# version 15
# baseline (speedup 1.0000x reference)
import contextlib
import numpy as np
import concourse.bass as bass
import concourse.mybir as mybir
from concourse.bass_utils import run_bass_kernel_spmd

F32 = mybir.dt.float32
BF16 = mybir.dt.bfloat16
AF = mybir.ActivationFunctionType
ALU = mybir.AluOpType
AX = mybir.AxisListType

N_CORES = 8
D = 2048
B_SZ = 4
T_SEQ = 2048
TOK_GLOBAL = B_SZ * T_SEQ
TOK_CORE = TOK_GLOBAL // N_CORES
EPS = 1e-5
SOFTMAX_SHIFT = 12.0


class Sem:
    def __init__(self, name):
        self.name = name
        self.count = 0
        self.handle = None

class Op:
    __slots__ = ("engine", "fn", "waits", "incs")
    def __init__(self, engine, fn, waits, incs):
        self.engine = engine
        self.fn = fn
        self.waits = waits
        self.incs = incs

class Sched:
    ENGINES = ("sync", "vector", "scalar", "tensor", "gpsimd")

    def __init__(self):
        self.ops = []
        self.sems = []
        self._pending = {}

    def snapshot(self):
        return [(s, s.count) for s in self.sems if s.count > 0]

    def set_pending(self, evs):
        self._pending = {e: list(evs) for e in self.ENGINES}

    def barrier(self, exclude=()):
        evs = [(s, s.count) for s in self.sems if s.count > 0 and s not in exclude]
        self._pending = {e: list(evs) for e in self.ENGINES}

    def sem(self, name):
        s = Sem(name)
        self.sems.append(s)
        return s

    def add(self, engine, fn, deps=(), inc=None, inc_n=1):
        waits = []
        pj = self._pending.pop(engine, None)
        if pj:
            waits.extend(pj)
        for d in deps:
            if d is None:
                continue
            if isinstance(d, Sem):
                waits.append((d, d.count))
            else:
                waits.append(d)
        ev = None
        incs = []
        if inc is not None:
            inc.count += inc_n
            incs.append((inc, inc_n))
            ev = (inc, inc.count)
        self.ops.append(Op(engine, fn, waits, incs))
        return ev

    def emit(self, nc):
        with contextlib.ExitStack() as stack:
            for s in self.sems:
                s.handle = stack.enter_context(nc.semaphore(s.name))
            block = stack.enter_context(nc.Block())
            by_engine = {e: [op for op in self.ops if op.engine == e] for e in self.ENGINES}

            def mk(engine_ops):
                def body(eng):
                    for op in engine_ops:
                        wmax = {}
                        for s, v in op.waits:
                            wmax[s] = max(wmax.get(s, 0), v)
                        for s, v in wmax.items():
                            if v > 0:
                                eng.wait_ge(s.handle, v)
                        ins = op.fn(eng)
                        for s, n in op.incs:
                            assert ins is not None, f"op with inc on {s.name} returned no instruction"
                            ins = ins.then_inc(s.handle, n)
                return body

            for e in self.ENGINES:
                if by_engine[e]:
                    getattr(block, e)(mk(by_engine[e]))



def emit_layernorm(sch, y, out, stats, eps_tile, n_feat, deps, v_sem, s_sem,
                   out_free_dep=None):
    deps = [d for d in deps if d is not None]
    e1 = sch.add("vector", lambda v: v.reduce_sum(stats["sum1"][:], y[:], axis=AX.X),
                 deps=deps, inc=v_sem)
    e2 = sch.add("scalar", lambda s: s.activation(stats["sq_scratch"][:, 0:n_feat], y[:], AF.Square,
                                                 accum_out=stats["sumsq"][:]),
                 deps=deps, inc=s_sem)
    e3a = sch.add("vector", lambda v: v.tensor_scalar(stats["mean"][:], stats["sum1"][:], 1.0 / n_feat, None, ALU.mult),
                  deps=[e1], inc=v_sem)
    e3b = sch.add("vector", lambda v: v.tensor_scalar(stats["var"][:], stats["sumsq"][:], 1.0 / n_feat, None, ALU.mult),
                  deps=[e2], inc=v_sem)
    e3 = sch.add("vector", lambda v: v.tensor_tensor(stats["m2"][:], stats["mean"][:], stats["mean"][:], ALU.mult),
                 deps=[e3a], inc=v_sem)
    e4 = sch.add("vector", lambda v: v.tensor_tensor(stats["var"][:], stats["var"][:], stats["m2"][:], ALU.subtract),
                 deps=[e3, e3b], inc=v_sem)
    e5 = sch.add("scalar", lambda s: s.activation(stats["std"][:], stats["var"][:], AF.Sqrt, bias=eps_tile[:]),
                 deps=[e4], inc=s_sem)
    e6 = sch.add("vector", lambda v: v.reciprocal(stats["rstd"][:], stats["std"][:]),
                 deps=[e5], inc=v_sem)
    e7 = sch.add("vector", lambda v: v.tensor_tensor(stats["nmr"][:], stats["mean"][:], stats["rstd"][:], ALU.mult),
                 deps=[e6], inc=v_sem)
    e8 = sch.add("vector", lambda v: v.tensor_scalar(stats["nmr"][:], stats["nmr"][:], -1.0, None, ALU.mult),
                 deps=[e7], inc=v_sem)
    e9 = sch.add("vector", lambda v: v.tensor_scalar(out[:], y[:], stats["rstd"][:], stats["nmr"][:],
                                                     ALU.mult, ALU.add),
                 deps=[e8, e6, out_free_dep] + deps, inc=v_sem)
    return e9


def alloc_ln_stats(stack, nc, tag, width=2048):
    t = {}
    for nm in ("sum1", "sumsq", "mean", "var", "m2", "std", "rstd", "nmr"):
        t[nm] = stack.enter_context(nc.sbuf_tensor(f"{tag}_{nm}", [128, 1], F32))
    t["sq_scratch"] = stack.enter_context(nc.sbuf_tensor(f"{tag}_sq", [128, width], F32))
    return t



def build_pmm_program(kch, a_is_bf16=False, with_partial=False, finalize=True,
                      with_colscale=False, scale64=False):
    nc = bass.Bass("TRN2", target_bir_lowering=False, debug=False)
    K = kch * 128
    a_fm = nc.dram_tensor("a_fm", [K, TOK_CORE], BF16 if a_is_bf16 else F32,
                          kind="ExternalInput").ap()
    w = nc.dram_tensor("w", [K, D], F32, kind="ExternalInput").ap()
    if with_partial:
        p_in = nc.dram_tensor("p_in", [TOK_CORE, D], F32, kind="ExternalInput").ap()
    if with_colscale:
        colscale_b = nc.dram_tensor("colscale_b", [128, D], F32, kind="ExternalInput").ap()
    if finalize:
        resid = nc.dram_tensor("resid", [TOK_CORE, D], F32, kind="ExternalInput").ap()
    out = nc.dram_tensor("out", [TOK_CORE, D], F32, kind="ExternalOutput").ap()

    TT = TOK_CORE // 128
    OC = D // 512

    sch = Sched()
    a_ld = [sch.sem(f"a_ld{i}") for i in range(2)]
    w_ld = [sch.sem(f"w_ld{i}") for i in range(2)]
    r_ld = [sch.sem(f"r_ld{i}") for i in range(2)]
    p_ld = [sch.sem(f"p_ld{i}") for i in range(2)]
    c_ld = sch.sem("c_ld")
    cast = sch.sem("cast")
    mm = sch.sem("mm")
    ev = sch.sem("ev")
    vs = sch.sem("vs")
    ss = sch.sem("ss")
    st_s = [sch.sem(f"st{i}") for i in range(2)]

    NB = 1 if kch > 16 else 2
    with contextlib.ExitStack() as stack:
        sb = lambda name, shape, dt=F32: stack.enter_context(nc.sbuf_tensor(name, shape, dt))
        a_bf = [sb(f"a_bf_{k}", [128, TOK_CORE], BF16) for k in range(kch)]
        w_bf = [sb(f"w_bf_{k}", [128, D], BF16) for k in range(kch)]
        a_f32 = None if a_is_bf16 else [sb(f"a_f32_{i}", [128, TOK_CORE]) for i in range(NB)]
        w_f32 = [sb(f"w_f32_{i}", [128, D]) for i in range(NB)]
        r_f32 = [sb(f"r_f32_{i}", [128, D]) for i in range(NB)] if finalize else None
        p_f32 = [sb(f"p_f32_{i}", [128, D]) for i in range(NB)] if with_partial else None
        cs_t = sb("cs_t", [128, D]) if with_colscale else None
        y = [sb(f"y_{i}", [128, D]) for i in range(NB)]
        o_t = [sb(f"o_{i}", [128, D]) for i in range(NB)]
        eps_t = sb("eps", [128, 1])
        stats = alloc_ln_stats(stack, nc, "ln") if finalize else None
        ps = [stack.enter_context(nc.psum_tensor(f"ps_{i}", [128, 512], F32)) for i in range(4)]

        ev_eps = sch.add("vector", lambda v: v.memset(eps_t[:], EPS), inc=vs)
        cs_ev = None
        if with_colscale:
            e_cl = sch.add("sync", lambda s: s.dma_start(out=cs_t[:], in_=colscale_b), inc=c_ld, inc_n=16)
            if scale64:
                cs_ev = sch.add("vector", lambda v: v.tensor_scalar(cs_t[:], cs_t[:], 64.0, None, ALU.mult),
                                deps=[e_cl], inc=vs)
            else:
                cs_ev = e_cl

        cast_evs = []
        for k in range(kch):
            buf = k % NB
            dep_a = [cast_evs[k - NB][0]] if k >= NB else []
            dep_w = [cast_evs[k - NB][1]] if k >= NB else []
            if a_is_bf16:
                e_ca = sch.add("sync", (lambda kk: lambda s: s.dma_start(
                    out=a_bf[kk][:], in_=a_fm[kk * 128:(kk + 1) * 128, :]))(k),
                    deps=dep_a, inc=a_ld[buf], inc_n=16)
            else:
                e_la = sch.add("sync", (lambda kk, bb: lambda s: s.dma_start(
                    out=a_f32[bb][:], in_=a_fm[kk * 128:(kk + 1) * 128, :]))(k, buf),
                    deps=dep_a, inc=a_ld[buf], inc_n=16)
                e_ca = sch.add("vector", (lambda kk, bb: lambda v: v.tensor_copy(a_bf[kk][:], a_f32[bb][:]))(k, buf),
                               deps=[e_la], inc=cast)
            e_lw = sch.add("sync", (lambda kk, bb: lambda s: s.dma_start(
                out=w_f32[bb][:], in_=w[kk * 128:(kk + 1) * 128, :]))(k, buf),
                deps=dep_w, inc=w_ld[buf], inc_n=16)
            e_cw = sch.add("vector", (lambda kk, bb: lambda v: v.tensor_copy(w_bf[kk][:], w_f32[bb][:]))(k, buf),
                           deps=[e_lw], inc=cast)
            cast_evs.append((e_ca, e_cw))
        loads_done = [cast_evs[-1][0], cast_evs[-1][1], (cast, cast.count),
                      (a_ld[0], a_ld[0].count), (a_ld[1], a_ld[1].count)]

        prev_ev = {}
        prev_r = {}
        prev_p = {}
        prev_out = {}
        prev_ln_out = None
        for t in range(TT):
            buf = t % NB
            e_lr = None
            e_lp = None
            if finalize:
                e_lr = sch.add("sync", (lambda tt, bb: lambda s: s.dma_start(
                    out=r_f32[bb][:], in_=resid[tt * 128:(tt + 1) * 128, :]))(t, buf),
                    deps=[prev_r.get(buf)], inc=r_ld[buf], inc_n=16)
            if with_partial:
                e_lp = sch.add("sync", (lambda tt, bb: lambda s: s.dma_start(
                    out=p_f32[bb][:], in_=p_in[tt * 128:(tt + 1) * 128, :]))(t, buf),
                    deps=[prev_p.get(buf)], inc=p_ld[buf], inc_n=16)
            mm_evs = []
            for o in range(OC):
                def mmgroup(tt, oo):
                    def f(te):
                        ins = None
                        for k in range(kch):
                            ins = te.matmul(ps[oo][:],
                                            a_bf[k][:, tt * 128:(tt + 1) * 128],
                                            w_bf[k][:, oo * 512:(oo + 1) * 512],
                                            start=(k == 0), stop=(k == kch - 1))
                        return ins
                    return f
                e_mm = sch.add("tensor", mmgroup(t, o),
                               deps=loads_done + [prev_ev.get(o)], inc=mm)
                mm_evs.append(e_mm)
            ev_evs = []
            for o in range(OC):
                sl = slice(o * 512, (o + 1) * 512)
                if with_partial:
                    e_a = sch.add("vector", (lambda oo, bb, sl_: lambda v: v.tensor_tensor(
                        y[bb][:, sl_], ps[oo][:], p_f32[bb][:, sl_], ALU.add))(o, buf, sl),
                        deps=[mm_evs[o], e_lp, prev_ln_out if finalize else prev_out.get(buf)], inc=ev)
                else:
                    e_a = sch.add("vector", (lambda oo, bb, sl_: lambda v: v.tensor_copy(
                        y[bb][:, sl_], ps[oo][:]))(o, buf, sl),
                        deps=[mm_evs[o], prev_ln_out if finalize else prev_out.get(buf)], inc=ev)
                step = e_a
                if finalize:
                    if with_colscale:
                        step = sch.add("vector", (lambda oo, bb, sl_: lambda v: v.tensor_tensor(
                            y[bb][:, sl_], y[bb][:, sl_], cs_t[:, sl_], ALU.mult))(o, buf, sl),
                            deps=[step, cs_ev], inc=ev)
                    step = sch.add("vector", (lambda oo, bb, sl_: lambda v: v.tensor_tensor(
                        y[bb][:, sl_], y[bb][:, sl_], r_f32[bb][:, sl_], ALU.add))(o, buf, sl),
                        deps=[step, e_lr], inc=ev)
                prev_ev[o] = e_a
                ev_evs.append(step)
            if finalize:
                e_out = emit_layernorm(sch, y[buf], o_t[buf], stats, eps_t, D,
                                       deps=ev_evs + [ev_eps], v_sem=vs, s_sem=ss,
                                       out_free_dep=prev_out.get(buf))
                prev_ln_out = e_out
                src_tile = o_t[buf]
            else:
                e_out = ev_evs[-1]
                src_tile = y[buf]
            e_st = sch.add("gpsimd", (lambda tt, s_, ev_: lambda g: g.dma_start(
                out=out[tt * 128:(tt + 1) * 128, :], in_=s_[:]))(t, src_tile, e_out),
                deps=ev_evs + [e_out], inc=st_s[buf], inc_n=16)
            prev_r[buf] = ev_evs[-1]
            prev_p[buf] = ev_evs[-1]
            prev_out[buf] = e_st

        sch.add("gpsimd", lambda g: None, deps=[(s, s.count) for s in st_s])
        sch.emit(nc)
    return nc



INTER0 = 5632
INTER1 = 8192
FWHT_SCALE_2048 = 1.0 / (64.0 * float(np.sqrt(2048.0)))
FWHT_SCALE_8192 = 1.0 / (64.0 * float(np.sqrt(8192.0)))


def build_ffn_up_program(layer):
    nc = bass.Bass("TRN2", target_bir_lowering=False, debug=False)
    NI = (INTER1 if layer == 1 else INTER0) // 128
    KC = D // 128
    a_dt = BF16 if layer == 1 else F32
    x_fm = nc.dram_tensor("x_fm", [D, TOK_CORE], a_dt, kind="ExternalInput").ap()
    gw = nc.dram_tensor("gw", [D, NI * 128], F32, kind="ExternalInput").ap()
    uw = nc.dram_tensor("uw", [D, NI * 128], F32, kind="ExternalInput").ap()
    gs = nc.dram_tensor("gs", [NI * 128, 1], F32, kind="ExternalInput").ap()
    us = nc.dram_tensor("us", [NI * 128, 1], F32, kind="ExternalInput").ap()
    if layer == 1:
        su = nc.dram_tensor("su", [NI * 128, 1], F32, kind="ExternalInput").ap()
        h128 = nc.dram_tensor("h128", [128, 128], F32, kind="ExternalInput").ap()
    h_out = nc.dram_tensor("h_out", [NI * 128, TOK_CORE], BF16, kind="ExternalOutput").ap()

    sch = Sched()
    x_ld = [sch.sem(f"x_ld{i}") for i in range(2)]
    gw_ld = [sch.sem(f"gw_ld{i}") for i in range(2)]
    uw_ld = [sch.sem(f"uw_ld{i}") for i in range(2)]
    sc_ld = [sch.sem(f"sc_ld{i}") for i in range(2)]
    c_ld = sch.sem("c_ld")
    cast = sch.sem("cast")
    mm = sch.sem("mm")
    vs = sch.sem("vs")
    ss = sch.sem("ss")
    st_s = [sch.sem(f"st{i}") for i in range(4)]

    with contextlib.ExitStack() as stack:
        sb = lambda name, shape, dt=F32: stack.enter_context(nc.sbuf_tensor(name, shape, dt))
        x_stage = None if layer == 1 else [sb(f"x_stage{i}", [128, TOK_CORE]) for i in range(2)]
        x_bf = [sb(f"x_bf{k}", [128, TOK_CORE], BF16) for k in range(KC)]
        w_stage = [sb("w_stage0", [128, 2048])]
        gw_bf = [sb("gw_bf0", [128, 2048], BF16)]
        uw_bf = [sb("uw_bf0", [128, 2048], BF16)]
        scs = [sb(f"scs{i}", [128, 4]) for i in range(2)]
        sg_t = [sb(f"sg{i}", [128, 512]) for i in range(2)]
        if layer == 1:
            hbuf = [sb(f"hbuf{c}", [128, TOK_CORE], BF16) for c in range(NI)]
            btmp = sb("btmp", [128, TOK_CORE], BF16)
            h128_f = sb("h128_f", [128, 128])
            h128_b = sb("h128_b", [128, 128], BF16)
            ho_t = [sb(f"ho{i}", [128, 512], BF16) for i in range(4)]
        else:
            ho_t = [sb(f"ho{i}", [128, TOK_CORE], BF16) for i in range(4)]
        ps_g = [stack.enter_context(nc.psum_tensor(f"psg{i}", [128, 512], F32)) for i in range(2)]
        ps_u = [stack.enter_context(nc.psum_tensor(f"psu{i}", [128, 512], F32)) for i in range(2)]

        h_evs = []
        if layer == 1:
            e_h = sch.add("sync", lambda s: s.dma_start(out=h128_f[:], in_=h128), inc=c_ld, inc_n=16)
            e_hc = sch.add("vector", lambda v: v.tensor_copy(h128_b[:], h128_f[:]), deps=[e_h], inc=cast)
            h_evs = [e_hc]

        x_cast_evs = []
        for k in range(KC):
            buf = k % 2
            if layer == 1:
                e_cx = sch.add("sync", (lambda kk: lambda s: s.dma_start(
                    out=x_bf[kk][:], in_=x_fm[kk * 128:(kk + 1) * 128, :]))(k),
                    deps=[x_cast_evs[k - 2]] if k >= 2 else [], inc=x_ld[buf], inc_n=16)
            else:
                e_lx = sch.add("sync", (lambda kk, bb: lambda s: s.dma_start(
                    out=x_stage[bb][:], in_=x_fm[kk * 128:(kk + 1) * 128, :]))(k, buf),
                    deps=[x_cast_evs[k - 2]] if k >= 2 else [], inc=x_ld[buf], inc_n=16)
                e_cx = sch.add("vector", (lambda kk, bb: lambda v: v.tensor_copy(x_bf[kk][:], x_stage[bb][:]))(k, buf),
                               deps=[e_lx], inc=cast)
            x_cast_evs.append(e_cx)
        x_done = [(cast, cast.count), (x_ld[0], x_ld[0].count), (x_ld[1], x_ld[1].count)]

        st_prev = [None] * 4
        prev_wcast = {}
        prev_wuse = {}
        prev_scuse = {}
        prev_sg = {}
        prev_psg = {}
        prev_psu = {}
        prev_hbuf = {}
        hbuf_evs = []

        for i in range(NI):
            wb = 0
            e_lg = sch.add("sync", (lambda i_, bb: lambda s: s.dma_start(
                out=w_stage[bb][:].rearrange("p (c f) -> p c f", c=KC),
                in_=gw[:, i_ * 128:(i_ + 1) * 128].rearrange("(c p) f -> p c f", p=128)))(i, wb),
                deps=[prev_wcast.get("stage")], inc=gw_ld[0], inc_n=16)
            e_cg = sch.add("vector", (lambda bb: lambda v: v.tensor_copy(gw_bf[bb][:], w_stage[bb][:]))(wb),
                           deps=[e_lg, prev_wuse.get("g")], inc=cast)
            e_lu = sch.add("scalar", (lambda i_, bb: lambda s: s.dma_start(
                out=w_stage[bb][:].rearrange("p (c f) -> p c f", c=KC),
                in_=uw[:, i_ * 128:(i_ + 1) * 128].rearrange("(c p) f -> p c f", p=128)))(i, wb),
                deps=[e_cg], inc=uw_ld[0], inc_n=16)
            e_cu = sch.add("vector", (lambda bb: lambda v: v.tensor_copy(uw_bf[bb][:], w_stage[bb][:]))(wb),
                           deps=[e_lu, prev_wuse.get("u")], inc=cast)
            prev_wcast["stage"] = e_cu
            wb2 = i % 2
            e_ls = sch.add("sync", (lambda i_, bb: lambda s: s.dma_start(
                out=scs[bb][:, 0:1], in_=gs[i_ * 128:(i_ + 1) * 128, :]))(i, wb2),
                deps=[prev_scuse.get(wb2)], inc=sc_ld[wb2], inc_n=16)
            e_ls2 = sch.add("sync", (lambda i_, bb: lambda s: s.dma_start(
                out=scs[bb][:, 1:2], in_=us[i_ * 128:(i_ + 1) * 128, :]))(i, wb2),
                deps=[e_ls], inc=sc_ld[wb2], inc_n=16)
            if layer == 1:
                e_ls3 = sch.add("sync", (lambda i_, bb: lambda s: s.dma_start(
                    out=scs[bb][:, 2:3], in_=su[i_ * 128:(i_ + 1) * 128, :]))(i, wb2),
                    deps=[e_ls2], inc=sc_ld[wb2], inc_n=16)
                e_sc1 = sch.add("vector", (lambda bb: lambda v: v.tensor_scalar(
                    scs[bb][:, 3:4], scs[bb][:, 1:2], 64.0, None, ALU.mult))(wb2),
                    deps=[e_ls3], inc=vs)
                e_sc2 = sch.add("vector", (lambda bb: lambda v: v.tensor_tensor(
                    scs[bb][:, 3:4], scs[bb][:, 3:4], scs[bb][:, 2:3], ALU.mult))(wb2),
                    deps=[e_sc1], inc=vs)
                e_sc3 = sch.add("vector", (lambda bb: lambda v: v.tensor_scalar(
                    scs[bb][:, 0:1], scs[bb][:, 0:1], 64.0, None, ALU.mult))(wb2),
                    deps=[e_sc2], inc=vs)
                e_scales = e_sc3
                comb_col = 3
            else:
                e_scales = e_ls2
                comb_col = 1
            for th in range(2):
                tsl = slice(th * 512, (th + 1) * 512)
                pp = th
                def gmm(i_, th_, bb, pp_, which):
                    wsrc = gw_bf if which == "g" else uw_bf
                    pdst = ps_g if which == "g" else ps_u
                    def f(te):
                        ins = None
                        for kc in range(KC):
                            ins = te.matmul(pdst[pp_][:],
                                            wsrc[bb][:, kc * 128:(kc + 1) * 128],
                                            x_bf[kc][:, th_ * 512:(th_ + 1) * 512],
                                            start=(kc == 0), stop=(kc == KC - 1))
                        return ins
                    return f
                e_gm = sch.add("tensor", gmm(i, th, wb, pp, "g"),
                               deps=x_done + [e_cg, prev_psg.get(pp)], inc=mm)
                e_um = sch.add("tensor", gmm(i, th, wb, pp, "u"),
                               deps=x_done + [e_cu, prev_psu.get(pp)], inc=mm)
                e_sg = sch.add("scalar", (lambda bb, pp_, sgb: lambda s: s.activation(
                    sg_t[sgb][:], ps_g[pp_][:], AF.Silu, scale=scs[bb][:, 0:1]))(wb2, pp, th),
                    deps=[e_gm, e_scales, prev_sg.get(th)], inc=ss)
                prev_psg[pp] = e_sg
                if layer == 1:
                    dst = hbuf[i]
                    ddep = prev_hbuf.get(i)
                else:
                    dst = ho_t[2 * (i % 2) + th]
                    ddep = st_prev[2 * (i % 2) + th]
                e_h2 = sch.add("vector", (lambda bb, pp_, sgb, d_, t2: lambda v: v.scalar_tensor_tensor(
                    d_[:, t2] if layer == 1 else d_[:, 0:512],
                    ps_u[pp_][:], scs[bb][:, comb_col:comb_col + 1], sg_t[sgb][:],
                    ALU.mult, ALU.mult))(wb2, pp, th, dst, tsl),
                    deps=[e_um, e_sg, e_scales, ddep], inc=vs)
                prev_psu[pp] = e_h2
                prev_sg[th] = e_h2
                if layer == 1:
                    prev_hbuf[i] = e_h2
                    hbuf_evs.append(e_h2)
                else:
                    pass
                prev_wuse["g"] = e_sg
                prev_wuse["u"] = e_h2
                prev_scuse[wb2] = e_h2
                if layer == 0:
                    sl4 = 2 * (i % 2) + th
                    e_st = sch.add("gpsimd", (lambda i_, th_, d_: lambda g: g.dma_start(
                        out=h_out[i_ * 128:(i_ + 1) * 128, th_ * 512:(th_ + 1) * 512],
                        in_=d_[:, 0:512]))(i, th, dst),
                        deps=[e_h2, st_prev[sl4]], inc=st_s[sl4], inc_n=16)
                    st_prev[sl4] = e_st

        if layer == 1:
            tile_ev = {c: hbuf_evs[2 * c + 1] for c in range(NI)}
            prev_btmp = None
            hstep = 1
            while hstep < NI:
                for c in range(NI):
                    if (c // hstep) % 2 == 0:
                        a, b = c, c + hstep
                        e1 = sch.add("vector", (lambda a_, b_: lambda v: v.tensor_tensor(
                            btmp[:], hbuf[a_][:], hbuf[b_][:], ALU.subtract))(a, b),
                            deps=[tile_ev[a], tile_ev[b], prev_btmp], inc=vs)
                        e2 = sch.add("vector", (lambda a_, b_: lambda v: v.tensor_tensor(
                            hbuf[a_][:], hbuf[a_][:], hbuf[b_][:], ALU.add))(a, b),
                            deps=[e1], inc=vs)
                        e3 = sch.add("vector", (lambda b_: lambda v: v.tensor_copy(hbuf[b_][:], btmp[:]))(b),
                            deps=[e2], inc=vs)
                        tile_ev[a] = e2
                        tile_ev[b] = e3
                        prev_btmp = e3
                hstep *= 2
            prev_ps = {}
            for c in range(NI):
                for th in range(2):
                    pp = th
                    e_m = sch.add("tensor", (lambda c_, th_, pp_: lambda te: te.matmul(
                        ps_g[pp_][:], h128_b[:], hbuf[c_][:, th_ * 512:(th_ + 1) * 512],
                        start=True, stop=True))(c, th, pp),
                        deps=[tile_ev[c]] + h_evs + [prev_ps.get(pp)], inc=mm)
                    sl4 = 2 * (c % 2) + th
                    e_e = sch.add("scalar", (lambda pp_, sl_: lambda s: s.activation(
                        ho_t[sl_][:], ps_g[pp_][:], AF.Copy, scale=FWHT_SCALE_8192))(pp, sl4),
                        deps=[e_m, st_prev[sl4]], inc=ss)
                    prev_ps[pp] = e_e
                    e_st = sch.add("gpsimd", (lambda c_, th_, sl_: lambda g: g.dma_start(
                        out=h_out[c_ * 128:(c_ + 1) * 128, th_ * 512:(th_ + 1) * 512],
                        in_=ho_t[sl_][:]))(c, th, sl4),
                        deps=[e_e], inc=st_s[sl4], inc_n=16)
                    st_prev[sl4] = e_st

        sch.add("gpsimd", lambda g: None, deps=[(s, s.count) for s in st_s])
        sch.emit(nc)
    return nc



def build_fwht_program():
    nc = bass.Bass("TRN2", target_bir_lowering=False, debug=False)
    KC = D // 128
    x_fm = nc.dram_tensor("x_fm", [D, TOK_CORE], F32, kind="ExternalInput").ap()
    su = nc.dram_tensor("su", [D, 1], F32, kind="ExternalInput").ap()
    h128 = nc.dram_tensor("h128", [128, 128], F32, kind="ExternalInput").ap()
    t_out = nc.dram_tensor("t_out", [D, TOK_CORE], BF16, kind="ExternalOutput").ap()

    sch = Sched()
    x_ld = [sch.sem(f"x_ld{i}") for i in range(2)]
    su_ld = sch.sem("su_ld")
    c_ld = sch.sem("c_ld")
    cast = sch.sem("cast")
    mm = sch.sem("mm")
    vs = sch.sem("vs")
    ss = sch.sem("ss")
    st_s = [sch.sem(f"st{i}") for i in range(2)]

    with contextlib.ExitStack() as stack:
        sb = lambda name, shape, dt=F32: stack.enter_context(nc.sbuf_tensor(name, shape, dt))
        x_stage = [sb(f"x_stage{i}", [128, TOK_CORE]) for i in range(2)]
        xs_bf = [sb(f"xs_bf{k}", [128, TOK_CORE], BF16) for k in range(KC)]
        t1 = [sb(f"t1_{k}", [128, TOK_CORE]) for k in range(KC)]
        btmp = sb("btmp", [128, TOK_CORE])
        su_t = sb("su_t", [128, KC])
        h128_f = sb("h128_f", [128, 128])
        h128_b = sb("h128_b", [128, 128], BF16)
        to_t = [sb(f"to{i}", [128, TOK_CORE], BF16) for i in range(2)]
        ps2 = [stack.enter_context(nc.psum_tensor(f"ps_{i}", [128, 512], F32)) for i in range(2)]

        e_h = sch.add("sync", lambda s: s.dma_start(out=h128_f[:], in_=h128), inc=c_ld, inc_n=16)
        e_hc = sch.add("vector", lambda v: v.tensor_copy(h128_b[:], h128_f[:]), deps=[e_h], inc=cast)
        for k in range(KC):
            sch.add("sync", (lambda kk: lambda s: s.dma_start(
                out=su_t[:, kk:kk + 1], in_=su[kk * 128:(kk + 1) * 128, :]))(k), inc=su_ld, inc_n=16)
        e_su = (su_ld, su_ld.count)

        xev = []
        for k in range(KC):
            buf = k % 2
            e_lx = sch.add("sync", (lambda kk, bb: lambda s: s.dma_start(
                out=x_stage[bb][:], in_=x_fm[kk * 128:(kk + 1) * 128, :]))(k, buf),
                deps=[xev[k - 2][0]] if k >= 2 else [], inc=x_ld[buf], inc_n=16)
            e_ps = sch.add("vector", (lambda kk, bb: lambda v: v.tensor_scalar(
                x_stage[bb][:], x_stage[bb][:], su_t[:, kk:kk + 1], None, ALU.mult))(k, buf),
                deps=[e_lx, e_su], inc=vs)
            e_cx = sch.add("vector", (lambda kk, bb: lambda v: v.tensor_copy(xs_bf[kk][:], x_stage[bb][:]))(k, buf),
                           deps=[e_ps], inc=cast)
            xev.append((e_cx,))
        t1_ev = {}
        prev_ps = {}
        for k in range(KC):
            for th in range(2):
                pp = th
                e_m = sch.add("tensor", (lambda k_, th_, pp_: lambda te: te.matmul(
                    ps2[pp_][:], h128_b[:], xs_bf[k_][:, th_ * 512:(th_ + 1) * 512],
                    start=True, stop=True))(k, th, pp),
                    deps=[xev[k][0], e_hc, prev_ps.get(pp)], inc=mm)
                e_e = sch.add("scalar", (lambda k_, th_, pp_: lambda s: s.activation(
                    t1[k_][:, th_ * 512:(th_ + 1) * 512], ps2[pp_][:], AF.Copy,
                    scale=FWHT_SCALE_2048))(k, th, pp),
                    deps=[e_m], inc=ss)
                prev_ps[pp] = e_e
                t1_ev[(k, th)] = e_e
        tile_ev = {k: t1_ev[(k, 1)] for k in range(KC)}
        for k in range(KC):
            tile_ev[k] = t1_ev[(k, 1)]
        full_ev = {k: [t1_ev[(k, 0)], t1_ev[(k, 1)]] for k in range(KC)}
        tile_dep = {k: full_ev[k] for k in range(KC)}
        cur = {k: tile_dep[k] for k in range(KC)}
        prev_btmp = None
        hstep = 1
        while hstep < KC:
            for c in range(KC):
                if (c // hstep) % 2 == 0:
                    a, b = c, c + hstep
                    e1 = sch.add("vector", (lambda a_, b_: lambda v: v.tensor_tensor(
                        btmp[:], t1[a_][:], t1[b_][:], ALU.subtract))(a, b),
                        deps=cur[a] + cur[b] + [prev_btmp], inc=vs)
                    e2 = sch.add("vector", (lambda a_, b_: lambda v: v.tensor_tensor(
                        t1[a_][:], t1[a_][:], t1[b_][:], ALU.add))(a, b),
                        deps=[e1], inc=vs)
                    e3 = sch.add("vector", (lambda b_: lambda v: v.tensor_copy(t1[b_][:], btmp[:]))(b),
                        deps=[e2], inc=vs)
                    cur[a] = [e2]
                    cur[b] = [e3]
                    prev_btmp = e3
            hstep *= 2
        st_prev = [None, None]
        for k in range(KC):
            sl2 = k % 2
            e_c = sch.add("vector", (lambda k_, sl_: lambda v: v.tensor_copy(to_t[sl_][:], t1[k_][:]))(k, sl2),
                          deps=cur[k] + [st_prev[sl2]], inc=vs)
            e_st = sch.add("gpsimd", (lambda k_, sl_: lambda g: g.dma_start(
                out=t_out[k_ * 128:(k_ + 1) * 128, :], in_=to_t[sl_][:]))(k, sl2),
                deps=[e_c], inc=st_s[sl2], inc_n=16)
            st_prev[sl2] = e_st

        sch.add("gpsimd", lambda g: None, deps=[(s, s.count) for s in st_s])
        sch.emit(nc)
    return nc




HEAD_DIM = 128
LAMBDA_INIT = 0.8 - 0.6 * float(np.exp(-0.3 * 12))
SM_SCALE = 1.0 / float(np.sqrt(128.0))


def build_attn_program(layer):
    nc = bass.Bass("TRN2", target_bir_lowering=False, debug=False)
    xfm = nc.dram_tensor("xfm", [D, T_SEQ], F32, kind="ExternalInput").ap()
    wq = nc.dram_tensor("wq", [D, 1024], F32, kind="ExternalInput").ap()
    wk = nc.dram_tensor("wk", [D, 1024], F32, kind="ExternalInput").ap()
    wv = nc.dram_tensor("wv", [D, 1024], F32, kind="ExternalInput").ap()
    masks = nc.dram_tensor("masks", [4, 128, 512], F32, kind="ExternalInput").ap()
    ident = nc.dram_tensor("ident", [128, 128], F32, kind="ExternalInput").ap()
    if layer == 1:
        rc = nc.dram_tensor("rc", [128, T_SEQ], F32, kind="ExternalInput").ap()
        rs = nc.dram_tensor("rs", [128, T_SEQ], F32, kind="ExternalInput").ap()
    else:
        lamv = nc.dram_tensor("lamv", [1, 512], F32, kind="ExternalInput").ap()
        subln_b = nc.dram_tensor("subln_b", [128, 256], F32, kind="ExternalInput").ap()
    a_out = nc.dram_tensor("a_out", [T_SEQ, 1024], F32, kind="ExternalOutput").ap()

    NH = 8 if layer == 1 else 4
    NBR = 1 if layer == 1 else 2
    DV = 128 if layer == 1 else 256
    DVA = DV + 1
    FH = 1024 // NH
    KC = D // 128
    NT = T_SEQ // 128
    NQ5 = T_SEQ // 512

    sch = Sched()
    x_ld = [sch.sem(f"x_ld{i}") for i in range(2)]
    w_lds = {nm: sch.sem(f"{nm}_ld") for nm in ("wq", "wk", "wv")}
    c_ld = sch.sem("c_ld")
    cast = sch.sem("cast")
    mm = sch.sem("mm")
    vs = sch.sem("vs")
    ss = sch.sem("ss")
    st_s = [sch.sem(f"st{i}") for i in range(4)]

    with contextlib.ExitStack() as stack:
        sb = lambda name, shape, dt=F32: stack.enter_context(nc.sbuf_tensor(name, shape, dt))
        x_stage = [sb(f"x_stage{i}", [128, T_SEQ]) for i in range(2)]
        x_bf = [sb(f"x_bf{k}", [128, T_SEQ], BF16) for k in range(KC)]
        w_stage = sb("w_stage", [128, 8 * FH])
        w_bf = {nm: sb(f"{nm}_bf", [128, KC * FH], BF16) for nm in ("wq", "wk", "wv")}
        q_bf = [sb(f"q_bf{b}", [128, T_SEQ], BF16) for b in range(NBR)]
        k_bf = [sb(f"k_bf{b}", [128, T_SEQ], BF16) for b in range(NBR)]
        v_tm = sb("v_tm", [128, NT * DVA], BF16)
        pt = sb("pt", [128, NT * 512], BF16)
        mask_bf = sb("mask_bf", [128, 4 * 512], BF16)
        mask_f32 = sb("mask_f32", [128, 4 * 512])
        ident_bf = sb("ident_bf", [128, 128], BF16)
        ident_f32 = sb("ident_f32", [128, 128])
        shift_t = sb("shift_t", [128, 1])
        eps_t = sb("eps_t", [128, 1])
        rec = [sb(f"rec{b}", [128, 1]) for b in range(NBR)]
        a_f32 = [sb(f"a_f32{i}", [128, DV]) for i in range(2)]
        if layer == 1:
            cos_t = sb("cos_t", [128, T_SEQ])
            sin_t = sb("sin_t", [128, T_SEQ])
            rope_sc = [sb(f"rope{i}", [128, 512]) for i in range(3)]
        else:
            lam_row = sb("lam_row", [1, 512])
            lam_sc = sb("lam_sc", [1, 8])
            ones_row = sb("ones_row", [1, 128])
            lam_b = sb("lam_b", [128, 1])
            nlam_rec = sb("nlam_rec", [128, 1])
            subln_t = sb("subln_t", [128, 256])
            sq_s = sb("sq_s", [128, 256])
            o_sb0 = sb("o_sb0", [128, 4 * 256])
            o1_sc = sb("o1_sc", [128, 256])
            cmb = sb("cmb", [128, 256])
            rms = {nm: sb(f"rms_{nm}", [128, 1]) for nm in ("sumsq", "ms", "std", "rstd")}
        ps_proj = [stack.enter_context(nc.psum_tensor(f"psp{i}", [128, 512], F32)) for i in range(2)]
        ps_sc = [stack.enter_context(nc.psum_tensor(f"pss{i}", [128, 512], F32)) for i in range(2)]
        ps_pv = [stack.enter_context(nc.psum_tensor(f"pspv{i}", [128, DVA], F32)) for i in range(2)]
        if layer == 0:
            ps_lam = stack.enter_context(nc.psum_tensor("ps_lam", [128, 1], F32))

        for r in range(4):
            sch.add("sync", (lambda r_: lambda s: s.dma_start(
                out=mask_f32[:, r_ * 512:(r_ + 1) * 512], in_=masks[r_]))(r), inc=c_ld, inc_n=16)
        sch.add("sync", lambda s: s.dma_start(out=ident_f32[:], in_=ident), inc=c_ld, inc_n=16)
        if layer == 1:
            sch.add("sync", lambda s: s.dma_start(out=cos_t[:], in_=rc), inc=c_ld, inc_n=16)
            sch.add("sync", lambda s: s.dma_start(out=sin_t[:], in_=rs), inc=c_ld, inc_n=16)
        else:
            sch.add("sync", lambda s: s.dma_start(out=lam_row[:], in_=lamv), inc=c_ld, inc_n=16)
            sch.add("sync", lambda s: s.dma_start(out=subln_t[:], in_=subln_b), inc=c_ld, inc_n=16)
        c_done = (c_ld, c_ld.count)
        rope_evs = [c_done]
        subln_ev = c_done

        e_mc = sch.add("vector", lambda v: v.tensor_copy(mask_bf[:], mask_f32[:]), deps=[c_done], inc=cast)
        e_ic = sch.add("vector", lambda v: v.tensor_copy(ident_bf[:], ident_f32[:]), deps=[c_done], inc=cast)
        e_sh = sch.add("vector", lambda v: v.memset(shift_t[:], -SOFTMAX_SHIFT), inc=vs)
        e_eps = sch.add("vector", lambda v: v.memset(eps_t[:], EPS), inc=vs)
        lam_evs = []
        if layer == 0:
            e_p1 = sch.add("vector", lambda v: v.tensor_tensor(
                lam_row[0:1, 0:128], lam_row[0:1, 0:128], lam_row[0:1, 128:256], ALU.mult),
                deps=[c_done], inc=vs)
            e_p2 = sch.add("vector", lambda v: v.tensor_tensor(
                lam_row[0:1, 256:384], lam_row[0:1, 256:384], lam_row[0:1, 384:512], ALU.mult),
                deps=[c_done, e_p1], inc=vs)
            e_s1 = sch.add("vector", lambda v: v.reduce_sum(lam_sc[0:1, 0:1], lam_row[0:1, 0:128], axis=AX.X),
                           deps=[e_p1], inc=vs)
            e_s2 = sch.add("vector", lambda v: v.reduce_sum(lam_sc[0:1, 1:2], lam_row[0:1, 256:384], axis=AX.X),
                           deps=[e_p2, e_s1], inc=vs)
            e_e1 = sch.add("scalar", lambda s: s.activation(lam_sc[0:1, 2:4], lam_sc[0:1, 0:2], AF.Exp),
                           deps=[e_s1, e_s2], inc=ss)
            e_d = sch.add("vector", lambda v: v.tensor_tensor(
                lam_sc[0:1, 4:5], lam_sc[0:1, 2:3], lam_sc[0:1, 3:4], ALU.subtract),
                deps=[e_e1], inc=vs)
            e_l = sch.add("vector", lambda v: v.tensor_scalar(
                lam_sc[0:1, 4:5], lam_sc[0:1, 4:5], LAMBDA_INIT, None, ALU.add),
                deps=[e_d], inc=vs)
            e_or = sch.add("vector", lambda v: v.memset(ones_row[:], 1.0), inc=vs)
            e_lb = sch.add("tensor", lambda t: t.matmul(ps_lam[:], ones_row[:], lam_sc[0:1, 4:5],
                                                        start=True, stop=True),
                           deps=[e_l, e_or], inc=mm)
            e_lbc = sch.add("vector", lambda v: v.tensor_copy(lam_b[:], ps_lam[:]), deps=[e_lb], inc=vs)
            lam_evs = [e_lbc]

        x_cast_evs = []
        for k in range(KC):
            buf = k % 2
            e_lx = sch.add("sync", (lambda kk, bb: lambda s: s.dma_start(
                out=x_stage[bb][:], in_=xfm[kk * 128:(kk + 1) * 128, :]))(k, buf),
                deps=[x_cast_evs[k - 2]] if k >= 2 else [], inc=x_ld[buf], inc_n=16)
            e_cx = sch.add("vector", (lambda kk, bb: lambda v: v.tensor_copy(x_bf[kk][:], x_stage[bb][:]))(k, buf),
                           deps=[e_lx], inc=cast)
            x_cast_evs.append(e_cx)
        x_done = (cast, cast.count)

        st_prev = [None] * 4
        st_ctr = [0]
        def next_store_slot():
            i = st_ctr[0] % 4
            st_ctr[0] += 1
            return i

        prev_w_use = {nm: None for nm in ("wq", "wk", "wv")}
        prev_w_cast = {nm: None for nm in ("wq", "wk", "wv")}
        prev_head_end = [None]
        prev_pproj = {}
        prev_psc = {}
        prev_ppv = {}
        prev_af = {}
        prev_rope = [None]
        prev_rms = [None]
        prev_osb0 = {}
        prev_o1 = [None]
        prev_cmb = [None]
        prev_rec = {}
        prev_nlam = [None]

        for h in range(NH):
            w_evs = {}
            prev_stage_cast = prev_w_cast["wv"]
            for nm, wap in (("wq", wq), ("wk", wk), ("wv", wv)):
                e_cw = None
                for hf in range(2):
                    e_lw = sch.add("sync", (lambda w_, r0, r1, c0, c1: lambda s: s.dma_start(
                        out=w_stage[:].rearrange("p (c f) -> p c f", c=8),
                        in_=w_[r0:r1, c0:c1].rearrange("(c p) f -> p c f", p=128)))(
                            wap, hf * 1024, (hf + 1) * 1024, h * FH, (h + 1) * FH),
                        deps=[prev_stage_cast], inc=w_lds[nm], inc_n=16)
                    e_cw = sch.add("vector", (lambda nm_, hf_: lambda v: v.tensor_copy(
                        w_bf[nm_][:, hf_ * 8 * FH:(hf_ + 1) * 8 * FH], w_stage[:]))(nm, hf),
                        deps=[e_lw, prev_w_use[nm]], inc=cast)
                    prev_stage_cast = e_cw
                prev_w_cast[nm] = e_cw
                w_evs[nm] = e_cw

            qk_done = []
            pcnt = 0
            for nm, dest in (("wq", q_bf), ("wk", k_bf)):
                for br in range(NBR):
                    for tsl in range(4):
                        pp = pcnt % 2
                        pcnt += 1
                        def proj(nm_, br_, tsl_, pp_):
                            def f(te):
                                ins = None
                                for kc in range(KC):
                                    ins = te.matmul(ps_proj[pp_][:],
                                                    w_bf[nm_][:, kc * FH + br_ * 128: kc * FH + br_ * 128 + 128],
                                                    x_bf[kc][:, tsl_ * 512:(tsl_ + 1) * 512],
                                                    start=(kc == 0), stop=(kc == KC - 1))
                                return ins
                            return f
                        e_mm = sch.add("tensor", proj(nm, br, tsl, pp),
                                       deps=[w_evs[nm], x_done, prev_pproj.get(pp)], inc=mm)
                        if layer == 1:
                            e_r1 = sch.add("vector", (lambda pp_: lambda v: v.tensor_scalar(
                                rope_sc[0][0:64, :], ps_proj[pp_][64:128, :], -1.0, None, ALU.mult))(pp),
                                deps=[e_mm, prev_rope[0]], inc=vs)
                            e_r2 = sch.add("vector", (lambda pp_: lambda v: v.tensor_copy(
                                rope_sc[0][64:128, :], ps_proj[pp_][0:64, :]))(pp),
                                deps=[e_mm, e_r1], inc=vs)
                            e_r3 = sch.add("vector", (lambda pp_, t_: lambda v: v.tensor_tensor(
                                rope_sc[1][:], ps_proj[pp_][:], cos_t[:, t_ * 512:(t_ + 1) * 512], ALU.mult))(pp, tsl),
                                deps=[e_mm, e_r1] + rope_evs, inc=vs)
                            e_r4 = sch.add("vector", (lambda t_: lambda v: v.tensor_tensor(
                                rope_sc[2][:], rope_sc[0][:], sin_t[:, t_ * 512:(t_ + 1) * 512], ALU.mult))(tsl),
                                deps=[e_r2, e_r3] + rope_evs, inc=vs)
                            e_evq = sch.add("vector", (lambda d_, t_: lambda v: v.tensor_tensor(
                                d_[:, t_ * 512:(t_ + 1) * 512], rope_sc[1][:], rope_sc[2][:], ALU.add))(dest[br], tsl),
                                deps=[e_r4, prev_head_end[0]], inc=vs)
                            prev_rope[0] = e_evq
                        else:
                            e_evq = sch.add("scalar", (lambda d_, t_, pp_: lambda s: s.copy(
                                d_[:, t_ * 512:(t_ + 1) * 512], ps_proj[pp_][:]))(dest[br], tsl, pp),
                                deps=[e_mm, prev_head_end[0]], inc=ss)
                        prev_pproj[pp] = e_evq
                        qk_done.append(e_evq)
            qk_all = qk_done[-1]

            e_vms = sch.add("vector", lambda v: v.memset(v_tm[:], 1.0),
                            deps=[prev_head_end[0]], inc=vs)
            v_done = []
            for kt in range(NT):
                pp = pcnt % 2
                pcnt += 1
                def vproj(kt_, pp_):
                    def f(te):
                        ins = None
                        for kc in range(KC):
                            ins = te.matmul(ps_proj[pp_][:, 0:DV],
                                            x_bf[kc][:, kt_ * 128:(kt_ + 1) * 128],
                                            w_bf["wv"][:, kc * FH: kc * FH + DV],
                                            start=(kc == 0), stop=(kc == KC - 1))
                        return ins
                    return f
                e_vm = sch.add("tensor", vproj(kt, pp),
                               deps=[w_evs["wv"], x_done, prev_pproj.get(pp)], inc=mm)
                e_ve = sch.add("scalar", (lambda kt_, pp_: lambda s: s.copy(
                    v_tm[:, kt_ * DVA: kt_ * DVA + DV], ps_proj[pp_][:, 0:DV]))(kt, pp),
                    deps=[e_vm, e_vms], inc=ss)
                prev_pproj[pp] = e_ve
                v_done.append(e_ve)
            v_all = v_done[-1]

            head_last = None
            for tq in range(NQ5):
                nch = 4 * tq + 4
                osb0_evs = {}
                for br in range(NBR):
                    pt_evs = []
                    for j in range(nch):
                        pb = j % 2
                        def sct(j_, tq_, br_, pb_):
                            def f(te):
                                return te.matmul(ps_sc[pb_][:],
                                                 k_bf[br_][:, j_ * 128:(j_ + 1) * 128],
                                                 q_bf[br_][:, tq_ * 512:(tq_ + 1) * 512],
                                                 start=True, stop=(j_ < 4 * tq_))
                            return f
                        e_s = sch.add("tensor", sct(j, tq, br, pb),
                                      deps=[qk_all, prev_psc.get(pb)], inc=mm)
                        if j >= 4 * tq:
                            r_i = j - 4 * tq
                            def mskmm(r_, pb_):
                                def f(te):
                                    return te.matmul(ps_sc[pb_][:], ident_bf[:],
                                                     mask_bf[:, r_ * 512:(r_ + 1) * 512],
                                                     start=False, stop=True)
                                return f
                            e_s = sch.add("tensor", mskmm(r_i, pb), deps=[e_s, e_mc, e_ic], inc=mm)
                        e_x = sch.add("scalar", (lambda j_, pb_: lambda s: s.activation(
                            pt[:, j_ * 512:(j_ + 1) * 512], ps_sc[pb_][:], AF.Exp,
                            bias=shift_t[:], scale=SM_SCALE))(j, pb),
                            deps=[e_s, e_sh], inc=ss)
                        prev_psc[pb] = e_x
                        pt_evs.append(e_x)
                    pt_all = pt_evs[-1]

                    for qs in range(4):
                        pb = qs % 2
                        def pvg(qs_, ncl_, pb_):
                            def f(te):
                                ins = None
                                for j in range(ncl_):
                                    ins = te.matmul(ps_pv[pb_][:],
                                                    pt[:, j * 512 + qs_ * 128: j * 512 + qs_ * 128 + 128],
                                                    v_tm[:, j * DVA:(j + 1) * DVA],
                                                    start=(j == 0), stop=(j == ncl_ - 1))
                                return ins
                            return f
                        e_pv = sch.add("tensor", pvg(qs, nch, pb),
                                       deps=[pt_all, v_all, prev_ppv.get(pb)], inc=mm)
                        e_rc2 = sch.add("vector", (lambda pb_, br_: lambda v: v.reciprocal(
                            rec[br_][:], ps_pv[pb_][:, DV:DVA]))(pb, br),
                            deps=[e_pv, prev_rec.get(br)], inc=vs)
                        if layer == 1:
                            sl = next_store_slot()
                            ab = sl % 2
                            e_n = sch.add("vector", (lambda pb_, ab_: lambda v: v.tensor_scalar(
                                a_f32[ab_][:, 0:DV], ps_pv[pb_][:, 0:DV], rec[0][:], None, ALU.mult))(pb, ab),
                                deps=[e_rc2, prev_af.get(ab)], inc=vs)
                            e_st = sch.add("gpsimd", (lambda tq_, qs_, h_, ab_: lambda g: g.dma_start(
                                out=a_out[(tq_ * 4 + qs_) * 128:(tq_ * 4 + qs_ + 1) * 128, h_ * FH:(h_ + 1) * FH],
                                in_=a_f32[ab_][:, 0:DV]))(tq, qs, h, ab),
                                deps=[e_n, st_prev[sl]], inc=st_s[sl], inc_n=16)
                            st_prev[sl] = e_st
                            prev_af[ab] = e_st
                            prev_rec[0] = e_n
                            prev_ppv[pb] = e_n
                            head_last = e_st
                        elif br == 0:
                            e_n = sch.add("vector", (lambda pb_, qs_: lambda v: v.tensor_scalar(
                                o_sb0[:, qs_ * DV:(qs_ + 1) * DV], ps_pv[pb_][:, 0:DV], rec[0][:], None, ALU.mult))(pb, qs),
                                deps=[e_rc2, prev_osb0.get(qs)], inc=vs)
                            osb0_evs[qs] = e_n
                            prev_rec[0] = e_n
                            prev_ppv[pb] = e_n
                        else:
                            e_c1 = sch.add("vector", lambda v: v.tensor_tensor(
                                nlam_rec[:], rec[1][:], lam_b[:], ALU.mult),
                                deps=[e_rc2, prev_nlam[0]] + lam_evs, inc=vs)
                            e_c2 = sch.add("vector", (lambda pb_: lambda v: v.tensor_scalar(
                                o1_sc[:], ps_pv[pb_][:, 0:DV], nlam_rec[:], None, ALU.mult))(pb),
                                deps=[e_c1, prev_o1[0]], inc=vs)
                            prev_rec[1] = e_c2
                            prev_ppv[pb] = e_c2
                            e_c3 = sch.add("vector", (lambda qs_: lambda v: v.tensor_tensor(
                                cmb[:], o_sb0[:, qs_ * DV:(qs_ + 1) * DV], o1_sc[:], ALU.subtract))(qs),
                                deps=[e_c2, osb0_evs[qs], prev_cmb[0]], inc=vs)
                            prev_osb0[qs] = e_c3
                            prev_o1[0] = e_c3
                            e_q1 = sch.add("scalar", lambda s: s.activation(
                                sq_s[:], cmb[:], AF.Square, accum_out=rms["sumsq"][:]),
                                deps=[e_c3, prev_rms[0]], inc=ss)
                            e_q2 = sch.add("vector", lambda v: v.tensor_scalar(
                                rms["ms"][:], rms["sumsq"][:], 1.0 / DV, None, ALU.mult),
                                deps=[e_q1], inc=vs)
                            e_q3 = sch.add("scalar", lambda s: s.activation(
                                rms["std"][:], rms["ms"][:], AF.Sqrt, bias=eps_t[:]),
                                deps=[e_q2, e_eps], inc=ss)
                            e_q4 = sch.add("vector", lambda v: v.reciprocal(rms["rstd"][:], rms["std"][:]),
                                           deps=[e_q3], inc=vs)
                            e_q5 = sch.add("vector", lambda v: v.tensor_scalar(
                                rms["rstd"][:], rms["rstd"][:], 1.0 - LAMBDA_INIT, None, ALU.mult),
                                deps=[e_q4], inc=vs)
                            sl = next_store_slot()
                            ab = sl % 2
                            e_q6 = sch.add("vector", (lambda ab_: lambda v: v.tensor_scalar(
                                a_f32[ab_][:], cmb[:], rms["rstd"][:], None, ALU.mult))(ab),
                                deps=[e_q5, e_c3, prev_af.get(ab)], inc=vs)
                            e_q7 = sch.add("vector", (lambda ab_: lambda v: v.tensor_tensor(
                                a_f32[ab_][:], a_f32[ab_][:], subln_t[:], ALU.mult))(ab),
                                deps=[e_q6, subln_ev], inc=vs)
                            prev_rms[0] = e_q7
                            prev_cmb[0] = e_q7
                            e_st = sch.add("gpsimd", (lambda tq_, qs_, h_, ab_: lambda g: g.dma_start(
                                out=a_out[(tq_ * 4 + qs_) * 128:(tq_ * 4 + qs_ + 1) * 128, h_ * FH:(h_ + 1) * FH],
                                in_=a_f32[ab_][:]))(tq, qs, h, ab),
                                deps=[e_q7, st_prev[sl]], inc=st_s[sl], inc_n=16)
                            st_prev[sl] = e_st
                            prev_af[ab] = e_st
                            head_last = e_st

            prev_head_end[0] = head_last
            for nm in ("wq", "wk", "wv"):
                prev_w_use[nm] = head_last

        sch.add("gpsimd", lambda g: None, deps=[(s, s.count) for s in st_s])
        sch.emit(nc)
    return nc


def _run(nc, in_maps):
    return run_bass_kernel_spmd(nc, in_maps, core_ids=list(range(N_CORES)))



def host_rope_tables():
    inv = 1.0 / (1e6 ** (np.arange(0, 128, 2, dtype=np.float64) / 128))
    t = np.arange(T_SEQ, dtype=np.float64)
    f = np.outer(t, inv)
    cos = np.concatenate([np.cos(f), np.cos(f)], axis=-1).astype(np.float32)
    sin = np.concatenate([np.sin(f), np.sin(f)], axis=-1).astype(np.float32)
    return np.ascontiguousarray(cos.T), np.ascontiguousarray(sin.T)


def host_masks():
    m = np.zeros((4, 128, 512), dtype=np.float32)
    kk = np.arange(128)[:, None]
    qq = np.arange(512)[None, :]
    for r in range(4):
        blk = qq // 128
        vis = (blk > r) | ((blk == r) & (kk <= (qq % 128)))
        m[r] = np.where(vis, 0.0, -1e30)
    return m


def host_hadamard128():
    H = np.array([[1.0]], dtype=np.float32)
    while H.shape[0] < 128:
        H = np.block([[H, H], [H, -H]])
    return np.ascontiguousarray(H.astype(np.float32))


def _ct(a):
    return np.ascontiguousarray(a)



_PROG_CACHE = {}

def _prog(key, builder, *args, **kwargs):
    if key not in _PROG_CACHE:
        _PROG_CACHE[key] = builder(*args, **kwargs)
    return _PROG_CACHE[key]


def kernel_multi(L0_attn_x, L0_attn_Wq, L0_attn_Wk, L0_attn_Wv, L0_attn_Wo,
           L0_attn_lambda_q1, L0_attn_lambda_k1, L0_attn_lambda_q2, L0_attn_lambda_k2,
           L0_attn_subln_weight,
           L0_ffn_gate_w, L0_ffn_up_w, L0_ffn_down_w, L0_ffn_gate_s, L0_ffn_up_s, L0_ffn_down_s,
           L1_attn_wq, L1_attn_wk, L1_attn_wv, L1_attn_wo,
           L1_ffn_SU_ug, L1_ffn_SU_dp, L1_ffn_Wscale_ug, L1_ffn_Wscale_dp,
           L1_ffn_W_up, L1_ffn_W_gate, L1_ffn_W_down):
    f32 = np.float32
    x = np.asarray(L0_attn_x, dtype=f32)
    x_flat = x.reshape(TOK_GLOBAL, D)
    masks = host_masks()
    ident = np.eye(128, dtype=f32)
    H128 = host_hadamard128()
    cosT, sinT = host_rope_tables()

    def rows(c):
        return slice(c * TOK_CORE, (c + 1) * TOK_CORE)

    nc_a0 = _prog("a0", build_attn_program, 0)
    lamv = np.concatenate([np.asarray(L0_attn_lambda_q1, f32), np.asarray(L0_attn_lambda_k1, f32),
                           np.asarray(L0_attn_lambda_q2, f32), np.asarray(L0_attn_lambda_k2, f32)])[None, :]
    subln_b = _ct(np.broadcast_to(np.asarray(L0_attn_subln_weight, f32), (128, 256)))
    Wq0 = np.asarray(L0_attn_Wq, f32); Wk0 = np.asarray(L0_attn_Wk, f32); Wv0 = np.asarray(L0_attn_Wv, f32)
    im = []
    for c in range(N_CORES):
        b, hh = c // 2, c % 2
        im.append({"xfm": _ct(x[b].T), "wq": _ct(Wq0[:, hh * 1024:(hh + 1) * 1024]),
                   "wk": _ct(Wk0[:, hh * 1024:(hh + 1) * 1024]),
                   "wv": _ct(Wv0[:, hh * 1024:(hh + 1) * 1024]),
                   "masks": masks, "ident": ident, "lamv": lamv, "subln_b": subln_b})
    res = _run(nc_a0, im)
    A0 = np.empty((B_SZ, T_SEQ, D), dtype=f32)
    for c in range(N_CORES):
        b, hh = c // 2, c % 2
        A0[b, :, hh * 1024:(hh + 1) * 1024] = res.results[c]["a_out"]
    A0_flat = A0.reshape(TOK_GLOBAL, D)

    nc_wo = _prog("pmm16_final", build_pmm_program, 16)
    Wo0 = np.asarray(L0_attn_Wo, f32)
    im = [{"a_fm": _ct(A0_flat[rows(c)].T), "w": Wo0, "resid": _ct(x_flat[rows(c)])}
          for c in range(N_CORES)]
    res = _run(nc_wo, im)
    R1 = np.concatenate([res.results[c]["out"] for c in range(N_CORES)], axis=0)

    nc_f0 = _prog("ffnup0", build_ffn_up_program, 0)
    gwT = _ct(np.asarray(L0_ffn_gate_w, f32).T)
    uwT = _ct(np.asarray(L0_ffn_up_w, f32).T)
    gs0 = _ct(np.asarray(L0_ffn_gate_s, f32)[:, None])
    us0 = _ct(np.asarray(L0_ffn_up_s, f32)[:, None])
    im = [{"x_fm": _ct(R1[rows(c)].T), "gw": gwT, "uw": uwT, "gs": gs0, "us": us0}
          for c in range(N_CORES)]
    res = _run(nc_f0, im)
    h0 = [res.results[c]["h_out"] for c in range(N_CORES)]

    dwT = _ct(np.asarray(L0_ffn_down_w, f32).T)
    ds_b = _ct(np.broadcast_to(np.asarray(L0_ffn_down_s, f32), (128, D)))
    nc_d1 = _prog("pmm22_part0", build_pmm_program, 22, True, False, False)
    im = [{"a_fm": _ct(h0[c][:2816]), "w": _ct(dwT[:2816])} for c in range(N_CORES)]
    res = _run(nc_d1, im)
    p = [res.results[c]["out"] for c in range(N_CORES)]
    nc_d2 = _prog("pmm22_fin_cs", build_pmm_program, 22, True, True, True, True, False)
    im = [{"a_fm": _ct(h0[c][2816:]), "w": _ct(dwT[2816:]), "p_in": p[c],
           "resid": _ct(R1[rows(c)]), "colscale_b": ds_b} for c in range(N_CORES)]
    res = _run(nc_d2, im)
    R2 = np.concatenate([res.results[c]["out"] for c in range(N_CORES)], axis=0)

    nc_a1 = _prog("a1", build_attn_program, 1)
    wq1 = np.asarray(L1_attn_wq, f32); wk1 = np.asarray(L1_attn_wk, f32); wv1 = np.asarray(L1_attn_wv, f32)
    R2b = R2.reshape(B_SZ, T_SEQ, D)
    im = []
    for c in range(N_CORES):
        b, hh = c // 2, c % 2
        im.append({"xfm": _ct(R2b[b].T), "wq": _ct(wq1[hh * 1024:(hh + 1) * 1024, :].T),
                   "wk": _ct(wk1[hh * 1024:(hh + 1) * 1024, :].T),
                   "wv": _ct(wv1[hh * 1024:(hh + 1) * 1024, :].T),
                   "masks": masks, "ident": ident, "rc": cosT, "rs": sinT})
    res = _run(nc_a1, im)
    A1 = np.empty((B_SZ, T_SEQ, D), dtype=f32)
    for c in range(N_CORES):
        b, hh = c // 2, c % 2
        A1[b, :, hh * 1024:(hh + 1) * 1024] = res.results[c]["a_out"]
    A1_flat = A1.reshape(TOK_GLOBAL, D)

    wo1T = _ct(np.asarray(L1_attn_wo, f32).T)
    im = [{"a_fm": _ct(A1_flat[rows(c)].T), "w": wo1T, "resid": _ct(R2[rows(c)])}
          for c in range(N_CORES)]
    res = _run(nc_wo, im)
    R3 = np.concatenate([res.results[c]["out"] for c in range(N_CORES)], axis=0)

    nc_fw = _prog("fwht", build_fwht_program)
    su_ug = _ct(np.asarray(L1_ffn_SU_ug, f32)[:, None])
    im = [{"x_fm": _ct(R3[rows(c)].T), "su": su_ug, "h128": H128} for c in range(N_CORES)]
    res = _run(nc_fw, im)
    t_fm = [res.results[c]["t_out"] for c in range(N_CORES)]

    nc_f1 = _prog("ffnup1", build_ffn_up_program, 1)
    wguT = _ct(np.asarray(L1_ffn_W_gate, f32).T)
    wupT = _ct(np.asarray(L1_ffn_W_up, f32).T)
    wsc = np.asarray(L1_ffn_Wscale_ug, f32)
    gs1 = _ct(wsc[INTER1:][:, None])
    us1 = _ct(wsc[:INTER1][:, None])
    su_dp = _ct(np.asarray(L1_ffn_SU_dp, f32)[:, None])
    im = [{"x_fm": t_fm[c], "gw": wguT, "uw": wupT, "gs": gs1, "us": us1,
           "su": su_dp, "h128": H128} for c in range(N_CORES)]
    res = _run(nc_f1, im)
    h2 = [res.results[c]["h_out"] for c in range(N_CORES)]

    wdT = _ct(np.asarray(L1_ffn_W_down, f32).T)
    wdp_b = _ct(np.broadcast_to(np.asarray(L1_ffn_Wscale_dp, f32), (128, D)))
    nc_p0 = _prog("pmm16_part0", build_pmm_program, 16, True, False, False)
    nc_pm = _prog("pmm16_partm", build_pmm_program, 16, True, True, False)
    nc_pf = _prog("pmm16_fin64", build_pmm_program, 16, True, True, True, True, True)
    im = [{"a_fm": _ct(h2[c][:2048]), "w": _ct(wdT[:2048])} for c in range(N_CORES)]
    res = _run(nc_p0, im)
    p = [res.results[c]["out"] for c in range(N_CORES)]
    for seg in (1, 2):
        im = [{"a_fm": _ct(h2[c][seg * 2048:(seg + 1) * 2048]),
               "w": _ct(wdT[seg * 2048:(seg + 1) * 2048]), "p_in": p[c]} for c in range(N_CORES)]
        res = _run(nc_pm, im)
        p = [res.results[c]["out"] for c in range(N_CORES)]
    im = [{"a_fm": _ct(h2[c][3 * 2048:]), "w": _ct(wdT[3 * 2048:]), "p_in": p[c],
           "resid": _ct(R3[rows(c)]), "colscale_b": wdp_b} for c in range(N_CORES)]
    res = _run(nc_pf, im)
    R4 = np.concatenate([res.results[c]["out"] for c in range(N_CORES)], axis=0)

    kernel_multi.dbg = {"A0": A0, "R1": R1, "R2": R2, "R3": R3, "h0": h0, "t_fm": t_fm,
                  "h2": h2, "R4": R4}
    return R4.reshape(B_SZ, T_SEQ, D)



from concourse.aot_env import allow_ici_all_to_all

SLAB_BF16_ELEMS = 106368


class Fctx:
    def __init__(self, nc, sch, slab, psum_f32, psum_bf16):
        self.nc = nc
        self.sch = sch
        self.slab = slab
        self.psum_f32 = psum_f32
        self.psum_bf16 = psum_bf16
        self.off = 0
        self.consts = {}

    def tile(self, n, dt=F32, parts=128):
        elems = n * (2 if dt == F32 else 1)
        off = (self.off + 1) & ~1
        self.off = off + elems
        assert self.off <= SLAB_BF16_ELEMS, f"slab overflow: {self.off}"
        sl = self.slab[:, off:off + elems]
        if dt == F32:
            sl = sl.bitcast(F32)
        elif dt != BF16:
            raise ValueError(dt)
        if parts != 128:
            sl = sl[0:parts, :]
        return sl

    def mark(self):
        return self.off

    def reset(self, m):
        self.off = m

    def set_base(self, m):
        self.off = m


def fsem_pool(sch):
    return {
        "x_ld": [sch.sem(f"fx_ld{i}") for i in range(4)],
        "w_ld": [sch.sem(f"fw_ld{i}") for i in range(6)],
        "r_ld": [sch.sem(f"fr_ld{i}") for i in range(2)],
        "p_ld": [sch.sem(f"fp_ld{i}") for i in range(2)],
        "sc_ld": [sch.sem(f"fsc_ld{i}") for i in range(2)],
        "x_ldB": [sch.sem(f"fx_ldB{i}") for i in range(4)],
        "w_ldB": [sch.sem(f"fw_ldB{i}") for i in range(2)],
        "p_ldB": [sch.sem(f"fp_ldB{i}") for i in range(2)],
        "c_ld": sch.sem("fc_ld"),
        "cc": sch.sem("fcc"),
        "cast": sch.sem("fcast"),
        "mm": sch.sem("fmm"),
        "vs": sch.sem("fvs"),
        "ss": sch.sem("fss"),
        "st": [sch.sem(f"fst{i}") for i in range(4)],
    }


def emit_const_phase(ctx, S, aps):
    sch = ctx.sch
    c_ld, cast, vs, ss, mm = S["c_ld"], S["cast"], S["vs"], S["ss"], S["mm"]
    C = {}
    C["mask_bf"] = ctx.tile(4 * 512, BF16)
    C["ident_bf"] = ctx.tile(128, BF16)
    C["h128_bf"] = ctx.tile(128, BF16)
    C["shift"] = ctx.tile(1, F32)
    C["eps"] = ctx.tile(1, F32)
    C["subln"] = ctx.tile(256, F32)
    C["lam_b"] = ctx.tile(1, F32)
    m_temp = ctx.mark()
    C["mask_f32"] = ctx.tile(4 * 512, F32)
    C["ident_f32"] = ctx.tile(128, F32)
    C["h128_f32"] = ctx.tile(128, F32)
    C["lam_row"] = ctx.tile(512, F32)
    C["lam_sc"] = ctx.tile(8, F32)
    C["ones_row"] = ctx.tile(128, F32)
    for r in range(4):
        sch.add("sync", (lambda r_: lambda s: s.dma_start(
            out=C["mask_f32"][:, r_ * 512:(r_ + 1) * 512], in_=aps["masks"][r_]))(r),
            inc=c_ld, inc_n=16)
    sch.add("sync", lambda s: s.dma_start(out=C["ident_f32"][:], in_=aps["ident"]), inc=c_ld, inc_n=16)
    sch.add("sync", lambda s: s.dma_start(out=C["h128_f32"][:], in_=aps["h128"]), inc=c_ld, inc_n=16)
    sch.add("sync", lambda s: s.dma_start(out=C["subln"][:], in_=aps["subln_b"]), inc=c_ld, inc_n=16)
    sch.add("sync", lambda s: s.dma_start(out=C["lam_row"][0:1, :], in_=aps["lamv"]), inc=c_ld, inc_n=16)
    c_done = (c_ld, c_ld.count)
    C["loaded"] = c_done
    e_mc = sch.add("vector", lambda v: v.tensor_copy(C["mask_bf"][:], C["mask_f32"][:]), deps=[c_done], inc=cast)
    e_ic = sch.add("vector", lambda v: v.tensor_copy(C["ident_bf"][:], C["ident_f32"][:]), deps=[c_done], inc=cast)
    e_hc = sch.add("vector", lambda v: v.tensor_copy(C["h128_bf"][:], C["h128_f32"][:]), deps=[c_done], inc=cast)
    e_sh = sch.add("vector", lambda v: v.memset(C["shift"][:], -SOFTMAX_SHIFT), inc=vs)
    e_ep = sch.add("vector", lambda v: v.memset(C["eps"][:], EPS), inc=vs)
    lr = C["lam_row"]; lsc = C["lam_sc"]
    e_p1 = sch.add("vector", lambda v: v.tensor_tensor(lr[0:1, 0:128], lr[0:1, 0:128], lr[0:1, 128:256], ALU.mult),
                   deps=[c_done], inc=vs)
    e_p2 = sch.add("vector", lambda v: v.tensor_tensor(lr[0:1, 256:384], lr[0:1, 256:384], lr[0:1, 384:512], ALU.mult),
                   deps=[c_done, e_p1], inc=vs)
    e_s1 = sch.add("vector", lambda v: v.reduce_sum(lsc[0:1, 0:1], lr[0:1, 0:128], axis=AX.X),
                   deps=[e_p1], inc=vs)
    e_s2 = sch.add("vector", lambda v: v.reduce_sum(lsc[0:1, 1:2], lr[0:1, 256:384], axis=AX.X),
                   deps=[e_p2, e_s1], inc=vs)
    e_e1 = sch.add("scalar", lambda s: s.activation(lsc[0:1, 2:4], lsc[0:1, 0:2], AF.Exp),
                   deps=[e_s1, e_s2], inc=ss)
    e_d = sch.add("vector", lambda v: v.tensor_tensor(lsc[0:1, 4:5], lsc[0:1, 2:3], lsc[0:1, 3:4], ALU.subtract),
                  deps=[e_e1], inc=vs)
    e_l = sch.add("vector", lambda v: v.tensor_scalar(lsc[0:1, 4:5], lsc[0:1, 4:5], LAMBDA_INIT, None, ALU.add),
                  deps=[e_d], inc=vs)
    e_or = sch.add("vector", lambda v: v.memset(C["ones_row"][0:1, :], 1.0), inc=vs)
    e_lb = sch.add("tensor", lambda t: t.matmul(ctx.psum_f32[4][:, 0:1], C["ones_row"][0:1, :],
                                                lsc[0:1, 4:5], start=True, stop=True),
                   deps=[e_l, e_or], inc=S["mm"])
    e_lbc = sch.add("vector", lambda v: v.tensor_copy(C["lam_b"][:], ctx.psum_f32[4][:, 0:1]),
                    deps=[e_lb], inc=vs)
    C["evs"] = [e_mc, e_ic, e_hc, e_sh, e_ep, e_lbc]
    ctx.consts = C
    ctx.reset(m_temp)


def emit_transpose_phase(ctx, S, src_dram, dst_dram, R, Cc, a2a_split=False):
    sch = ctx.sch
    m0 = ctx.mark()
    NR, NC = R // 128, Cc // 128
    sbf = [ctx.tile(Cc, BF16) for _ in range(2)]
    obuf = [ctx.tile(R, BF16) for _ in range(NC)]
    x_ld, cast, mm, vs, ss = S["x_ld"], S["cast"], S["mm"], S["vs"], S["ss"]
    st = S["st"]
    prev_cast = {}
    prev_ps = {}
    ev_by_j = {j: [] for j in range(NC)}
    for i in range(NR):
        buf = i % 2
        e_c = sch.add("gpsimd", (lambda i_, bb: lambda g: g.dma_start(
            out=sbf[bb][:], in_=src_dram[i_ * 128:(i_ + 1) * 128, :]))(i, buf),
            deps=[prev_cast.get(("b", buf))], inc=x_ld[buf], inc_n=16)
        prev_cast[buf] = e_c
        for j in range(NC):
            pb = j % 2
            e_t = sch.add("tensor", (lambda bb, j_, pb_: lambda te: te.transpose(
                ctx.psum_bf16[pb_][:], sbf[bb][:, j_ * 128:(j_ + 1) * 128],
                ctx.consts["ident_bf"][:]))(buf, j, pb),
                deps=[e_c, prev_ps.get(pb)] + [ctx.consts["evs"][1]], inc=mm)
            e_e = sch.add("vector", (lambda j_, i_, pb_: lambda v: v.tensor_copy(
                obuf[j_][:, i_ * 128:(i_ + 1) * 128], ctx.psum_bf16[pb_][:]))(j, i, pb),
                deps=[e_t], inc=vs)
            prev_ps[pb] = e_e
            ev_by_j[j].append(e_e)
        prev_cast[("b", buf)] = ev_by_j[NC - 1][-1]
    for j in range(NC):
        sl = j % 4
        if a2a_split:
            H = R // 2
            for bb in range(2):
                sch.add("gpsimd", (lambda j_, b_: lambda g: g.dma_start(
                    out=dst_dram[b_ * Cc + j_ * 128: b_ * Cc + (j_ + 1) * 128, :],
                    in_=obuf[j_][:, b_ * H:(b_ + 1) * H]))(j, bb),
                    deps=ev_by_j[j], inc=st[sl], inc_n=16)
        else:
            sch.add("gpsimd", (lambda j_: lambda g: g.dma_start(
                out=dst_dram[j_ * 128:(j_ + 1) * 128, :], in_=obuf[j_][:]))(j),
                deps=ev_by_j[j], inc=st[sl], inc_n=16)
    ctx.reset(m0)


def emit_collective(ctx, S, kind, in_ap, out_ap, op=None):
    sch = ctx.sch
    op = ALU.bypass if op is None else op
    def f(g):
        with allow_ici_all_to_all():
            return g.collective_compute(
                kind, op,
                replica_groups=[[0, 1], [2, 3], [4, 5], [6, 7]],
                ins=[in_ap.ap().opt()], outs=[out_ap.ap().opt()])
    return sch.add("gpsimd", f, inc=S["cc"], inc_n=1)


def emit_attn_phase(ctx, S, layer, cc_x_out, wq, wk, wv, afm_tiles, x_dep=None,
                    rc_ap=None, rs_ap=None):
    sch = ctx.sch
    m0 = ctx.mark()
    NH = 8 if layer == 1 else 4
    NBR = 1 if layer == 1 else 2
    DV = 128 if layer == 1 else 256
    DVA = DV + 1
    FH = 1024 // NH
    KC = D // 128
    NT = T_SEQ // 128
    NQ5 = T_SEQ // 512
    C = ctx.consts
    x_ldS, w_lds, castS, mmS, vsS, ssS, stS = (S["x_ld"], S["w_ld"], S["cast"],
                                               S["mm"], S["vs"], S["ss"], S["st"])
    sc_ld = S["sc_ld"]

    x_bf = [ctx.tile(T_SEQ, BF16) for _ in range(KC)]
    w_bf = [{nm: ctx.tile(KC * FH, BF16) for nm in ("wq", "wk", "wv")} for _ in range(2)]
    q_bf = [ctx.tile(T_SEQ, BF16) for _ in range(NBR)]
    k_bf = [ctx.tile(T_SEQ, BF16) for _ in range(NBR)]
    v_tm = ctx.tile(NT * DVA, BF16)
    pt = ctx.tile(NT * 512, BF16)
    rec = [ctx.tile(1, F32) for _ in range(NBR)]
    a_f32 = [ctx.tile(DV, F32) for _ in range(2)]
    a_bfs_t = ctx.tile(DV, BF16)
    a_bfs = [a_bfs_t, a_bfs_t]
    if layer == 1:
        rope_sc = [ctx.tile(512, F32) for _ in range(3)]
        cos_l = ctx.tile(2048, F32)
        sin_l = ctx.tile(2048, F32)
    else:
        nlam_rec = ctx.tile(1, F32)
        sq_s = ctx.tile(256, F32)
        o_sb0 = ctx.tile(4 * 256, F32)
        o1_sc = ctx.tile(256, F32)
        cmb = ctx.tile(256, F32)
        rms = {nm: ctx.tile(1, F32) for nm in ("sumsq", "ms", "std", "rstd")}
    ps_proj = [ctx.psum_f32[0], ctx.psum_f32[1]]
    ps_sc = [ctx.psum_f32[2], ctx.psum_f32[3]]
    ps_pv = [ctx.psum_f32[4][:, 0:DVA], ctx.psum_f32[5][:, 0:DVA]]

    x_evs = []
    for k in range(KC):
        buf = k % 2
        dep = [x_evs[k - 2][1]] if k >= 2 else []
        if x_dep is not None:
            dep = dep + [x_dep]
        e0 = sch.add("sync", (lambda kk: lambda s: s.dma_start(
            out=x_bf[kk][:, 0:1024], in_=cc_x_out[kk * 128:(kk + 1) * 128, :]))(k),
            deps=dep, inc=x_ldS[buf], inc_n=16)
        e1 = sch.add("sync", (lambda kk: lambda s: s.dma_start(
            out=x_bf[kk][:, 1024:2048], in_=cc_x_out[2048 + kk * 128: 2048 + (kk + 1) * 128, :]))(k),
            deps=[e0], inc=x_ldS[buf], inc_n=16)
        x_evs.append((e0, e1))
    x_done = [(x_ldS[0], x_ldS[0].count), (x_ldS[1], x_ldS[1].count)]

    st_prev = [None] * 4
    st_ctr = [0]
    def next_store_slot():
        i = st_ctr[0] % 4
        st_ctr[0] += 1
        return i

    prev_tp = {}
    prev_abfs = {}
    NDS = DV // 128
    def emit_a_out(e_src, ab, h, tq, qs):
        e_c = sch.add("vector", (lambda ab_: lambda v: v.tensor_copy(a_bfs[ab_][:], a_f32[ab_][:]))(ab),
                      deps=[e_src, prev_abfs.get(0), prev_abfs.get(1)], inc=vsS)
        last = None
        for dsub in range(NDS):
            tb = (st_ctr[0] + dsub) % 2
            fc = h * NDS + dsub
            qcol = (tq * 4 + qs) * 128
            e_t = sch.add("tensor", (lambda ab_, d_, tb_: lambda te: te.transpose(
                ctx.psum_bf16[tb_][:], a_bfs[ab_][:, d_ * 128:(d_ + 1) * 128],
                C["ident_bf"][:]))(ab, dsub, tb),
                deps=[e_c, prev_tp.get(tb), C["evs"][1]], inc=mmS)
            e_e = sch.add("vector", (lambda fc_, qc_, tb_: lambda v: v.tensor_copy(
                afm_tiles[fc_][:, qc_:qc_ + 128], ctx.psum_bf16[tb_][:]))(fc, qcol, tb),
                deps=[e_t], inc=vsS)
            prev_tp[tb] = e_e
            last = e_e
        st_ctr[0] += NDS
        prev_abfs[ab] = last
        return last

    prev_w_use = {}
    prev_head_end = [None]
    w_ev_by_head = {}
    prev_pproj = {}
    prev_psc = {}
    prev_ppv = {}
    prev_af = {}
    prev_rope = [None]
    prev_rms = [None]
    prev_osb0 = {}
    prev_o1 = [None]
    prev_cmb = [None]
    prev_rec = {}
    prev_nlam = [None]
    e_mc, e_ic, e_hc, e_sh, e_ep, e_lbc = C["evs"]
    lam_evs = [e_lbc]
    eps_t = C["eps"]; shift_t = C["shift"]
    mask_bf = C["mask_bf"]; ident_bf = C["ident_bf"]
    subln_t = C["subln"]; lam_b = C["lam_b"]
    subln_ev = C["loaded"]
    rope_evs = []
    if layer == 1:
        cos_t, sin_t = cos_l, sin_l
        e_rc = sch.add("sync", lambda s: s.dma_start(out=cos_t[:], in_=rc_ap), inc=S["r_ld"][0], inc_n=16)
        e_rs = sch.add("sync", lambda s: s.dma_start(out=sin_t[:], in_=rs_ap), deps=[e_rc], inc=S["r_ld"][1], inc_n=16)
        rope_evs = [e_rc, e_rs]
    else:
        cos_t = sin_t = None

    def load_head_weights(h):
        hb = h % 2
        evs = {}
        for wi, (nm, wap) in enumerate((("wq", wq), ("wk", wk), ("wv", wv))):
            e_lw = sch.add("gpsimd", (lambda w_, c0, c1, nm_, hb_: lambda g: g.dma_start(
                out=w_bf[hb_][nm_][:].rearrange("p (c f) -> p c f", c=KC),
                in_=w_[:, c0:c1].rearrange("(c p) f -> p c f", p=128)))(
                    wap, h * FH, (h + 1) * FH, nm, hb),
                deps=[prev_w_use.get((nm, hb))], inc=w_lds[wi + 3 * hb], inc_n=16)
            evs[nm] = e_lw
        w_ev_by_head[h] = evs

    load_head_weights(0)
    if NH > 1:
        load_head_weights(1)

    for h in range(NH):
        hb = h % 2
        w_evs = w_ev_by_head[h]

        qk_done = []
        pcnt = 0
        for nm, dest in (("wq", q_bf), ("wk", k_bf)):
            for br in range(NBR):
                for tsl in range(4):
                    pp = pcnt % 2
                    pcnt += 1
                    def proj(nm_, br_, tsl_, pp_, hb_=hb):
                        def f(te):
                            ins = None
                            for kc in range(KC):
                                ins = te.matmul(ps_proj[pp_][:],
                                                w_bf[hb_][nm_][:, kc * FH + br_ * 128: kc * FH + br_ * 128 + 128],
                                                x_bf[kc][:, tsl_ * 512:(tsl_ + 1) * 512],
                                                start=(kc == 0), stop=(kc == KC - 1))
                            return ins
                        return f
                    e_mm = sch.add("tensor", proj(nm, br, tsl, pp),
                                   deps=[w_evs[nm], prev_pproj.get(pp)] + x_done, inc=mmS)
                    if layer == 1:
                        e_r1 = sch.add("vector", (lambda pp_: lambda v: v.tensor_scalar(
                            rope_sc[0][0:64, :], ps_proj[pp_][64:128, :], -1.0, None, ALU.mult))(pp),
                            deps=[e_mm, prev_rope[0]], inc=vsS)
                        e_r2 = sch.add("vector", (lambda pp_: lambda v: v.tensor_copy(
                            rope_sc[0][64:128, :], ps_proj[pp_][0:64, :]))(pp),
                            deps=[e_mm, e_r1], inc=vsS)
                        e_r3 = sch.add("vector", (lambda pp_, t_: lambda v: v.tensor_tensor(
                            rope_sc[1][:], ps_proj[pp_][:], cos_t[:, t_ * 512:(t_ + 1) * 512], ALU.mult))(pp, tsl),
                            deps=[e_mm, e_r1] + rope_evs, inc=vsS)
                        e_r4 = sch.add("vector", (lambda t_: lambda v: v.tensor_tensor(
                            rope_sc[2][:], rope_sc[0][:], sin_t[:, t_ * 512:(t_ + 1) * 512], ALU.mult))(tsl),
                            deps=[e_r2, e_r3] + rope_evs, inc=vsS)
                        e_evq = sch.add("vector", (lambda d_, t_: lambda v: v.tensor_tensor(
                            d_[:, t_ * 512:(t_ + 1) * 512], rope_sc[1][:], rope_sc[2][:], ALU.add))(dest[br], tsl),
                            deps=[e_r4, prev_head_end[0]], inc=vsS)
                        prev_rope[0] = e_evq
                    else:
                        e_evq = sch.add("scalar", (lambda d_, t_, pp_: lambda s: s.copy(
                            d_[:, t_ * 512:(t_ + 1) * 512], ps_proj[pp_][:]))(dest[br], tsl, pp),
                            deps=[e_mm, prev_head_end[0]], inc=ssS)
                    prev_pproj[pp] = e_evq
                    qk_done.append(e_evq)
        qk_all = qk_done[-1]

        e_vms = sch.add("vector", lambda v: v.memset(v_tm[:], 1.0),
                        deps=[prev_head_end[0]], inc=vsS)
        v_done = []
        for kt in range(NT):
            pp = pcnt % 2
            pcnt += 1
            def vproj(kt_, pp_, hb_=hb):
                def f(te):
                    ins = None
                    for kc in range(KC):
                        ins = te.matmul(ps_proj[pp_][:, 0:DV],
                                        x_bf[kc][:, kt_ * 128:(kt_ + 1) * 128],
                                        w_bf[hb_]["wv"][:, kc * FH: kc * FH + DV],
                                        start=(kc == 0), stop=(kc == KC - 1))
                    return ins
                return f
            e_vm = sch.add("tensor", vproj(kt, pp),
                           deps=[w_evs["wv"], prev_pproj.get(pp)] + x_done, inc=mmS)
            e_ve = sch.add("scalar", (lambda kt_, pp_: lambda s: s.copy(
                v_tm[:, kt_ * DVA: kt_ * DVA + DV], ps_proj[pp_][:, 0:DV]))(kt, pp),
                deps=[e_vm, e_vms], inc=ssS)
            prev_pproj[pp] = e_ve
            v_done.append(e_ve)
        v_all = v_done[-1]

        head_last = None
        for tq in range(NQ5):
            nch = 4 * tq + 4
            osb0_evs = {}
            for br in range(NBR):
                pt_evs = []
                for j in range(nch):
                    pb = j % 2
                    def sct(j_, tq_, br_, pb_):
                        def f(te):
                            return te.matmul(ps_sc[pb_][:],
                                             k_bf[br_][:, j_ * 128:(j_ + 1) * 128],
                                             q_bf[br_][:, tq_ * 512:(tq_ + 1) * 512],
                                             start=True, stop=(j_ < 4 * tq_))
                        return f
                    e_s = sch.add("tensor", sct(j, tq, br, pb),
                                  deps=[qk_all, prev_psc.get(pb)], inc=mmS)
                    if j >= 4 * tq:
                        r_i = j - 4 * tq
                        def mskmm(r_, pb_):
                            def f(te):
                                return te.matmul(ps_sc[pb_][:], ident_bf[:],
                                                 mask_bf[:, r_ * 512:(r_ + 1) * 512],
                                                 start=False, stop=True)
                            return f
                        e_s = sch.add("tensor", mskmm(r_i, pb), deps=[e_s, e_mc, e_ic], inc=mmS)
                    e_x = sch.add("scalar", (lambda j_, pb_: lambda s: s.activation(
                        pt[:, j_ * 512:(j_ + 1) * 512], ps_sc[pb_][:], AF.Exp,
                        bias=shift_t[:], scale=SM_SCALE))(j, pb),
                        deps=[e_s, e_sh], inc=ssS)
                    prev_psc[pb] = e_x
                    pt_evs.append(e_x)
                pt_all = pt_evs[-1]

                for qs in range(4):
                    pb = qs % 2
                    def pvg(qs_, ncl_, pb_):
                        def f(te):
                            ins = None
                            for j in range(ncl_):
                                ins = te.matmul(ps_pv[pb_][:],
                                                pt[:, j * 512 + qs_ * 128: j * 512 + qs_ * 128 + 128],
                                                v_tm[:, j * DVA:(j + 1) * DVA],
                                                start=(j == 0), stop=(j == ncl_ - 1))
                            return ins
                        return f
                    e_pv = sch.add("tensor", pvg(qs, nch, pb),
                                   deps=[pt_all, v_all, prev_ppv.get(pb), e_lbc], inc=mmS)
                    e_rc2 = sch.add("vector", (lambda pb_, br_: lambda v: v.reciprocal(
                        rec[br_][:], ps_pv[pb_][:, DV:DVA]))(pb, br),
                        deps=[e_pv, prev_rec.get(br)], inc=vsS)
                    if layer == 1:
                        ab = st_ctr[0] % 2
                        e_n = sch.add("vector", (lambda pb_, ab_: lambda v: v.tensor_scalar(
                            a_f32[ab_][:, 0:DV], ps_pv[pb_][:, 0:DV], rec[0][:], None, ALU.mult))(pb, ab),
                            deps=[e_rc2, prev_af.get(ab)], inc=vsS)
                        e_st = emit_a_out(e_n, ab, h, tq, qs)
                        prev_af[ab] = e_st
                        prev_rec[0] = e_n
                        prev_ppv[pb] = e_n
                        head_last = e_st
                    elif br == 0:
                        e_n = sch.add("vector", (lambda pb_, qs_: lambda v: v.tensor_scalar(
                            o_sb0[:, qs_ * DV:(qs_ + 1) * DV], ps_pv[pb_][:, 0:DV], rec[0][:], None, ALU.mult))(pb, qs),
                            deps=[e_rc2, prev_osb0.get(qs)], inc=vsS)
                        osb0_evs[qs] = e_n
                        prev_rec[0] = e_n
                        prev_ppv[pb] = e_n
                    else:
                        e_c1 = sch.add("vector", lambda v: v.tensor_tensor(
                            nlam_rec[:], rec[1][:], lam_b[:], ALU.mult),
                            deps=[e_rc2, prev_nlam[0]] + lam_evs, inc=vsS)
                        e_c2 = sch.add("vector", (lambda pb_: lambda v: v.tensor_scalar(
                            o1_sc[:], ps_pv[pb_][:, 0:DV], nlam_rec[:], None, ALU.mult))(pb),
                            deps=[e_c1, prev_o1[0]], inc=vsS)
                        prev_rec[1] = e_c2
                        prev_ppv[pb] = e_c2
                        prev_nlam[0] = e_c2
                        e_c3 = sch.add("vector", (lambda qs_: lambda v: v.tensor_tensor(
                            cmb[:], o_sb0[:, qs_ * 256:(qs_ + 1) * 256], o1_sc[:], ALU.subtract))(qs),
                            deps=[e_c2, osb0_evs[qs], prev_cmb[0]], inc=vsS)
                        prev_osb0[qs] = e_c3
                        prev_o1[0] = e_c3
                        e_q1 = sch.add("scalar", lambda s: s.activation(
                            sq_s[:], cmb[:], AF.Square, accum_out=rms["sumsq"][:]),
                            deps=[e_c3, prev_rms[0]], inc=ssS)
                        e_q2 = sch.add("vector", lambda v: v.tensor_scalar(
                            rms["ms"][:], rms["sumsq"][:], 1.0 / DV, None, ALU.mult),
                            deps=[e_q1], inc=vsS)
                        e_q3 = sch.add("scalar", lambda s: s.activation(
                            rms["std"][:], rms["ms"][:], AF.Sqrt, bias=eps_t[:]),
                            deps=[e_q2, e_ep], inc=ssS)
                        e_q4 = sch.add("vector", lambda v: v.reciprocal(rms["rstd"][:], rms["std"][:]),
                                       deps=[e_q3], inc=vsS)
                        e_q5 = sch.add("vector", lambda v: v.tensor_scalar(
                            rms["rstd"][:], rms["rstd"][:], 1.0 - LAMBDA_INIT, None, ALU.mult),
                            deps=[e_q4], inc=vsS)
                        ab = st_ctr[0] % 2
                        e_q6 = sch.add("vector", (lambda ab_: lambda v: v.tensor_scalar(
                            a_f32[ab_][:], cmb[:], rms["rstd"][:], None, ALU.mult))(ab),
                            deps=[e_q5, e_c3, prev_af.get(ab)], inc=vsS)
                        e_q7 = sch.add("vector", (lambda ab_: lambda v: v.tensor_tensor(
                            a_f32[ab_][:], a_f32[ab_][:], subln_t[:], ALU.mult))(ab),
                            deps=[e_q6, subln_ev], inc=vsS)
                        prev_rms[0] = e_q7
                        prev_cmb[0] = e_q7
                        e_st = emit_a_out(e_q7, ab, h, tq, qs)
                        prev_af[ab] = e_st
                        head_last = e_st

        prev_head_end[0] = head_last
        for nm in ("wq", "wk", "wv"):
            prev_w_use[(nm, hb)] = head_last
        if h + 2 < NH:
            load_head_weights(h + 2)
    ctx.reset(m0)


def emit_pmm_phase(ctx, S, kch, a_src, w_src, out_dram, a_is_bf16=True,
                   p_src=None, resid_src=None, colscale_src=None, scale64=False,
                   finalize=False, bank_base=0, alt_sems=False, p_dep=(), st_base=0,
                   nb=None):
    sch = ctx.sch
    m0 = ctx.mark()
    TT = TOK_CORE // 128
    OC = D // 512
    NB = nb if nb is not None else (1 if kch > 16 else 2)
    if alt_sems:
        a_ld, w_ld, r_ld, p_ld = S["x_ldB"], S["w_ldB"], S["r_ld"], S["p_ldB"]
    else:
        a_ld, w_ld, r_ld, p_ld = S["x_ld"], S["w_ld"], S["r_ld"], S["p_ld"]
    c_ld, cast, mm, vs, ss = S["c_ld"], S["cast"], S["mm"], S["vs"], S["ss"]
    st_sems = S["st"]
    a_bf = [ctx.tile(TOK_CORE, BF16) for _ in range(kch)]
    w_bf = [ctx.tile(D, BF16) for _ in range(kch)]
    a_f32 = None if a_is_bf16 else [ctx.tile(TOK_CORE, F32) for _ in range(NB)]
    r_f32 = [ctx.tile(D, F32) for _ in range(NB)] if finalize else None
    p_f32 = [ctx.tile(D, F32) for _ in range(NB)] if p_src is not None else None
    cs_t = ctx.tile(D, F32) if colscale_src is not None else None
    y = [ctx.tile(D, F32) for _ in range(NB)]
    o_t = y
    stats = None
    if finalize:
        stats = {nm: ctx.tile(1, F32) for nm in ("sum1", "sumsq", "mean", "var", "m2", "std", "rstd", "nmr")}
        stats["sq_scratch"] = ctx.tile(D, F32)
    eps_t = ctx.consts["eps"]
    ps = [ctx.psum_f32[bank_base + i] for i in range(4)]

    cs_ev = None
    if colscale_src is not None:
        e_cl = sch.add("sync", lambda s: s.dma_start(out=cs_t[:], in_=colscale_src), inc=c_ld, inc_n=16)
        if scale64:
            cs_ev = sch.add("vector", lambda v: v.tensor_scalar(cs_t[:], cs_t[:], 64.0, None, ALU.mult),
                            deps=[e_cl], inc=vs)
        else:
            cs_ev = e_cl

    cast_evs = []
    for k in range(kch):
        buf = k % NB
        abuf = k % 4
        dep_a = [cast_evs[k - 4][0]] if k >= 4 else []
        dep_w = [cast_evs[k - 4][1]] if k >= 4 else []
        if a_is_bf16:
            e_ca = sch.add("sync", (lambda kk: lambda s: s.dma_start(
                out=a_bf[kk][:], in_=a_src(kk)))(k),
                deps=dep_a, inc=a_ld[abuf], inc_n=16)
        else:
            e_la = sch.add("sync", (lambda kk, bb: lambda s: s.dma_start(
                out=a_f32[bb][:], in_=a_src(kk)))(k, buf),
                deps=[cast_evs[k - NB][0]] if k >= NB else [], inc=a_ld[buf], inc_n=16)
            e_ca = sch.add("vector", (lambda kk, bb: lambda v: v.tensor_copy(a_bf[kk][:], a_f32[bb][:]))(k, buf),
                           deps=[e_la], inc=cast)
        e_cw = sch.add("gpsimd", (lambda kk: lambda g: g.dma_start(
            out=w_bf[kk][:], in_=w_src(kk)))(k),
            deps=dep_w, inc=w_ld[k % 2], inc_n=16)
        cast_evs.append((e_ca, e_cw))
    kh = kch // 2
    loads_h1 = [e for pair in cast_evs[:kh] for e in pair if e is not None]
    loads_done = [cast_evs[-1][0], cast_evs[-1][1], (cast, cast.count)] + \
                 [(a_ld[i], a_ld[i].count) for i in range(4)] + \
                 [(w_ld[0], w_ld[0].count), (w_ld[1], w_ld[1].count)]

    prev_ev = {}
    prev_r = {}
    prev_p = {}
    prev_out = {}
    prev_ln_out = None
    for t in range(TT):
        buf = t % NB
        e_lr = None
        e_lp = None
        if finalize:
            e_lr = sch.add("sync", (lambda tt, bb: lambda s: s.dma_start(
                out=r_f32[bb][:], in_=resid_src[tt * 128:(tt + 1) * 128, :]))(t, buf),
                deps=[prev_r.get(buf)], inc=r_ld[buf % 2], inc_n=16)
        if p_src is not None:
            e_lp = sch.add("sync", (lambda tt, bb: lambda s: s.dma_start(
                out=p_f32[bb][:], in_=p_src[tt * 128:(tt + 1) * 128, :]))(t, buf),
                deps=[prev_p.get(buf)] + list(p_dep), inc=p_ld[buf % 2], inc_n=16)
        mm_evs = []
        for o in range(OC):
            def mmgroup(tt, oo, k0, k1):
                def f(te):
                    ins = None
                    for k in range(k0, k1):
                        ins = te.matmul(ps[oo][:],
                                        a_bf[k][:, tt * 128:(tt + 1) * 128],
                                        w_bf[k][:, oo * 512:(oo + 1) * 512],
                                        start=(k == 0), stop=(k == kch - 1))
                    return ins
                return f
            e_mm = sch.add("tensor", mmgroup(t, o, 0, kch),
                           deps=loads_done + [prev_ev.get(o)], inc=mm)
            mm_evs.append(e_mm)
        ev_evs = []
        for o in range(OC):
            sl = slice(o * 512, (o + 1) * 512)
            if p_src is not None:
                e_a = sch.add("vector", (lambda oo, bb, sl_: lambda v: v.tensor_tensor(
                    y[bb][:, sl_], ps[oo][:], p_f32[bb][:, sl_], ALU.add))(o, buf, sl),
                    deps=[mm_evs[o], e_lp, prev_ln_out, prev_out.get(buf)], inc=vs)
            else:
                e_a = sch.add("vector", (lambda oo, bb, sl_: lambda v: v.tensor_copy(
                    y[bb][:, sl_], ps[oo][:]))(o, buf, sl),
                    deps=[mm_evs[o], prev_ln_out, prev_out.get(buf)], inc=vs)
            step = e_a
            if finalize:
                if colscale_src is not None:
                    step = sch.add("vector", (lambda oo, bb, sl_: lambda v: v.tensor_tensor(
                        y[bb][:, sl_], y[bb][:, sl_], cs_t[:, sl_], ALU.mult))(o, buf, sl),
                        deps=[step, cs_ev], inc=vs)
                step = sch.add("vector", (lambda oo, bb, sl_: lambda v: v.tensor_tensor(
                    y[bb][:, sl_], y[bb][:, sl_], r_f32[bb][:, sl_], ALU.add))(o, buf, sl),
                    deps=[step, e_lr], inc=vs)
            prev_ev[o] = e_a
            ev_evs.append(step)
        if finalize:
            e_out = emit_layernorm(sch, y[buf], o_t[buf], stats, eps_t, D,
                                   deps=ev_evs + [ctx.consts["evs"][4]], v_sem=vs, s_sem=ss,
                                   out_free_dep=prev_out.get(buf))
            prev_ln_out = e_out
            src_tile = o_t[buf]
        else:
            e_out = ev_evs[-1]
            src_tile = y[buf]
        e_st = sch.add("gpsimd", (lambda tt, s_: lambda g: g.dma_start(
            out=out_dram[tt * 128:(tt + 1) * 128, :], in_=s_[:]))(t, src_tile),
            deps=ev_evs + [e_out], inc=st_sems[st_base + (buf % 2)], inc_n=16)
        prev_r[buf] = ev_evs[-1]
        prev_p[buf] = ev_evs[-1]
        prev_out[buf] = e_st
    ctx.reset(m0)


def emit_fwht_phase(ctx, S, x_src, su_src, t_out, h2048):
    sch = ctx.sch
    m0 = ctx.mark()
    KC = D // 128
    x_ld, w_ld, mm, vs, ss = S["x_ld"], S["w_ld"], S["mm"], S["vs"], S["ss"]
    sc_ld, st_sems = S["sc_ld"], S["st"]
    xs_bf = [ctx.tile(TOK_CORE, BF16) for _ in range(KC)]
    h_bf = [ctx.tile(D, BF16) for _ in range(KC)]
    su_t = ctx.tile(KC, F32)
    to_t = [ctx.tile(TOK_CORE, BF16) for _ in range(2)]
    ps2 = [ctx.psum_f32[0], ctx.psum_f32[1]]

    for k in range(KC):
        sch.add("sync", (lambda kk: lambda s: s.dma_start(
            out=su_t[:, kk:kk + 1], in_=su_src[kk * 128:(kk + 1) * 128, :]))(k),
            inc=sc_ld[0], inc_n=16)
    e_su = (sc_ld[0], sc_ld[0].count)

    xev = []
    hev = []
    for k in range(KC):
        buf = k % 2
        e_lx = sch.add("sync", (lambda kk: lambda s: s.dma_start(
            out=xs_bf[kk][:], in_=x_src[kk * 128:(kk + 1) * 128, :]))(k),
            deps=[xev[k - 2]] if k >= 2 else [], inc=x_ld[buf], inc_n=16)
        e_ps = sch.add("vector", (lambda kk: lambda v: v.tensor_scalar(
            xs_bf[kk][:], xs_bf[kk][:], su_t[:, kk:kk + 1], None, ALU.mult))(k),
            deps=[e_lx, e_su], inc=vs)
        xev.append(e_ps)
        e_lh = sch.add("scalar", (lambda kk: lambda s: s.dma_start(
            out=h_bf[kk][:], in_=h2048[kk * 128:(kk + 1) * 128, :]))(k),
            deps=[hev[k - 2]] if k >= 2 else [], inc=w_ld[k % 2], inc_n=16)
        hev.append(e_lh)
    loads = [xev[-1], hev[-1], (vs, vs.count),
             (x_ld[0], x_ld[0].count), (x_ld[1], x_ld[1].count),
             (w_ld[0], w_ld[0].count), (w_ld[1], w_ld[1].count)]

    st_prev = [None, None]
    prev_ps = {}
    for fo in range(KC):
        sl2 = fo % 2
        ev_th = []
        for th in range(2):
            pp = th
            def mmg(fo_, th_, pp_):
                def f(te):
                    ins = None
                    for fi in range(KC):
                        ins = te.matmul(ps2[pp_][:],
                                        h_bf[fi][:, fo_ * 128:(fo_ + 1) * 128],
                                        xs_bf[fi][:, th_ * 512:(th_ + 1) * 512],
                                        start=(fi == 0), stop=(fi == KC - 1))
                    return ins
                return f
            e_m = sch.add("tensor", mmg(fo, th, pp), deps=loads + [prev_ps.get(pp)], inc=mm)
            e_e = sch.add("scalar", (lambda th_, pp_, sl_: lambda s: s.activation(
                to_t[sl_][:, th_ * 512:(th_ + 1) * 512], ps2[pp_][:], AF.Copy,
                scale=FWHT_SCALE_2048))(th, pp, sl2),
                deps=[e_m, st_prev[sl2]], inc=ss)
            prev_ps[pp] = e_e
            ev_th.append(e_e)
        e_st = sch.add("gpsimd", (lambda fo_, sl_: lambda g: g.dma_start(
            out=t_out[fo_ * 128:(fo_ + 1) * 128, :], in_=to_t[sl_][:]))(fo, sl2),
            deps=ev_th, inc=st_sems[sl2], inc_n=16)
        st_prev[sl2] = e_st
    ctx.reset(m0)


def emit_ffn_up_phase(ctx, S, layer, x_src, gw, uw, gs, us, su, h_out):
    sch = ctx.sch
    m0 = ctx.mark()
    NI = (INTER1 if layer == 1 else INTER0) // 128
    KC = D // 128
    x_ld, gw_ld, uw_ld, sc_ld = S["x_ld"], S["w_ld"][0:2], S["w_ld"][2:4], S["sc_ld"]
    c_ld, cast, mm, vs, ss, st_sems = S["c_ld"], S["cast"], S["mm"], S["vs"], S["ss"], S["st"]
    x_bf = [ctx.tile(TOK_CORE, BF16) for _ in range(KC)]
    gw_bf = [ctx.tile(2048, BF16) for _ in range(2)]
    uw_bf = [ctx.tile(2048, BF16) for _ in range(2)]
    scs = [ctx.tile(4, F32) for _ in range(2)]
    sg_t = [ctx.tile(512, F32) for _ in range(2)]
    if layer == 1:
        hbuf = [ctx.tile(TOK_CORE, BF16) for _ in range(NI)]
        btmp = ctx.tile(TOK_CORE, BF16)
        ho_t = [ctx.tile(512, BF16) for _ in range(4)]
    else:
        ho_t = [ctx.tile(TOK_CORE, BF16) for _ in range(4)]
    ps_g = [ctx.psum_f32[0], ctx.psum_f32[1]]
    ps_u = [ctx.psum_f32[2], ctx.psum_f32[3]]
    h128_b = ctx.consts["h128_bf"]
    h_evs = [ctx.consts["evs"][2]]

    x_cast_evs = []
    for k in range(KC):
        buf = k % 2
        e_cx = sch.add("sync", (lambda kk: lambda s: s.dma_start(
            out=x_bf[kk][:], in_=x_src[kk * 128:(kk + 1) * 128, :]))(k),
            deps=[x_cast_evs[k - 2]] if k >= 2 else [], inc=x_ld[buf], inc_n=16)
        x_cast_evs.append(e_cx)
    x_done = [(x_ld[0], x_ld[0].count), (x_ld[1], x_ld[1].count)]

    st_prev = [None] * 4
    prev_wcast = {}
    prev_wuse = {}
    prev_scuse = {}
    prev_sg = {}
    prev_psg = {}
    prev_psu = {}
    prev_hbuf = {}
    hbuf_evs = []
    tile_ev = {}
    prev_btmp = [None]

    for i in range(NI):
        wb = i % 2
        e_cg = sch.add("gpsimd", (lambda i_, bb: lambda g: g.dma_start(
            out=gw_bf[bb][:].rearrange("p (c f) -> p c f", c=KC),
            in_=gw[:, i_ * 128:(i_ + 1) * 128].rearrange("(c p) f -> p c f", p=128)))(i, wb),
            deps=[prev_wuse.get(("g", wb))], inc=gw_ld[0], inc_n=16)
        e_cu = sch.add("gpsimd", (lambda i_, bb: lambda g: g.dma_start(
            out=uw_bf[bb][:].rearrange("p (c f) -> p c f", c=KC),
            in_=uw[:, i_ * 128:(i_ + 1) * 128].rearrange("(c p) f -> p c f", p=128)))(i, wb),
            deps=[prev_wuse.get(("u", wb)), e_cg], inc=uw_ld[0], inc_n=16)
        wb2 = i % 2
        e_ls = sch.add("sync", (lambda i_, bb: lambda s: s.dma_start(
            out=scs[bb][:, 0:1], in_=gs[i_ * 128:(i_ + 1) * 128, :]))(i, wb2),
            deps=[prev_scuse.get(wb2)], inc=sc_ld[wb2], inc_n=16)
        e_ls2 = sch.add("sync", (lambda i_, bb: lambda s: s.dma_start(
            out=scs[bb][:, 1:2], in_=us[i_ * 128:(i_ + 1) * 128, :]))(i, wb2),
            deps=[e_ls], inc=sc_ld[wb2], inc_n=16)
        if layer == 1:
            e_ls3 = sch.add("sync", (lambda i_, bb: lambda s: s.dma_start(
                out=scs[bb][:, 2:3], in_=su[i_ * 128:(i_ + 1) * 128, :]))(i, wb2),
                deps=[e_ls2], inc=sc_ld[wb2], inc_n=16)
            e_sc1 = sch.add("vector", (lambda bb: lambda v: v.tensor_scalar(
                scs[bb][:, 3:4], scs[bb][:, 1:2], 64.0, None, ALU.mult))(wb2),
                deps=[e_ls3], inc=vs)
            e_sc2 = sch.add("vector", (lambda bb: lambda v: v.tensor_tensor(
                scs[bb][:, 3:4], scs[bb][:, 3:4], scs[bb][:, 2:3], ALU.mult))(wb2),
                deps=[e_sc1], inc=vs)
            e_sc3 = sch.add("vector", (lambda bb: lambda v: v.tensor_scalar(
                scs[bb][:, 0:1], scs[bb][:, 0:1], 64.0, None, ALU.mult))(wb2),
                deps=[e_sc2], inc=vs)
            e_scales = e_sc3
            comb_col = 3
        else:
            e_scales = e_ls2
            comb_col = 1
        for th in range(2):
            tsl = slice(th * 512, (th + 1) * 512)
            pp = th
            def gmm(th_, bb, pp_, which):
                wsrc = gw_bf if which == "g" else uw_bf
                pdst = ps_g if which == "g" else ps_u
                def f(te):
                    ins = None
                    for kc in range(KC):
                        ins = te.matmul(pdst[pp_][:],
                                        wsrc[bb][:, kc * 128:(kc + 1) * 128],
                                        x_bf[kc][:, th_ * 512:(th_ + 1) * 512],
                                        start=(kc == 0), stop=(kc == KC - 1))
                    return ins
                return f
            e_gm = sch.add("tensor", gmm(th, wb, pp, "g"),
                           deps=x_done + [e_cg, prev_psg.get(pp)], inc=mm)
            e_um = sch.add("tensor", gmm(th, wb, pp, "u"),
                           deps=x_done + [e_cu, prev_psu.get(pp)], inc=mm)
            e_sg = sch.add("scalar", (lambda bb, pp_, sgb: lambda s: s.activation(
                sg_t[sgb][:], ps_g[pp_][:], AF.Silu, scale=scs[bb][:, 0:1]))(wb2, pp, th),
                deps=[e_gm, e_scales, prev_sg.get(th)], inc=ss)
            prev_psg[pp] = e_sg
            if layer == 1:
                dst = hbuf[i]
                ddep = prev_hbuf.get(i)
            else:
                dst = ho_t[2 * (i % 2) + th]
                ddep = st_prev[2 * (i % 2) + th]
            e_h2 = sch.add("vector", (lambda bb, pp_, sgb, d_, t2: lambda v: v.scalar_tensor_tensor(
                d_[:, t2] if layer == 1 else d_[:, 0:512],
                ps_u[pp_][:], scs[bb][:, comb_col:comb_col + 1], sg_t[sgb][:],
                ALU.mult, ALU.mult))(wb2, pp, th, dst, tsl),
                deps=[e_um, e_sg, e_scales, ddep], inc=vs)
            prev_psu[pp] = e_h2
            prev_sg[th] = e_h2
            if layer == 1:
                prev_hbuf[i] = e_h2
                hbuf_evs.append(e_h2)
            prev_wuse[("g", wb)] = e_sg
            prev_wuse[("u", wb)] = e_h2
            prev_scuse[wb2] = e_h2
            if layer == 0:
                sl4 = 2 * (i % 2) + th
                e_st = sch.add("gpsimd", (lambda i_, th_, d_: lambda g: g.dma_start(
                    out=h_out[i_ * 128:(i_ + 1) * 128, th_ * 512:(th_ + 1) * 512],
                    in_=d_[:, 0:512]))(i, th, dst),
                    deps=[e_h2, st_prev[sl4]], inc=st_sems[sl4], inc_n=16)
                st_prev[sl4] = e_st
        if layer == 1:
            tile_ev[i] = hbuf_evs[2 * i + 1]
            g2 = i
            for s in range(6):
                span = 1 << (s + 1)
                if (g2 + 1) % span == 0:
                    base = g2 + 1 - span
                    for c in range(base, base + (span // 2)):
                        a, b = c, c + (span // 2)
                        e1 = sch.add("vector", (lambda a_, b_: lambda v: v.tensor_tensor(
                            btmp[:], hbuf[a_][:], hbuf[b_][:], ALU.subtract))(a, b),
                            deps=[tile_ev[a], tile_ev[b], prev_btmp[0]], inc=vs)
                        e2 = sch.add("vector", (lambda a_, b_: lambda v: v.tensor_tensor(
                            hbuf[a_][:], hbuf[a_][:], hbuf[b_][:], ALU.add))(a, b),
                            deps=[e1], inc=vs)
                        e3 = sch.add("vector", (lambda b_: lambda v: v.tensor_copy(hbuf[b_][:], btmp[:]))(b),
                            deps=[e2], inc=vs)
                        tile_ev[a] = e2
                        tile_ev[b] = e3
                        prev_btmp[0] = e3

    if layer == 1:
        prev_ps = {}
        for c in range(NI):
            for th in range(2):
                pp = th
                e_m = sch.add("tensor", (lambda c_, th_, pp_: lambda te: te.matmul(
                    ps_g[pp_][:], h128_b[:], hbuf[c_][:, th_ * 512:(th_ + 1) * 512],
                    start=True, stop=True))(c, th, pp),
                    deps=[tile_ev[c]] + h_evs + [prev_ps.get(pp)], inc=mm)
                sl4 = 2 * (c % 2) + th
                e_e = sch.add("scalar", (lambda pp_, sl_: lambda s: s.activation(
                    ho_t[sl_][:], ps_g[pp_][:], AF.Copy, scale=FWHT_SCALE_8192))(pp, sl4),
                    deps=[e_m, st_prev[sl4]], inc=ss)
                prev_ps[pp] = e_e
                e_st = sch.add("gpsimd", (lambda c_, th_, sl_: lambda g: g.dma_start(
                    out=h_out[c_ * 128:(c_ + 1) * 128, th_ * 512:(th_ + 1) * 512],
                    in_=ho_t[sl_][:]))(c, th, sl4),
                    deps=[e_e], inc=st_sems[sl4], inc_n=16)
                st_prev[sl4] = e_st
    ctx.reset(m0)



def emit_wopart_phase(ctx, S, a_tiles, wo_my, part_out_d):
    sch = ctx.sch
    m0 = ctx.mark()
    a_ld, w_ld, cast, mm, vs, ss, st_sems = (S["x_ld"], S["w_ld"], S["cast"],
                                             S["mm"], S["vs"], S["ss"], S["st"])
    a_bf = a_tiles
    w_bf = [ctx.tile(D, BF16) for _ in range(8)]
    ctile = [ctx.tile(D, BF16) for _ in range(2)]
    ps = [ctx.psum_f32[i] for i in range(4)]
    cast_evs = []
    for k in range(8):
        buf = k % 2
        e_cw = sch.add("gpsimd", (lambda kk: lambda g: g.dma_start(
            out=w_bf[kk][:], in_=wo_my[kk * 128:(kk + 1) * 128, :]))(k),
            deps=[cast_evs[k - 2] if k >= 2 else None], inc=w_ld[buf], inc_n=16)
        cast_evs.append(e_cw)
    loads = [cast_evs[-1],
             (w_ld[0], w_ld[0].count), (w_ld[1], w_ld[1].count)]
    prev_ev = {}
    prev_ct = {}
    for tt in range(16):
        cb = tt % 2
        ev_o = []
        for o in range(4):
            def mmg(tt_, oo):
                def f(te):
                    ins = None
                    for k in range(8):
                        ins = te.matmul(ps[oo][:],
                                        a_bf[k][:, tt_ * 128:(tt_ + 1) * 128],
                                        w_bf[k][:, oo * 512:(oo + 1) * 512],
                                        start=(k == 0), stop=(k == 7))
                    return ins
                return f
            e_mm = sch.add("tensor", mmg(tt, o), deps=loads + [prev_ev.get(o)], inc=mm)
            e_ev = sch.add("scalar", (lambda oo, cb_: lambda s: s.copy(
                ctile[cb_][:, oo * 512:(oo + 1) * 512], ps[oo][:]))(o, cb),
                deps=[e_mm, prev_ct.get(cb)], inc=ss)
            prev_ev[o] = e_ev
            ev_o.append(e_ev)
        e_st = sch.add("gpsimd", (lambda tt_, cb_: lambda g: g.dma_start(
            out=part_out_d[tt_ * 128:(tt_ + 1) * 128, :], in_=ctile[cb_][:]))(tt, cb),
            deps=ev_o, inc=st_sems[cb], inc_n=16)
        prev_ct[cb] = e_st
    ctx.reset(m0)


def emit_resln_phase(ctx, S, rs_out_d, resid_src, out_d, x_dep=None):
    sch = ctx.sch
    m0 = ctx.mark()
    a_ld, r_ld, cast, vs, ss, st_sems = (S["x_ld"], S["r_ld"], S["cast"],
                                         S["vs"], S["ss"], S["st"])
    w_bf = [ctx.tile(D, BF16) for _ in range(2)]
    r_f32 = [ctx.tile(D, F32) for _ in range(2)]
    y = [ctx.tile(D, F32) for _ in range(2)]
    stats = {nm: ctx.tile(1, F32) for nm in ("sum1", "sumsq", "mean", "var", "m2", "std", "rstd", "nmr")}
    stats["sq_scratch"] = ctx.tile(D, F32)
    eps_t = ctx.consts["eps"]
    prev = {}
    prev_ln = None
    for t in range(8):
        buf = t % 2
        e_lw = sch.add("sync", (lambda tt, bb: lambda s: s.dma_start(
            out=w_bf[bb][:], in_=rs_out_d[tt * 128:(tt + 1) * 128, :]))(t, buf),
            deps=[prev.get(("w", buf)), x_dep], inc=a_ld[buf], inc_n=16)
        e_lr = sch.add("sync", (lambda tt, bb: lambda s: s.dma_start(
            out=r_f32[bb][:], in_=resid_src[tt * 128:(tt + 1) * 128, :]))(t, buf),
            deps=[prev.get(("r", buf))], inc=r_ld[buf], inc_n=16)
        e_y = sch.add("vector", (lambda bb: lambda v: v.tensor_tensor(
            y[bb][:], w_bf[bb][:], r_f32[bb][:], ALU.add))(buf),
            deps=[e_lw, e_lr, prev_ln, prev.get(("st", buf))], inc=S["vs"])
        e_out = emit_layernorm(sch, y[buf], y[buf], stats, eps_t, D,
                               deps=[e_y, ctx.consts["evs"][4]], v_sem=S["vs"], s_sem=S["ss"])
        prev_ln = e_out
        e_st = sch.add("gpsimd", (lambda tt, bb: lambda g: g.dma_start(
            out=out_d[tt * 128:(tt + 1) * 128, :], in_=y[bb][:]))(t, buf),
            deps=[e_out], inc=st_sems[buf], inc_n=16)
        prev[("w", buf)] = e_y
        prev[("r", buf)] = e_y
        prev[("st", buf)] = e_st
    ctx.reset(m0)



def down_chain(ctx, S, sch, widths, h_d, w_ap, pa_d, pb_d, out_d,
               resid_src, colscale_src, scale64, out_final=None):
    base = ctx.mark()
    regions = [base, base + 46 * 1024]
    snap_half = [None, None]
    snap_prev = sch.snapshot()
    pcur, pnxt = pa_d, pb_d
    off = 0
    n = len(widths)
    prev_store_evs = []
    for i, kw in enumerate(widths):
        half = i % 2
        ctx.set_base(regions[half])
        sch.set_pending(snap_half[half] or snap_prev)
        last = (i == n - 1)
        out_ap = (out_final if out_final is not None else out_d.ap()) if last else pnxt.ap()
        emit_pmm_phase(
            ctx, S, kw,
            a_src=(lambda o: lambda k: h_d.ap()[(o + k) * 128:(o + k + 1) * 128, :])(off),
            w_src=(lambda o: lambda k: w_ap[(o + k) * 128:(o + k + 1) * 128, :])(off),
            out_dram=out_ap,
            p_src=pcur.ap() if i > 0 else None,
            resid_src=resid_src if last else None,
            colscale_src=colscale_src if last else None,
            scale64=scale64 if last else False,
            finalize=last,
            bank_base=4 * half,
            alt_sems=(half == 1),
            p_dep=prev_store_evs,
            st_base=2 * half,
            nb=1)
        snap_half[half] = sch.snapshot()
        prev_store_evs = [(S["st"][2 * half], S["st"][2 * half].count),
                          (S["st"][2 * half + 1], S["st"][2 * half + 1].count)]
        off += kw
        pcur, pnxt = pnxt, pcur
    ctx.set_base(base)
    sch.set_pending(snap_half[0] + (snap_half[1] or []))


def build_fused_program(dbg=False):
    nc = bass.Bass("TRN2", target_bir_lowering=False, debug=False)
    f32 = F32
    ap = {}
    ap["x_rows"] = nc.dram_tensor("x_rows", [TOK_CORE, D], f32, kind="ExternalInput").ap()
    for nm in ("wq0", "wk0", "wv0"):
        ap[nm] = nc.dram_tensor(nm, [D, 1024], f32, kind="ExternalInput").ap()
    ap["wo0"] = nc.dram_tensor("wo0", [1024, D], f32, kind="ExternalInput").ap()
    ap["gwT0"] = nc.dram_tensor("gwT0", [D, INTER0], f32, kind="ExternalInput").ap()
    ap["uwT0"] = nc.dram_tensor("uwT0", [D, INTER0], f32, kind="ExternalInput").ap()
    ap["dwT0"] = nc.dram_tensor("dwT0", [INTER0, D], f32, kind="ExternalInput").ap()
    ap["gs0"] = nc.dram_tensor("gs0", [INTER0, 1], f32, kind="ExternalInput").ap()
    ap["us0"] = nc.dram_tensor("us0", [INTER0, 1], f32, kind="ExternalInput").ap()
    ap["ds_b"] = nc.dram_tensor("ds_b", [128, D], f32, kind="ExternalInput").ap()
    for nm in ("wq1", "wk1", "wv1"):
        ap[nm] = nc.dram_tensor(nm, [D, 1024], f32, kind="ExternalInput").ap()
    ap["wo1T"] = nc.dram_tensor("wo1T", [1024, D], f32, kind="ExternalInput").ap()
    ap["su_ug"] = nc.dram_tensor("su_ug", [D, 1], f32, kind="ExternalInput").ap()
    ap["gwT1"] = nc.dram_tensor("gwT1", [D, INTER1], f32, kind="ExternalInput").ap()
    ap["uwT1"] = nc.dram_tensor("uwT1", [D, INTER1], f32, kind="ExternalInput").ap()
    ap["gs1"] = nc.dram_tensor("gs1", [INTER1, 1], f32, kind="ExternalInput").ap()
    ap["us1"] = nc.dram_tensor("us1", [INTER1, 1], f32, kind="ExternalInput").ap()
    ap["su_dp"] = nc.dram_tensor("su_dp", [INTER1, 1], f32, kind="ExternalInput").ap()
    ap["wdT1"] = nc.dram_tensor("wdT1", [INTER1, D], f32, kind="ExternalInput").ap()
    ap["wdp_b"] = nc.dram_tensor("wdp_b", [128, D], f32, kind="ExternalInput").ap()
    ap["masks"] = nc.dram_tensor("masks", [4, 128, 512], f32, kind="ExternalInput").ap()
    ap["ident"] = nc.dram_tensor("ident", [128, 128], f32, kind="ExternalInput").ap()
    ap["h128"] = nc.dram_tensor("h128", [128, 128], f32, kind="ExternalInput").ap()
    ap["h2048"] = nc.dram_tensor("h2048", [D, D], BF16, kind="ExternalInput").ap()
    ap["rc"] = nc.dram_tensor("rc", [128, T_SEQ], f32, kind="ExternalInput").ap()
    ap["rs"] = nc.dram_tensor("rs", [128, T_SEQ], f32, kind="ExternalInput").ap()
    ap["lamv"] = nc.dram_tensor("lamv", [1, 512], f32, kind="ExternalInput").ap()
    ap["subln_b"] = nc.dram_tensor("subln_b", [128, 256], f32, kind="ExternalInput").ap()
    out_rows = nc.dram_tensor("out_rows", [TOK_CORE, D], f32, kind="ExternalOutput").ap()

    dr = lambda nm, shp, dt: nc.dram_tensor(nm, shp, dt)
    cc_x_in = [dr(f"cc_x_in{l}", [D, TOK_CORE], BF16) for l in range(2)]
    cc_x_out = [dr(f"cc_x_out{l}", [2 * D, TOK_CORE], BF16) for l in range(2)]
    cc_p_in = [dr(f"cc_p_in{l}", [T_SEQ, D], BF16) for l in range(2)]
    cc_p_out = [dr(f"cc_p_out{l}", [TOK_CORE, D], BF16) for l in range(2)]
    r1_d = dr("r1_d", [TOK_CORE, D], f32)
    r2_d = dr("r2_d", [TOK_CORE, D], f32)
    r3_d = dr("r3_d", [TOK_CORE, D], f32)
    xfm_d = dr("xfm_d", [D, TOK_CORE], BF16)
    t_d = dr("t_d", [D, TOK_CORE], BF16)
    h0_d = dr("h0_d", [INTER0, TOK_CORE], BF16)
    h2_d = dr("h2_d", [INTER1, TOK_CORE], BF16)
    pa_d = dr("pa_d", [TOK_CORE, D], f32)
    pb_d = dr("pb_d", [TOK_CORE, D], f32)

    sch = Sched()
    S = fsem_pool(sch)

    with contextlib.ExitStack() as stack:
        slab = stack.enter_context(nc.sbuf_tensor("slab", [128, SLAB_BF16_ELEMS], BF16))
        psum_f32 = [stack.enter_context(nc.psum_tensor(f"pb_{i}", [128, 512], F32)) for i in range(8)]
        ctx = Fctx(nc, sch, slab, [p for p in psum_f32], None)
        ctx.psum_f32 = [p[:] for p in psum_f32]
        ctx.psum_bf16 = [psum_f32[6][:].bitcast(BF16)[:, 0:128],
                         psum_f32[7][:].bitcast(BF16)[:, 0:128]]

        emit_const_phase(ctx, S, ap)
        base_mark = ctx.mark()
        sch.barrier()

        def bar():
            sch.barrier()

        emit_transpose_phase(ctx, S, ap["x_rows"], cc_x_in[0].ap(), TOK_CORE, D); bar()
        cc_ev = emit_collective(ctx, S, "AllGather", cc_x_in[0], cc_x_out[0])
        sch.barrier(exclude=[S["cc"]])
        afm_mark = ctx.mark()
        afm_tiles = [ctx.tile(T_SEQ, BF16) for _ in range(8)]
        emit_attn_phase(ctx, S, 0, cc_x_out[0].ap(), ap["wq0"], ap["wk0"], ap["wv0"], afm_tiles,
                        x_dep=cc_ev); bar()
        emit_wopart_phase(ctx, S, afm_tiles, ap["wo0"], cc_p_in[0].ap()); bar()
        ctx.reset(afm_mark)
        cc_ev = emit_collective(ctx, S, "ReduceScatter", cc_p_in[0], cc_p_out[0], op=ALU.add)
        sch.barrier(exclude=[S["cc"]])
        emit_resln_phase(ctx, S, cc_p_out[0].ap(), ap["x_rows"], r1_d.ap(), x_dep=cc_ev); bar()
        emit_transpose_phase(ctx, S, r1_d.ap(), xfm_d.ap(), TOK_CORE, D); bar()
        emit_ffn_up_phase(ctx, S, 0, xfm_d.ap(), ap["gwT0"], ap["uwT0"],
                          ap["gs0"], ap["us0"], None, h0_d.ap()); bar()
        chain0 = [8, 8, 8, 8, 8, 4]
        down_chain(ctx, S, sch, chain0, h0_d, ap["dwT0"], pa_d, pb_d, r2_d,
                   resid_src=r1_d.ap(), colscale_src=ap["ds_b"], scale64=False)
        bar()

        emit_transpose_phase(ctx, S, r2_d.ap(), cc_x_in[1].ap(), TOK_CORE, D); bar()
        cc_ev = emit_collective(ctx, S, "AllGather", cc_x_in[1], cc_x_out[1])
        sch.barrier(exclude=[S["cc"]])
        afm_mark = ctx.mark()
        afm_tiles = [ctx.tile(T_SEQ, BF16) for _ in range(8)]
        emit_attn_phase(ctx, S, 1, cc_x_out[1].ap(), ap["wq1"], ap["wk1"], ap["wv1"], afm_tiles,
                        x_dep=cc_ev, rc_ap=ap["rc"], rs_ap=ap["rs"]); bar()
        emit_wopart_phase(ctx, S, afm_tiles, ap["wo1T"], cc_p_in[1].ap()); bar()
        ctx.reset(afm_mark)
        cc_ev = emit_collective(ctx, S, "ReduceScatter", cc_p_in[1], cc_p_out[1], op=ALU.add)
        sch.barrier(exclude=[S["cc"]])
        emit_resln_phase(ctx, S, cc_p_out[1].ap(), r2_d.ap(), r3_d.ap(), x_dep=cc_ev); bar()
        emit_transpose_phase(ctx, S, r3_d.ap(), xfm_d.ap(), TOK_CORE, D); bar()
        emit_fwht_phase(ctx, S, xfm_d.ap(), ap["su_ug"], t_d.ap(), ap["h2048"]); bar()
        emit_ffn_up_phase(ctx, S, 1, t_d.ap(), ap["gwT1"], ap["uwT1"],
                          ap["gs1"], ap["us1"], ap["su_dp"], h2_d.ap()); bar()
        chain1 = [8] * 8
        down_chain(ctx, S, sch, chain1, h2_d, ap["wdT1"], pa_d, pb_d, None,
                   resid_src=r3_d.ap(), colscale_src=ap["wdp_b"], scale64=True,
                   out_final=out_rows)

        if dbg:
            sch.barrier()
            dbg_outs = {
                "dbg_xfm0": (cc_x_in[0], [D, TOK_CORE], BF16),
                "dbg_xag0": (cc_x_out[0], [2 * D, TOK_CORE], BF16),
                "dbg_rs0": (cc_p_out[0], [TOK_CORE, D], BF16),
                "dbg_r1": (r1_d, [TOK_CORE, D], f32),
                "dbg_h0": (h0_d, [INTER0, TOK_CORE], BF16),
                "dbg_r2": (r2_d, [TOK_CORE, D], f32),
                "dbg_r3": (r3_d, [TOK_CORE, D], f32),
                "dbg_t": (t_d, [D, TOK_CORE], BF16),
                "dbg_h2": (h2_d, [INTER1, TOK_CORE], BF16),
            }
            for i, (nm, (src_t, shp, dt_)) in enumerate(dbg_outs.items()):
                o = nc.dram_tensor(nm, shp, dt_, kind="ExternalOutput").ap()
                sch.add("sync", (lambda o_, s_: lambda s: s.dma_start(out=o_, in_=s_.ap()))(o, src_t),
                        inc=S["cc"], inc_n=16)
        sch.add("gpsimd", lambda g: None,
                deps=[(s, s.count) for s in S["st"]] + [(S["cc"], S["cc"].count)])
        sch.emit(nc)
    return nc


def kernel_fused(**inputs):
    f32 = np.float32
    g = {k: np.asarray(v, f32) for k, v in inputs.items()}
    x = g["L0_attn_x"]
    x_flat = x.reshape(TOK_GLOBAL, D)
    masks = host_masks()
    ident = np.eye(128, dtype=f32)
    H128 = host_hadamard128()
    import ml_dtypes
    H2 = np.array([[1.0]], dtype=np.float64)
    while H2.shape[0] < D:
        H2 = np.block([[H2, H2], [H2, -H2]])
    H2048 = np.ascontiguousarray(H2.astype(ml_dtypes.bfloat16))
    cosT, sinT = host_rope_tables()
    lamv = np.concatenate([g["L0_attn_lambda_q1"], g["L0_attn_lambda_k1"],
                           g["L0_attn_lambda_q2"], g["L0_attn_lambda_k2"]])[None, :]
    subln_b = _ct(np.broadcast_to(g["L0_attn_subln_weight"], (128, 256)))
    ds_b = _ct(np.broadcast_to(g["L0_ffn_down_s"], (128, D)))
    wdp_b = _ct(np.broadcast_to(g["L1_ffn_Wscale_dp"], (128, D)))
    gwT0 = _ct(g["L0_ffn_gate_w"].T); uwT0 = _ct(g["L0_ffn_up_w"].T)
    dwT0 = _ct(g["L0_ffn_down_w"].T)
    gwT1 = _ct(g["L1_ffn_W_gate"].T); uwT1 = _ct(g["L1_ffn_W_up"].T)
    wdT1 = _ct(g["L1_ffn_W_down"].T)
    wo1T = _ct(g["L1_attn_wo"].T)
    wsc = g["L1_ffn_Wscale_ug"]

    import os
    dbg = bool(os.environ.get("FUSED_DBG"))
    nc = _prog("fused_dbg" if dbg else "fused", build_fused_program, dbg)
    in_maps = []
    for c in range(N_CORES):
        b, s = c // 2, c % 2
        hs = slice(s * 1024, (s + 1) * 1024)
        in_maps.append({
            "x_rows": _ct(x_flat[c * TOK_CORE:(c + 1) * TOK_CORE]),
            "wq0": _ct(g["L0_attn_Wq"][:, hs]), "wk0": _ct(g["L0_attn_Wk"][:, hs]),
            "wv0": _ct(g["L0_attn_Wv"][:, hs]), "wo0": _ct(g["L0_attn_Wo"][hs, :]),
            "gwT0": gwT0, "uwT0": uwT0, "dwT0": dwT0,
            "gs0": _ct(g["L0_ffn_gate_s"][:, None]), "us0": _ct(g["L0_ffn_up_s"][:, None]),
            "ds_b": ds_b,
            "wq1": _ct(g["L1_attn_wq"][hs, :].T), "wk1": _ct(g["L1_attn_wk"][hs, :].T),
            "wv1": _ct(g["L1_attn_wv"][hs, :].T), "wo1T": _ct(wo1T[hs, :]),
            "su_ug": _ct(g["L1_ffn_SU_ug"][:, None]),
            "gwT1": gwT1, "uwT1": uwT1,
            "gs1": _ct(wsc[INTER1:][:, None]), "us1": _ct(wsc[:INTER1][:, None]),
            "su_dp": _ct(g["L1_ffn_SU_dp"][:, None]), "wdT1": wdT1, "wdp_b": wdp_b,
            "masks": masks, "ident": ident, "h128": H128, "h2048": H2048, "rc": cosT, "rs": sinT,
            "lamv": _ct(lamv), "subln_b": subln_b,
        })
    res = _run(nc, in_maps)
    if dbg:
        kernel_fused.dbg_results = res.results
    out = np.concatenate([res.results[c]["out_rows"] for c in range(N_CORES)], axis=0)
    return out.reshape(B_SZ, T_SEQ, D)


def kernel(**inputs):
    return kernel_fused(**inputs)


# version 16
# speedup vs baseline: 1.0606x; 1.0606x over previous
import contextlib
import numpy as np
import concourse.bass as bass
import concourse.mybir as mybir
from concourse.bass_utils import run_bass_kernel_spmd

F32 = mybir.dt.float32
BF16 = mybir.dt.bfloat16
AF = mybir.ActivationFunctionType
ALU = mybir.AluOpType
AX = mybir.AxisListType

N_CORES = 8
D = 2048
B_SZ = 4
T_SEQ = 2048
TOK_GLOBAL = B_SZ * T_SEQ
TOK_CORE = TOK_GLOBAL // N_CORES
EPS = 1e-5
SOFTMAX_SHIFT = 12.0


class Sem:
    def __init__(self, name):
        self.name = name
        self.count = 0
        self.handle = None

class Op:
    __slots__ = ("engine", "fn", "waits", "incs")
    def __init__(self, engine, fn, waits, incs):
        self.engine = engine
        self.fn = fn
        self.waits = waits
        self.incs = incs

class Sched:
    ENGINES = ("sync", "vector", "scalar", "tensor", "gpsimd")

    def __init__(self):
        self.ops = []
        self.sems = []
        self._pending = {}

    def barrier(self, exclude=()):
        evs = [(s, s.count) for s in self.sems if s.count > 0 and s not in exclude]
        self._pending = {e: list(evs) for e in self.ENGINES}

    def sem(self, name):
        s = Sem(name)
        self.sems.append(s)
        return s

    def add(self, engine, fn, deps=(), inc=None, inc_n=1):
        waits = []
        pj = self._pending.pop(engine, None)
        if pj:
            waits.extend(pj)
        for d in deps:
            if d is None:
                continue
            if isinstance(d, Sem):
                waits.append((d, d.count))
            else:
                waits.append(d)
        ev = None
        incs = []
        if inc is not None:
            inc.count += inc_n
            incs.append((inc, inc_n))
            ev = (inc, inc.count)
        self.ops.append(Op(engine, fn, waits, incs))
        return ev

    def emit(self, nc):
        with contextlib.ExitStack() as stack:
            for s in self.sems:
                s.handle = stack.enter_context(nc.semaphore(s.name))
            block = stack.enter_context(nc.Block())
            by_engine = {e: [op for op in self.ops if op.engine == e] for e in self.ENGINES}

            def mk(engine_ops):
                def body(eng):
                    for op in engine_ops:
                        wmax = {}
                        for s, v in op.waits:
                            wmax[s] = max(wmax.get(s, 0), v)
                        for s, v in wmax.items():
                            if v > 0:
                                eng.wait_ge(s.handle, v)
                        ins = op.fn(eng)
                        for s, n in op.incs:
                            assert ins is not None, f"op with inc on {s.name} returned no instruction"
                            ins = ins.then_inc(s.handle, n)
                return body

            for e in self.ENGINES:
                if by_engine[e]:
                    getattr(block, e)(mk(by_engine[e]))



def emit_layernorm(sch, y, out, stats, eps_tile, n_feat, deps, v_sem, s_sem,
                   out_free_dep=None):
    deps = [d for d in deps if d is not None]
    e1 = sch.add("vector", lambda v: v.reduce_sum(stats["sum1"][:], y[:], axis=AX.X),
                 deps=deps, inc=v_sem)
    e2 = sch.add("scalar", lambda s: s.activation(stats["sq_scratch"][:, 0:n_feat], y[:], AF.Square,
                                                 accum_out=stats["sumsq"][:]),
                 deps=deps, inc=s_sem)
    e3a = sch.add("vector", lambda v: v.tensor_scalar(stats["mean"][:], stats["sum1"][:], 1.0 / n_feat, None, ALU.mult),
                  deps=[e1], inc=v_sem)
    e3b = sch.add("vector", lambda v: v.tensor_scalar(stats["var"][:], stats["sumsq"][:], 1.0 / n_feat, None, ALU.mult),
                  deps=[e2], inc=v_sem)
    e3 = sch.add("vector", lambda v: v.tensor_tensor(stats["m2"][:], stats["mean"][:], stats["mean"][:], ALU.mult),
                 deps=[e3a], inc=v_sem)
    e4 = sch.add("vector", lambda v: v.tensor_tensor(stats["var"][:], stats["var"][:], stats["m2"][:], ALU.subtract),
                 deps=[e3, e3b], inc=v_sem)
    e5 = sch.add("scalar", lambda s: s.activation(stats["std"][:], stats["var"][:], AF.Sqrt, bias=eps_tile[:]),
                 deps=[e4], inc=s_sem)
    e6 = sch.add("vector", lambda v: v.reciprocal(stats["rstd"][:], stats["std"][:]),
                 deps=[e5], inc=v_sem)
    e7 = sch.add("vector", lambda v: v.tensor_tensor(stats["nmr"][:], stats["mean"][:], stats["rstd"][:], ALU.mult),
                 deps=[e6], inc=v_sem)
    e8 = sch.add("vector", lambda v: v.tensor_scalar(stats["nmr"][:], stats["nmr"][:], -1.0, None, ALU.mult),
                 deps=[e7], inc=v_sem)
    e9 = sch.add("vector", lambda v: v.tensor_scalar(out[:], y[:], stats["rstd"][:], stats["nmr"][:],
                                                     ALU.mult, ALU.add),
                 deps=[e8, e6, out_free_dep] + deps, inc=v_sem)
    return e9


def alloc_ln_stats(stack, nc, tag, width=2048):
    t = {}
    for nm in ("sum1", "sumsq", "mean", "var", "m2", "std", "rstd", "nmr"):
        t[nm] = stack.enter_context(nc.sbuf_tensor(f"{tag}_{nm}", [128, 1], F32))
    t["sq_scratch"] = stack.enter_context(nc.sbuf_tensor(f"{tag}_sq", [128, width], F32))
    return t



def build_pmm_program(kch, a_is_bf16=False, with_partial=False, finalize=True,
                      with_colscale=False, scale64=False):
    nc = bass.Bass("TRN2", target_bir_lowering=False, debug=False)
    K = kch * 128
    a_fm = nc.dram_tensor("a_fm", [K, TOK_CORE], BF16 if a_is_bf16 else F32,
                          kind="ExternalInput").ap()
    w = nc.dram_tensor("w", [K, D], F32, kind="ExternalInput").ap()
    if with_partial:
        p_in = nc.dram_tensor("p_in", [TOK_CORE, D], F32, kind="ExternalInput").ap()
    if with_colscale:
        colscale_b = nc.dram_tensor("colscale_b", [128, D], F32, kind="ExternalInput").ap()
    if finalize:
        resid = nc.dram_tensor("resid", [TOK_CORE, D], F32, kind="ExternalInput").ap()
    out = nc.dram_tensor("out", [TOK_CORE, D], F32, kind="ExternalOutput").ap()

    TT = TOK_CORE // 128
    OC = D // 512

    sch = Sched()
    a_ld = [sch.sem(f"a_ld{i}") for i in range(2)]
    w_ld = [sch.sem(f"w_ld{i}") for i in range(2)]
    r_ld = [sch.sem(f"r_ld{i}") for i in range(2)]
    p_ld = [sch.sem(f"p_ld{i}") for i in range(2)]
    c_ld = sch.sem("c_ld")
    cast = sch.sem("cast")
    mm = sch.sem("mm")
    ev = sch.sem("ev")
    vs = sch.sem("vs")
    ss = sch.sem("ss")
    st_s = [sch.sem(f"st{i}") for i in range(2)]

    NB = 1 if kch > 16 else 2
    with contextlib.ExitStack() as stack:
        sb = lambda name, shape, dt=F32: stack.enter_context(nc.sbuf_tensor(name, shape, dt))
        a_bf = [sb(f"a_bf_{k}", [128, TOK_CORE], BF16) for k in range(kch)]
        w_bf = [sb(f"w_bf_{k}", [128, D], BF16) for k in range(kch)]
        a_f32 = None if a_is_bf16 else [sb(f"a_f32_{i}", [128, TOK_CORE]) for i in range(NB)]
        w_f32 = [sb(f"w_f32_{i}", [128, D]) for i in range(NB)]
        r_f32 = [sb(f"r_f32_{i}", [128, D]) for i in range(NB)] if finalize else None
        p_f32 = [sb(f"p_f32_{i}", [128, D]) for i in range(NB)] if with_partial else None
        cs_t = sb("cs_t", [128, D]) if with_colscale else None
        y = [sb(f"y_{i}", [128, D]) for i in range(NB)]
        o_t = [sb(f"o_{i}", [128, D]) for i in range(NB)]
        eps_t = sb("eps", [128, 1])
        stats = alloc_ln_stats(stack, nc, "ln") if finalize else None
        ps = [stack.enter_context(nc.psum_tensor(f"ps_{i}", [128, 512], F32)) for i in range(4)]

        ev_eps = sch.add("vector", lambda v: v.memset(eps_t[:], EPS), inc=vs)
        cs_ev = None
        if with_colscale:
            e_cl = sch.add("sync", lambda s: s.dma_start(out=cs_t[:], in_=colscale_b), inc=c_ld, inc_n=16)
            if scale64:
                cs_ev = sch.add("vector", lambda v: v.tensor_scalar(cs_t[:], cs_t[:], 64.0, None, ALU.mult),
                                deps=[e_cl], inc=vs)
            else:
                cs_ev = e_cl

        cast_evs = []
        for k in range(kch):
            buf = k % NB
            dep_a = [cast_evs[k - NB][0]] if k >= NB else []
            dep_w = [cast_evs[k - NB][1]] if k >= NB else []
            if a_is_bf16:
                e_ca = sch.add("sync", (lambda kk: lambda s: s.dma_start(
                    out=a_bf[kk][:], in_=a_fm[kk * 128:(kk + 1) * 128, :]))(k),
                    deps=dep_a, inc=a_ld[buf], inc_n=16)
            else:
                e_la = sch.add("sync", (lambda kk, bb: lambda s: s.dma_start(
                    out=a_f32[bb][:], in_=a_fm[kk * 128:(kk + 1) * 128, :]))(k, buf),
                    deps=dep_a, inc=a_ld[buf], inc_n=16)
                e_ca = sch.add("vector", (lambda kk, bb: lambda v: v.tensor_copy(a_bf[kk][:], a_f32[bb][:]))(k, buf),
                               deps=[e_la], inc=cast)
            e_lw = sch.add("sync", (lambda kk, bb: lambda s: s.dma_start(
                out=w_f32[bb][:], in_=w[kk * 128:(kk + 1) * 128, :]))(k, buf),
                deps=dep_w, inc=w_ld[buf], inc_n=16)
            e_cw = sch.add("vector", (lambda kk, bb: lambda v: v.tensor_copy(w_bf[kk][:], w_f32[bb][:]))(k, buf),
                           deps=[e_lw], inc=cast)
            cast_evs.append((e_ca, e_cw))
        loads_done = [cast_evs[-1][0], cast_evs[-1][1], (cast, cast.count),
                      (a_ld[0], a_ld[0].count), (a_ld[1], a_ld[1].count)]

        prev_ev = {}
        prev_r = {}
        prev_p = {}
        prev_out = {}
        prev_ln_out = None
        for t in range(TT):
            buf = t % NB
            e_lr = None
            e_lp = None
            if finalize:
                e_lr = sch.add("sync", (lambda tt, bb: lambda s: s.dma_start(
                    out=r_f32[bb][:], in_=resid[tt * 128:(tt + 1) * 128, :]))(t, buf),
                    deps=[prev_r.get(buf)], inc=r_ld[buf], inc_n=16)
            if with_partial:
                e_lp = sch.add("sync", (lambda tt, bb: lambda s: s.dma_start(
                    out=p_f32[bb][:], in_=p_in[tt * 128:(tt + 1) * 128, :]))(t, buf),
                    deps=[prev_p.get(buf)], inc=p_ld[buf], inc_n=16)
            mm_evs = []
            for o in range(OC):
                def mmgroup(tt, oo):
                    def f(te):
                        ins = None
                        for k in range(kch):
                            ins = te.matmul(ps[oo][:],
                                            a_bf[k][:, tt * 128:(tt + 1) * 128],
                                            w_bf[k][:, oo * 512:(oo + 1) * 512],
                                            start=(k == 0), stop=(k == kch - 1))
                        return ins
                    return f
                e_mm = sch.add("tensor", mmgroup(t, o),
                               deps=loads_done + [prev_ev.get(o)], inc=mm)
                mm_evs.append(e_mm)
            ev_evs = []
            for o in range(OC):
                sl = slice(o * 512, (o + 1) * 512)
                if with_partial:
                    e_a = sch.add("vector", (lambda oo, bb, sl_: lambda v: v.tensor_tensor(
                        y[bb][:, sl_], ps[oo][:], p_f32[bb][:, sl_], ALU.add))(o, buf, sl),
                        deps=[mm_evs[o], e_lp, prev_ln_out if finalize else prev_out.get(buf)], inc=ev)
                else:
                    e_a = sch.add("vector", (lambda oo, bb, sl_: lambda v: v.tensor_copy(
                        y[bb][:, sl_], ps[oo][:]))(o, buf, sl),
                        deps=[mm_evs[o], prev_ln_out if finalize else prev_out.get(buf)], inc=ev)
                step = e_a
                if finalize:
                    if with_colscale:
                        step = sch.add("vector", (lambda oo, bb, sl_: lambda v: v.tensor_tensor(
                            y[bb][:, sl_], y[bb][:, sl_], cs_t[:, sl_], ALU.mult))(o, buf, sl),
                            deps=[step, cs_ev], inc=ev)
                    step = sch.add("vector", (lambda oo, bb, sl_: lambda v: v.tensor_tensor(
                        y[bb][:, sl_], y[bb][:, sl_], r_f32[bb][:, sl_], ALU.add))(o, buf, sl),
                        deps=[step, e_lr], inc=ev)
                prev_ev[o] = e_a
                ev_evs.append(step)
            if finalize:
                e_out = emit_layernorm(sch, y[buf], o_t[buf], stats, eps_t, D,
                                       deps=ev_evs + [ev_eps], v_sem=vs, s_sem=ss,
                                       out_free_dep=prev_out.get(buf))
                prev_ln_out = e_out
                src_tile = o_t[buf]
            else:
                e_out = ev_evs[-1]
                src_tile = y[buf]
            e_st = sch.add("gpsimd", (lambda tt, s_, ev_: lambda g: g.dma_start(
                out=out[tt * 128:(tt + 1) * 128, :], in_=s_[:]))(t, src_tile, e_out),
                deps=ev_evs + [e_out], inc=st_s[buf], inc_n=16)
            prev_r[buf] = ev_evs[-1]
            prev_p[buf] = ev_evs[-1]
            prev_out[buf] = e_st

        sch.add("gpsimd", lambda g: None, deps=[(s, s.count) for s in st_s])
        sch.emit(nc)
    return nc



INTER0 = 5632
INTER1 = 8192
FWHT_SCALE_2048 = 1.0 / (64.0 * float(np.sqrt(2048.0)))
FWHT_SCALE_8192 = 1.0 / (64.0 * float(np.sqrt(8192.0)))


def build_ffn_up_program(layer):
    nc = bass.Bass("TRN2", target_bir_lowering=False, debug=False)
    NI = (INTER1 if layer == 1 else INTER0) // 128
    KC = D // 128
    a_dt = BF16 if layer == 1 else F32
    x_fm = nc.dram_tensor("x_fm", [D, TOK_CORE], a_dt, kind="ExternalInput").ap()
    gw = nc.dram_tensor("gw", [D, NI * 128], F32, kind="ExternalInput").ap()
    uw = nc.dram_tensor("uw", [D, NI * 128], F32, kind="ExternalInput").ap()
    gs = nc.dram_tensor("gs", [NI * 128, 1], F32, kind="ExternalInput").ap()
    us = nc.dram_tensor("us", [NI * 128, 1], F32, kind="ExternalInput").ap()
    if layer == 1:
        su = nc.dram_tensor("su", [NI * 128, 1], F32, kind="ExternalInput").ap()
        h128 = nc.dram_tensor("h128", [128, 128], F32, kind="ExternalInput").ap()
    h_out = nc.dram_tensor("h_out", [NI * 128, TOK_CORE], BF16, kind="ExternalOutput").ap()

    sch = Sched()
    x_ld = [sch.sem(f"x_ld{i}") for i in range(2)]
    gw_ld = [sch.sem(f"gw_ld{i}") for i in range(2)]
    uw_ld = [sch.sem(f"uw_ld{i}") for i in range(2)]
    sc_ld = [sch.sem(f"sc_ld{i}") for i in range(2)]
    c_ld = sch.sem("c_ld")
    cast = sch.sem("cast")
    mm = sch.sem("mm")
    vs = sch.sem("vs")
    ss = sch.sem("ss")
    st_s = [sch.sem(f"st{i}") for i in range(4)]

    with contextlib.ExitStack() as stack:
        sb = lambda name, shape, dt=F32: stack.enter_context(nc.sbuf_tensor(name, shape, dt))
        x_stage = None if layer == 1 else [sb(f"x_stage{i}", [128, TOK_CORE]) for i in range(2)]
        x_bf = [sb(f"x_bf{k}", [128, TOK_CORE], BF16) for k in range(KC)]
        w_stage = [sb("w_stage0", [128, 2048])]
        gw_bf = [sb("gw_bf0", [128, 2048], BF16)]
        uw_bf = [sb("uw_bf0", [128, 2048], BF16)]
        scs = [sb(f"scs{i}", [128, 4]) for i in range(2)]
        sg_t = [sb(f"sg{i}", [128, 512]) for i in range(2)]
        if layer == 1:
            hbuf = [sb(f"hbuf{c}", [128, TOK_CORE], BF16) for c in range(NI)]
            btmp = sb("btmp", [128, TOK_CORE], BF16)
            h128_f = sb("h128_f", [128, 128])
            h128_b = sb("h128_b", [128, 128], BF16)
            ho_t = [sb(f"ho{i}", [128, 512], BF16) for i in range(4)]
        else:
            ho_t = [sb(f"ho{i}", [128, TOK_CORE], BF16) for i in range(4)]
        ps_g = [stack.enter_context(nc.psum_tensor(f"psg{i}", [128, 512], F32)) for i in range(2)]
        ps_u = [stack.enter_context(nc.psum_tensor(f"psu{i}", [128, 512], F32)) for i in range(2)]

        h_evs = []
        if layer == 1:
            e_h = sch.add("sync", lambda s: s.dma_start(out=h128_f[:], in_=h128), inc=c_ld, inc_n=16)
            e_hc = sch.add("vector", lambda v: v.tensor_copy(h128_b[:], h128_f[:]), deps=[e_h], inc=cast)
            h_evs = [e_hc]

        x_cast_evs = []
        for k in range(KC):
            buf = k % 2
            if layer == 1:
                e_cx = sch.add("sync", (lambda kk: lambda s: s.dma_start(
                    out=x_bf[kk][:], in_=x_fm[kk * 128:(kk + 1) * 128, :]))(k),
                    deps=[x_cast_evs[k - 2]] if k >= 2 else [], inc=x_ld[buf], inc_n=16)
            else:
                e_lx = sch.add("sync", (lambda kk, bb: lambda s: s.dma_start(
                    out=x_stage[bb][:], in_=x_fm[kk * 128:(kk + 1) * 128, :]))(k, buf),
                    deps=[x_cast_evs[k - 2]] if k >= 2 else [], inc=x_ld[buf], inc_n=16)
                e_cx = sch.add("vector", (lambda kk, bb: lambda v: v.tensor_copy(x_bf[kk][:], x_stage[bb][:]))(k, buf),
                               deps=[e_lx], inc=cast)
            x_cast_evs.append(e_cx)
        x_done = [(cast, cast.count), (x_ld[0], x_ld[0].count), (x_ld[1], x_ld[1].count)]

        st_prev = [None] * 4
        prev_wcast = {}
        prev_wuse = {}
        prev_scuse = {}
        prev_sg = {}
        prev_psg = {}
        prev_psu = {}
        prev_hbuf = {}
        hbuf_evs = []

        for i in range(NI):
            wb = 0
            e_lg = sch.add("sync", (lambda i_, bb: lambda s: s.dma_start(
                out=w_stage[bb][:].rearrange("p (c f) -> p c f", c=KC),
                in_=gw[:, i_ * 128:(i_ + 1) * 128].rearrange("(c p) f -> p c f", p=128)))(i, wb),
                deps=[prev_wcast.get("stage")], inc=gw_ld[0], inc_n=16)
            e_cg = sch.add("vector", (lambda bb: lambda v: v.tensor_copy(gw_bf[bb][:], w_stage[bb][:]))(wb),
                           deps=[e_lg, prev_wuse.get("g")], inc=cast)
            e_lu = sch.add("scalar", (lambda i_, bb: lambda s: s.dma_start(
                out=w_stage[bb][:].rearrange("p (c f) -> p c f", c=KC),
                in_=uw[:, i_ * 128:(i_ + 1) * 128].rearrange("(c p) f -> p c f", p=128)))(i, wb),
                deps=[e_cg], inc=uw_ld[0], inc_n=16)
            e_cu = sch.add("vector", (lambda bb: lambda v: v.tensor_copy(uw_bf[bb][:], w_stage[bb][:]))(wb),
                           deps=[e_lu, prev_wuse.get("u")], inc=cast)
            prev_wcast["stage"] = e_cu
            wb2 = i % 2
            e_ls = sch.add("sync", (lambda i_, bb: lambda s: s.dma_start(
                out=scs[bb][:, 0:1], in_=gs[i_ * 128:(i_ + 1) * 128, :]))(i, wb2),
                deps=[prev_scuse.get(wb2)], inc=sc_ld[wb2], inc_n=16)
            e_ls2 = sch.add("sync", (lambda i_, bb: lambda s: s.dma_start(
                out=scs[bb][:, 1:2], in_=us[i_ * 128:(i_ + 1) * 128, :]))(i, wb2),
                deps=[e_ls], inc=sc_ld[wb2], inc_n=16)
            if layer == 1:
                e_ls3 = sch.add("sync", (lambda i_, bb: lambda s: s.dma_start(
                    out=scs[bb][:, 2:3], in_=su[i_ * 128:(i_ + 1) * 128, :]))(i, wb2),
                    deps=[e_ls2], inc=sc_ld[wb2], inc_n=16)
                e_sc1 = sch.add("vector", (lambda bb: lambda v: v.tensor_scalar(
                    scs[bb][:, 3:4], scs[bb][:, 1:2], 64.0, None, ALU.mult))(wb2),
                    deps=[e_ls3], inc=vs)
                e_sc2 = sch.add("vector", (lambda bb: lambda v: v.tensor_tensor(
                    scs[bb][:, 3:4], scs[bb][:, 3:4], scs[bb][:, 2:3], ALU.mult))(wb2),
                    deps=[e_sc1], inc=vs)
                e_sc3 = sch.add("vector", (lambda bb: lambda v: v.tensor_scalar(
                    scs[bb][:, 0:1], scs[bb][:, 0:1], 64.0, None, ALU.mult))(wb2),
                    deps=[e_sc2], inc=vs)
                e_scales = e_sc3
                comb_col = 3
            else:
                e_scales = e_ls2
                comb_col = 1
            for th in range(2):
                tsl = slice(th * 512, (th + 1) * 512)
                pp = th
                def gmm(i_, th_, bb, pp_, which):
                    wsrc = gw_bf if which == "g" else uw_bf
                    pdst = ps_g if which == "g" else ps_u
                    def f(te):
                        ins = None
                        for kc in range(KC):
                            ins = te.matmul(pdst[pp_][:],
                                            wsrc[bb][:, kc * 128:(kc + 1) * 128],
                                            x_bf[kc][:, th_ * 512:(th_ + 1) * 512],
                                            start=(kc == 0), stop=(kc == KC - 1))
                        return ins
                    return f
                e_gm = sch.add("tensor", gmm(i, th, wb, pp, "g"),
                               deps=x_done + [e_cg, prev_psg.get(pp)], inc=mm)
                e_um = sch.add("tensor", gmm(i, th, wb, pp, "u"),
                               deps=x_done + [e_cu, prev_psu.get(pp)], inc=mm)
                e_sg = sch.add("scalar", (lambda bb, pp_, sgb: lambda s: s.activation(
                    sg_t[sgb][:], ps_g[pp_][:], AF.Silu, scale=scs[bb][:, 0:1]))(wb2, pp, th),
                    deps=[e_gm, e_scales, prev_sg.get(th)], inc=ss)
                prev_psg[pp] = e_sg
                if layer == 1:
                    dst = hbuf[i]
                    ddep = prev_hbuf.get(i)
                else:
                    dst = ho_t[2 * (i % 2) + th]
                    ddep = st_prev[2 * (i % 2) + th]
                e_h2 = sch.add("vector", (lambda bb, pp_, sgb, d_, t2: lambda v: v.scalar_tensor_tensor(
                    d_[:, t2] if layer == 1 else d_[:, 0:512],
                    ps_u[pp_][:], scs[bb][:, comb_col:comb_col + 1], sg_t[sgb][:],
                    ALU.mult, ALU.mult))(wb2, pp, th, dst, tsl),
                    deps=[e_um, e_sg, e_scales, ddep], inc=vs)
                prev_psu[pp] = e_h2
                prev_sg[th] = e_h2
                if layer == 1:
                    prev_hbuf[i] = e_h2
                    hbuf_evs.append(e_h2)
                else:
                    pass
                prev_wuse["g"] = e_sg
                prev_wuse["u"] = e_h2
                prev_scuse[wb2] = e_h2
                if layer == 0:
                    sl4 = 2 * (i % 2) + th
                    e_st = sch.add("gpsimd", (lambda i_, th_, d_: lambda g: g.dma_start(
                        out=h_out[i_ * 128:(i_ + 1) * 128, th_ * 512:(th_ + 1) * 512],
                        in_=d_[:, 0:512]))(i, th, dst),
                        deps=[e_h2, st_prev[sl4]], inc=st_s[sl4], inc_n=16)
                    st_prev[sl4] = e_st

        if layer == 1:
            tile_ev = {c: hbuf_evs[2 * c + 1] for c in range(NI)}
            prev_btmp = None
            hstep = 1
            while hstep < NI:
                for c in range(NI):
                    if (c // hstep) % 2 == 0:
                        a, b = c, c + hstep
                        e1 = sch.add("vector", (lambda a_, b_: lambda v: v.tensor_tensor(
                            btmp[:], hbuf[a_][:], hbuf[b_][:], ALU.subtract))(a, b),
                            deps=[tile_ev[a], tile_ev[b], prev_btmp], inc=vs)
                        e2 = sch.add("vector", (lambda a_, b_: lambda v: v.tensor_tensor(
                            hbuf[a_][:], hbuf[a_][:], hbuf[b_][:], ALU.add))(a, b),
                            deps=[e1], inc=vs)
                        e3 = sch.add("vector", (lambda b_: lambda v: v.tensor_copy(hbuf[b_][:], btmp[:]))(b),
                            deps=[e2], inc=vs)
                        tile_ev[a] = e2
                        tile_ev[b] = e3
                        prev_btmp = e3
                hstep *= 2
            prev_ps = {}
            for c in range(NI):
                for th in range(2):
                    pp = th
                    e_m = sch.add("tensor", (lambda c_, th_, pp_: lambda te: te.matmul(
                        ps_g[pp_][:], h128_b[:], hbuf[c_][:, th_ * 512:(th_ + 1) * 512],
                        start=True, stop=True))(c, th, pp),
                        deps=[tile_ev[c]] + h_evs + [prev_ps.get(pp)], inc=mm)
                    sl4 = 2 * (c % 2) + th
                    e_e = sch.add("scalar", (lambda pp_, sl_: lambda s: s.activation(
                        ho_t[sl_][:], ps_g[pp_][:], AF.Copy, scale=FWHT_SCALE_8192))(pp, sl4),
                        deps=[e_m, st_prev[sl4]], inc=ss)
                    prev_ps[pp] = e_e
                    e_st = sch.add("gpsimd", (lambda c_, th_, sl_: lambda g: g.dma_start(
                        out=h_out[c_ * 128:(c_ + 1) * 128, th_ * 512:(th_ + 1) * 512],
                        in_=ho_t[sl_][:]))(c, th, sl4),
                        deps=[e_e], inc=st_s[sl4], inc_n=16)
                    st_prev[sl4] = e_st

        sch.add("gpsimd", lambda g: None, deps=[(s, s.count) for s in st_s])
        sch.emit(nc)
    return nc



def build_fwht_program():
    nc = bass.Bass("TRN2", target_bir_lowering=False, debug=False)
    KC = D // 128
    x_fm = nc.dram_tensor("x_fm", [D, TOK_CORE], F32, kind="ExternalInput").ap()
    su = nc.dram_tensor("su", [D, 1], F32, kind="ExternalInput").ap()
    h128 = nc.dram_tensor("h128", [128, 128], F32, kind="ExternalInput").ap()
    t_out = nc.dram_tensor("t_out", [D, TOK_CORE], BF16, kind="ExternalOutput").ap()

    sch = Sched()
    x_ld = [sch.sem(f"x_ld{i}") for i in range(2)]
    su_ld = sch.sem("su_ld")
    c_ld = sch.sem("c_ld")
    cast = sch.sem("cast")
    mm = sch.sem("mm")
    vs = sch.sem("vs")
    ss = sch.sem("ss")
    st_s = [sch.sem(f"st{i}") for i in range(2)]

    with contextlib.ExitStack() as stack:
        sb = lambda name, shape, dt=F32: stack.enter_context(nc.sbuf_tensor(name, shape, dt))
        x_stage = [sb(f"x_stage{i}", [128, TOK_CORE]) for i in range(2)]
        xs_bf = [sb(f"xs_bf{k}", [128, TOK_CORE], BF16) for k in range(KC)]
        t1 = [sb(f"t1_{k}", [128, TOK_CORE]) for k in range(KC)]
        btmp = sb("btmp", [128, TOK_CORE])
        su_t = sb("su_t", [128, KC])
        h128_f = sb("h128_f", [128, 128])
        h128_b = sb("h128_b", [128, 128], BF16)
        to_t = [sb(f"to{i}", [128, TOK_CORE], BF16) for i in range(2)]
        ps2 = [stack.enter_context(nc.psum_tensor(f"ps_{i}", [128, 512], F32)) for i in range(2)]

        e_h = sch.add("sync", lambda s: s.dma_start(out=h128_f[:], in_=h128), inc=c_ld, inc_n=16)
        e_hc = sch.add("vector", lambda v: v.tensor_copy(h128_b[:], h128_f[:]), deps=[e_h], inc=cast)
        for k in range(KC):
            sch.add("sync", (lambda kk: lambda s: s.dma_start(
                out=su_t[:, kk:kk + 1], in_=su[kk * 128:(kk + 1) * 128, :]))(k), inc=su_ld, inc_n=16)
        e_su = (su_ld, su_ld.count)

        xev = []
        for k in range(KC):
            buf = k % 2
            e_lx = sch.add("sync", (lambda kk, bb: lambda s: s.dma_start(
                out=x_stage[bb][:], in_=x_fm[kk * 128:(kk + 1) * 128, :]))(k, buf),
                deps=[xev[k - 2][0]] if k >= 2 else [], inc=x_ld[buf], inc_n=16)
            e_ps = sch.add("vector", (lambda kk, bb: lambda v: v.tensor_scalar(
                x_stage[bb][:], x_stage[bb][:], su_t[:, kk:kk + 1], None, ALU.mult))(k, buf),
                deps=[e_lx, e_su], inc=vs)
            e_cx = sch.add("vector", (lambda kk, bb: lambda v: v.tensor_copy(xs_bf[kk][:], x_stage[bb][:]))(k, buf),
                           deps=[e_ps], inc=cast)
            xev.append((e_cx,))
        t1_ev = {}
        prev_ps = {}
        for k in range(KC):
            for th in range(2):
                pp = th
                e_m = sch.add("tensor", (lambda k_, th_, pp_: lambda te: te.matmul(
                    ps2[pp_][:], h128_b[:], xs_bf[k_][:, th_ * 512:(th_ + 1) * 512],
                    start=True, stop=True))(k, th, pp),
                    deps=[xev[k][0], e_hc, prev_ps.get(pp)], inc=mm)
                e_e = sch.add("scalar", (lambda k_, th_, pp_: lambda s: s.activation(
                    t1[k_][:, th_ * 512:(th_ + 1) * 512], ps2[pp_][:], AF.Copy,
                    scale=FWHT_SCALE_2048))(k, th, pp),
                    deps=[e_m], inc=ss)
                prev_ps[pp] = e_e
                t1_ev[(k, th)] = e_e
        tile_ev = {k: t1_ev[(k, 1)] for k in range(KC)}
        for k in range(KC):
            tile_ev[k] = t1_ev[(k, 1)]
        full_ev = {k: [t1_ev[(k, 0)], t1_ev[(k, 1)]] for k in range(KC)}
        tile_dep = {k: full_ev[k] for k in range(KC)}
        cur = {k: tile_dep[k] for k in range(KC)}
        prev_btmp = None
        hstep = 1
        while hstep < KC:
            for c in range(KC):
                if (c // hstep) % 2 == 0:
                    a, b = c, c + hstep
                    e1 = sch.add("vector", (lambda a_, b_: lambda v: v.tensor_tensor(
                        btmp[:], t1[a_][:], t1[b_][:], ALU.subtract))(a, b),
                        deps=cur[a] + cur[b] + [prev_btmp], inc=vs)
                    e2 = sch.add("vector", (lambda a_, b_: lambda v: v.tensor_tensor(
                        t1[a_][:], t1[a_][:], t1[b_][:], ALU.add))(a, b),
                        deps=[e1], inc=vs)
                    e3 = sch.add("vector", (lambda b_: lambda v: v.tensor_copy(t1[b_][:], btmp[:]))(b),
                        deps=[e2], inc=vs)
                    cur[a] = [e2]
                    cur[b] = [e3]
                    prev_btmp = e3
            hstep *= 2
        st_prev = [None, None]
        for k in range(KC):
            sl2 = k % 2
            e_c = sch.add("vector", (lambda k_, sl_: lambda v: v.tensor_copy(to_t[sl_][:], t1[k_][:]))(k, sl2),
                          deps=cur[k] + [st_prev[sl2]], inc=vs)
            e_st = sch.add("gpsimd", (lambda k_, sl_: lambda g: g.dma_start(
                out=t_out[k_ * 128:(k_ + 1) * 128, :], in_=to_t[sl_][:]))(k, sl2),
                deps=[e_c], inc=st_s[sl2], inc_n=16)
            st_prev[sl2] = e_st

        sch.add("gpsimd", lambda g: None, deps=[(s, s.count) for s in st_s])
        sch.emit(nc)
    return nc




HEAD_DIM = 128
LAMBDA_INIT = 0.8 - 0.6 * float(np.exp(-0.3 * 12))
SM_SCALE = 1.0 / float(np.sqrt(128.0))


def build_attn_program(layer):
    nc = bass.Bass("TRN2", target_bir_lowering=False, debug=False)
    xfm = nc.dram_tensor("xfm", [D, T_SEQ], F32, kind="ExternalInput").ap()
    wq = nc.dram_tensor("wq", [D, 1024], F32, kind="ExternalInput").ap()
    wk = nc.dram_tensor("wk", [D, 1024], F32, kind="ExternalInput").ap()
    wv = nc.dram_tensor("wv", [D, 1024], F32, kind="ExternalInput").ap()
    masks = nc.dram_tensor("masks", [4, 128, 512], F32, kind="ExternalInput").ap()
    ident = nc.dram_tensor("ident", [128, 128], F32, kind="ExternalInput").ap()
    if layer == 1:
        rc = nc.dram_tensor("rc", [128, T_SEQ], F32, kind="ExternalInput").ap()
        rs = nc.dram_tensor("rs", [128, T_SEQ], F32, kind="ExternalInput").ap()
    else:
        lamv = nc.dram_tensor("lamv", [1, 512], F32, kind="ExternalInput").ap()
        subln_b = nc.dram_tensor("subln_b", [128, 256], F32, kind="ExternalInput").ap()
    a_out = nc.dram_tensor("a_out", [T_SEQ, 1024], F32, kind="ExternalOutput").ap()

    NH = 8 if layer == 1 else 4
    NBR = 1 if layer == 1 else 2
    DV = 128 if layer == 1 else 256
    DVA = DV + 1
    FH = 1024 // NH
    KC = D // 128
    NT = T_SEQ // 128
    NQ5 = T_SEQ // 512

    sch = Sched()
    x_ld = [sch.sem(f"x_ld{i}") for i in range(2)]
    w_lds = {nm: sch.sem(f"{nm}_ld") for nm in ("wq", "wk", "wv")}
    c_ld = sch.sem("c_ld")
    cast = sch.sem("cast")
    mm = sch.sem("mm")
    vs = sch.sem("vs")
    ss = sch.sem("ss")
    st_s = [sch.sem(f"st{i}") for i in range(4)]

    with contextlib.ExitStack() as stack:
        sb = lambda name, shape, dt=F32: stack.enter_context(nc.sbuf_tensor(name, shape, dt))
        x_stage = [sb(f"x_stage{i}", [128, T_SEQ]) for i in range(2)]
        x_bf = [sb(f"x_bf{k}", [128, T_SEQ], BF16) for k in range(KC)]
        w_stage = sb("w_stage", [128, 8 * FH])
        w_bf = {nm: sb(f"{nm}_bf", [128, KC * FH], BF16) for nm in ("wq", "wk", "wv")}
        q_bf = [sb(f"q_bf{b}", [128, T_SEQ], BF16) for b in range(NBR)]
        k_bf = [sb(f"k_bf{b}", [128, T_SEQ], BF16) for b in range(NBR)]
        v_tm = sb("v_tm", [128, NT * DVA], BF16)
        pt = sb("pt", [128, NT * 512], BF16)
        mask_bf = sb("mask_bf", [128, 4 * 512], BF16)
        mask_f32 = sb("mask_f32", [128, 4 * 512])
        ident_bf = sb("ident_bf", [128, 128], BF16)
        ident_f32 = sb("ident_f32", [128, 128])
        shift_t = sb("shift_t", [128, 1])
        eps_t = sb("eps_t", [128, 1])
        rec = [sb(f"rec{b}", [128, 1]) for b in range(NBR)]
        a_f32 = [sb(f"a_f32{i}", [128, DV]) for i in range(2)]
        if layer == 1:
            cos_t = sb("cos_t", [128, T_SEQ])
            sin_t = sb("sin_t", [128, T_SEQ])
            rope_sc = [sb(f"rope{i}", [128, 512]) for i in range(3)]
        else:
            lam_row = sb("lam_row", [1, 512])
            lam_sc = sb("lam_sc", [1, 8])
            ones_row = sb("ones_row", [1, 128])
            lam_b = sb("lam_b", [128, 1])
            nlam_rec = sb("nlam_rec", [128, 1])
            subln_t = sb("subln_t", [128, 256])
            sq_s = sb("sq_s", [128, 256])
            o_sb0 = sb("o_sb0", [128, 4 * 256])
            o1_sc = sb("o1_sc", [128, 256])
            cmb = sb("cmb", [128, 256])
            rms = {nm: sb(f"rms_{nm}", [128, 1]) for nm in ("sumsq", "ms", "std", "rstd")}
        ps_proj = [stack.enter_context(nc.psum_tensor(f"psp{i}", [128, 512], F32)) for i in range(2)]
        ps_sc = [stack.enter_context(nc.psum_tensor(f"pss{i}", [128, 512], F32)) for i in range(2)]
        ps_pv = [stack.enter_context(nc.psum_tensor(f"pspv{i}", [128, DVA], F32)) for i in range(2)]
        if layer == 0:
            ps_lam = stack.enter_context(nc.psum_tensor("ps_lam", [128, 1], F32))

        for r in range(4):
            sch.add("sync", (lambda r_: lambda s: s.dma_start(
                out=mask_f32[:, r_ * 512:(r_ + 1) * 512], in_=masks[r_]))(r), inc=c_ld, inc_n=16)
        sch.add("sync", lambda s: s.dma_start(out=ident_f32[:], in_=ident), inc=c_ld, inc_n=16)
        if layer == 1:
            sch.add("sync", lambda s: s.dma_start(out=cos_t[:], in_=rc), inc=c_ld, inc_n=16)
            sch.add("sync", lambda s: s.dma_start(out=sin_t[:], in_=rs), inc=c_ld, inc_n=16)
        else:
            sch.add("sync", lambda s: s.dma_start(out=lam_row[:], in_=lamv), inc=c_ld, inc_n=16)
            sch.add("sync", lambda s: s.dma_start(out=subln_t[:], in_=subln_b), inc=c_ld, inc_n=16)
        c_done = (c_ld, c_ld.count)
        rope_evs = [c_done]
        subln_ev = c_done

        e_mc = sch.add("vector", lambda v: v.tensor_copy(mask_bf[:], mask_f32[:]), deps=[c_done], inc=cast)
        e_ic = sch.add("vector", lambda v: v.tensor_copy(ident_bf[:], ident_f32[:]), deps=[c_done], inc=cast)
        e_sh = sch.add("vector", lambda v: v.memset(shift_t[:], -SOFTMAX_SHIFT), inc=vs)
        e_eps = sch.add("vector", lambda v: v.memset(eps_t[:], EPS), inc=vs)
        lam_evs = []
        if layer == 0:
            e_p1 = sch.add("vector", lambda v: v.tensor_tensor(
                lam_row[0:1, 0:128], lam_row[0:1, 0:128], lam_row[0:1, 128:256], ALU.mult),
                deps=[c_done], inc=vs)
            e_p2 = sch.add("vector", lambda v: v.tensor_tensor(
                lam_row[0:1, 256:384], lam_row[0:1, 256:384], lam_row[0:1, 384:512], ALU.mult),
                deps=[c_done, e_p1], inc=vs)
            e_s1 = sch.add("vector", lambda v: v.reduce_sum(lam_sc[0:1, 0:1], lam_row[0:1, 0:128], axis=AX.X),
                           deps=[e_p1], inc=vs)
            e_s2 = sch.add("vector", lambda v: v.reduce_sum(lam_sc[0:1, 1:2], lam_row[0:1, 256:384], axis=AX.X),
                           deps=[e_p2, e_s1], inc=vs)
            e_e1 = sch.add("scalar", lambda s: s.activation(lam_sc[0:1, 2:4], lam_sc[0:1, 0:2], AF.Exp),
                           deps=[e_s1, e_s2], inc=ss)
            e_d = sch.add("vector", lambda v: v.tensor_tensor(
                lam_sc[0:1, 4:5], lam_sc[0:1, 2:3], lam_sc[0:1, 3:4], ALU.subtract),
                deps=[e_e1], inc=vs)
            e_l = sch.add("vector", lambda v: v.tensor_scalar(
                lam_sc[0:1, 4:5], lam_sc[0:1, 4:5], LAMBDA_INIT, None, ALU.add),
                deps=[e_d], inc=vs)
            e_or = sch.add("vector", lambda v: v.memset(ones_row[:], 1.0), inc=vs)
            e_lb = sch.add("tensor", lambda t: t.matmul(ps_lam[:], ones_row[:], lam_sc[0:1, 4:5],
                                                        start=True, stop=True),
                           deps=[e_l, e_or], inc=mm)
            e_lbc = sch.add("vector", lambda v: v.tensor_copy(lam_b[:], ps_lam[:]), deps=[e_lb], inc=vs)
            lam_evs = [e_lbc]

        x_cast_evs = []
        for k in range(KC):
            buf = k % 2
            e_lx = sch.add("sync", (lambda kk, bb: lambda s: s.dma_start(
                out=x_stage[bb][:], in_=xfm[kk * 128:(kk + 1) * 128, :]))(k, buf),
                deps=[x_cast_evs[k - 2]] if k >= 2 else [], inc=x_ld[buf], inc_n=16)
            e_cx = sch.add("vector", (lambda kk, bb: lambda v: v.tensor_copy(x_bf[kk][:], x_stage[bb][:]))(k, buf),
                           deps=[e_lx], inc=cast)
            x_cast_evs.append(e_cx)
        x_done = (cast, cast.count)

        st_prev = [None] * 4
        st_ctr = [0]
        def next_store_slot():
            i = st_ctr[0] % 4
            st_ctr[0] += 1
            return i

        prev_w_use = {nm: None for nm in ("wq", "wk", "wv")}
        prev_w_cast = {nm: None for nm in ("wq", "wk", "wv")}
        prev_head_end = [None]
        prev_pproj = {}
        prev_psc = {}
        prev_ppv = {}
        prev_af = {}
        prev_rope = [None]
        prev_rms = [None]
        prev_osb0 = {}
        prev_o1 = [None]
        prev_cmb = [None]
        prev_rec = {}
        prev_nlam = [None]

        for h in range(NH):
            w_evs = {}
            prev_stage_cast = prev_w_cast["wv"]
            for nm, wap in (("wq", wq), ("wk", wk), ("wv", wv)):
                e_cw = None
                for hf in range(2):
                    e_lw = sch.add("sync", (lambda w_, r0, r1, c0, c1: lambda s: s.dma_start(
                        out=w_stage[:].rearrange("p (c f) -> p c f", c=8),
                        in_=w_[r0:r1, c0:c1].rearrange("(c p) f -> p c f", p=128)))(
                            wap, hf * 1024, (hf + 1) * 1024, h * FH, (h + 1) * FH),
                        deps=[prev_stage_cast], inc=w_lds[nm], inc_n=16)
                    e_cw = sch.add("vector", (lambda nm_, hf_: lambda v: v.tensor_copy(
                        w_bf[nm_][:, hf_ * 8 * FH:(hf_ + 1) * 8 * FH], w_stage[:]))(nm, hf),
                        deps=[e_lw, prev_w_use[nm]], inc=cast)
                    prev_stage_cast = e_cw
                prev_w_cast[nm] = e_cw
                w_evs[nm] = e_cw

            qk_done = []
            pcnt = 0
            for nm, dest in (("wq", q_bf), ("wk", k_bf)):
                for br in range(NBR):
                    for tsl in range(4):
                        pp = pcnt % 2
                        pcnt += 1
                        def proj(nm_, br_, tsl_, pp_):
                            def f(te):
                                ins = None
                                for kc in range(KC):
                                    ins = te.matmul(ps_proj[pp_][:],
                                                    w_bf[nm_][:, kc * FH + br_ * 128: kc * FH + br_ * 128 + 128],
                                                    x_bf[kc][:, tsl_ * 512:(tsl_ + 1) * 512],
                                                    start=(kc == 0), stop=(kc == KC - 1))
                                return ins
                            return f
                        e_mm = sch.add("tensor", proj(nm, br, tsl, pp),
                                       deps=[w_evs[nm], x_done, prev_pproj.get(pp)], inc=mm)
                        if layer == 1:
                            e_r1 = sch.add("vector", (lambda pp_: lambda v: v.tensor_scalar(
                                rope_sc[0][0:64, :], ps_proj[pp_][64:128, :], -1.0, None, ALU.mult))(pp),
                                deps=[e_mm, prev_rope[0]], inc=vs)
                            e_r2 = sch.add("vector", (lambda pp_: lambda v: v.tensor_copy(
                                rope_sc[0][64:128, :], ps_proj[pp_][0:64, :]))(pp),
                                deps=[e_mm, e_r1], inc=vs)
                            e_r3 = sch.add("vector", (lambda pp_, t_: lambda v: v.tensor_tensor(
                                rope_sc[1][:], ps_proj[pp_][:], cos_t[:, t_ * 512:(t_ + 1) * 512], ALU.mult))(pp, tsl),
                                deps=[e_mm, e_r1] + rope_evs, inc=vs)
                            e_r4 = sch.add("vector", (lambda t_: lambda v: v.tensor_tensor(
                                rope_sc[2][:], rope_sc[0][:], sin_t[:, t_ * 512:(t_ + 1) * 512], ALU.mult))(tsl),
                                deps=[e_r2, e_r3] + rope_evs, inc=vs)
                            e_evq = sch.add("vector", (lambda d_, t_: lambda v: v.tensor_tensor(
                                d_[:, t_ * 512:(t_ + 1) * 512], rope_sc[1][:], rope_sc[2][:], ALU.add))(dest[br], tsl),
                                deps=[e_r4, prev_head_end[0]], inc=vs)
                            prev_rope[0] = e_evq
                        else:
                            e_evq = sch.add("scalar", (lambda d_, t_, pp_: lambda s: s.copy(
                                d_[:, t_ * 512:(t_ + 1) * 512], ps_proj[pp_][:]))(dest[br], tsl, pp),
                                deps=[e_mm, prev_head_end[0]], inc=ss)
                        prev_pproj[pp] = e_evq
                        qk_done.append(e_evq)
            qk_all = qk_done[-1]

            e_vms = sch.add("vector", lambda v: v.memset(v_tm[:], 1.0),
                            deps=[prev_head_end[0]], inc=vs)
            v_done = []
            for kt in range(NT):
                pp = pcnt % 2
                pcnt += 1
                def vproj(kt_, pp_):
                    def f(te):
                        ins = None
                        for kc in range(KC):
                            ins = te.matmul(ps_proj[pp_][:, 0:DV],
                                            x_bf[kc][:, kt_ * 128:(kt_ + 1) * 128],
                                            w_bf["wv"][:, kc * FH: kc * FH + DV],
                                            start=(kc == 0), stop=(kc == KC - 1))
                        return ins
                    return f
                e_vm = sch.add("tensor", vproj(kt, pp),
                               deps=[w_evs["wv"], x_done, prev_pproj.get(pp)], inc=mm)
                e_ve = sch.add("scalar", (lambda kt_, pp_: lambda s: s.copy(
                    v_tm[:, kt_ * DVA: kt_ * DVA + DV], ps_proj[pp_][:, 0:DV]))(kt, pp),
                    deps=[e_vm, e_vms], inc=ss)
                prev_pproj[pp] = e_ve
                v_done.append(e_ve)
            v_all = v_done[-1]

            head_last = None
            for tq in range(NQ5):
                nch = 4 * tq + 4
                osb0_evs = {}
                for br in range(NBR):
                    pt_evs = []
                    for j in range(nch):
                        pb = j % 2
                        def sct(j_, tq_, br_, pb_):
                            def f(te):
                                return te.matmul(ps_sc[pb_][:],
                                                 k_bf[br_][:, j_ * 128:(j_ + 1) * 128],
                                                 q_bf[br_][:, tq_ * 512:(tq_ + 1) * 512],
                                                 start=True, stop=(j_ < 4 * tq_))
                            return f
                        e_s = sch.add("tensor", sct(j, tq, br, pb),
                                      deps=[qk_all, prev_psc.get(pb)], inc=mm)
                        if j >= 4 * tq:
                            r_i = j - 4 * tq
                            def mskmm(r_, pb_):
                                def f(te):
                                    return te.matmul(ps_sc[pb_][:], ident_bf[:],
                                                     mask_bf[:, r_ * 512:(r_ + 1) * 512],
                                                     start=False, stop=True)
                                return f
                            e_s = sch.add("tensor", mskmm(r_i, pb), deps=[e_s, e_mc, e_ic], inc=mm)
                        e_x = sch.add("scalar", (lambda j_, pb_: lambda s: s.activation(
                            pt[:, j_ * 512:(j_ + 1) * 512], ps_sc[pb_][:], AF.Exp,
                            bias=shift_t[:], scale=SM_SCALE))(j, pb),
                            deps=[e_s, e_sh], inc=ss)
                        prev_psc[pb] = e_x
                        pt_evs.append(e_x)
                    pt_all = pt_evs[-1]

                    for qs in range(4):
                        pb = qs % 2
                        def pvg(qs_, ncl_, pb_):
                            def f(te):
                                ins = None
                                for j in range(ncl_):
                                    ins = te.matmul(ps_pv[pb_][:],
                                                    pt[:, j * 512 + qs_ * 128: j * 512 + qs_ * 128 + 128],
                                                    v_tm[:, j * DVA:(j + 1) * DVA],
                                                    start=(j == 0), stop=(j == ncl_ - 1))
                                return ins
                            return f
                        e_pv = sch.add("tensor", pvg(qs, nch, pb),
                                       deps=[pt_all, v_all, prev_ppv.get(pb)], inc=mm)
                        e_rc2 = sch.add("vector", (lambda pb_, br_: lambda v: v.reciprocal(
                            rec[br_][:], ps_pv[pb_][:, DV:DVA]))(pb, br),
                            deps=[e_pv, prev_rec.get(br)], inc=vs)
                        if layer == 1:
                            sl = next_store_slot()
                            ab = sl % 2
                            e_n = sch.add("vector", (lambda pb_, ab_: lambda v: v.tensor_scalar(
                                a_f32[ab_][:, 0:DV], ps_pv[pb_][:, 0:DV], rec[0][:], None, ALU.mult))(pb, ab),
                                deps=[e_rc2, prev_af.get(ab)], inc=vs)
                            e_st = sch.add("gpsimd", (lambda tq_, qs_, h_, ab_: lambda g: g.dma_start(
                                out=a_out[(tq_ * 4 + qs_) * 128:(tq_ * 4 + qs_ + 1) * 128, h_ * FH:(h_ + 1) * FH],
                                in_=a_f32[ab_][:, 0:DV]))(tq, qs, h, ab),
                                deps=[e_n, st_prev[sl]], inc=st_s[sl], inc_n=16)
                            st_prev[sl] = e_st
                            prev_af[ab] = e_st
                            prev_rec[0] = e_n
                            prev_ppv[pb] = e_n
                            head_last = e_st
                        elif br == 0:
                            e_n = sch.add("vector", (lambda pb_, qs_: lambda v: v.tensor_scalar(
                                o_sb0[:, qs_ * DV:(qs_ + 1) * DV], ps_pv[pb_][:, 0:DV], rec[0][:], None, ALU.mult))(pb, qs),
                                deps=[e_rc2, prev_osb0.get(qs)], inc=vs)
                            osb0_evs[qs] = e_n
                            prev_rec[0] = e_n
                            prev_ppv[pb] = e_n
                        else:
                            e_c1 = sch.add("vector", lambda v: v.tensor_tensor(
                                nlam_rec[:], rec[1][:], lam_b[:], ALU.mult),
                                deps=[e_rc2, prev_nlam[0]] + lam_evs, inc=vs)
                            e_c2 = sch.add("vector", (lambda pb_: lambda v: v.tensor_scalar(
                                o1_sc[:], ps_pv[pb_][:, 0:DV], nlam_rec[:], None, ALU.mult))(pb),
                                deps=[e_c1, prev_o1[0]], inc=vs)
                            prev_rec[1] = e_c2
                            prev_ppv[pb] = e_c2
                            e_c3 = sch.add("vector", (lambda qs_: lambda v: v.tensor_tensor(
                                cmb[:], o_sb0[:, qs_ * DV:(qs_ + 1) * DV], o1_sc[:], ALU.subtract))(qs),
                                deps=[e_c2, osb0_evs[qs], prev_cmb[0]], inc=vs)
                            prev_osb0[qs] = e_c3
                            prev_o1[0] = e_c3
                            e_q1 = sch.add("scalar", lambda s: s.activation(
                                sq_s[:], cmb[:], AF.Square, accum_out=rms["sumsq"][:]),
                                deps=[e_c3, prev_rms[0]], inc=ss)
                            e_q2 = sch.add("vector", lambda v: v.tensor_scalar(
                                rms["ms"][:], rms["sumsq"][:], 1.0 / DV, None, ALU.mult),
                                deps=[e_q1], inc=vs)
                            e_q3 = sch.add("scalar", lambda s: s.activation(
                                rms["std"][:], rms["ms"][:], AF.Sqrt, bias=eps_t[:]),
                                deps=[e_q2, e_eps], inc=ss)
                            e_q4 = sch.add("vector", lambda v: v.reciprocal(rms["rstd"][:], rms["std"][:]),
                                           deps=[e_q3], inc=vs)
                            e_q5 = sch.add("vector", lambda v: v.tensor_scalar(
                                rms["rstd"][:], rms["rstd"][:], 1.0 - LAMBDA_INIT, None, ALU.mult),
                                deps=[e_q4], inc=vs)
                            sl = next_store_slot()
                            ab = sl % 2
                            e_q6 = sch.add("vector", (lambda ab_: lambda v: v.tensor_scalar(
                                a_f32[ab_][:], cmb[:], rms["rstd"][:], None, ALU.mult))(ab),
                                deps=[e_q5, e_c3, prev_af.get(ab)], inc=vs)
                            e_q7 = sch.add("vector", (lambda ab_: lambda v: v.tensor_tensor(
                                a_f32[ab_][:], a_f32[ab_][:], subln_t[:], ALU.mult))(ab),
                                deps=[e_q6, subln_ev], inc=vs)
                            prev_rms[0] = e_q7
                            prev_cmb[0] = e_q7
                            e_st = sch.add("gpsimd", (lambda tq_, qs_, h_, ab_: lambda g: g.dma_start(
                                out=a_out[(tq_ * 4 + qs_) * 128:(tq_ * 4 + qs_ + 1) * 128, h_ * FH:(h_ + 1) * FH],
                                in_=a_f32[ab_][:]))(tq, qs, h, ab),
                                deps=[e_q7, st_prev[sl]], inc=st_s[sl], inc_n=16)
                            st_prev[sl] = e_st
                            prev_af[ab] = e_st
                            head_last = e_st

            prev_head_end[0] = head_last
            for nm in ("wq", "wk", "wv"):
                prev_w_use[nm] = head_last

        sch.add("gpsimd", lambda g: None, deps=[(s, s.count) for s in st_s])
        sch.emit(nc)
    return nc


def _run(nc, in_maps):
    return run_bass_kernel_spmd(nc, in_maps, core_ids=list(range(N_CORES)))



def host_rope_tables():
    inv = 1.0 / (1e6 ** (np.arange(0, 128, 2, dtype=np.float64) / 128))
    t = np.arange(T_SEQ, dtype=np.float64)
    f = np.outer(t, inv)
    cos = np.concatenate([np.cos(f), np.cos(f)], axis=-1).astype(np.float32)
    sin = np.concatenate([np.sin(f), np.sin(f)], axis=-1).astype(np.float32)
    return np.ascontiguousarray(cos.T), np.ascontiguousarray(sin.T)


def host_masks():
    m = np.zeros((4, 128, 512), dtype=np.float32)
    kk = np.arange(128)[:, None]
    qq = np.arange(512)[None, :]
    for r in range(4):
        blk = qq // 128
        vis = (blk > r) | ((blk == r) & (kk <= (qq % 128)))
        m[r] = np.where(vis, 0.0, -1e30)
    return m


def host_hadamard128():
    H = np.array([[1.0]], dtype=np.float32)
    while H.shape[0] < 128:
        H = np.block([[H, H], [H, -H]])
    return np.ascontiguousarray(H.astype(np.float32))


def _ct(a):
    return np.ascontiguousarray(a)



_PROG_CACHE = {}

def _prog(key, builder, *args, **kwargs):
    if key not in _PROG_CACHE:
        _PROG_CACHE[key] = builder(*args, **kwargs)
    return _PROG_CACHE[key]


def kernel_multi(L0_attn_x, L0_attn_Wq, L0_attn_Wk, L0_attn_Wv, L0_attn_Wo,
           L0_attn_lambda_q1, L0_attn_lambda_k1, L0_attn_lambda_q2, L0_attn_lambda_k2,
           L0_attn_subln_weight,
           L0_ffn_gate_w, L0_ffn_up_w, L0_ffn_down_w, L0_ffn_gate_s, L0_ffn_up_s, L0_ffn_down_s,
           L1_attn_wq, L1_attn_wk, L1_attn_wv, L1_attn_wo,
           L1_ffn_SU_ug, L1_ffn_SU_dp, L1_ffn_Wscale_ug, L1_ffn_Wscale_dp,
           L1_ffn_W_up, L1_ffn_W_gate, L1_ffn_W_down):
    f32 = np.float32
    x = np.asarray(L0_attn_x, dtype=f32)
    x_flat = x.reshape(TOK_GLOBAL, D)
    masks = host_masks()
    ident = np.eye(128, dtype=f32)
    H128 = host_hadamard128()
    cosT, sinT = host_rope_tables()

    def rows(c):
        return slice(c * TOK_CORE, (c + 1) * TOK_CORE)

    nc_a0 = _prog("a0", build_attn_program, 0)
    lamv = np.concatenate([np.asarray(L0_attn_lambda_q1, f32), np.asarray(L0_attn_lambda_k1, f32),
                           np.asarray(L0_attn_lambda_q2, f32), np.asarray(L0_attn_lambda_k2, f32)])[None, :]
    subln_b = _ct(np.broadcast_to(np.asarray(L0_attn_subln_weight, f32), (128, 256)))
    Wq0 = np.asarray(L0_attn_Wq, f32); Wk0 = np.asarray(L0_attn_Wk, f32); Wv0 = np.asarray(L0_attn_Wv, f32)
    im = []
    for c in range(N_CORES):
        b, hh = c // 2, c % 2
        im.append({"xfm": _ct(x[b].T), "wq": _ct(Wq0[:, hh * 1024:(hh + 1) * 1024]),
                   "wk": _ct(Wk0[:, hh * 1024:(hh + 1) * 1024]),
                   "wv": _ct(Wv0[:, hh * 1024:(hh + 1) * 1024]),
                   "masks": masks, "ident": ident, "lamv": lamv, "subln_b": subln_b})
    res = _run(nc_a0, im)
    A0 = np.empty((B_SZ, T_SEQ, D), dtype=f32)
    for c in range(N_CORES):
        b, hh = c // 2, c % 2
        A0[b, :, hh * 1024:(hh + 1) * 1024] = res.results[c]["a_out"]
    A0_flat = A0.reshape(TOK_GLOBAL, D)

    nc_wo = _prog("pmm16_final", build_pmm_program, 16)
    Wo0 = np.asarray(L0_attn_Wo, f32)
    im = [{"a_fm": _ct(A0_flat[rows(c)].T), "w": Wo0, "resid": _ct(x_flat[rows(c)])}
          for c in range(N_CORES)]
    res = _run(nc_wo, im)
    R1 = np.concatenate([res.results[c]["out"] for c in range(N_CORES)], axis=0)

    nc_f0 = _prog("ffnup0", build_ffn_up_program, 0)
    gwT = _ct(np.asarray(L0_ffn_gate_w, f32).T)
    uwT = _ct(np.asarray(L0_ffn_up_w, f32).T)
    gs0 = _ct(np.asarray(L0_ffn_gate_s, f32)[:, None])
    us0 = _ct(np.asarray(L0_ffn_up_s, f32)[:, None])
    im = [{"x_fm": _ct(R1[rows(c)].T), "gw": gwT, "uw": uwT, "gs": gs0, "us": us0}
          for c in range(N_CORES)]
    res = _run(nc_f0, im)
    h0 = [res.results[c]["h_out"] for c in range(N_CORES)]

    dwT = _ct(np.asarray(L0_ffn_down_w, f32).T)
    ds_b = _ct(np.broadcast_to(np.asarray(L0_ffn_down_s, f32), (128, D)))
    nc_d1 = _prog("pmm22_part0", build_pmm_program, 22, True, False, False)
    im = [{"a_fm": _ct(h0[c][:2816]), "w": _ct(dwT[:2816])} for c in range(N_CORES)]
    res = _run(nc_d1, im)
    p = [res.results[c]["out"] for c in range(N_CORES)]
    nc_d2 = _prog("pmm22_fin_cs", build_pmm_program, 22, True, True, True, True, False)
    im = [{"a_fm": _ct(h0[c][2816:]), "w": _ct(dwT[2816:]), "p_in": p[c],
           "resid": _ct(R1[rows(c)]), "colscale_b": ds_b} for c in range(N_CORES)]
    res = _run(nc_d2, im)
    R2 = np.concatenate([res.results[c]["out"] for c in range(N_CORES)], axis=0)

    nc_a1 = _prog("a1", build_attn_program, 1)
    wq1 = np.asarray(L1_attn_wq, f32); wk1 = np.asarray(L1_attn_wk, f32); wv1 = np.asarray(L1_attn_wv, f32)
    R2b = R2.reshape(B_SZ, T_SEQ, D)
    im = []
    for c in range(N_CORES):
        b, hh = c // 2, c % 2
        im.append({"xfm": _ct(R2b[b].T), "wq": _ct(wq1[hh * 1024:(hh + 1) * 1024, :].T),
                   "wk": _ct(wk1[hh * 1024:(hh + 1) * 1024, :].T),
                   "wv": _ct(wv1[hh * 1024:(hh + 1) * 1024, :].T),
                   "masks": masks, "ident": ident, "rc": cosT, "rs": sinT})
    res = _run(nc_a1, im)
    A1 = np.empty((B_SZ, T_SEQ, D), dtype=f32)
    for c in range(N_CORES):
        b, hh = c // 2, c % 2
        A1[b, :, hh * 1024:(hh + 1) * 1024] = res.results[c]["a_out"]
    A1_flat = A1.reshape(TOK_GLOBAL, D)

    wo1T = _ct(np.asarray(L1_attn_wo, f32).T)
    im = [{"a_fm": _ct(A1_flat[rows(c)].T), "w": wo1T, "resid": _ct(R2[rows(c)])}
          for c in range(N_CORES)]
    res = _run(nc_wo, im)
    R3 = np.concatenate([res.results[c]["out"] for c in range(N_CORES)], axis=0)

    nc_fw = _prog("fwht", build_fwht_program)
    su_ug = _ct(np.asarray(L1_ffn_SU_ug, f32)[:, None])
    im = [{"x_fm": _ct(R3[rows(c)].T), "su": su_ug, "h128": H128} for c in range(N_CORES)]
    res = _run(nc_fw, im)
    t_fm = [res.results[c]["t_out"] for c in range(N_CORES)]

    nc_f1 = _prog("ffnup1", build_ffn_up_program, 1)
    wguT = _ct(np.asarray(L1_ffn_W_gate, f32).T)
    wupT = _ct(np.asarray(L1_ffn_W_up, f32).T)
    wsc = np.asarray(L1_ffn_Wscale_ug, f32)
    gs1 = _ct(wsc[INTER1:][:, None])
    us1 = _ct(wsc[:INTER1][:, None])
    su_dp = _ct(np.asarray(L1_ffn_SU_dp, f32)[:, None])
    im = [{"x_fm": t_fm[c], "gw": wguT, "uw": wupT, "gs": gs1, "us": us1,
           "su": su_dp, "h128": H128} for c in range(N_CORES)]
    res = _run(nc_f1, im)
    h2 = [res.results[c]["h_out"] for c in range(N_CORES)]

    wdT = _ct(np.asarray(L1_ffn_W_down, f32).T)
    wdp_b = _ct(np.broadcast_to(np.asarray(L1_ffn_Wscale_dp, f32), (128, D)))
    nc_p0 = _prog("pmm16_part0", build_pmm_program, 16, True, False, False)
    nc_pm = _prog("pmm16_partm", build_pmm_program, 16, True, True, False)
    nc_pf = _prog("pmm16_fin64", build_pmm_program, 16, True, True, True, True, True)
    im = [{"a_fm": _ct(h2[c][:2048]), "w": _ct(wdT[:2048])} for c in range(N_CORES)]
    res = _run(nc_p0, im)
    p = [res.results[c]["out"] for c in range(N_CORES)]
    for seg in (1, 2):
        im = [{"a_fm": _ct(h2[c][seg * 2048:(seg + 1) * 2048]),
               "w": _ct(wdT[seg * 2048:(seg + 1) * 2048]), "p_in": p[c]} for c in range(N_CORES)]
        res = _run(nc_pm, im)
        p = [res.results[c]["out"] for c in range(N_CORES)]
    im = [{"a_fm": _ct(h2[c][3 * 2048:]), "w": _ct(wdT[3 * 2048:]), "p_in": p[c],
           "resid": _ct(R3[rows(c)]), "colscale_b": wdp_b} for c in range(N_CORES)]
    res = _run(nc_pf, im)
    R4 = np.concatenate([res.results[c]["out"] for c in range(N_CORES)], axis=0)

    kernel_multi.dbg = {"A0": A0, "R1": R1, "R2": R2, "R3": R3, "h0": h0, "t_fm": t_fm,
                  "h2": h2, "R4": R4}
    return R4.reshape(B_SZ, T_SEQ, D)



from concourse.aot_env import allow_ici_all_to_all

SLAB_BF16_ELEMS = 106368


class Fctx:
    def __init__(self, nc, sch, slab, psum_f32, psum_bf16):
        self.nc = nc
        self.sch = sch
        self.slab = slab
        self.psum_f32 = psum_f32
        self.psum_bf16 = psum_bf16
        self.off = 0
        self.consts = {}

    def tile(self, n, dt=F32, parts=128):
        elems = n * (2 if dt == F32 else 1)
        off = (self.off + 1) & ~1
        self.off = off + elems
        assert self.off <= SLAB_BF16_ELEMS, f"slab overflow: {self.off}"
        sl = self.slab[:, off:off + elems]
        if dt == F32:
            sl = sl.bitcast(F32)
        elif dt != BF16:
            raise ValueError(dt)
        if parts != 128:
            sl = sl[0:parts, :]
        return sl

    def mark(self):
        return self.off

    def reset(self, m):
        self.off = m


def fsem_pool(sch):
    return {
        "x_ld": [sch.sem(f"fx_ld{i}") for i in range(4)],
        "w_ld": [sch.sem(f"fw_ld{i}") for i in range(6)],
        "r_ld": [sch.sem(f"fr_ld{i}") for i in range(2)],
        "p_ld": [sch.sem(f"fp_ld{i}") for i in range(2)],
        "sc_ld": [sch.sem(f"fsc_ld{i}") for i in range(2)],
        "c_ld": sch.sem("fc_ld"),
        "cc": sch.sem("fcc"),
        "cast": sch.sem("fcast"),
        "mm": sch.sem("fmm"),
        "vs": sch.sem("fvs"),
        "ss": sch.sem("fss"),
        "st": [sch.sem(f"fst{i}") for i in range(4)],
    }


def emit_const_phase(ctx, S, aps):
    sch = ctx.sch
    c_ld, cast, vs, ss, mm = S["c_ld"], S["cast"], S["vs"], S["ss"], S["mm"]
    C = {}
    C["mask_bf"] = ctx.tile(4 * 512, BF16)
    C["ident_bf"] = ctx.tile(128, BF16)
    C["h128_bf"] = ctx.tile(128, BF16)
    C["shift"] = ctx.tile(1, F32)
    C["eps"] = ctx.tile(1, F32)
    C["subln"] = ctx.tile(256, F32)
    C["lam_b"] = ctx.tile(1, F32)
    m_temp = ctx.mark()
    C["mask_f32"] = ctx.tile(4 * 512, F32)
    C["ident_f32"] = ctx.tile(128, F32)
    C["h128_f32"] = ctx.tile(128, F32)
    C["lam_row"] = ctx.tile(512, F32)
    C["lam_sc"] = ctx.tile(8, F32)
    C["ones_row"] = ctx.tile(128, F32)
    for r in range(4):
        sch.add("sync", (lambda r_: lambda s: s.dma_start(
            out=C["mask_f32"][:, r_ * 512:(r_ + 1) * 512], in_=aps["masks"][r_]))(r),
            inc=c_ld, inc_n=16)
    sch.add("sync", lambda s: s.dma_start(out=C["ident_f32"][:], in_=aps["ident"]), inc=c_ld, inc_n=16)
    sch.add("sync", lambda s: s.dma_start(out=C["h128_f32"][:], in_=aps["h128"]), inc=c_ld, inc_n=16)
    sch.add("sync", lambda s: s.dma_start(out=C["subln"][:], in_=aps["subln_b"]), inc=c_ld, inc_n=16)
    sch.add("sync", lambda s: s.dma_start(out=C["lam_row"][0:1, :], in_=aps["lamv"]), inc=c_ld, inc_n=16)
    c_done = (c_ld, c_ld.count)
    C["loaded"] = c_done
    e_mc = sch.add("vector", lambda v: v.tensor_copy(C["mask_bf"][:], C["mask_f32"][:]), deps=[c_done], inc=cast)
    e_ic = sch.add("vector", lambda v: v.tensor_copy(C["ident_bf"][:], C["ident_f32"][:]), deps=[c_done], inc=cast)
    e_hc = sch.add("vector", lambda v: v.tensor_copy(C["h128_bf"][:], C["h128_f32"][:]), deps=[c_done], inc=cast)
    e_sh = sch.add("vector", lambda v: v.memset(C["shift"][:], -SOFTMAX_SHIFT), inc=vs)
    e_ep = sch.add("vector", lambda v: v.memset(C["eps"][:], EPS), inc=vs)
    lr = C["lam_row"]; lsc = C["lam_sc"]
    e_p1 = sch.add("vector", lambda v: v.tensor_tensor(lr[0:1, 0:128], lr[0:1, 0:128], lr[0:1, 128:256], ALU.mult),
                   deps=[c_done], inc=vs)
    e_p2 = sch.add("vector", lambda v: v.tensor_tensor(lr[0:1, 256:384], lr[0:1, 256:384], lr[0:1, 384:512], ALU.mult),
                   deps=[c_done, e_p1], inc=vs)
    e_s1 = sch.add("vector", lambda v: v.reduce_sum(lsc[0:1, 0:1], lr[0:1, 0:128], axis=AX.X),
                   deps=[e_p1], inc=vs)
    e_s2 = sch.add("vector", lambda v: v.reduce_sum(lsc[0:1, 1:2], lr[0:1, 256:384], axis=AX.X),
                   deps=[e_p2, e_s1], inc=vs)
    e_e1 = sch.add("scalar", lambda s: s.activation(lsc[0:1, 2:4], lsc[0:1, 0:2], AF.Exp),
                   deps=[e_s1, e_s2], inc=ss)
    e_d = sch.add("vector", lambda v: v.tensor_tensor(lsc[0:1, 4:5], lsc[0:1, 2:3], lsc[0:1, 3:4], ALU.subtract),
                  deps=[e_e1], inc=vs)
    e_l = sch.add("vector", lambda v: v.tensor_scalar(lsc[0:1, 4:5], lsc[0:1, 4:5], LAMBDA_INIT, None, ALU.add),
                  deps=[e_d], inc=vs)
    e_or = sch.add("vector", lambda v: v.memset(C["ones_row"][0:1, :], 1.0), inc=vs)
    e_lb = sch.add("tensor", lambda t: t.matmul(ctx.psum_f32[4][:, 0:1], C["ones_row"][0:1, :],
                                                lsc[0:1, 4:5], start=True, stop=True),
                   deps=[e_l, e_or], inc=S["mm"])
    e_lbc = sch.add("vector", lambda v: v.tensor_copy(C["lam_b"][:], ctx.psum_f32[4][:, 0:1]),
                    deps=[e_lb], inc=vs)
    C["evs"] = [e_mc, e_ic, e_hc, e_sh, e_ep, e_lbc]
    ctx.consts = C
    ctx.reset(m_temp)


def emit_transpose_phase(ctx, S, src_dram, dst_dram, R, Cc, a2a_split=False):
    sch = ctx.sch
    m0 = ctx.mark()
    NR, NC = R // 128, Cc // 128
    sbf = [ctx.tile(Cc, BF16) for _ in range(2)]
    obuf = [ctx.tile(R, BF16) for _ in range(NC)]
    x_ld, cast, mm, vs, ss = S["x_ld"], S["cast"], S["mm"], S["vs"], S["ss"]
    st = S["st"]
    prev_cast = {}
    prev_ps = {}
    ev_by_j = {j: [] for j in range(NC)}
    for i in range(NR):
        buf = i % 2
        e_c = sch.add("gpsimd", (lambda i_, bb: lambda g: g.dma_start(
            out=sbf[bb][:], in_=src_dram[i_ * 128:(i_ + 1) * 128, :]))(i, buf),
            deps=[prev_cast.get(("b", buf))], inc=x_ld[buf], inc_n=16)
        prev_cast[buf] = e_c
        for j in range(NC):
            pb = j % 2
            e_t = sch.add("tensor", (lambda bb, j_, pb_: lambda te: te.transpose(
                ctx.psum_bf16[pb_][:], sbf[bb][:, j_ * 128:(j_ + 1) * 128],
                ctx.consts["ident_bf"][:]))(buf, j, pb),
                deps=[e_c, prev_ps.get(pb)] + [ctx.consts["evs"][1]], inc=mm)
            e_e = sch.add("vector", (lambda j_, i_, pb_: lambda v: v.tensor_copy(
                obuf[j_][:, i_ * 128:(i_ + 1) * 128], ctx.psum_bf16[pb_][:]))(j, i, pb),
                deps=[e_t], inc=vs)
            prev_ps[pb] = e_e
            ev_by_j[j].append(e_e)
        prev_cast[("b", buf)] = ev_by_j[NC - 1][-1]
    for j in range(NC):
        sl = j % 4
        if a2a_split:
            H = R // 2
            for bb in range(2):
                sch.add("gpsimd", (lambda j_, b_: lambda g: g.dma_start(
                    out=dst_dram[b_ * Cc + j_ * 128: b_ * Cc + (j_ + 1) * 128, :],
                    in_=obuf[j_][:, b_ * H:(b_ + 1) * H]))(j, bb),
                    deps=ev_by_j[j], inc=st[sl], inc_n=16)
        else:
            sch.add("gpsimd", (lambda j_: lambda g: g.dma_start(
                out=dst_dram[j_ * 128:(j_ + 1) * 128, :], in_=obuf[j_][:]))(j),
                deps=ev_by_j[j], inc=st[sl], inc_n=16)
    ctx.reset(m0)


def emit_collective(ctx, S, kind, in_ap, out_ap, op=None):
    sch = ctx.sch
    op = ALU.bypass if op is None else op
    def f(g):
        with allow_ici_all_to_all():
            return g.collective_compute(
                kind, op,
                replica_groups=[[0, 1], [2, 3], [4, 5], [6, 7]],
                ins=[in_ap.ap().opt()], outs=[out_ap.ap().opt()])
    return sch.add("gpsimd", f, inc=S["cc"], inc_n=1)


def emit_attn_phase(ctx, S, layer, cc_x_out, wq, wk, wv, afm_tiles, x_dep=None,
                    rc_ap=None, rs_ap=None):
    sch = ctx.sch
    m0 = ctx.mark()
    NH = 8 if layer == 1 else 4
    NBR = 1 if layer == 1 else 2
    DV = 128 if layer == 1 else 256
    DVA = DV + 1
    FH = 1024 // NH
    KC = D // 128
    NT = T_SEQ // 128
    NQ5 = T_SEQ // 512
    C = ctx.consts
    x_ldS, w_lds, castS, mmS, vsS, ssS, stS = (S["x_ld"], S["w_ld"], S["cast"],
                                               S["mm"], S["vs"], S["ss"], S["st"])
    sc_ld = S["sc_ld"]

    x_bf = [ctx.tile(T_SEQ, BF16) for _ in range(KC)]
    w_bf = [{nm: ctx.tile(KC * FH, BF16) for nm in ("wq", "wk", "wv")} for _ in range(2)]
    q_bf = [ctx.tile(T_SEQ, BF16) for _ in range(NBR)]
    k_bf = [ctx.tile(T_SEQ, BF16) for _ in range(NBR)]
    v_tm = ctx.tile(NT * DVA, BF16)
    pt = ctx.tile(NT * 512, BF16)
    rec = [ctx.tile(1, F32) for _ in range(NBR)]
    a_f32 = [ctx.tile(DV, F32) for _ in range(2)]
    a_bfs_t = ctx.tile(DV, BF16)
    a_bfs = [a_bfs_t, a_bfs_t]
    if layer == 1:
        rope_sc = [ctx.tile(512, F32) for _ in range(3)]
        cos_l = ctx.tile(2048, F32)
        sin_l = ctx.tile(2048, F32)
    else:
        nlam_rec = ctx.tile(1, F32)
        sq_s = ctx.tile(256, F32)
        o_sb0 = ctx.tile(4 * 256, F32)
        o1_sc = ctx.tile(256, F32)
        cmb = ctx.tile(256, F32)
        rms = {nm: ctx.tile(1, F32) for nm in ("sumsq", "ms", "std", "rstd")}
    ps_proj = [ctx.psum_f32[0], ctx.psum_f32[1]]
    ps_sc = [ctx.psum_f32[2], ctx.psum_f32[3]]
    ps_pv = [ctx.psum_f32[4][:, 0:DVA], ctx.psum_f32[5][:, 0:DVA]]

    x_evs = []
    for k in range(KC):
        buf = k % 2
        dep = [x_evs[k - 2][1]] if k >= 2 else []
        if x_dep is not None:
            dep = dep + [x_dep]
        e0 = sch.add("sync", (lambda kk: lambda s: s.dma_start(
            out=x_bf[kk][:, 0:1024], in_=cc_x_out[kk * 128:(kk + 1) * 128, :]))(k),
            deps=dep, inc=x_ldS[buf], inc_n=16)
        e1 = sch.add("sync", (lambda kk: lambda s: s.dma_start(
            out=x_bf[kk][:, 1024:2048], in_=cc_x_out[2048 + kk * 128: 2048 + (kk + 1) * 128, :]))(k),
            deps=[e0], inc=x_ldS[buf], inc_n=16)
        x_evs.append((e0, e1))
    x_done = [(x_ldS[0], x_ldS[0].count), (x_ldS[1], x_ldS[1].count)]

    st_prev = [None] * 4
    st_ctr = [0]
    def next_store_slot():
        i = st_ctr[0] % 4
        st_ctr[0] += 1
        return i

    prev_tp = {}
    prev_abfs = {}
    NDS = DV // 128
    def emit_a_out(e_src, ab, h, tq, qs):
        e_c = sch.add("vector", (lambda ab_: lambda v: v.tensor_copy(a_bfs[ab_][:], a_f32[ab_][:]))(ab),
                      deps=[e_src, prev_abfs.get(0), prev_abfs.get(1)], inc=vsS)
        last = None
        for dsub in range(NDS):
            tb = (st_ctr[0] + dsub) % 2
            fc = h * NDS + dsub
            qcol = (tq * 4 + qs) * 128
            e_t = sch.add("tensor", (lambda ab_, d_, tb_: lambda te: te.transpose(
                ctx.psum_bf16[tb_][:], a_bfs[ab_][:, d_ * 128:(d_ + 1) * 128],
                C["ident_bf"][:]))(ab, dsub, tb),
                deps=[e_c, prev_tp.get(tb), C["evs"][1]], inc=mmS)
            e_e = sch.add("vector", (lambda fc_, qc_, tb_: lambda v: v.tensor_copy(
                afm_tiles[fc_][:, qc_:qc_ + 128], ctx.psum_bf16[tb_][:]))(fc, qcol, tb),
                deps=[e_t], inc=vsS)
            prev_tp[tb] = e_e
            last = e_e
        st_ctr[0] += NDS
        prev_abfs[ab] = last
        return last

    prev_w_use = {}
    prev_head_end = [None]
    w_ev_by_head = {}
    prev_pproj = {}
    prev_psc = {}
    prev_ppv = {}
    prev_af = {}
    prev_rope = [None]
    prev_rms = [None]
    prev_osb0 = {}
    prev_o1 = [None]
    prev_cmb = [None]
    prev_rec = {}
    prev_nlam = [None]
    e_mc, e_ic, e_hc, e_sh, e_ep, e_lbc = C["evs"]
    lam_evs = [e_lbc]
    eps_t = C["eps"]; shift_t = C["shift"]
    mask_bf = C["mask_bf"]; ident_bf = C["ident_bf"]
    subln_t = C["subln"]; lam_b = C["lam_b"]
    subln_ev = C["loaded"]
    rope_evs = []
    if layer == 1:
        cos_t, sin_t = cos_l, sin_l
        e_rc = sch.add("sync", lambda s: s.dma_start(out=cos_t[:], in_=rc_ap), inc=S["r_ld"][0], inc_n=16)
        e_rs = sch.add("sync", lambda s: s.dma_start(out=sin_t[:], in_=rs_ap), deps=[e_rc], inc=S["r_ld"][1], inc_n=16)
        rope_evs = [e_rc, e_rs]
    else:
        cos_t = sin_t = None

    def load_head_weights(h):
        hb = h % 2
        evs = {}
        for wi, (nm, wap) in enumerate((("wq", wq), ("wk", wk), ("wv", wv))):
            e_lw = sch.add("gpsimd", (lambda w_, c0, c1, nm_, hb_: lambda g: g.dma_start(
                out=w_bf[hb_][nm_][:].rearrange("p (c f) -> p c f", c=KC),
                in_=w_[:, c0:c1].rearrange("(c p) f -> p c f", p=128)))(
                    wap, h * FH, (h + 1) * FH, nm, hb),
                deps=[prev_w_use.get((nm, hb))], inc=w_lds[wi + 3 * hb], inc_n=16)
            evs[nm] = e_lw
        w_ev_by_head[h] = evs

    load_head_weights(0)
    if NH > 1:
        load_head_weights(1)

    for h in range(NH):
        hb = h % 2
        w_evs = w_ev_by_head[h]

        qk_done = []
        pcnt = 0
        for nm, dest in (("wq", q_bf), ("wk", k_bf)):
            for br in range(NBR):
                for tsl in range(4):
                    pp = pcnt % 2
                    pcnt += 1
                    def proj(nm_, br_, tsl_, pp_, hb_=hb):
                        def f(te):
                            ins = None
                            for kc in range(KC):
                                ins = te.matmul(ps_proj[pp_][:],
                                                w_bf[hb_][nm_][:, kc * FH + br_ * 128: kc * FH + br_ * 128 + 128],
                                                x_bf[kc][:, tsl_ * 512:(tsl_ + 1) * 512],
                                                start=(kc == 0), stop=(kc == KC - 1))
                            return ins
                        return f
                    e_mm = sch.add("tensor", proj(nm, br, tsl, pp),
                                   deps=[w_evs[nm], prev_pproj.get(pp)] + x_done, inc=mmS)
                    if layer == 1:
                        e_r1 = sch.add("vector", (lambda pp_: lambda v: v.tensor_scalar(
                            rope_sc[0][0:64, :], ps_proj[pp_][64:128, :], -1.0, None, ALU.mult))(pp),
                            deps=[e_mm, prev_rope[0]], inc=vsS)
                        e_r2 = sch.add("vector", (lambda pp_: lambda v: v.tensor_copy(
                            rope_sc[0][64:128, :], ps_proj[pp_][0:64, :]))(pp),
                            deps=[e_mm, e_r1], inc=vsS)
                        e_r3 = sch.add("vector", (lambda pp_, t_: lambda v: v.tensor_tensor(
                            rope_sc[1][:], ps_proj[pp_][:], cos_t[:, t_ * 512:(t_ + 1) * 512], ALU.mult))(pp, tsl),
                            deps=[e_mm, e_r1] + rope_evs, inc=vsS)
                        e_r4 = sch.add("vector", (lambda t_: lambda v: v.tensor_tensor(
                            rope_sc[2][:], rope_sc[0][:], sin_t[:, t_ * 512:(t_ + 1) * 512], ALU.mult))(tsl),
                            deps=[e_r2, e_r3] + rope_evs, inc=vsS)
                        e_evq = sch.add("vector", (lambda d_, t_: lambda v: v.tensor_tensor(
                            d_[:, t_ * 512:(t_ + 1) * 512], rope_sc[1][:], rope_sc[2][:], ALU.add))(dest[br], tsl),
                            deps=[e_r4, prev_head_end[0]], inc=vsS)
                        prev_rope[0] = e_evq
                    else:
                        e_evq = sch.add("scalar", (lambda d_, t_, pp_: lambda s: s.copy(
                            d_[:, t_ * 512:(t_ + 1) * 512], ps_proj[pp_][:]))(dest[br], tsl, pp),
                            deps=[e_mm, prev_head_end[0]], inc=ssS)
                    prev_pproj[pp] = e_evq
                    qk_done.append(e_evq)
        qk_all = qk_done[-1]

        e_vms = sch.add("vector", lambda v: v.memset(v_tm[:], 1.0),
                        deps=[prev_head_end[0]], inc=vsS)
        v_done = []
        for kt in range(NT):
            pp = pcnt % 2
            pcnt += 1
            def vproj(kt_, pp_, hb_=hb):
                def f(te):
                    ins = None
                    for kc in range(KC):
                        ins = te.matmul(ps_proj[pp_][:, 0:DV],
                                        x_bf[kc][:, kt_ * 128:(kt_ + 1) * 128],
                                        w_bf[hb_]["wv"][:, kc * FH: kc * FH + DV],
                                        start=(kc == 0), stop=(kc == KC - 1))
                    return ins
                return f
            e_vm = sch.add("tensor", vproj(kt, pp),
                           deps=[w_evs["wv"], prev_pproj.get(pp)] + x_done, inc=mmS)
            e_ve = sch.add("scalar", (lambda kt_, pp_: lambda s: s.copy(
                v_tm[:, kt_ * DVA: kt_ * DVA + DV], ps_proj[pp_][:, 0:DV]))(kt, pp),
                deps=[e_vm, e_vms], inc=ssS)
            prev_pproj[pp] = e_ve
            v_done.append(e_ve)
        v_all = v_done[-1]

        head_last = None
        for tq in range(NQ5):
            nch = 4 * tq + 4
            osb0_evs = {}
            for br in range(NBR):
                pt_evs = []
                for j in range(nch):
                    pb = j % 2
                    def sct(j_, tq_, br_, pb_):
                        def f(te):
                            return te.matmul(ps_sc[pb_][:],
                                             k_bf[br_][:, j_ * 128:(j_ + 1) * 128],
                                             q_bf[br_][:, tq_ * 512:(tq_ + 1) * 512],
                                             start=True, stop=(j_ < 4 * tq_))
                        return f
                    e_s = sch.add("tensor", sct(j, tq, br, pb),
                                  deps=[qk_all, prev_psc.get(pb)], inc=mmS)
                    if j >= 4 * tq:
                        r_i = j - 4 * tq
                        def mskmm(r_, pb_):
                            def f(te):
                                return te.matmul(ps_sc[pb_][:], ident_bf[:],
                                                 mask_bf[:, r_ * 512:(r_ + 1) * 512],
                                                 start=False, stop=True)
                            return f
                        e_s = sch.add("tensor", mskmm(r_i, pb), deps=[e_s, e_mc, e_ic], inc=mmS)
                    e_x = sch.add("scalar", (lambda j_, pb_: lambda s: s.activation(
                        pt[:, j_ * 512:(j_ + 1) * 512], ps_sc[pb_][:], AF.Exp,
                        bias=shift_t[:], scale=SM_SCALE))(j, pb),
                        deps=[e_s, e_sh], inc=ssS)
                    prev_psc[pb] = e_x
                    pt_evs.append(e_x)
                pt_all = pt_evs[-1]

                for qs in range(4):
                    pb = qs % 2
                    def pvg(qs_, ncl_, pb_):
                        def f(te):
                            ins = None
                            for j in range(ncl_):
                                ins = te.matmul(ps_pv[pb_][:],
                                                pt[:, j * 512 + qs_ * 128: j * 512 + qs_ * 128 + 128],
                                                v_tm[:, j * DVA:(j + 1) * DVA],
                                                start=(j == 0), stop=(j == ncl_ - 1))
                            return ins
                        return f
                    e_pv = sch.add("tensor", pvg(qs, nch, pb),
                                   deps=[pt_all, v_all, prev_ppv.get(pb), e_lbc], inc=mmS)
                    e_rc2 = sch.add("vector", (lambda pb_, br_: lambda v: v.reciprocal(
                        rec[br_][:], ps_pv[pb_][:, DV:DVA]))(pb, br),
                        deps=[e_pv, prev_rec.get(br)], inc=vsS)
                    if layer == 1:
                        ab = st_ctr[0] % 2
                        e_n = sch.add("vector", (lambda pb_, ab_: lambda v: v.tensor_scalar(
                            a_f32[ab_][:, 0:DV], ps_pv[pb_][:, 0:DV], rec[0][:], None, ALU.mult))(pb, ab),
                            deps=[e_rc2, prev_af.get(ab)], inc=vsS)
                        e_st = emit_a_out(e_n, ab, h, tq, qs)
                        prev_af[ab] = e_st
                        prev_rec[0] = e_n
                        prev_ppv[pb] = e_n
                        head_last = e_st
                    elif br == 0:
                        e_n = sch.add("vector", (lambda pb_, qs_: lambda v: v.tensor_scalar(
                            o_sb0[:, qs_ * DV:(qs_ + 1) * DV], ps_pv[pb_][:, 0:DV], rec[0][:], None, ALU.mult))(pb, qs),
                            deps=[e_rc2, prev_osb0.get(qs)], inc=vsS)
                        osb0_evs[qs] = e_n
                        prev_rec[0] = e_n
                        prev_ppv[pb] = e_n
                    else:
                        e_c1 = sch.add("vector", lambda v: v.tensor_tensor(
                            nlam_rec[:], rec[1][:], lam_b[:], ALU.mult),
                            deps=[e_rc2, prev_nlam[0]] + lam_evs, inc=vsS)
                        e_c2 = sch.add("vector", (lambda pb_: lambda v: v.tensor_scalar(
                            o1_sc[:], ps_pv[pb_][:, 0:DV], nlam_rec[:], None, ALU.mult))(pb),
                            deps=[e_c1, prev_o1[0]], inc=vsS)
                        prev_rec[1] = e_c2
                        prev_ppv[pb] = e_c2
                        prev_nlam[0] = e_c2
                        e_c3 = sch.add("vector", (lambda qs_: lambda v: v.tensor_tensor(
                            cmb[:], o_sb0[:, qs_ * 256:(qs_ + 1) * 256], o1_sc[:], ALU.subtract))(qs),
                            deps=[e_c2, osb0_evs[qs], prev_cmb[0]], inc=vsS)
                        prev_osb0[qs] = e_c3
                        prev_o1[0] = e_c3
                        e_q1 = sch.add("scalar", lambda s: s.activation(
                            sq_s[:], cmb[:], AF.Square, accum_out=rms["sumsq"][:]),
                            deps=[e_c3, prev_rms[0]], inc=ssS)
                        e_q2 = sch.add("vector", lambda v: v.tensor_scalar(
                            rms["ms"][:], rms["sumsq"][:], 1.0 / DV, None, ALU.mult),
                            deps=[e_q1], inc=vsS)
                        e_q3 = sch.add("scalar", lambda s: s.activation(
                            rms["std"][:], rms["ms"][:], AF.Sqrt, bias=eps_t[:]),
                            deps=[e_q2, e_ep], inc=ssS)
                        e_q4 = sch.add("vector", lambda v: v.reciprocal(rms["rstd"][:], rms["std"][:]),
                                       deps=[e_q3], inc=vsS)
                        e_q5 = sch.add("vector", lambda v: v.tensor_scalar(
                            rms["rstd"][:], rms["rstd"][:], 1.0 - LAMBDA_INIT, None, ALU.mult),
                            deps=[e_q4], inc=vsS)
                        ab = st_ctr[0] % 2
                        e_q6 = sch.add("vector", (lambda ab_: lambda v: v.tensor_scalar(
                            a_f32[ab_][:], cmb[:], rms["rstd"][:], None, ALU.mult))(ab),
                            deps=[e_q5, e_c3, prev_af.get(ab)], inc=vsS)
                        e_q7 = sch.add("vector", (lambda ab_: lambda v: v.tensor_tensor(
                            a_f32[ab_][:], a_f32[ab_][:], subln_t[:], ALU.mult))(ab),
                            deps=[e_q6, subln_ev], inc=vsS)
                        prev_rms[0] = e_q7
                        prev_cmb[0] = e_q7
                        e_st = emit_a_out(e_q7, ab, h, tq, qs)
                        prev_af[ab] = e_st
                        head_last = e_st

        prev_head_end[0] = head_last
        for nm in ("wq", "wk", "wv"):
            prev_w_use[(nm, hb)] = head_last
        if h + 2 < NH:
            load_head_weights(h + 2)
    ctx.reset(m0)


def emit_pmm_phase(ctx, S, kch, a_src, w_src, out_dram, a_is_bf16=True,
                   p_src=None, resid_src=None, colscale_src=None, scale64=False,
                   finalize=False):
    sch = ctx.sch
    m0 = ctx.mark()
    TT = TOK_CORE // 128
    OC = D // 512
    NB = 1 if kch > 16 else 2
    a_ld, w_ld, r_ld, p_ld = S["x_ld"], S["w_ld"], S["r_ld"], S["p_ld"]
    c_ld, cast, mm, vs, ss = S["c_ld"], S["cast"], S["mm"], S["vs"], S["ss"]
    st_sems = S["st"]
    a_bf = [ctx.tile(TOK_CORE, BF16) for _ in range(kch)]
    w_bf = [ctx.tile(D, BF16) for _ in range(kch)]
    a_f32 = None if a_is_bf16 else [ctx.tile(TOK_CORE, F32) for _ in range(NB)]
    r_f32 = [ctx.tile(D, F32) for _ in range(NB)] if finalize else None
    p_f32 = [ctx.tile(D, F32) for _ in range(NB)] if p_src is not None else None
    cs_t = ctx.tile(D, F32) if colscale_src is not None else None
    y = [ctx.tile(D, F32) for _ in range(NB)]
    o_t = y
    stats = None
    if finalize:
        stats = {nm: ctx.tile(1, F32) for nm in ("sum1", "sumsq", "mean", "var", "m2", "std", "rstd", "nmr")}
        stats["sq_scratch"] = ctx.tile(D, F32)
    eps_t = ctx.consts["eps"]
    ps = [ctx.psum_f32[i] for i in range(4)]

    cs_ev = None
    if colscale_src is not None:
        e_cl = sch.add("sync", lambda s: s.dma_start(out=cs_t[:], in_=colscale_src), inc=c_ld, inc_n=16)
        if scale64:
            cs_ev = sch.add("vector", lambda v: v.tensor_scalar(cs_t[:], cs_t[:], 64.0, None, ALU.mult),
                            deps=[e_cl], inc=vs)
        else:
            cs_ev = e_cl

    cast_evs = []
    for k in range(kch):
        buf = k % NB
        abuf = k % 4
        dep_a = [cast_evs[k - 4][0]] if k >= 4 else []
        dep_w = [cast_evs[k - 4][1]] if k >= 4 else []
        if a_is_bf16:
            e_ca = sch.add("sync", (lambda kk: lambda s: s.dma_start(
                out=a_bf[kk][:], in_=a_src(kk)))(k),
                deps=dep_a, inc=a_ld[abuf], inc_n=16)
        else:
            e_la = sch.add("sync", (lambda kk, bb: lambda s: s.dma_start(
                out=a_f32[bb][:], in_=a_src(kk)))(k, buf),
                deps=[cast_evs[k - NB][0]] if k >= NB else [], inc=a_ld[buf], inc_n=16)
            e_ca = sch.add("vector", (lambda kk, bb: lambda v: v.tensor_copy(a_bf[kk][:], a_f32[bb][:]))(k, buf),
                           deps=[e_la], inc=cast)
        e_cw = sch.add("gpsimd", (lambda kk: lambda g: g.dma_start(
            out=w_bf[kk][:], in_=w_src(kk)))(k),
            deps=dep_w, inc=w_ld[k % 2], inc_n=16)
        cast_evs.append((e_ca, e_cw))
    kh = kch // 2
    loads_h1 = [e for pair in cast_evs[:kh] for e in pair if e is not None]
    loads_done = [cast_evs[-1][0], cast_evs[-1][1], (cast, cast.count)] + \
                 [(a_ld[i], a_ld[i].count) for i in range(4)] + \
                 [(w_ld[0], w_ld[0].count), (w_ld[1], w_ld[1].count)]

    prev_ev = {}
    prev_r = {}
    prev_p = {}
    prev_out = {}
    prev_ln_out = None
    for t in range(TT):
        buf = t % NB
        e_lr = None
        e_lp = None
        if finalize:
            e_lr = sch.add("sync", (lambda tt, bb: lambda s: s.dma_start(
                out=r_f32[bb][:], in_=resid_src[tt * 128:(tt + 1) * 128, :]))(t, buf),
                deps=[prev_r.get(buf)], inc=r_ld[buf % 2], inc_n=16)
        if p_src is not None:
            e_lp = sch.add("sync", (lambda tt, bb: lambda s: s.dma_start(
                out=p_f32[bb][:], in_=p_src[tt * 128:(tt + 1) * 128, :]))(t, buf),
                deps=[prev_p.get(buf)], inc=p_ld[buf % 2], inc_n=16)
        mm_evs = []
        for o in range(OC):
            def mmgroup(tt, oo, k0, k1):
                def f(te):
                    ins = None
                    for k in range(k0, k1):
                        ins = te.matmul(ps[oo][:],
                                        a_bf[k][:, tt * 128:(tt + 1) * 128],
                                        w_bf[k][:, oo * 512:(oo + 1) * 512],
                                        start=(k == 0), stop=(k == kch - 1))
                    return ins
                return f
            e_mm = sch.add("tensor", mmgroup(t, o, 0, kch),
                           deps=loads_done + [prev_ev.get(o)], inc=mm)
            mm_evs.append(e_mm)
        ev_evs = []
        for o in range(OC):
            sl = slice(o * 512, (o + 1) * 512)
            if p_src is not None:
                e_a = sch.add("vector", (lambda oo, bb, sl_: lambda v: v.tensor_tensor(
                    y[bb][:, sl_], ps[oo][:], p_f32[bb][:, sl_], ALU.add))(o, buf, sl),
                    deps=[mm_evs[o], e_lp, prev_ln_out, prev_out.get(buf)], inc=vs)
            else:
                e_a = sch.add("vector", (lambda oo, bb, sl_: lambda v: v.tensor_copy(
                    y[bb][:, sl_], ps[oo][:]))(o, buf, sl),
                    deps=[mm_evs[o], prev_ln_out, prev_out.get(buf)], inc=vs)
            step = e_a
            if finalize:
                if colscale_src is not None:
                    step = sch.add("vector", (lambda oo, bb, sl_: lambda v: v.tensor_tensor(
                        y[bb][:, sl_], y[bb][:, sl_], cs_t[:, sl_], ALU.mult))(o, buf, sl),
                        deps=[step, cs_ev], inc=vs)
                step = sch.add("vector", (lambda oo, bb, sl_: lambda v: v.tensor_tensor(
                    y[bb][:, sl_], y[bb][:, sl_], r_f32[bb][:, sl_], ALU.add))(o, buf, sl),
                    deps=[step, e_lr], inc=vs)
            prev_ev[o] = e_a
            ev_evs.append(step)
        if finalize:
            e_out = emit_layernorm(sch, y[buf], o_t[buf], stats, eps_t, D,
                                   deps=ev_evs + [ctx.consts["evs"][4]], v_sem=vs, s_sem=ss,
                                   out_free_dep=prev_out.get(buf))
            prev_ln_out = e_out
            src_tile = o_t[buf]
        else:
            e_out = ev_evs[-1]
            src_tile = y[buf]
        e_st = sch.add("gpsimd", (lambda tt, s_: lambda g: g.dma_start(
            out=out_dram[tt * 128:(tt + 1) * 128, :], in_=s_[:]))(t, src_tile),
            deps=ev_evs + [e_out], inc=st_sems[buf % 2], inc_n=16)
        prev_r[buf] = ev_evs[-1]
        prev_p[buf] = ev_evs[-1]
        prev_out[buf] = e_st
    ctx.reset(m0)


def emit_fwht_phase(ctx, S, x_src, su_src, t_out, h2048, x_tiles=None):
    sch = ctx.sch
    m0 = ctx.mark()
    KC = D // 128
    x_ld, w_ld, mm, vs, ss = S["x_ld"], S["w_ld"], S["mm"], S["vs"], S["ss"]
    sc_ld, st_sems = S["sc_ld"], S["st"]
    xs_bf = x_tiles if x_tiles is not None else [ctx.tile(TOK_CORE, BF16) for _ in range(KC)]
    h_bf = [ctx.tile(D, BF16) for _ in range(KC)]
    su_t = ctx.tile(KC, F32)
    to_t = [ctx.tile(TOK_CORE, BF16) for _ in range(2)]
    ps2 = [ctx.psum_f32[0], ctx.psum_f32[1]]

    for k in range(KC):
        sch.add("sync", (lambda kk: lambda s: s.dma_start(
            out=su_t[:, kk:kk + 1], in_=su_src[kk * 128:(kk + 1) * 128, :]))(k),
            inc=sc_ld[0], inc_n=16)
    e_su = (sc_ld[0], sc_ld[0].count)

    xev = []
    hev = []
    for k in range(KC):
        buf = k % 2
        if x_tiles is None:
            e_lx = sch.add("sync", (lambda kk: lambda s: s.dma_start(
                out=xs_bf[kk][:], in_=x_src[kk * 128:(kk + 1) * 128, :]))(k),
                deps=[xev[k - 2]] if k >= 2 else [], inc=x_ld[buf], inc_n=16)
            lx_dep = [e_lx, e_su]
        else:
            lx_dep = [e_su]
        e_ps = sch.add("vector", (lambda kk: lambda v: v.tensor_scalar(
            xs_bf[kk][:], xs_bf[kk][:], su_t[:, kk:kk + 1], None, ALU.mult))(k),
            deps=lx_dep, inc=vs)
        xev.append(e_ps)
        e_lh = sch.add("scalar", (lambda kk: lambda s: s.dma_start(
            out=h_bf[kk][:], in_=h2048[kk * 128:(kk + 1) * 128, :]))(k),
            deps=[hev[k - 2]] if k >= 2 else [], inc=w_ld[k % 2], inc_n=16)
        hev.append(e_lh)
    loads = [xev[-1], hev[-1], (vs, vs.count),
             (x_ld[0], x_ld[0].count), (x_ld[1], x_ld[1].count),
             (w_ld[0], w_ld[0].count), (w_ld[1], w_ld[1].count)]

    st_prev = [None, None]
    prev_ps = {}
    for fo in range(KC):
        sl2 = fo % 2
        ev_th = []
        for th in range(2):
            pp = th
            def mmg(fo_, th_, pp_):
                def f(te):
                    ins = None
                    for fi in range(KC):
                        ins = te.matmul(ps2[pp_][:],
                                        h_bf[fi][:, fo_ * 128:(fo_ + 1) * 128],
                                        xs_bf[fi][:, th_ * 512:(th_ + 1) * 512],
                                        start=(fi == 0), stop=(fi == KC - 1))
                    return ins
                return f
            e_m = sch.add("tensor", mmg(fo, th, pp), deps=loads + [prev_ps.get(pp)], inc=mm)
            e_e = sch.add("scalar", (lambda th_, pp_, sl_: lambda s: s.activation(
                to_t[sl_][:, th_ * 512:(th_ + 1) * 512], ps2[pp_][:], AF.Copy,
                scale=FWHT_SCALE_2048))(th, pp, sl2),
                deps=[e_m, st_prev[sl2]], inc=ss)
            prev_ps[pp] = e_e
            ev_th.append(e_e)
        e_st = sch.add("gpsimd", (lambda fo_, sl_: lambda g: g.dma_start(
            out=t_out[fo_ * 128:(fo_ + 1) * 128, :], in_=to_t[sl_][:]))(fo, sl2),
            deps=ev_th, inc=st_sems[sl2], inc_n=16)
        st_prev[sl2] = e_st
    ctx.reset(m0)


def emit_ffn_up_phase(ctx, S, layer, x_src, gw, uw, gs, us, su, h_out, x_tiles=None):
    sch = ctx.sch
    m0 = ctx.mark()
    NI = (INTER1 if layer == 1 else INTER0) // 128
    KC = D // 128
    x_ld, gw_ld, uw_ld, sc_ld = S["x_ld"], S["w_ld"][0:2], S["w_ld"][2:4], S["sc_ld"]
    c_ld, cast, mm, vs, ss, st_sems = S["c_ld"], S["cast"], S["mm"], S["vs"], S["ss"], S["st"]
    x_bf = x_tiles if x_tiles is not None else [ctx.tile(TOK_CORE, BF16) for _ in range(KC)]
    gw_bf = [ctx.tile(2048, BF16) for _ in range(2)]
    uw_bf = [ctx.tile(2048, BF16) for _ in range(2)]
    scs = [ctx.tile(4, F32) for _ in range(2)]
    sg_t = [ctx.tile(512, F32) for _ in range(2)]
    if layer == 1:
        hbuf = [ctx.tile(TOK_CORE, BF16) for _ in range(NI)]
        btmp = ctx.tile(TOK_CORE, BF16)
        ho_t = [ctx.tile(512, BF16) for _ in range(4)]
    else:
        ho_t = [ctx.tile(TOK_CORE, BF16) for _ in range(4)]
    ps_g = [ctx.psum_f32[0], ctx.psum_f32[1]]
    ps_u = [ctx.psum_f32[2], ctx.psum_f32[3]]
    h128_b = ctx.consts["h128_bf"]
    h_evs = [ctx.consts["evs"][2]]

    if x_tiles is None:
        x_cast_evs = []
        for k in range(KC):
            buf = k % 2
            e_cx = sch.add("sync", (lambda kk: lambda s: s.dma_start(
                out=x_bf[kk][:], in_=x_src[kk * 128:(kk + 1) * 128, :]))(k),
                deps=[x_cast_evs[k - 2]] if k >= 2 else [], inc=x_ld[buf], inc_n=16)
            x_cast_evs.append(e_cx)
        x_done = [(x_ld[0], x_ld[0].count), (x_ld[1], x_ld[1].count)]
    else:
        x_done = []

    st_prev = [None] * 4
    prev_wcast = {}
    prev_wuse = {}
    prev_scuse = {}
    prev_sg = {}
    prev_psg = {}
    prev_psu = {}
    prev_hbuf = {}
    hbuf_evs = []
    tile_ev = {}
    prev_btmp = [None]

    for i in range(NI):
        wb = i % 2
        e_cg = sch.add("gpsimd", (lambda i_, bb: lambda g: g.dma_start(
            out=gw_bf[bb][:].rearrange("p (c f) -> p c f", c=KC),
            in_=gw[:, i_ * 128:(i_ + 1) * 128].rearrange("(c p) f -> p c f", p=128)))(i, wb),
            deps=[prev_wuse.get(("g", wb))], inc=gw_ld[0], inc_n=16)
        e_cu = sch.add("gpsimd", (lambda i_, bb: lambda g: g.dma_start(
            out=uw_bf[bb][:].rearrange("p (c f) -> p c f", c=KC),
            in_=uw[:, i_ * 128:(i_ + 1) * 128].rearrange("(c p) f -> p c f", p=128)))(i, wb),
            deps=[prev_wuse.get(("u", wb)), e_cg], inc=uw_ld[0], inc_n=16)
        wb2 = i % 2
        e_ls = sch.add("sync", (lambda i_, bb: lambda s: s.dma_start(
            out=scs[bb][:, 0:1], in_=gs[i_ * 128:(i_ + 1) * 128, :]))(i, wb2),
            deps=[prev_scuse.get(wb2)], inc=sc_ld[wb2], inc_n=16)
        e_ls2 = sch.add("sync", (lambda i_, bb: lambda s: s.dma_start(
            out=scs[bb][:, 1:2], in_=us[i_ * 128:(i_ + 1) * 128, :]))(i, wb2),
            deps=[e_ls], inc=sc_ld[wb2], inc_n=16)
        if layer == 1:
            e_ls3 = sch.add("sync", (lambda i_, bb: lambda s: s.dma_start(
                out=scs[bb][:, 2:3], in_=su[i_ * 128:(i_ + 1) * 128, :]))(i, wb2),
                deps=[e_ls2], inc=sc_ld[wb2], inc_n=16)
            e_sc1 = sch.add("vector", (lambda bb: lambda v: v.tensor_scalar(
                scs[bb][:, 3:4], scs[bb][:, 1:2], 64.0, None, ALU.mult))(wb2),
                deps=[e_ls3], inc=vs)
            e_sc2 = sch.add("vector", (lambda bb: lambda v: v.tensor_tensor(
                scs[bb][:, 3:4], scs[bb][:, 3:4], scs[bb][:, 2:3], ALU.mult))(wb2),
                deps=[e_sc1], inc=vs)
            e_sc3 = sch.add("vector", (lambda bb: lambda v: v.tensor_scalar(
                scs[bb][:, 0:1], scs[bb][:, 0:1], 64.0, None, ALU.mult))(wb2),
                deps=[e_sc2], inc=vs)
            e_scales = e_sc3
            comb_col = 3
        else:
            e_scales = e_ls2
            comb_col = 1
        for th in range(2):
            tsl = slice(th * 512, (th + 1) * 512)
            pp = th
            def gmm(th_, bb, pp_, which):
                wsrc = gw_bf if which == "g" else uw_bf
                pdst = ps_g if which == "g" else ps_u
                def f(te):
                    ins = None
                    for kc in range(KC):
                        ins = te.matmul(pdst[pp_][:],
                                        wsrc[bb][:, kc * 128:(kc + 1) * 128],
                                        x_bf[kc][:, th_ * 512:(th_ + 1) * 512],
                                        start=(kc == 0), stop=(kc == KC - 1))
                    return ins
                return f
            e_gm = sch.add("tensor", gmm(th, wb, pp, "g"),
                           deps=x_done + [e_cg, prev_psg.get(pp)], inc=mm)
            e_um = sch.add("tensor", gmm(th, wb, pp, "u"),
                           deps=x_done + [e_cu, prev_psu.get(pp)], inc=mm)
            e_sg = sch.add("scalar", (lambda bb, pp_, sgb: lambda s: s.activation(
                sg_t[sgb][:], ps_g[pp_][:], AF.Silu, scale=scs[bb][:, 0:1]))(wb2, pp, th),
                deps=[e_gm, e_scales, prev_sg.get(th)], inc=ss)
            prev_psg[pp] = e_sg
            if layer == 1:
                dst = hbuf[i]
                ddep = prev_hbuf.get(i)
            else:
                dst = ho_t[2 * (i % 2) + th]
                ddep = st_prev[2 * (i % 2) + th]
            e_h2 = sch.add("vector", (lambda bb, pp_, sgb, d_, t2: lambda v: v.scalar_tensor_tensor(
                d_[:, t2] if layer == 1 else d_[:, 0:512],
                ps_u[pp_][:], scs[bb][:, comb_col:comb_col + 1], sg_t[sgb][:],
                ALU.mult, ALU.mult))(wb2, pp, th, dst, tsl),
                deps=[e_um, e_sg, e_scales, ddep], inc=vs)
            prev_psu[pp] = e_h2
            prev_sg[th] = e_h2
            if layer == 1:
                prev_hbuf[i] = e_h2
                hbuf_evs.append(e_h2)
            prev_wuse[("g", wb)] = e_sg
            prev_wuse[("u", wb)] = e_h2
            prev_scuse[wb2] = e_h2
            if layer == 0:
                sl4 = 2 * (i % 2) + th
                e_st = sch.add("gpsimd", (lambda i_, th_, d_: lambda g: g.dma_start(
                    out=h_out[i_ * 128:(i_ + 1) * 128, th_ * 512:(th_ + 1) * 512],
                    in_=d_[:, 0:512]))(i, th, dst),
                    deps=[e_h2, st_prev[sl4]], inc=st_sems[sl4], inc_n=16)
                st_prev[sl4] = e_st
        if layer == 1:
            tile_ev[i] = hbuf_evs[2 * i + 1]
            g2 = i
            for s in range(6):
                span = 1 << (s + 1)
                if (g2 + 1) % span == 0:
                    base = g2 + 1 - span
                    for c in range(base, base + (span // 2)):
                        a, b = c, c + (span // 2)
                        e1 = sch.add("vector", (lambda a_, b_: lambda v: v.tensor_tensor(
                            btmp[:], hbuf[a_][:], hbuf[b_][:], ALU.subtract))(a, b),
                            deps=[tile_ev[a], tile_ev[b], prev_btmp[0]], inc=vs)
                        e2 = sch.add("vector", (lambda a_, b_: lambda v: v.tensor_tensor(
                            hbuf[a_][:], hbuf[a_][:], hbuf[b_][:], ALU.add))(a, b),
                            deps=[e1], inc=vs)
                        e3 = sch.add("vector", (lambda b_: lambda v: v.tensor_copy(hbuf[b_][:], btmp[:]))(b),
                            deps=[e2], inc=vs)
                        tile_ev[a] = e2
                        tile_ev[b] = e3
                        prev_btmp[0] = e3

    if layer == 1:
        prev_ps = {}
        for c in range(NI):
            for th in range(2):
                pp = th
                e_m = sch.add("tensor", (lambda c_, th_, pp_: lambda te: te.matmul(
                    ps_g[pp_][:], h128_b[:], hbuf[c_][:, th_ * 512:(th_ + 1) * 512],
                    start=True, stop=True))(c, th, pp),
                    deps=[tile_ev[c]] + h_evs + [prev_ps.get(pp)], inc=mm)
                sl4 = 2 * (c % 2) + th
                e_e = sch.add("scalar", (lambda pp_, sl_: lambda s: s.activation(
                    ho_t[sl_][:], ps_g[pp_][:], AF.Copy, scale=FWHT_SCALE_8192))(pp, sl4),
                    deps=[e_m, st_prev[sl4]], inc=ss)
                prev_ps[pp] = e_e
                e_st = sch.add("gpsimd", (lambda c_, th_, sl_: lambda g: g.dma_start(
                    out=h_out[c_ * 128:(c_ + 1) * 128, th_ * 512:(th_ + 1) * 512],
                    in_=ho_t[sl_][:]))(c, th, sl4),
                    deps=[e_e], inc=st_sems[sl4], inc_n=16)
                st_prev[sl4] = e_st
    ctx.reset(m0)



def emit_wopart_phase(ctx, S, a_tiles, wo_my, part_out_d):
    sch = ctx.sch
    m0 = ctx.mark()
    a_ld, w_ld, cast, mm, vs, ss, st_sems = (S["x_ld"], S["w_ld"], S["cast"],
                                             S["mm"], S["vs"], S["ss"], S["st"])
    a_bf = a_tiles
    w_bf = [ctx.tile(D, BF16) for _ in range(8)]
    ctile = [ctx.tile(D, BF16) for _ in range(2)]
    ps = [ctx.psum_f32[i] for i in range(4)]
    cast_evs = []
    for k in range(8):
        buf = k % 2
        e_cw = sch.add("gpsimd", (lambda kk: lambda g: g.dma_start(
            out=w_bf[kk][:], in_=wo_my[kk * 128:(kk + 1) * 128, :]))(k),
            deps=[cast_evs[k - 2] if k >= 2 else None], inc=w_ld[buf], inc_n=16)
        cast_evs.append(e_cw)
    loads = [cast_evs[-1],
             (w_ld[0], w_ld[0].count), (w_ld[1], w_ld[1].count)]
    prev_ev = {}
    prev_ct = {}
    for tt in range(16):
        cb = tt % 2
        ev_o = []
        for o in range(4):
            def mmg(tt_, oo):
                def f(te):
                    ins = None
                    for k in range(8):
                        ins = te.matmul(ps[oo][:],
                                        a_bf[k][:, tt_ * 128:(tt_ + 1) * 128],
                                        w_bf[k][:, oo * 512:(oo + 1) * 512],
                                        start=(k == 0), stop=(k == 7))
                    return ins
                return f
            e_mm = sch.add("tensor", mmg(tt, o), deps=loads + [prev_ev.get(o)], inc=mm)
            e_ev = sch.add("scalar", (lambda oo, cb_: lambda s: s.copy(
                ctile[cb_][:, oo * 512:(oo + 1) * 512], ps[oo][:]))(o, cb),
                deps=[e_mm, prev_ct.get(cb)], inc=ss)
            prev_ev[o] = e_ev
            ev_o.append(e_ev)
        e_st = sch.add("gpsimd", (lambda tt_, cb_: lambda g: g.dma_start(
            out=part_out_d[tt_ * 128:(tt_ + 1) * 128, :], in_=ctile[cb_][:]))(tt, cb),
            deps=ev_o, inc=st_sems[cb], inc_n=16)
        prev_ct[cb] = e_st
    ctx.reset(m0)


def emit_resln_phase(ctx, S, rs_out_d, resid_src, out_d, x_dep=None, xfm_tiles=None):
    sch = ctx.sch
    m0 = ctx.mark()
    a_ld, r_ld, cast, vs, ss, st_sems = (S["x_ld"], S["r_ld"], S["cast"],
                                         S["vs"], S["ss"], S["st"])
    w_bf = [ctx.tile(D, BF16) for _ in range(2)]
    r_f32 = [ctx.tile(D, F32) for _ in range(2)]
    y = [ctx.tile(D, F32) for _ in range(2)]
    ybf = [ctx.tile(D, BF16) for _ in range(2)] if xfm_tiles is not None else None
    stats = {nm: ctx.tile(1, F32) for nm in ("sum1", "sumsq", "mean", "var", "m2", "std", "rstd", "nmr")}
    stats["sq_scratch"] = ctx.tile(D, F32)
    eps_t = ctx.consts["eps"]
    prev = {}
    prev_tp = {}
    prev_ln = None
    for t in range(8):
        buf = t % 2
        e_lw = sch.add("sync", (lambda tt, bb: lambda s: s.dma_start(
            out=w_bf[bb][:], in_=rs_out_d[tt * 128:(tt + 1) * 128, :]))(t, buf),
            deps=[prev.get(("w", buf)), x_dep], inc=a_ld[buf], inc_n=16)
        e_lr = sch.add("sync", (lambda tt, bb: lambda s: s.dma_start(
            out=r_f32[bb][:], in_=resid_src[tt * 128:(tt + 1) * 128, :]))(t, buf),
            deps=[prev.get(("r", buf))], inc=r_ld[buf], inc_n=16)
        e_y = sch.add("vector", (lambda bb: lambda v: v.tensor_tensor(
            y[bb][:], w_bf[bb][:], r_f32[bb][:], ALU.add))(buf),
            deps=[e_lw, e_lr, prev_ln, prev.get(("st", buf)), prev.get(("yb", buf))], inc=S["vs"])
        e_out = emit_layernorm(sch, y[buf], y[buf], stats, eps_t, D,
                               deps=[e_y, ctx.consts["evs"][4]], v_sem=S["vs"], s_sem=S["ss"])
        prev_ln = e_out
        e_st = sch.add("gpsimd", (lambda tt, bb: lambda g: g.dma_start(
            out=out_d[tt * 128:(tt + 1) * 128, :], in_=y[bb][:]))(t, buf),
            deps=[e_out], inc=st_sems[buf], inc_n=16)
        if xfm_tiles is not None:
            e_cb = sch.add("scalar", (lambda bb: lambda s: s.copy(ybf[bb][:], y[bb][:]))(buf),
                           deps=[e_out, prev.get(("yb2", buf))], inc=ss)
            last_e = None
            for j in range(16):
                tb = j % 2
                e_t = sch.add("tensor", (lambda bb, j_, tb_: lambda te: te.transpose(
                    ctx.psum_bf16[tb_][:], ybf[bb][:, j_ * 128:(j_ + 1) * 128],
                    ctx.consts["ident_bf"][:]))(buf, j, tb),
                    deps=[e_cb, prev_tp.get(tb), ctx.consts["evs"][1]], inc=S["mm"])
                e_e = sch.add("vector", (lambda j_, tt, tb_: lambda v: v.tensor_copy(
                    xfm_tiles[j_][:, tt * 128:(tt + 1) * 128], ctx.psum_bf16[tb_][:]))(j, t, tb),
                    deps=[e_t], inc=vs)
                prev_tp[tb] = e_e
                last_e = e_e
            prev[("yb", buf)] = e_cb
            prev[("yb2", buf)] = last_e
        prev[("w", buf)] = e_y
        prev[("r", buf)] = e_y
        prev[("st", buf)] = e_st
    ctx.reset(m0)


def build_fused_program(dbg=False):
    nc = bass.Bass("TRN2", target_bir_lowering=False, debug=False)
    f32 = F32
    ap = {}
    ap["x_rows"] = nc.dram_tensor("x_rows", [TOK_CORE, D], f32, kind="ExternalInput").ap()
    for nm in ("wq0", "wk0", "wv0"):
        ap[nm] = nc.dram_tensor(nm, [D, 1024], f32, kind="ExternalInput").ap()
    ap["wo0"] = nc.dram_tensor("wo0", [1024, D], f32, kind="ExternalInput").ap()
    ap["gwT0"] = nc.dram_tensor("gwT0", [D, INTER0], f32, kind="ExternalInput").ap()
    ap["uwT0"] = nc.dram_tensor("uwT0", [D, INTER0], f32, kind="ExternalInput").ap()
    ap["dwT0"] = nc.dram_tensor("dwT0", [INTER0, D], f32, kind="ExternalInput").ap()
    ap["gs0"] = nc.dram_tensor("gs0", [INTER0, 1], f32, kind="ExternalInput").ap()
    ap["us0"] = nc.dram_tensor("us0", [INTER0, 1], f32, kind="ExternalInput").ap()
    ap["ds_b"] = nc.dram_tensor("ds_b", [128, D], f32, kind="ExternalInput").ap()
    for nm in ("wq1", "wk1", "wv1"):
        ap[nm] = nc.dram_tensor(nm, [D, 1024], f32, kind="ExternalInput").ap()
    ap["wo1T"] = nc.dram_tensor("wo1T", [1024, D], f32, kind="ExternalInput").ap()
    ap["su_ug"] = nc.dram_tensor("su_ug", [D, 1], f32, kind="ExternalInput").ap()
    ap["gwT1"] = nc.dram_tensor("gwT1", [D, INTER1], f32, kind="ExternalInput").ap()
    ap["uwT1"] = nc.dram_tensor("uwT1", [D, INTER1], f32, kind="ExternalInput").ap()
    ap["gs1"] = nc.dram_tensor("gs1", [INTER1, 1], f32, kind="ExternalInput").ap()
    ap["us1"] = nc.dram_tensor("us1", [INTER1, 1], f32, kind="ExternalInput").ap()
    ap["su_dp"] = nc.dram_tensor("su_dp", [INTER1, 1], f32, kind="ExternalInput").ap()
    ap["wdT1"] = nc.dram_tensor("wdT1", [INTER1, D], f32, kind="ExternalInput").ap()
    ap["wdp_b"] = nc.dram_tensor("wdp_b", [128, D], f32, kind="ExternalInput").ap()
    ap["masks"] = nc.dram_tensor("masks", [4, 128, 512], f32, kind="ExternalInput").ap()
    ap["ident"] = nc.dram_tensor("ident", [128, 128], f32, kind="ExternalInput").ap()
    ap["h128"] = nc.dram_tensor("h128", [128, 128], f32, kind="ExternalInput").ap()
    ap["h2048"] = nc.dram_tensor("h2048", [D, D], BF16, kind="ExternalInput").ap()
    ap["rc"] = nc.dram_tensor("rc", [128, T_SEQ], f32, kind="ExternalInput").ap()
    ap["rs"] = nc.dram_tensor("rs", [128, T_SEQ], f32, kind="ExternalInput").ap()
    ap["lamv"] = nc.dram_tensor("lamv", [1, 512], f32, kind="ExternalInput").ap()
    ap["subln_b"] = nc.dram_tensor("subln_b", [128, 256], f32, kind="ExternalInput").ap()
    out_rows = nc.dram_tensor("out_rows", [TOK_CORE, D], f32, kind="ExternalOutput").ap()

    dr = lambda nm, shp, dt: nc.dram_tensor(nm, shp, dt)
    cc_x_in = [dr(f"cc_x_in{l}", [D, TOK_CORE], BF16) for l in range(2)]
    cc_x_out = [dr(f"cc_x_out{l}", [2 * D, TOK_CORE], BF16) for l in range(2)]
    cc_p_in = [dr(f"cc_p_in{l}", [T_SEQ, D], BF16) for l in range(2)]
    cc_p_out = [dr(f"cc_p_out{l}", [TOK_CORE, D], BF16) for l in range(2)]
    r1_d = dr("r1_d", [TOK_CORE, D], f32)
    r2_d = dr("r2_d", [TOK_CORE, D], f32)
    r3_d = dr("r3_d", [TOK_CORE, D], f32)
    xfm_d = dr("xfm_d", [D, TOK_CORE], BF16)
    t_d = dr("t_d", [D, TOK_CORE], BF16)
    h0_d = dr("h0_d", [INTER0, TOK_CORE], BF16)
    h2_d = dr("h2_d", [INTER1, TOK_CORE], BF16)
    pa_d = dr("pa_d", [TOK_CORE, D], f32)
    pb_d = dr("pb_d", [TOK_CORE, D], f32)

    sch = Sched()
    S = fsem_pool(sch)

    with contextlib.ExitStack() as stack:
        slab = stack.enter_context(nc.sbuf_tensor("slab", [128, SLAB_BF16_ELEMS], BF16))
        psum_f32 = [stack.enter_context(nc.psum_tensor(f"pb_{i}", [128, 512], F32)) for i in range(6)]
        psum_bf16 = [stack.enter_context(nc.psum_tensor(f"pbb_{i}", [128, 128], BF16)) for i in range(2)]
        ctx = Fctx(nc, sch, slab, [p[:] for p in psum_f32] and [p for p in psum_f32], psum_bf16)
        ctx.psum_f32 = [p[:] for p in psum_f32]
        ctx.psum_bf16 = [p[:] for p in psum_bf16]

        emit_const_phase(ctx, S, ap)
        base_mark = ctx.mark()
        sch.barrier()

        def bar():
            sch.barrier()

        emit_transpose_phase(ctx, S, ap["x_rows"], cc_x_in[0].ap(), TOK_CORE, D); bar()
        cc_ev = emit_collective(ctx, S, "AllGather", cc_x_in[0], cc_x_out[0])
        sch.barrier(exclude=[S["cc"]])
        afm_mark = ctx.mark()
        afm_tiles = [ctx.tile(T_SEQ, BF16) for _ in range(8)]
        emit_attn_phase(ctx, S, 0, cc_x_out[0].ap(), ap["wq0"], ap["wk0"], ap["wv0"], afm_tiles,
                        x_dep=cc_ev); bar()
        emit_wopart_phase(ctx, S, afm_tiles, ap["wo0"], cc_p_in[0].ap()); bar()
        ctx.reset(afm_mark)
        cc_ev = emit_collective(ctx, S, "ReduceScatter", cc_p_in[0], cc_p_out[0], op=ALU.add)
        sch.barrier(exclude=[S["cc"]])
        xfm_mark = ctx.mark()
        xfm_tiles = [ctx.tile(TOK_CORE, BF16) for _ in range(16)]
        emit_resln_phase(ctx, S, cc_p_out[0].ap(), ap["x_rows"], r1_d.ap(), x_dep=cc_ev,
                         xfm_tiles=xfm_tiles); bar()
        emit_ffn_up_phase(ctx, S, 0, None, ap["gwT0"], ap["uwT0"],
                          ap["gs0"], ap["us0"], None, h0_d.ap(), x_tiles=xfm_tiles); bar()
        ctx.reset(xfm_mark)
        emit_pmm_phase(ctx, S, 22,
                       a_src=lambda k: h0_d.ap()[k * 128:(k + 1) * 128, :],
                       w_src=lambda k: ap["dwT0"][k * 128:(k + 1) * 128, :],
                       out_dram=pa_d.ap()); bar()
        emit_pmm_phase(ctx, S, 22,
                       a_src=lambda k: h0_d.ap()[(22 + k) * 128:(23 + k) * 128, :],
                       w_src=lambda k: ap["dwT0"][(22 + k) * 128:(23 + k) * 128, :],
                       out_dram=r2_d.ap(), p_src=pa_d.ap(), resid_src=r1_d.ap(),
                       colscale_src=ap["ds_b"], scale64=False, finalize=True); bar()

        emit_transpose_phase(ctx, S, r2_d.ap(), cc_x_in[1].ap(), TOK_CORE, D); bar()
        cc_ev = emit_collective(ctx, S, "AllGather", cc_x_in[1], cc_x_out[1])
        sch.barrier(exclude=[S["cc"]])
        afm_mark = ctx.mark()
        afm_tiles = [ctx.tile(T_SEQ, BF16) for _ in range(8)]
        emit_attn_phase(ctx, S, 1, cc_x_out[1].ap(), ap["wq1"], ap["wk1"], ap["wv1"], afm_tiles,
                        x_dep=cc_ev, rc_ap=ap["rc"], rs_ap=ap["rs"]); bar()
        emit_wopart_phase(ctx, S, afm_tiles, ap["wo1T"], cc_p_in[1].ap()); bar()
        ctx.reset(afm_mark)
        cc_ev = emit_collective(ctx, S, "ReduceScatter", cc_p_in[1], cc_p_out[1], op=ALU.add)
        sch.barrier(exclude=[S["cc"]])
        xfm_mark = ctx.mark()
        xfm_tiles = [ctx.tile(TOK_CORE, BF16) for _ in range(16)]
        emit_resln_phase(ctx, S, cc_p_out[1].ap(), r2_d.ap(), r3_d.ap(), x_dep=cc_ev,
                         xfm_tiles=xfm_tiles); bar()
        emit_fwht_phase(ctx, S, None, ap["su_ug"], t_d.ap(), ap["h2048"], x_tiles=xfm_tiles); bar()
        ctx.reset(xfm_mark)
        emit_ffn_up_phase(ctx, S, 1, t_d.ap(), ap["gwT1"], ap["uwT1"],
                          ap["gs1"], ap["us1"], ap["su_dp"], h2_d.ap()); bar()
        pcur, pnxt = pa_d, pb_d
        emit_pmm_phase(ctx, S, 16,
                       a_src=lambda k: h2_d.ap()[k * 128:(k + 1) * 128, :],
                       w_src=lambda k: ap["wdT1"][k * 128:(k + 1) * 128, :],
                       out_dram=pcur.ap()); bar()
        for seg in (1, 2):
            emit_pmm_phase(ctx, S, 16,
                           a_src=(lambda sg: lambda k: h2_d.ap()[(sg * 16 + k) * 128:(sg * 16 + k + 1) * 128, :])(seg),
                           w_src=(lambda sg: lambda k: ap["wdT1"][(sg * 16 + k) * 128:(sg * 16 + k + 1) * 128, :])(seg),
                           out_dram=pnxt.ap(), p_src=pcur.ap()); bar()
            pcur, pnxt = pnxt, pcur
        emit_pmm_phase(ctx, S, 16,
                       a_src=lambda k: h2_d.ap()[(48 + k) * 128:(49 + k) * 128, :],
                       w_src=lambda k: ap["wdT1"][(48 + k) * 128:(49 + k) * 128, :],
                       out_dram=out_rows, p_src=pcur.ap(), resid_src=r3_d.ap(),
                       colscale_src=ap["wdp_b"], scale64=True, finalize=True)

        if dbg:
            sch.barrier()
            dbg_outs = {
                "dbg_xfm0": (cc_x_in[0], [D, TOK_CORE], BF16),
                "dbg_xag0": (cc_x_out[0], [2 * D, TOK_CORE], BF16),
                "dbg_rs0": (cc_p_out[0], [TOK_CORE, D], BF16),
                "dbg_r1": (r1_d, [TOK_CORE, D], f32),
                "dbg_h0": (h0_d, [INTER0, TOK_CORE], BF16),
                "dbg_r2": (r2_d, [TOK_CORE, D], f32),
                "dbg_r3": (r3_d, [TOK_CORE, D], f32),
                "dbg_t": (t_d, [D, TOK_CORE], BF16),
                "dbg_h2": (h2_d, [INTER1, TOK_CORE], BF16),
            }
            for i, (nm, (src_t, shp, dt_)) in enumerate(dbg_outs.items()):
                o = nc.dram_tensor(nm, shp, dt_, kind="ExternalOutput").ap()
                sch.add("sync", (lambda o_, s_: lambda s: s.dma_start(out=o_, in_=s_.ap()))(o, src_t),
                        inc=S["cc"], inc_n=16)
        sch.add("gpsimd", lambda g: None,
                deps=[(s, s.count) for s in S["st"]] + [(S["cc"], S["cc"].count)])
        sch.emit(nc)
    return nc


def kernel_fused(**inputs):
    f32 = np.float32
    g = {k: np.asarray(v, f32) for k, v in inputs.items()}
    x = g["L0_attn_x"]
    x_flat = x.reshape(TOK_GLOBAL, D)
    masks = host_masks()
    ident = np.eye(128, dtype=f32)
    H128 = host_hadamard128()
    import ml_dtypes
    H2 = np.array([[1.0]], dtype=np.float64)
    while H2.shape[0] < D:
        H2 = np.block([[H2, H2], [H2, -H2]])
    H2048 = np.ascontiguousarray(H2.astype(ml_dtypes.bfloat16))
    cosT, sinT = host_rope_tables()
    lamv = np.concatenate([g["L0_attn_lambda_q1"], g["L0_attn_lambda_k1"],
                           g["L0_attn_lambda_q2"], g["L0_attn_lambda_k2"]])[None, :]
    subln_b = _ct(np.broadcast_to(g["L0_attn_subln_weight"], (128, 256)))
    ds_b = _ct(np.broadcast_to(g["L0_ffn_down_s"], (128, D)))
    wdp_b = _ct(np.broadcast_to(g["L1_ffn_Wscale_dp"], (128, D)))
    gwT0 = _ct(g["L0_ffn_gate_w"].T); uwT0 = _ct(g["L0_ffn_up_w"].T)
    dwT0 = _ct(g["L0_ffn_down_w"].T)
    gwT1 = _ct(g["L1_ffn_W_gate"].T); uwT1 = _ct(g["L1_ffn_W_up"].T)
    wdT1 = _ct(g["L1_ffn_W_down"].T)
    wo1T = _ct(g["L1_attn_wo"].T)
    wsc = g["L1_ffn_Wscale_ug"]

    import os
    dbg = bool(os.environ.get("FUSED_DBG"))
    nc = _prog("fused_dbg" if dbg else "fused", build_fused_program, dbg)
    in_maps = []
    for c in range(N_CORES):
        b, s = c // 2, c % 2
        hs = slice(s * 1024, (s + 1) * 1024)
        in_maps.append({
            "x_rows": _ct(x_flat[c * TOK_CORE:(c + 1) * TOK_CORE]),
            "wq0": _ct(g["L0_attn_Wq"][:, hs]), "wk0": _ct(g["L0_attn_Wk"][:, hs]),
            "wv0": _ct(g["L0_attn_Wv"][:, hs]), "wo0": _ct(g["L0_attn_Wo"][hs, :]),
            "gwT0": gwT0, "uwT0": uwT0, "dwT0": dwT0,
            "gs0": _ct(g["L0_ffn_gate_s"][:, None]), "us0": _ct(g["L0_ffn_up_s"][:, None]),
            "ds_b": ds_b,
            "wq1": _ct(g["L1_attn_wq"][hs, :].T), "wk1": _ct(g["L1_attn_wk"][hs, :].T),
            "wv1": _ct(g["L1_attn_wv"][hs, :].T), "wo1T": _ct(wo1T[hs, :]),
            "su_ug": _ct(g["L1_ffn_SU_ug"][:, None]),
            "gwT1": gwT1, "uwT1": uwT1,
            "gs1": _ct(wsc[INTER1:][:, None]), "us1": _ct(wsc[:INTER1][:, None]),
            "su_dp": _ct(g["L1_ffn_SU_dp"][:, None]), "wdT1": wdT1, "wdp_b": wdp_b,
            "masks": masks, "ident": ident, "h128": H128, "h2048": H2048, "rc": cosT, "rs": sinT,
            "lamv": _ct(lamv), "subln_b": subln_b,
        })
    res = _run(nc, in_maps)
    if dbg:
        kernel_fused.dbg_results = res.results
    out = np.concatenate([res.results[c]["out_rows"] for c in range(N_CORES)], axis=0)
    return out.reshape(B_SZ, T_SEQ, D)


def kernel(**inputs):
    return kernel_fused(**inputs)


# version 17
# speedup vs baseline: 1.1027x; 1.0397x over previous
import contextlib
import numpy as np
import concourse.bass as bass
import concourse.mybir as mybir
from concourse.bass_utils import run_bass_kernel_spmd

F32 = mybir.dt.float32
BF16 = mybir.dt.bfloat16
AF = mybir.ActivationFunctionType
ALU = mybir.AluOpType
AX = mybir.AxisListType

N_CORES = 8
D = 2048
B_SZ = 4
T_SEQ = 2048
TOK_GLOBAL = B_SZ * T_SEQ
TOK_CORE = TOK_GLOBAL // N_CORES
EPS = 1e-5
SOFTMAX_SHIFT = 12.0


class Sem:
    def __init__(self, name):
        self.name = name
        self.count = 0
        self.handle = None

class Op:
    __slots__ = ("engine", "fn", "waits", "incs")
    def __init__(self, engine, fn, waits, incs):
        self.engine = engine
        self.fn = fn
        self.waits = waits
        self.incs = incs

class Sched:
    ENGINES = ("sync", "vector", "scalar", "tensor", "gpsimd")

    def __init__(self):
        self.ops = []
        self.sems = []
        self._pending = {}

    def barrier(self, exclude=()):
        evs = [(s, s.count) for s in self.sems if s.count > 0 and s not in exclude]
        self._pending = {e: list(evs) for e in self.ENGINES}

    def sem(self, name):
        s = Sem(name)
        self.sems.append(s)
        return s

    def add(self, engine, fn, deps=(), inc=None, inc_n=1):
        waits = []
        pj = self._pending.pop(engine, None)
        if pj:
            waits.extend(pj)
        for d in deps:
            if d is None:
                continue
            if isinstance(d, Sem):
                waits.append((d, d.count))
            else:
                waits.append(d)
        ev = None
        incs = []
        if inc is not None:
            inc.count += inc_n
            incs.append((inc, inc_n))
            ev = (inc, inc.count)
        self.ops.append(Op(engine, fn, waits, incs))
        return ev

    def emit(self, nc):
        with contextlib.ExitStack() as stack:
            for s in self.sems:
                s.handle = stack.enter_context(nc.semaphore(s.name))
            block = stack.enter_context(nc.Block())
            by_engine = {e: [op for op in self.ops if op.engine == e] for e in self.ENGINES}

            def mk(engine_ops):
                def body(eng):
                    for op in engine_ops:
                        wmax = {}
                        for s, v in op.waits:
                            wmax[s] = max(wmax.get(s, 0), v)
                        for s, v in wmax.items():
                            if v > 0:
                                eng.wait_ge(s.handle, v)
                        ins = op.fn(eng)
                        for s, n in op.incs:
                            assert ins is not None, f"op with inc on {s.name} returned no instruction"
                            ins = ins.then_inc(s.handle, n)
                return body

            for e in self.ENGINES:
                if by_engine[e]:
                    getattr(block, e)(mk(by_engine[e]))



def emit_layernorm(sch, y, out, stats, eps_tile, n_feat, deps, v_sem, s_sem,
                   out_free_dep=None):
    deps = [d for d in deps if d is not None]
    e1 = sch.add("vector", lambda v: v.reduce_sum(stats["sum1"][:], y[:], axis=AX.X),
                 deps=deps, inc=v_sem)
    e2 = sch.add("scalar", lambda s: s.activation(stats["sq_scratch"][:, 0:n_feat], y[:], AF.Square,
                                                 accum_out=stats["sumsq"][:]),
                 deps=deps, inc=s_sem)
    e3a = sch.add("vector", lambda v: v.tensor_scalar(stats["mean"][:], stats["sum1"][:], 1.0 / n_feat, None, ALU.mult),
                  deps=[e1], inc=v_sem)
    e3b = sch.add("vector", lambda v: v.tensor_scalar(stats["var"][:], stats["sumsq"][:], 1.0 / n_feat, None, ALU.mult),
                  deps=[e2], inc=v_sem)
    e3 = sch.add("vector", lambda v: v.tensor_tensor(stats["m2"][:], stats["mean"][:], stats["mean"][:], ALU.mult),
                 deps=[e3a], inc=v_sem)
    e4 = sch.add("vector", lambda v: v.tensor_tensor(stats["var"][:], stats["var"][:], stats["m2"][:], ALU.subtract),
                 deps=[e3, e3b], inc=v_sem)
    e5 = sch.add("scalar", lambda s: s.activation(stats["std"][:], stats["var"][:], AF.Sqrt, bias=eps_tile[:]),
                 deps=[e4], inc=s_sem)
    e6 = sch.add("vector", lambda v: v.reciprocal(stats["rstd"][:], stats["std"][:]),
                 deps=[e5], inc=v_sem)
    e7 = sch.add("vector", lambda v: v.tensor_tensor(stats["nmr"][:], stats["mean"][:], stats["rstd"][:], ALU.mult),
                 deps=[e6], inc=v_sem)
    e8 = sch.add("vector", lambda v: v.tensor_scalar(stats["nmr"][:], stats["nmr"][:], -1.0, None, ALU.mult),
                 deps=[e7], inc=v_sem)
    e9 = sch.add("vector", lambda v: v.tensor_scalar(out[:], y[:], stats["rstd"][:], stats["nmr"][:],
                                                     ALU.mult, ALU.add),
                 deps=[e8, e6, out_free_dep] + deps, inc=v_sem)
    return e9


def alloc_ln_stats(stack, nc, tag, width=2048):
    t = {}
    for nm in ("sum1", "sumsq", "mean", "var", "m2", "std", "rstd", "nmr"):
        t[nm] = stack.enter_context(nc.sbuf_tensor(f"{tag}_{nm}", [128, 1], F32))
    t["sq_scratch"] = stack.enter_context(nc.sbuf_tensor(f"{tag}_sq", [128, width], F32))
    return t



def build_pmm_program(kch, a_is_bf16=False, with_partial=False, finalize=True,
                      with_colscale=False, scale64=False):
    nc = bass.Bass("TRN2", target_bir_lowering=False, debug=False)
    K = kch * 128
    a_fm = nc.dram_tensor("a_fm", [K, TOK_CORE], BF16 if a_is_bf16 else F32,
                          kind="ExternalInput").ap()
    w = nc.dram_tensor("w", [K, D], F32, kind="ExternalInput").ap()
    if with_partial:
        p_in = nc.dram_tensor("p_in", [TOK_CORE, D], F32, kind="ExternalInput").ap()
    if with_colscale:
        colscale_b = nc.dram_tensor("colscale_b", [128, D], F32, kind="ExternalInput").ap()
    if finalize:
        resid = nc.dram_tensor("resid", [TOK_CORE, D], F32, kind="ExternalInput").ap()
    out = nc.dram_tensor("out", [TOK_CORE, D], F32, kind="ExternalOutput").ap()

    TT = TOK_CORE // 128
    OC = D // 512

    sch = Sched()
    a_ld = [sch.sem(f"a_ld{i}") for i in range(2)]
    w_ld = [sch.sem(f"w_ld{i}") for i in range(2)]
    r_ld = [sch.sem(f"r_ld{i}") for i in range(2)]
    p_ld = [sch.sem(f"p_ld{i}") for i in range(2)]
    c_ld = sch.sem("c_ld")
    cast = sch.sem("cast")
    mm = sch.sem("mm")
    ev = sch.sem("ev")
    vs = sch.sem("vs")
    ss = sch.sem("ss")
    st_s = [sch.sem(f"st{i}") for i in range(2)]

    NB = 1 if kch > 16 else 2
    with contextlib.ExitStack() as stack:
        sb = lambda name, shape, dt=F32: stack.enter_context(nc.sbuf_tensor(name, shape, dt))
        a_bf = [sb(f"a_bf_{k}", [128, TOK_CORE], BF16) for k in range(kch)]
        w_bf = [sb(f"w_bf_{k}", [128, D], BF16) for k in range(kch)]
        a_f32 = None if a_is_bf16 else [sb(f"a_f32_{i}", [128, TOK_CORE]) for i in range(NB)]
        w_f32 = [sb(f"w_f32_{i}", [128, D]) for i in range(NB)]
        r_f32 = [sb(f"r_f32_{i}", [128, D]) for i in range(NB)] if finalize else None
        p_f32 = [sb(f"p_f32_{i}", [128, D]) for i in range(NB)] if with_partial else None
        cs_t = sb("cs_t", [128, D]) if with_colscale else None
        y = [sb(f"y_{i}", [128, D]) for i in range(NB)]
        o_t = [sb(f"o_{i}", [128, D]) for i in range(NB)]
        eps_t = sb("eps", [128, 1])
        stats = alloc_ln_stats(stack, nc, "ln") if finalize else None
        ps = [stack.enter_context(nc.psum_tensor(f"ps_{i}", [128, 512], F32)) for i in range(4)]

        ev_eps = sch.add("vector", lambda v: v.memset(eps_t[:], EPS), inc=vs)
        cs_ev = None
        if with_colscale:
            e_cl = sch.add("sync", lambda s: s.dma_start(out=cs_t[:], in_=colscale_b), inc=c_ld, inc_n=16)
            if scale64:
                cs_ev = sch.add("vector", lambda v: v.tensor_scalar(cs_t[:], cs_t[:], 64.0, None, ALU.mult),
                                deps=[e_cl], inc=vs)
            else:
                cs_ev = e_cl

        cast_evs = []
        for k in range(kch):
            buf = k % NB
            dep_a = [cast_evs[k - NB][0]] if k >= NB else []
            dep_w = [cast_evs[k - NB][1]] if k >= NB else []
            if a_is_bf16:
                e_ca = sch.add("sync", (lambda kk: lambda s: s.dma_start(
                    out=a_bf[kk][:], in_=a_fm[kk * 128:(kk + 1) * 128, :]))(k),
                    deps=dep_a, inc=a_ld[buf], inc_n=16)
            else:
                e_la = sch.add("sync", (lambda kk, bb: lambda s: s.dma_start(
                    out=a_f32[bb][:], in_=a_fm[kk * 128:(kk + 1) * 128, :]))(k, buf),
                    deps=dep_a, inc=a_ld[buf], inc_n=16)
                e_ca = sch.add("vector", (lambda kk, bb: lambda v: v.tensor_copy(a_bf[kk][:], a_f32[bb][:]))(k, buf),
                               deps=[e_la], inc=cast)
            e_lw = sch.add("sync", (lambda kk, bb: lambda s: s.dma_start(
                out=w_f32[bb][:], in_=w[kk * 128:(kk + 1) * 128, :]))(k, buf),
                deps=dep_w, inc=w_ld[buf], inc_n=16)
            e_cw = sch.add("vector", (lambda kk, bb: lambda v: v.tensor_copy(w_bf[kk][:], w_f32[bb][:]))(k, buf),
                           deps=[e_lw], inc=cast)
            cast_evs.append((e_ca, e_cw))
        loads_done = [cast_evs[-1][0], cast_evs[-1][1], (cast, cast.count),
                      (a_ld[0], a_ld[0].count), (a_ld[1], a_ld[1].count)]

        prev_ev = {}
        prev_r = {}
        prev_p = {}
        prev_out = {}
        prev_ln_out = None
        for t in range(TT):
            buf = t % NB
            e_lr = None
            e_lp = None
            if finalize:
                e_lr = sch.add("sync", (lambda tt, bb: lambda s: s.dma_start(
                    out=r_f32[bb][:], in_=resid[tt * 128:(tt + 1) * 128, :]))(t, buf),
                    deps=[prev_r.get(buf)], inc=r_ld[buf], inc_n=16)
            if with_partial:
                e_lp = sch.add("sync", (lambda tt, bb: lambda s: s.dma_start(
                    out=p_f32[bb][:], in_=p_in[tt * 128:(tt + 1) * 128, :]))(t, buf),
                    deps=[prev_p.get(buf)], inc=p_ld[buf], inc_n=16)
            mm_evs = []
            for o in range(OC):
                def mmgroup(tt, oo):
                    def f(te):
                        ins = None
                        for k in range(kch):
                            ins = te.matmul(ps[oo][:],
                                            a_bf[k][:, tt * 128:(tt + 1) * 128],
                                            w_bf[k][:, oo * 512:(oo + 1) * 512],
                                            start=(k == 0), stop=(k == kch - 1))
                        return ins
                    return f
                e_mm = sch.add("tensor", mmgroup(t, o),
                               deps=loads_done + [prev_ev.get(o)], inc=mm)
                mm_evs.append(e_mm)
            ev_evs = []
            for o in range(OC):
                sl = slice(o * 512, (o + 1) * 512)
                if with_partial:
                    e_a = sch.add("vector", (lambda oo, bb, sl_: lambda v: v.tensor_tensor(
                        y[bb][:, sl_], ps[oo][:], p_f32[bb][:, sl_], ALU.add))(o, buf, sl),
                        deps=[mm_evs[o], e_lp, prev_ln_out if finalize else prev_out.get(buf)], inc=ev)
                else:
                    e_a = sch.add("vector", (lambda oo, bb, sl_: lambda v: v.tensor_copy(
                        y[bb][:, sl_], ps[oo][:]))(o, buf, sl),
                        deps=[mm_evs[o], prev_ln_out if finalize else prev_out.get(buf)], inc=ev)
                step = e_a
                if finalize:
                    if with_colscale:
                        step = sch.add("vector", (lambda oo, bb, sl_: lambda v: v.tensor_tensor(
                            y[bb][:, sl_], y[bb][:, sl_], cs_t[:, sl_], ALU.mult))(o, buf, sl),
                            deps=[step, cs_ev], inc=ev)
                    step = sch.add("vector", (lambda oo, bb, sl_: lambda v: v.tensor_tensor(
                        y[bb][:, sl_], y[bb][:, sl_], r_f32[bb][:, sl_], ALU.add))(o, buf, sl),
                        deps=[step, e_lr], inc=ev)
                prev_ev[o] = e_a
                ev_evs.append(step)
            if finalize:
                e_out = emit_layernorm(sch, y[buf], o_t[buf], stats, eps_t, D,
                                       deps=ev_evs + [ev_eps], v_sem=vs, s_sem=ss,
                                       out_free_dep=prev_out.get(buf))
                prev_ln_out = e_out
                src_tile = o_t[buf]
            else:
                e_out = ev_evs[-1]
                src_tile = y[buf]
            e_st = sch.add("gpsimd", (lambda tt, s_, ev_: lambda g: g.dma_start(
                out=out[tt * 128:(tt + 1) * 128, :], in_=s_[:]))(t, src_tile, e_out),
                deps=ev_evs + [e_out], inc=st_s[buf], inc_n=16)
            prev_r[buf] = ev_evs[-1]
            prev_p[buf] = ev_evs[-1]
            prev_out[buf] = e_st

        sch.add("gpsimd", lambda g: None, deps=[(s, s.count) for s in st_s])
        sch.emit(nc)
    return nc



INTER0 = 5632
INTER1 = 8192
FWHT_SCALE_2048 = 1.0 / (64.0 * float(np.sqrt(2048.0)))
FWHT_SCALE_8192 = 1.0 / (64.0 * float(np.sqrt(8192.0)))


def build_ffn_up_program(layer):
    nc = bass.Bass("TRN2", target_bir_lowering=False, debug=False)
    NI = (INTER1 if layer == 1 else INTER0) // 128
    KC = D // 128
    a_dt = BF16 if layer == 1 else F32
    x_fm = nc.dram_tensor("x_fm", [D, TOK_CORE], a_dt, kind="ExternalInput").ap()
    gw = nc.dram_tensor("gw", [D, NI * 128], F32, kind="ExternalInput").ap()
    uw = nc.dram_tensor("uw", [D, NI * 128], F32, kind="ExternalInput").ap()
    gs = nc.dram_tensor("gs", [NI * 128, 1], F32, kind="ExternalInput").ap()
    us = nc.dram_tensor("us", [NI * 128, 1], F32, kind="ExternalInput").ap()
    if layer == 1:
        su = nc.dram_tensor("su", [NI * 128, 1], F32, kind="ExternalInput").ap()
        h128 = nc.dram_tensor("h128", [128, 128], F32, kind="ExternalInput").ap()
    h_out = nc.dram_tensor("h_out", [NI * 128, TOK_CORE], BF16, kind="ExternalOutput").ap()

    sch = Sched()
    x_ld = [sch.sem(f"x_ld{i}") for i in range(2)]
    gw_ld = [sch.sem(f"gw_ld{i}") for i in range(2)]
    uw_ld = [sch.sem(f"uw_ld{i}") for i in range(2)]
    sc_ld = [sch.sem(f"sc_ld{i}") for i in range(2)]
    c_ld = sch.sem("c_ld")
    cast = sch.sem("cast")
    mm = sch.sem("mm")
    vs = sch.sem("vs")
    ss = sch.sem("ss")
    st_s = [sch.sem(f"st{i}") for i in range(4)]

    with contextlib.ExitStack() as stack:
        sb = lambda name, shape, dt=F32: stack.enter_context(nc.sbuf_tensor(name, shape, dt))
        x_stage = None if layer == 1 else [sb(f"x_stage{i}", [128, TOK_CORE]) for i in range(2)]
        x_bf = [sb(f"x_bf{k}", [128, TOK_CORE], BF16) for k in range(KC)]
        w_stage = [sb("w_stage0", [128, 2048])]
        gw_bf = [sb("gw_bf0", [128, 2048], BF16)]
        uw_bf = [sb("uw_bf0", [128, 2048], BF16)]
        scs = [sb(f"scs{i}", [128, 4]) for i in range(2)]
        sg_t = [sb(f"sg{i}", [128, 512]) for i in range(2)]
        if layer == 1:
            hbuf = [sb(f"hbuf{c}", [128, TOK_CORE], BF16) for c in range(NI)]
            btmp = sb("btmp", [128, TOK_CORE], BF16)
            h128_f = sb("h128_f", [128, 128])
            h128_b = sb("h128_b", [128, 128], BF16)
            ho_t = [sb(f"ho{i}", [128, 512], BF16) for i in range(4)]
        else:
            ho_t = [sb(f"ho{i}", [128, TOK_CORE], BF16) for i in range(4)]
        ps_g = [stack.enter_context(nc.psum_tensor(f"psg{i}", [128, 512], F32)) for i in range(2)]
        ps_u = [stack.enter_context(nc.psum_tensor(f"psu{i}", [128, 512], F32)) for i in range(2)]

        h_evs = []
        if layer == 1:
            e_h = sch.add("sync", lambda s: s.dma_start(out=h128_f[:], in_=h128), inc=c_ld, inc_n=16)
            e_hc = sch.add("vector", lambda v: v.tensor_copy(h128_b[:], h128_f[:]), deps=[e_h], inc=cast)
            h_evs = [e_hc]

        x_cast_evs = []
        for k in range(KC):
            buf = k % 2
            if layer == 1:
                e_cx = sch.add("sync", (lambda kk: lambda s: s.dma_start(
                    out=x_bf[kk][:], in_=x_fm[kk * 128:(kk + 1) * 128, :]))(k),
                    deps=[x_cast_evs[k - 2]] if k >= 2 else [], inc=x_ld[buf], inc_n=16)
            else:
                e_lx = sch.add("sync", (lambda kk, bb: lambda s: s.dma_start(
                    out=x_stage[bb][:], in_=x_fm[kk * 128:(kk + 1) * 128, :]))(k, buf),
                    deps=[x_cast_evs[k - 2]] if k >= 2 else [], inc=x_ld[buf], inc_n=16)
                e_cx = sch.add("vector", (lambda kk, bb: lambda v: v.tensor_copy(x_bf[kk][:], x_stage[bb][:]))(k, buf),
                               deps=[e_lx], inc=cast)
            x_cast_evs.append(e_cx)
        x_done = [(cast, cast.count), (x_ld[0], x_ld[0].count), (x_ld[1], x_ld[1].count)]

        st_prev = [None] * 4
        prev_wcast = {}
        prev_wuse = {}
        prev_scuse = {}
        prev_sg = {}
        prev_psg = {}
        prev_psu = {}
        prev_hbuf = {}
        hbuf_evs = []

        for i in range(NI):
            wb = 0
            e_lg = sch.add("sync", (lambda i_, bb: lambda s: s.dma_start(
                out=w_stage[bb][:].rearrange("p (c f) -> p c f", c=KC),
                in_=gw[:, i_ * 128:(i_ + 1) * 128].rearrange("(c p) f -> p c f", p=128)))(i, wb),
                deps=[prev_wcast.get("stage")], inc=gw_ld[0], inc_n=16)
            e_cg = sch.add("vector", (lambda bb: lambda v: v.tensor_copy(gw_bf[bb][:], w_stage[bb][:]))(wb),
                           deps=[e_lg, prev_wuse.get("g")], inc=cast)
            e_lu = sch.add("scalar", (lambda i_, bb: lambda s: s.dma_start(
                out=w_stage[bb][:].rearrange("p (c f) -> p c f", c=KC),
                in_=uw[:, i_ * 128:(i_ + 1) * 128].rearrange("(c p) f -> p c f", p=128)))(i, wb),
                deps=[e_cg], inc=uw_ld[0], inc_n=16)
            e_cu = sch.add("vector", (lambda bb: lambda v: v.tensor_copy(uw_bf[bb][:], w_stage[bb][:]))(wb),
                           deps=[e_lu, prev_wuse.get("u")], inc=cast)
            prev_wcast["stage"] = e_cu
            wb2 = i % 2
            e_ls = sch.add("sync", (lambda i_, bb: lambda s: s.dma_start(
                out=scs[bb][:, 0:1], in_=gs[i_ * 128:(i_ + 1) * 128, :]))(i, wb2),
                deps=[prev_scuse.get(wb2)], inc=sc_ld[wb2], inc_n=16)
            e_ls2 = sch.add("sync", (lambda i_, bb: lambda s: s.dma_start(
                out=scs[bb][:, 1:2], in_=us[i_ * 128:(i_ + 1) * 128, :]))(i, wb2),
                deps=[e_ls], inc=sc_ld[wb2], inc_n=16)
            if layer == 1:
                e_ls3 = sch.add("sync", (lambda i_, bb: lambda s: s.dma_start(
                    out=scs[bb][:, 2:3], in_=su[i_ * 128:(i_ + 1) * 128, :]))(i, wb2),
                    deps=[e_ls2], inc=sc_ld[wb2], inc_n=16)
                e_sc1 = sch.add("vector", (lambda bb: lambda v: v.tensor_scalar(
                    scs[bb][:, 3:4], scs[bb][:, 1:2], 64.0, None, ALU.mult))(wb2),
                    deps=[e_ls3], inc=vs)
                e_sc2 = sch.add("vector", (lambda bb: lambda v: v.tensor_tensor(
                    scs[bb][:, 3:4], scs[bb][:, 3:4], scs[bb][:, 2:3], ALU.mult))(wb2),
                    deps=[e_sc1], inc=vs)
                e_sc3 = sch.add("vector", (lambda bb: lambda v: v.tensor_scalar(
                    scs[bb][:, 0:1], scs[bb][:, 0:1], 64.0, None, ALU.mult))(wb2),
                    deps=[e_sc2], inc=vs)
                e_scales = e_sc3
                comb_col = 3
            else:
                e_scales = e_ls2
                comb_col = 1
            for th in range(2):
                tsl = slice(th * 512, (th + 1) * 512)
                pp = th
                def gmm(i_, th_, bb, pp_, which):
                    wsrc = gw_bf if which == "g" else uw_bf
                    pdst = ps_g if which == "g" else ps_u
                    def f(te):
                        ins = None
                        for kc in range(KC):
                            ins = te.matmul(pdst[pp_][:],
                                            wsrc[bb][:, kc * 128:(kc + 1) * 128],
                                            x_bf[kc][:, th_ * 512:(th_ + 1) * 512],
                                            start=(kc == 0), stop=(kc == KC - 1))
                        return ins
                    return f
                e_gm = sch.add("tensor", gmm(i, th, wb, pp, "g"),
                               deps=x_done + [e_cg, prev_psg.get(pp)], inc=mm)
                e_um = sch.add("tensor", gmm(i, th, wb, pp, "u"),
                               deps=x_done + [e_cu, prev_psu.get(pp)], inc=mm)
                e_sg = sch.add("scalar", (lambda bb, pp_, sgb: lambda s: s.activation(
                    sg_t[sgb][:], ps_g[pp_][:], AF.Silu, scale=scs[bb][:, 0:1]))(wb2, pp, th),
                    deps=[e_gm, e_scales, prev_sg.get(th)], inc=ss)
                prev_psg[pp] = e_sg
                if layer == 1:
                    dst = hbuf[i]
                    ddep = prev_hbuf.get(i)
                else:
                    dst = ho_t[2 * (i % 2) + th]
                    ddep = st_prev[2 * (i % 2) + th]
                e_h2 = sch.add("vector", (lambda bb, pp_, sgb, d_, t2: lambda v: v.scalar_tensor_tensor(
                    d_[:, t2] if layer == 1 else d_[:, 0:512],
                    ps_u[pp_][:], scs[bb][:, comb_col:comb_col + 1], sg_t[sgb][:],
                    ALU.mult, ALU.mult))(wb2, pp, th, dst, tsl),
                    deps=[e_um, e_sg, e_scales, ddep], inc=vs)
                prev_psu[pp] = e_h2
                prev_sg[th] = e_h2
                if layer == 1:
                    prev_hbuf[i] = e_h2
                    hbuf_evs.append(e_h2)
                else:
                    pass
                prev_wuse["g"] = e_sg
                prev_wuse["u"] = e_h2
                prev_scuse[wb2] = e_h2
                if layer == 0:
                    sl4 = 2 * (i % 2) + th
                    e_st = sch.add("gpsimd", (lambda i_, th_, d_: lambda g: g.dma_start(
                        out=h_out[i_ * 128:(i_ + 1) * 128, th_ * 512:(th_ + 1) * 512],
                        in_=d_[:, 0:512]))(i, th, dst),
                        deps=[e_h2, st_prev[sl4]], inc=st_s[sl4], inc_n=16)
                    st_prev[sl4] = e_st

        if layer == 1:
            tile_ev = {c: hbuf_evs[2 * c + 1] for c in range(NI)}
            prev_btmp = None
            hstep = 1
            while hstep < NI:
                for c in range(NI):
                    if (c // hstep) % 2 == 0:
                        a, b = c, c + hstep
                        e1 = sch.add("vector", (lambda a_, b_: lambda v: v.tensor_tensor(
                            btmp[:], hbuf[a_][:], hbuf[b_][:], ALU.subtract))(a, b),
                            deps=[tile_ev[a], tile_ev[b], prev_btmp], inc=vs)
                        e2 = sch.add("vector", (lambda a_, b_: lambda v: v.tensor_tensor(
                            hbuf[a_][:], hbuf[a_][:], hbuf[b_][:], ALU.add))(a, b),
                            deps=[e1], inc=vs)
                        e3 = sch.add("vector", (lambda b_: lambda v: v.tensor_copy(hbuf[b_][:], btmp[:]))(b),
                            deps=[e2], inc=vs)
                        tile_ev[a] = e2
                        tile_ev[b] = e3
                        prev_btmp = e3
                hstep *= 2
            prev_ps = {}
            for c in range(NI):
                for th in range(2):
                    pp = th
                    e_m = sch.add("tensor", (lambda c_, th_, pp_: lambda te: te.matmul(
                        ps_g[pp_][:], h128_b[:], hbuf[c_][:, th_ * 512:(th_ + 1) * 512],
                        start=True, stop=True))(c, th, pp),
                        deps=[tile_ev[c]] + h_evs + [prev_ps.get(pp)], inc=mm)
                    sl4 = 2 * (c % 2) + th
                    e_e = sch.add("scalar", (lambda pp_, sl_: lambda s: s.activation(
                        ho_t[sl_][:], ps_g[pp_][:], AF.Copy, scale=FWHT_SCALE_8192))(pp, sl4),
                        deps=[e_m, st_prev[sl4]], inc=ss)
                    prev_ps[pp] = e_e
                    e_st = sch.add("gpsimd", (lambda c_, th_, sl_: lambda g: g.dma_start(
                        out=h_out[c_ * 128:(c_ + 1) * 128, th_ * 512:(th_ + 1) * 512],
                        in_=ho_t[sl_][:]))(c, th, sl4),
                        deps=[e_e], inc=st_s[sl4], inc_n=16)
                    st_prev[sl4] = e_st

        sch.add("gpsimd", lambda g: None, deps=[(s, s.count) for s in st_s])
        sch.emit(nc)
    return nc



def build_fwht_program():
    nc = bass.Bass("TRN2", target_bir_lowering=False, debug=False)
    KC = D // 128
    x_fm = nc.dram_tensor("x_fm", [D, TOK_CORE], F32, kind="ExternalInput").ap()
    su = nc.dram_tensor("su", [D, 1], F32, kind="ExternalInput").ap()
    h128 = nc.dram_tensor("h128", [128, 128], F32, kind="ExternalInput").ap()
    t_out = nc.dram_tensor("t_out", [D, TOK_CORE], BF16, kind="ExternalOutput").ap()

    sch = Sched()
    x_ld = [sch.sem(f"x_ld{i}") for i in range(2)]
    su_ld = sch.sem("su_ld")
    c_ld = sch.sem("c_ld")
    cast = sch.sem("cast")
    mm = sch.sem("mm")
    vs = sch.sem("vs")
    ss = sch.sem("ss")
    st_s = [sch.sem(f"st{i}") for i in range(2)]

    with contextlib.ExitStack() as stack:
        sb = lambda name, shape, dt=F32: stack.enter_context(nc.sbuf_tensor(name, shape, dt))
        x_stage = [sb(f"x_stage{i}", [128, TOK_CORE]) for i in range(2)]
        xs_bf = [sb(f"xs_bf{k}", [128, TOK_CORE], BF16) for k in range(KC)]
        t1 = [sb(f"t1_{k}", [128, TOK_CORE]) for k in range(KC)]
        btmp = sb("btmp", [128, TOK_CORE])
        su_t = sb("su_t", [128, KC])
        h128_f = sb("h128_f", [128, 128])
        h128_b = sb("h128_b", [128, 128], BF16)
        to_t = [sb(f"to{i}", [128, TOK_CORE], BF16) for i in range(2)]
        ps2 = [stack.enter_context(nc.psum_tensor(f"ps_{i}", [128, 512], F32)) for i in range(2)]

        e_h = sch.add("sync", lambda s: s.dma_start(out=h128_f[:], in_=h128), inc=c_ld, inc_n=16)
        e_hc = sch.add("vector", lambda v: v.tensor_copy(h128_b[:], h128_f[:]), deps=[e_h], inc=cast)
        for k in range(KC):
            sch.add("sync", (lambda kk: lambda s: s.dma_start(
                out=su_t[:, kk:kk + 1], in_=su[kk * 128:(kk + 1) * 128, :]))(k), inc=su_ld, inc_n=16)
        e_su = (su_ld, su_ld.count)

        xev = []
        for k in range(KC):
            buf = k % 2
            e_lx = sch.add("sync", (lambda kk, bb: lambda s: s.dma_start(
                out=x_stage[bb][:], in_=x_fm[kk * 128:(kk + 1) * 128, :]))(k, buf),
                deps=[xev[k - 2][0]] if k >= 2 else [], inc=x_ld[buf], inc_n=16)
            e_ps = sch.add("vector", (lambda kk, bb: lambda v: v.tensor_scalar(
                x_stage[bb][:], x_stage[bb][:], su_t[:, kk:kk + 1], None, ALU.mult))(k, buf),
                deps=[e_lx, e_su], inc=vs)
            e_cx = sch.add("vector", (lambda kk, bb: lambda v: v.tensor_copy(xs_bf[kk][:], x_stage[bb][:]))(k, buf),
                           deps=[e_ps], inc=cast)
            xev.append((e_cx,))
        t1_ev = {}
        prev_ps = {}
        for k in range(KC):
            for th in range(2):
                pp = th
                e_m = sch.add("tensor", (lambda k_, th_, pp_: lambda te: te.matmul(
                    ps2[pp_][:], h128_b[:], xs_bf[k_][:, th_ * 512:(th_ + 1) * 512],
                    start=True, stop=True))(k, th, pp),
                    deps=[xev[k][0], e_hc, prev_ps.get(pp)], inc=mm)
                e_e = sch.add("scalar", (lambda k_, th_, pp_: lambda s: s.activation(
                    t1[k_][:, th_ * 512:(th_ + 1) * 512], ps2[pp_][:], AF.Copy,
                    scale=FWHT_SCALE_2048))(k, th, pp),
                    deps=[e_m], inc=ss)
                prev_ps[pp] = e_e
                t1_ev[(k, th)] = e_e
        tile_ev = {k: t1_ev[(k, 1)] for k in range(KC)}
        for k in range(KC):
            tile_ev[k] = t1_ev[(k, 1)]
        full_ev = {k: [t1_ev[(k, 0)], t1_ev[(k, 1)]] for k in range(KC)}
        tile_dep = {k: full_ev[k] for k in range(KC)}
        cur = {k: tile_dep[k] for k in range(KC)}
        prev_btmp = None
        hstep = 1
        while hstep < KC:
            for c in range(KC):
                if (c // hstep) % 2 == 0:
                    a, b = c, c + hstep
                    e1 = sch.add("vector", (lambda a_, b_: lambda v: v.tensor_tensor(
                        btmp[:], t1[a_][:], t1[b_][:], ALU.subtract))(a, b),
                        deps=cur[a] + cur[b] + [prev_btmp], inc=vs)
                    e2 = sch.add("vector", (lambda a_, b_: lambda v: v.tensor_tensor(
                        t1[a_][:], t1[a_][:], t1[b_][:], ALU.add))(a, b),
                        deps=[e1], inc=vs)
                    e3 = sch.add("vector", (lambda b_: lambda v: v.tensor_copy(t1[b_][:], btmp[:]))(b),
                        deps=[e2], inc=vs)
                    cur[a] = [e2]
                    cur[b] = [e3]
                    prev_btmp = e3
            hstep *= 2
        st_prev = [None, None]
        for k in range(KC):
            sl2 = k % 2
            e_c = sch.add("vector", (lambda k_, sl_: lambda v: v.tensor_copy(to_t[sl_][:], t1[k_][:]))(k, sl2),
                          deps=cur[k] + [st_prev[sl2]], inc=vs)
            e_st = sch.add("gpsimd", (lambda k_, sl_: lambda g: g.dma_start(
                out=t_out[k_ * 128:(k_ + 1) * 128, :], in_=to_t[sl_][:]))(k, sl2),
                deps=[e_c], inc=st_s[sl2], inc_n=16)
            st_prev[sl2] = e_st

        sch.add("gpsimd", lambda g: None, deps=[(s, s.count) for s in st_s])
        sch.emit(nc)
    return nc




HEAD_DIM = 128
LAMBDA_INIT = 0.8 - 0.6 * float(np.exp(-0.3 * 12))
SM_SCALE = 1.0 / float(np.sqrt(128.0))


def build_attn_program(layer):
    nc = bass.Bass("TRN2", target_bir_lowering=False, debug=False)
    xfm = nc.dram_tensor("xfm", [D, T_SEQ], F32, kind="ExternalInput").ap()
    wq = nc.dram_tensor("wq", [D, 1024], F32, kind="ExternalInput").ap()
    wk = nc.dram_tensor("wk", [D, 1024], F32, kind="ExternalInput").ap()
    wv = nc.dram_tensor("wv", [D, 1024], F32, kind="ExternalInput").ap()
    masks = nc.dram_tensor("masks", [4, 128, 512], F32, kind="ExternalInput").ap()
    ident = nc.dram_tensor("ident", [128, 128], F32, kind="ExternalInput").ap()
    if layer == 1:
        rc = nc.dram_tensor("rc", [128, T_SEQ], F32, kind="ExternalInput").ap()
        rs = nc.dram_tensor("rs", [128, T_SEQ], F32, kind="ExternalInput").ap()
    else:
        lamv = nc.dram_tensor("lamv", [1, 512], F32, kind="ExternalInput").ap()
        subln_b = nc.dram_tensor("subln_b", [128, 256], F32, kind="ExternalInput").ap()
    a_out = nc.dram_tensor("a_out", [T_SEQ, 1024], F32, kind="ExternalOutput").ap()

    NH = 8 if layer == 1 else 4
    NBR = 1 if layer == 1 else 2
    DV = 128 if layer == 1 else 256
    DVA = DV + 1
    FH = 1024 // NH
    KC = D // 128
    NT = T_SEQ // 128
    NQ5 = T_SEQ // 512

    sch = Sched()
    x_ld = [sch.sem(f"x_ld{i}") for i in range(2)]
    w_lds = {nm: sch.sem(f"{nm}_ld") for nm in ("wq", "wk", "wv")}
    c_ld = sch.sem("c_ld")
    cast = sch.sem("cast")
    mm = sch.sem("mm")
    vs = sch.sem("vs")
    ss = sch.sem("ss")
    st_s = [sch.sem(f"st{i}") for i in range(4)]

    with contextlib.ExitStack() as stack:
        sb = lambda name, shape, dt=F32: stack.enter_context(nc.sbuf_tensor(name, shape, dt))
        x_stage = [sb(f"x_stage{i}", [128, T_SEQ]) for i in range(2)]
        x_bf = [sb(f"x_bf{k}", [128, T_SEQ], BF16) for k in range(KC)]
        w_stage = sb("w_stage", [128, 8 * FH])
        w_bf = {nm: sb(f"{nm}_bf", [128, KC * FH], BF16) for nm in ("wq", "wk", "wv")}
        q_bf = [sb(f"q_bf{b}", [128, T_SEQ], BF16) for b in range(NBR)]
        k_bf = [sb(f"k_bf{b}", [128, T_SEQ], BF16) for b in range(NBR)]
        v_tm = sb("v_tm", [128, NT * DVA], BF16)
        pt = sb("pt", [128, NT * 512], BF16)
        mask_bf = sb("mask_bf", [128, 4 * 512], BF16)
        mask_f32 = sb("mask_f32", [128, 4 * 512])
        ident_bf = sb("ident_bf", [128, 128], BF16)
        ident_f32 = sb("ident_f32", [128, 128])
        shift_t = sb("shift_t", [128, 1])
        eps_t = sb("eps_t", [128, 1])
        rec = [sb(f"rec{b}", [128, 1]) for b in range(NBR)]
        a_f32 = [sb(f"a_f32{i}", [128, DV]) for i in range(2)]
        if layer == 1:
            cos_t = sb("cos_t", [128, T_SEQ])
            sin_t = sb("sin_t", [128, T_SEQ])
            rope_sc = [sb(f"rope{i}", [128, 512]) for i in range(3)]
        else:
            lam_row = sb("lam_row", [1, 512])
            lam_sc = sb("lam_sc", [1, 8])
            ones_row = sb("ones_row", [1, 128])
            lam_b = sb("lam_b", [128, 1])
            nlam_rec = sb("nlam_rec", [128, 1])
            subln_t = sb("subln_t", [128, 256])
            sq_s = sb("sq_s", [128, 256])
            o_sb0 = sb("o_sb0", [128, 4 * 256])
            o1_sc = sb("o1_sc", [128, 256])
            cmb = sb("cmb", [128, 256])
            rms = {nm: sb(f"rms_{nm}", [128, 1]) for nm in ("sumsq", "ms", "std", "rstd")}
        ps_proj = [stack.enter_context(nc.psum_tensor(f"psp{i}", [128, 512], F32)) for i in range(2)]
        ps_sc = [stack.enter_context(nc.psum_tensor(f"pss{i}", [128, 512], F32)) for i in range(2)]
        ps_pv = [stack.enter_context(nc.psum_tensor(f"pspv{i}", [128, DVA], F32)) for i in range(2)]
        if layer == 0:
            ps_lam = stack.enter_context(nc.psum_tensor("ps_lam", [128, 1], F32))

        for r in range(4):
            sch.add("sync", (lambda r_: lambda s: s.dma_start(
                out=mask_f32[:, r_ * 512:(r_ + 1) * 512], in_=masks[r_]))(r), inc=c_ld, inc_n=16)
        sch.add("sync", lambda s: s.dma_start(out=ident_f32[:], in_=ident), inc=c_ld, inc_n=16)
        if layer == 1:
            sch.add("sync", lambda s: s.dma_start(out=cos_t[:], in_=rc), inc=c_ld, inc_n=16)
            sch.add("sync", lambda s: s.dma_start(out=sin_t[:], in_=rs), inc=c_ld, inc_n=16)
        else:
            sch.add("sync", lambda s: s.dma_start(out=lam_row[:], in_=lamv), inc=c_ld, inc_n=16)
            sch.add("sync", lambda s: s.dma_start(out=subln_t[:], in_=subln_b), inc=c_ld, inc_n=16)
        c_done = (c_ld, c_ld.count)
        rope_evs = [c_done]
        subln_ev = c_done

        e_mc = sch.add("vector", lambda v: v.tensor_copy(mask_bf[:], mask_f32[:]), deps=[c_done], inc=cast)
        e_ic = sch.add("vector", lambda v: v.tensor_copy(ident_bf[:], ident_f32[:]), deps=[c_done], inc=cast)
        e_sh = sch.add("vector", lambda v: v.memset(shift_t[:], -SOFTMAX_SHIFT), inc=vs)
        e_eps = sch.add("vector", lambda v: v.memset(eps_t[:], EPS), inc=vs)
        lam_evs = []
        if layer == 0:
            e_p1 = sch.add("vector", lambda v: v.tensor_tensor(
                lam_row[0:1, 0:128], lam_row[0:1, 0:128], lam_row[0:1, 128:256], ALU.mult),
                deps=[c_done], inc=vs)
            e_p2 = sch.add("vector", lambda v: v.tensor_tensor(
                lam_row[0:1, 256:384], lam_row[0:1, 256:384], lam_row[0:1, 384:512], ALU.mult),
                deps=[c_done, e_p1], inc=vs)
            e_s1 = sch.add("vector", lambda v: v.reduce_sum(lam_sc[0:1, 0:1], lam_row[0:1, 0:128], axis=AX.X),
                           deps=[e_p1], inc=vs)
            e_s2 = sch.add("vector", lambda v: v.reduce_sum(lam_sc[0:1, 1:2], lam_row[0:1, 256:384], axis=AX.X),
                           deps=[e_p2, e_s1], inc=vs)
            e_e1 = sch.add("scalar", lambda s: s.activation(lam_sc[0:1, 2:4], lam_sc[0:1, 0:2], AF.Exp),
                           deps=[e_s1, e_s2], inc=ss)
            e_d = sch.add("vector", lambda v: v.tensor_tensor(
                lam_sc[0:1, 4:5], lam_sc[0:1, 2:3], lam_sc[0:1, 3:4], ALU.subtract),
                deps=[e_e1], inc=vs)
            e_l = sch.add("vector", lambda v: v.tensor_scalar(
                lam_sc[0:1, 4:5], lam_sc[0:1, 4:5], LAMBDA_INIT, None, ALU.add),
                deps=[e_d], inc=vs)
            e_or = sch.add("vector", lambda v: v.memset(ones_row[:], 1.0), inc=vs)
            e_lb = sch.add("tensor", lambda t: t.matmul(ps_lam[:], ones_row[:], lam_sc[0:1, 4:5],
                                                        start=True, stop=True),
                           deps=[e_l, e_or], inc=mm)
            e_lbc = sch.add("vector", lambda v: v.tensor_copy(lam_b[:], ps_lam[:]), deps=[e_lb], inc=vs)
            lam_evs = [e_lbc]

        x_cast_evs = []
        for k in range(KC):
            buf = k % 2
            e_lx = sch.add("sync", (lambda kk, bb: lambda s: s.dma_start(
                out=x_stage[bb][:], in_=xfm[kk * 128:(kk + 1) * 128, :]))(k, buf),
                deps=[x_cast_evs[k - 2]] if k >= 2 else [], inc=x_ld[buf], inc_n=16)
            e_cx = sch.add("vector", (lambda kk, bb: lambda v: v.tensor_copy(x_bf[kk][:], x_stage[bb][:]))(k, buf),
                           deps=[e_lx], inc=cast)
            x_cast_evs.append(e_cx)
        x_done = (cast, cast.count)

        st_prev = [None] * 4
        st_ctr = [0]
        def next_store_slot():
            i = st_ctr[0] % 4
            st_ctr[0] += 1
            return i

        prev_w_use = {nm: None for nm in ("wq", "wk", "wv")}
        prev_w_cast = {nm: None for nm in ("wq", "wk", "wv")}
        prev_head_end = [None]
        prev_pproj = {}
        prev_psc = {}
        prev_ppv = {}
        prev_af = {}
        prev_rope = [None]
        prev_rms = [None]
        prev_osb0 = {}
        prev_o1 = [None]
        prev_cmb = [None]
        prev_rec = {}
        prev_nlam = [None]

        for h in range(NH):
            w_evs = {}
            prev_stage_cast = prev_w_cast["wv"]
            for nm, wap in (("wq", wq), ("wk", wk), ("wv", wv)):
                e_cw = None
                for hf in range(2):
                    e_lw = sch.add("sync", (lambda w_, r0, r1, c0, c1: lambda s: s.dma_start(
                        out=w_stage[:].rearrange("p (c f) -> p c f", c=8),
                        in_=w_[r0:r1, c0:c1].rearrange("(c p) f -> p c f", p=128)))(
                            wap, hf * 1024, (hf + 1) * 1024, h * FH, (h + 1) * FH),
                        deps=[prev_stage_cast], inc=w_lds[nm], inc_n=16)
                    e_cw = sch.add("vector", (lambda nm_, hf_: lambda v: v.tensor_copy(
                        w_bf[nm_][:, hf_ * 8 * FH:(hf_ + 1) * 8 * FH], w_stage[:]))(nm, hf),
                        deps=[e_lw, prev_w_use[nm]], inc=cast)
                    prev_stage_cast = e_cw
                prev_w_cast[nm] = e_cw
                w_evs[nm] = e_cw

            qk_done = []
            pcnt = 0
            for nm, dest in (("wq", q_bf), ("wk", k_bf)):
                for br in range(NBR):
                    for tsl in range(4):
                        pp = pcnt % 2
                        pcnt += 1
                        def proj(nm_, br_, tsl_, pp_):
                            def f(te):
                                ins = None
                                for kc in range(KC):
                                    ins = te.matmul(ps_proj[pp_][:],
                                                    w_bf[nm_][:, kc * FH + br_ * 128: kc * FH + br_ * 128 + 128],
                                                    x_bf[kc][:, tsl_ * 512:(tsl_ + 1) * 512],
                                                    start=(kc == 0), stop=(kc == KC - 1))
                                return ins
                            return f
                        e_mm = sch.add("tensor", proj(nm, br, tsl, pp),
                                       deps=[w_evs[nm], x_done, prev_pproj.get(pp)], inc=mm)
                        if layer == 1:
                            e_r1 = sch.add("vector", (lambda pp_: lambda v: v.tensor_scalar(
                                rope_sc[0][0:64, :], ps_proj[pp_][64:128, :], -1.0, None, ALU.mult))(pp),
                                deps=[e_mm, prev_rope[0]], inc=vs)
                            e_r2 = sch.add("vector", (lambda pp_: lambda v: v.tensor_copy(
                                rope_sc[0][64:128, :], ps_proj[pp_][0:64, :]))(pp),
                                deps=[e_mm, e_r1], inc=vs)
                            e_r3 = sch.add("vector", (lambda pp_, t_: lambda v: v.tensor_tensor(
                                rope_sc[1][:], ps_proj[pp_][:], cos_t[:, t_ * 512:(t_ + 1) * 512], ALU.mult))(pp, tsl),
                                deps=[e_mm, e_r1] + rope_evs, inc=vs)
                            e_r4 = sch.add("vector", (lambda t_: lambda v: v.tensor_tensor(
                                rope_sc[2][:], rope_sc[0][:], sin_t[:, t_ * 512:(t_ + 1) * 512], ALU.mult))(tsl),
                                deps=[e_r2, e_r3] + rope_evs, inc=vs)
                            e_evq = sch.add("vector", (lambda d_, t_: lambda v: v.tensor_tensor(
                                d_[:, t_ * 512:(t_ + 1) * 512], rope_sc[1][:], rope_sc[2][:], ALU.add))(dest[br], tsl),
                                deps=[e_r4, prev_head_end[0]], inc=vs)
                            prev_rope[0] = e_evq
                        else:
                            e_evq = sch.add("scalar", (lambda d_, t_, pp_: lambda s: s.copy(
                                d_[:, t_ * 512:(t_ + 1) * 512], ps_proj[pp_][:]))(dest[br], tsl, pp),
                                deps=[e_mm, prev_head_end[0]], inc=ss)
                        prev_pproj[pp] = e_evq
                        qk_done.append(e_evq)
            qk_all = qk_done[-1]

            e_vms = sch.add("vector", lambda v: v.memset(v_tm[:], 1.0),
                            deps=[prev_head_end[0]], inc=vs)
            v_done = []
            for kt in range(NT):
                pp = pcnt % 2
                pcnt += 1
                def vproj(kt_, pp_):
                    def f(te):
                        ins = None
                        for kc in range(KC):
                            ins = te.matmul(ps_proj[pp_][:, 0:DV],
                                            x_bf[kc][:, kt_ * 128:(kt_ + 1) * 128],
                                            w_bf["wv"][:, kc * FH: kc * FH + DV],
                                            start=(kc == 0), stop=(kc == KC - 1))
                        return ins
                    return f
                e_vm = sch.add("tensor", vproj(kt, pp),
                               deps=[w_evs["wv"], x_done, prev_pproj.get(pp)], inc=mm)
                e_ve = sch.add("scalar", (lambda kt_, pp_: lambda s: s.copy(
                    v_tm[:, kt_ * DVA: kt_ * DVA + DV], ps_proj[pp_][:, 0:DV]))(kt, pp),
                    deps=[e_vm, e_vms], inc=ss)
                prev_pproj[pp] = e_ve
                v_done.append(e_ve)
            v_all = v_done[-1]

            head_last = None
            for tq in range(NQ5):
                nch = 4 * tq + 4
                osb0_evs = {}
                for br in range(NBR):
                    pt_evs = []
                    for j in range(nch):
                        pb = j % 2
                        def sct(j_, tq_, br_, pb_):
                            def f(te):
                                return te.matmul(ps_sc[pb_][:],
                                                 k_bf[br_][:, j_ * 128:(j_ + 1) * 128],
                                                 q_bf[br_][:, tq_ * 512:(tq_ + 1) * 512],
                                                 start=True, stop=(j_ < 4 * tq_))
                            return f
                        e_s = sch.add("tensor", sct(j, tq, br, pb),
                                      deps=[qk_all, prev_psc.get(pb)], inc=mm)
                        if j >= 4 * tq:
                            r_i = j - 4 * tq
                            def mskmm(r_, pb_):
                                def f(te):
                                    return te.matmul(ps_sc[pb_][:], ident_bf[:],
                                                     mask_bf[:, r_ * 512:(r_ + 1) * 512],
                                                     start=False, stop=True)
                                return f
                            e_s = sch.add("tensor", mskmm(r_i, pb), deps=[e_s, e_mc, e_ic], inc=mm)
                        e_x = sch.add("scalar", (lambda j_, pb_: lambda s: s.activation(
                            pt[:, j_ * 512:(j_ + 1) * 512], ps_sc[pb_][:], AF.Exp,
                            bias=shift_t[:], scale=SM_SCALE))(j, pb),
                            deps=[e_s, e_sh], inc=ss)
                        prev_psc[pb] = e_x
                        pt_evs.append(e_x)
                    pt_all = pt_evs[-1]

                    for qs in range(4):
                        pb = qs % 2
                        def pvg(qs_, ncl_, pb_):
                            def f(te):
                                ins = None
                                for j in range(ncl_):
                                    ins = te.matmul(ps_pv[pb_][:],
                                                    pt[:, j * 512 + qs_ * 128: j * 512 + qs_ * 128 + 128],
                                                    v_tm[:, j * DVA:(j + 1) * DVA],
                                                    start=(j == 0), stop=(j == ncl_ - 1))
                                return ins
                            return f
                        e_pv = sch.add("tensor", pvg(qs, nch, pb),
                                       deps=[pt_all, v_all, prev_ppv.get(pb)], inc=mm)
                        e_rc2 = sch.add("vector", (lambda pb_, br_: lambda v: v.reciprocal(
                            rec[br_][:], ps_pv[pb_][:, DV:DVA]))(pb, br),
                            deps=[e_pv, prev_rec.get(br)], inc=vs)
                        if layer == 1:
                            sl = next_store_slot()
                            ab = sl % 2
                            e_n = sch.add("vector", (lambda pb_, ab_: lambda v: v.tensor_scalar(
                                a_f32[ab_][:, 0:DV], ps_pv[pb_][:, 0:DV], rec[0][:], None, ALU.mult))(pb, ab),
                                deps=[e_rc2, prev_af.get(ab)], inc=vs)
                            e_st = sch.add("gpsimd", (lambda tq_, qs_, h_, ab_: lambda g: g.dma_start(
                                out=a_out[(tq_ * 4 + qs_) * 128:(tq_ * 4 + qs_ + 1) * 128, h_ * FH:(h_ + 1) * FH],
                                in_=a_f32[ab_][:, 0:DV]))(tq, qs, h, ab),
                                deps=[e_n, st_prev[sl]], inc=st_s[sl], inc_n=16)
                            st_prev[sl] = e_st
                            prev_af[ab] = e_st
                            prev_rec[0] = e_n
                            prev_ppv[pb] = e_n
                            head_last = e_st
                        elif br == 0:
                            e_n = sch.add("vector", (lambda pb_, qs_: lambda v: v.tensor_scalar(
                                o_sb0[:, qs_ * DV:(qs_ + 1) * DV], ps_pv[pb_][:, 0:DV], rec[0][:], None, ALU.mult))(pb, qs),
                                deps=[e_rc2, prev_osb0.get(qs)], inc=vs)
                            osb0_evs[qs] = e_n
                            prev_rec[0] = e_n
                            prev_ppv[pb] = e_n
                        else:
                            e_c1 = sch.add("vector", lambda v: v.tensor_tensor(
                                nlam_rec[:], rec[1][:], lam_b[:], ALU.mult),
                                deps=[e_rc2, prev_nlam[0]] + lam_evs, inc=vs)
                            e_c2 = sch.add("vector", (lambda pb_: lambda v: v.tensor_scalar(
                                o1_sc[:], ps_pv[pb_][:, 0:DV], nlam_rec[:], None, ALU.mult))(pb),
                                deps=[e_c1, prev_o1[0]], inc=vs)
                            prev_rec[1] = e_c2
                            prev_ppv[pb] = e_c2
                            e_c3 = sch.add("vector", (lambda qs_: lambda v: v.tensor_tensor(
                                cmb[:], o_sb0[:, qs_ * DV:(qs_ + 1) * DV], o1_sc[:], ALU.subtract))(qs),
                                deps=[e_c2, osb0_evs[qs], prev_cmb[0]], inc=vs)
                            prev_osb0[qs] = e_c3
                            prev_o1[0] = e_c3
                            e_q1 = sch.add("scalar", lambda s: s.activation(
                                sq_s[:], cmb[:], AF.Square, accum_out=rms["sumsq"][:]),
                                deps=[e_c3, prev_rms[0]], inc=ss)
                            e_q2 = sch.add("vector", lambda v: v.tensor_scalar(
                                rms["ms"][:], rms["sumsq"][:], 1.0 / DV, None, ALU.mult),
                                deps=[e_q1], inc=vs)
                            e_q3 = sch.add("scalar", lambda s: s.activation(
                                rms["std"][:], rms["ms"][:], AF.Sqrt, bias=eps_t[:]),
                                deps=[e_q2, e_eps], inc=ss)
                            e_q4 = sch.add("vector", lambda v: v.reciprocal(rms["rstd"][:], rms["std"][:]),
                                           deps=[e_q3], inc=vs)
                            e_q5 = sch.add("vector", lambda v: v.tensor_scalar(
                                rms["rstd"][:], rms["rstd"][:], 1.0 - LAMBDA_INIT, None, ALU.mult),
                                deps=[e_q4], inc=vs)
                            sl = next_store_slot()
                            ab = sl % 2
                            e_q6 = sch.add("vector", (lambda ab_: lambda v: v.tensor_scalar(
                                a_f32[ab_][:], cmb[:], rms["rstd"][:], None, ALU.mult))(ab),
                                deps=[e_q5, e_c3, prev_af.get(ab)], inc=vs)
                            e_q7 = sch.add("vector", (lambda ab_: lambda v: v.tensor_tensor(
                                a_f32[ab_][:], a_f32[ab_][:], subln_t[:], ALU.mult))(ab),
                                deps=[e_q6, subln_ev], inc=vs)
                            prev_rms[0] = e_q7
                            prev_cmb[0] = e_q7
                            e_st = sch.add("gpsimd", (lambda tq_, qs_, h_, ab_: lambda g: g.dma_start(
                                out=a_out[(tq_ * 4 + qs_) * 128:(tq_ * 4 + qs_ + 1) * 128, h_ * FH:(h_ + 1) * FH],
                                in_=a_f32[ab_][:]))(tq, qs, h, ab),
                                deps=[e_q7, st_prev[sl]], inc=st_s[sl], inc_n=16)
                            st_prev[sl] = e_st
                            prev_af[ab] = e_st
                            head_last = e_st

            prev_head_end[0] = head_last
            for nm in ("wq", "wk", "wv"):
                prev_w_use[nm] = head_last

        sch.add("gpsimd", lambda g: None, deps=[(s, s.count) for s in st_s])
        sch.emit(nc)
    return nc


def _run(nc, in_maps):
    return run_bass_kernel_spmd(nc, in_maps, core_ids=list(range(N_CORES)))



def host_rope_tables():
    inv = 1.0 / (1e6 ** (np.arange(0, 128, 2, dtype=np.float64) / 128))
    t = np.arange(T_SEQ, dtype=np.float64)
    f = np.outer(t, inv)
    cos = np.concatenate([np.cos(f), np.cos(f)], axis=-1).astype(np.float32)
    sin = np.concatenate([np.sin(f), np.sin(f)], axis=-1).astype(np.float32)
    return np.ascontiguousarray(cos.T), np.ascontiguousarray(sin.T)


def host_masks():
    m = np.zeros((4, 128, 512), dtype=np.float32)
    kk = np.arange(128)[:, None]
    qq = np.arange(512)[None, :]
    for r in range(4):
        blk = qq // 128
        vis = (blk > r) | ((blk == r) & (kk <= (qq % 128)))
        m[r] = np.where(vis, 0.0, -1e30)
    return m


def host_hadamard128():
    H = np.array([[1.0]], dtype=np.float32)
    while H.shape[0] < 128:
        H = np.block([[H, H], [H, -H]])
    return np.ascontiguousarray(H.astype(np.float32))


def _ct(a):
    return np.ascontiguousarray(a)



_PROG_CACHE = {}

def _prog(key, builder, *args, **kwargs):
    if key not in _PROG_CACHE:
        _PROG_CACHE[key] = builder(*args, **kwargs)
    return _PROG_CACHE[key]


def kernel_multi(L0_attn_x, L0_attn_Wq, L0_attn_Wk, L0_attn_Wv, L0_attn_Wo,
           L0_attn_lambda_q1, L0_attn_lambda_k1, L0_attn_lambda_q2, L0_attn_lambda_k2,
           L0_attn_subln_weight,
           L0_ffn_gate_w, L0_ffn_up_w, L0_ffn_down_w, L0_ffn_gate_s, L0_ffn_up_s, L0_ffn_down_s,
           L1_attn_wq, L1_attn_wk, L1_attn_wv, L1_attn_wo,
           L1_ffn_SU_ug, L1_ffn_SU_dp, L1_ffn_Wscale_ug, L1_ffn_Wscale_dp,
           L1_ffn_W_up, L1_ffn_W_gate, L1_ffn_W_down):
    f32 = np.float32
    x = np.asarray(L0_attn_x, dtype=f32)
    x_flat = x.reshape(TOK_GLOBAL, D)
    masks = host_masks()
    ident = np.eye(128, dtype=f32)
    H128 = host_hadamard128()
    cosT, sinT = host_rope_tables()

    def rows(c):
        return slice(c * TOK_CORE, (c + 1) * TOK_CORE)

    nc_a0 = _prog("a0", build_attn_program, 0)
    lamv = np.concatenate([np.asarray(L0_attn_lambda_q1, f32), np.asarray(L0_attn_lambda_k1, f32),
                           np.asarray(L0_attn_lambda_q2, f32), np.asarray(L0_attn_lambda_k2, f32)])[None, :]
    subln_b = _ct(np.broadcast_to(np.asarray(L0_attn_subln_weight, f32), (128, 256)))
    Wq0 = np.asarray(L0_attn_Wq, f32); Wk0 = np.asarray(L0_attn_Wk, f32); Wv0 = np.asarray(L0_attn_Wv, f32)
    im = []
    for c in range(N_CORES):
        b, hh = c // 2, c % 2
        im.append({"xfm": _ct(x[b].T), "wq": _ct(Wq0[:, hh * 1024:(hh + 1) * 1024]),
                   "wk": _ct(Wk0[:, hh * 1024:(hh + 1) * 1024]),
                   "wv": _ct(Wv0[:, hh * 1024:(hh + 1) * 1024]),
                   "masks": masks, "ident": ident, "lamv": lamv, "subln_b": subln_b})
    res = _run(nc_a0, im)
    A0 = np.empty((B_SZ, T_SEQ, D), dtype=f32)
    for c in range(N_CORES):
        b, hh = c // 2, c % 2
        A0[b, :, hh * 1024:(hh + 1) * 1024] = res.results[c]["a_out"]
    A0_flat = A0.reshape(TOK_GLOBAL, D)

    nc_wo = _prog("pmm16_final", build_pmm_program, 16)
    Wo0 = np.asarray(L0_attn_Wo, f32)
    im = [{"a_fm": _ct(A0_flat[rows(c)].T), "w": Wo0, "resid": _ct(x_flat[rows(c)])}
          for c in range(N_CORES)]
    res = _run(nc_wo, im)
    R1 = np.concatenate([res.results[c]["out"] for c in range(N_CORES)], axis=0)

    nc_f0 = _prog("ffnup0", build_ffn_up_program, 0)
    gwT = _ct(np.asarray(L0_ffn_gate_w, f32).T)
    uwT = _ct(np.asarray(L0_ffn_up_w, f32).T)
    gs0 = _ct(np.asarray(L0_ffn_gate_s, f32)[:, None])
    us0 = _ct(np.asarray(L0_ffn_up_s, f32)[:, None])
    im = [{"x_fm": _ct(R1[rows(c)].T), "gw": gwT, "uw": uwT, "gs": gs0, "us": us0}
          for c in range(N_CORES)]
    res = _run(nc_f0, im)
    h0 = [res.results[c]["h_out"] for c in range(N_CORES)]

    dwT = _ct(np.asarray(L0_ffn_down_w, f32).T)
    ds_b = _ct(np.broadcast_to(np.asarray(L0_ffn_down_s, f32), (128, D)))
    nc_d1 = _prog("pmm22_part0", build_pmm_program, 22, True, False, False)
    im = [{"a_fm": _ct(h0[c][:2816]), "w": _ct(dwT[:2816])} for c in range(N_CORES)]
    res = _run(nc_d1, im)
    p = [res.results[c]["out"] for c in range(N_CORES)]
    nc_d2 = _prog("pmm22_fin_cs", build_pmm_program, 22, True, True, True, True, False)
    im = [{"a_fm": _ct(h0[c][2816:]), "w": _ct(dwT[2816:]), "p_in": p[c],
           "resid": _ct(R1[rows(c)]), "colscale_b": ds_b} for c in range(N_CORES)]
    res = _run(nc_d2, im)
    R2 = np.concatenate([res.results[c]["out"] for c in range(N_CORES)], axis=0)

    nc_a1 = _prog("a1", build_attn_program, 1)
    wq1 = np.asarray(L1_attn_wq, f32); wk1 = np.asarray(L1_attn_wk, f32); wv1 = np.asarray(L1_attn_wv, f32)
    R2b = R2.reshape(B_SZ, T_SEQ, D)
    im = []
    for c in range(N_CORES):
        b, hh = c // 2, c % 2
        im.append({"xfm": _ct(R2b[b].T), "wq": _ct(wq1[hh * 1024:(hh + 1) * 1024, :].T),
                   "wk": _ct(wk1[hh * 1024:(hh + 1) * 1024, :].T),
                   "wv": _ct(wv1[hh * 1024:(hh + 1) * 1024, :].T),
                   "masks": masks, "ident": ident, "rc": cosT, "rs": sinT})
    res = _run(nc_a1, im)
    A1 = np.empty((B_SZ, T_SEQ, D), dtype=f32)
    for c in range(N_CORES):
        b, hh = c // 2, c % 2
        A1[b, :, hh * 1024:(hh + 1) * 1024] = res.results[c]["a_out"]
    A1_flat = A1.reshape(TOK_GLOBAL, D)

    wo1T = _ct(np.asarray(L1_attn_wo, f32).T)
    im = [{"a_fm": _ct(A1_flat[rows(c)].T), "w": wo1T, "resid": _ct(R2[rows(c)])}
          for c in range(N_CORES)]
    res = _run(nc_wo, im)
    R3 = np.concatenate([res.results[c]["out"] for c in range(N_CORES)], axis=0)

    nc_fw = _prog("fwht", build_fwht_program)
    su_ug = _ct(np.asarray(L1_ffn_SU_ug, f32)[:, None])
    im = [{"x_fm": _ct(R3[rows(c)].T), "su": su_ug, "h128": H128} for c in range(N_CORES)]
    res = _run(nc_fw, im)
    t_fm = [res.results[c]["t_out"] for c in range(N_CORES)]

    nc_f1 = _prog("ffnup1", build_ffn_up_program, 1)
    wguT = _ct(np.asarray(L1_ffn_W_gate, f32).T)
    wupT = _ct(np.asarray(L1_ffn_W_up, f32).T)
    wsc = np.asarray(L1_ffn_Wscale_ug, f32)
    gs1 = _ct(wsc[INTER1:][:, None])
    us1 = _ct(wsc[:INTER1][:, None])
    su_dp = _ct(np.asarray(L1_ffn_SU_dp, f32)[:, None])
    im = [{"x_fm": t_fm[c], "gw": wguT, "uw": wupT, "gs": gs1, "us": us1,
           "su": su_dp, "h128": H128} for c in range(N_CORES)]
    res = _run(nc_f1, im)
    h2 = [res.results[c]["h_out"] for c in range(N_CORES)]

    wdT = _ct(np.asarray(L1_ffn_W_down, f32).T)
    wdp_b = _ct(np.broadcast_to(np.asarray(L1_ffn_Wscale_dp, f32), (128, D)))
    nc_p0 = _prog("pmm16_part0", build_pmm_program, 16, True, False, False)
    nc_pm = _prog("pmm16_partm", build_pmm_program, 16, True, True, False)
    nc_pf = _prog("pmm16_fin64", build_pmm_program, 16, True, True, True, True, True)
    im = [{"a_fm": _ct(h2[c][:2048]), "w": _ct(wdT[:2048])} for c in range(N_CORES)]
    res = _run(nc_p0, im)
    p = [res.results[c]["out"] for c in range(N_CORES)]
    for seg in (1, 2):
        im = [{"a_fm": _ct(h2[c][seg * 2048:(seg + 1) * 2048]),
               "w": _ct(wdT[seg * 2048:(seg + 1) * 2048]), "p_in": p[c]} for c in range(N_CORES)]
        res = _run(nc_pm, im)
        p = [res.results[c]["out"] for c in range(N_CORES)]
    im = [{"a_fm": _ct(h2[c][3 * 2048:]), "w": _ct(wdT[3 * 2048:]), "p_in": p[c],
           "resid": _ct(R3[rows(c)]), "colscale_b": wdp_b} for c in range(N_CORES)]
    res = _run(nc_pf, im)
    R4 = np.concatenate([res.results[c]["out"] for c in range(N_CORES)], axis=0)

    kernel_multi.dbg = {"A0": A0, "R1": R1, "R2": R2, "R3": R3, "h0": h0, "t_fm": t_fm,
                  "h2": h2, "R4": R4}
    return R4.reshape(B_SZ, T_SEQ, D)



from concourse.aot_env import allow_ici_all_to_all

SLAB_BF16_ELEMS = 106368


class Fctx:
    def __init__(self, nc, sch, slab, psum_f32, psum_bf16):
        self.nc = nc
        self.sch = sch
        self.slab = slab
        self.psum_f32 = psum_f32
        self.psum_bf16 = psum_bf16
        self.off = 0
        self.consts = {}

    def tile(self, n, dt=F32, parts=128):
        elems = n * (2 if dt == F32 else 1)
        off = (self.off + 1) & ~1
        self.off = off + elems
        assert self.off <= SLAB_BF16_ELEMS, f"slab overflow: {self.off}"
        sl = self.slab[:, off:off + elems]
        if dt == F32:
            sl = sl.bitcast(F32)
        elif dt != BF16:
            raise ValueError(dt)
        if parts != 128:
            sl = sl[0:parts, :]
        return sl

    def mark(self):
        return self.off

    def reset(self, m):
        self.off = m


def fsem_pool(sch):
    return {
        "x_ld": [sch.sem(f"fx_ld{i}") for i in range(4)],
        "w_ld": [sch.sem(f"fw_ld{i}") for i in range(6)],
        "r_ld": [sch.sem(f"fr_ld{i}") for i in range(2)],
        "p_ld": [sch.sem(f"fp_ld{i}") for i in range(2)],
        "sc_ld": [sch.sem(f"fsc_ld{i}") for i in range(2)],
        "g_ld": [sch.sem(f"fg_ld{i}") for i in range(2)],
        "c_ld": sch.sem("fc_ld"),
        "cc": sch.sem("fcc"),
        "cast": sch.sem("fcast"),
        "mm": sch.sem("fmm"),
        "vs": sch.sem("fvs"),
        "ss": sch.sem("fss"),
        "st": [sch.sem(f"fst{i}") for i in range(4)],
    }


def emit_const_phase(ctx, S, aps):
    sch = ctx.sch
    c_ld, cast, vs, ss, mm = S["c_ld"], S["cast"], S["vs"], S["ss"], S["mm"]
    C = {}
    C["mask_bf"] = ctx.tile(4 * 512, BF16)
    C["ident_bf"] = ctx.tile(128, BF16)
    C["h128_bf"] = ctx.tile(128, BF16)
    C["shift"] = ctx.tile(1, F32)
    C["eps"] = ctx.tile(1, F32)
    C["subln"] = ctx.tile(256, F32)
    C["lam_b"] = ctx.tile(1, F32)
    m_temp = ctx.mark()
    C["mask_f32"] = ctx.tile(4 * 512, F32)
    C["ident_f32"] = ctx.tile(128, F32)
    C["h128_f32"] = ctx.tile(128, F32)
    C["lam_row"] = ctx.tile(512, F32)
    C["lam_sc"] = ctx.tile(8, F32)
    C["ones_row"] = ctx.tile(128, F32)
    for r in range(4):
        sch.add("sync", (lambda r_: lambda s: s.dma_start(
            out=C["mask_f32"][:, r_ * 512:(r_ + 1) * 512], in_=aps["masks"][r_]))(r),
            inc=c_ld, inc_n=16)
    sch.add("sync", lambda s: s.dma_start(out=C["ident_f32"][:], in_=aps["ident"]), inc=c_ld, inc_n=16)
    sch.add("sync", lambda s: s.dma_start(out=C["h128_f32"][:], in_=aps["h128"]), inc=c_ld, inc_n=16)
    sch.add("sync", lambda s: s.dma_start(out=C["subln"][:], in_=aps["subln_b"]), inc=c_ld, inc_n=16)
    sch.add("sync", lambda s: s.dma_start(out=C["lam_row"][0:1, :], in_=aps["lamv"]), inc=c_ld, inc_n=16)
    c_done = (c_ld, c_ld.count)
    C["loaded"] = c_done
    e_mc = sch.add("vector", lambda v: v.tensor_copy(C["mask_bf"][:], C["mask_f32"][:]), deps=[c_done], inc=cast)
    e_ic = sch.add("vector", lambda v: v.tensor_copy(C["ident_bf"][:], C["ident_f32"][:]), deps=[c_done], inc=cast)
    e_hc = sch.add("vector", lambda v: v.tensor_copy(C["h128_bf"][:], C["h128_f32"][:]), deps=[c_done], inc=cast)
    e_sh = sch.add("vector", lambda v: v.memset(C["shift"][:], -SOFTMAX_SHIFT), inc=vs)
    e_ep = sch.add("vector", lambda v: v.memset(C["eps"][:], EPS), inc=vs)
    lr = C["lam_row"]; lsc = C["lam_sc"]
    e_p1 = sch.add("vector", lambda v: v.tensor_tensor(lr[0:1, 0:128], lr[0:1, 0:128], lr[0:1, 128:256], ALU.mult),
                   deps=[c_done], inc=vs)
    e_p2 = sch.add("vector", lambda v: v.tensor_tensor(lr[0:1, 256:384], lr[0:1, 256:384], lr[0:1, 384:512], ALU.mult),
                   deps=[c_done, e_p1], inc=vs)
    e_s1 = sch.add("vector", lambda v: v.reduce_sum(lsc[0:1, 0:1], lr[0:1, 0:128], axis=AX.X),
                   deps=[e_p1], inc=vs)
    e_s2 = sch.add("vector", lambda v: v.reduce_sum(lsc[0:1, 1:2], lr[0:1, 256:384], axis=AX.X),
                   deps=[e_p2, e_s1], inc=vs)
    e_e1 = sch.add("scalar", lambda s: s.activation(lsc[0:1, 2:4], lsc[0:1, 0:2], AF.Exp),
                   deps=[e_s1, e_s2], inc=ss)
    e_d = sch.add("vector", lambda v: v.tensor_tensor(lsc[0:1, 4:5], lsc[0:1, 2:3], lsc[0:1, 3:4], ALU.subtract),
                  deps=[e_e1], inc=vs)
    e_l = sch.add("vector", lambda v: v.tensor_scalar(lsc[0:1, 4:5], lsc[0:1, 4:5], LAMBDA_INIT, None, ALU.add),
                  deps=[e_d], inc=vs)
    e_or = sch.add("vector", lambda v: v.memset(C["ones_row"][0:1, :], 1.0), inc=vs)
    e_lb = sch.add("tensor", lambda t: t.matmul(ctx.psum_f32[4][:, 0:1], C["ones_row"][0:1, :],
                                                lsc[0:1, 4:5], start=True, stop=True),
                   deps=[e_l, e_or], inc=S["mm"])
    e_lbc = sch.add("vector", lambda v: v.tensor_copy(C["lam_b"][:], ctx.psum_f32[4][:, 0:1]),
                    deps=[e_lb], inc=vs)
    C["evs"] = [e_mc, e_ic, e_hc, e_sh, e_ep, e_lbc]
    ctx.consts = C
    ctx.reset(m_temp)


def emit_transpose_phase(ctx, S, src_dram, dst_dram, R, Cc, a2a_split=False):
    sch = ctx.sch
    m0 = ctx.mark()
    NR, NC = R // 128, Cc // 128
    sbf = [ctx.tile(Cc, BF16) for _ in range(2)]
    obuf = [ctx.tile(R, BF16) for _ in range(NC)]
    x_ld, cast, mm, vs, ss = S["g_ld"], S["cast"], S["mm"], S["vs"], S["ss"]
    st = S["st"]
    prev_cast = {}
    prev_ps = {}
    ev_by_j = {j: [] for j in range(NC)}
    for i in range(NR):
        buf = i % 2
        e_c = sch.add("gpsimd", (lambda i_, bb: lambda g: g.dma_start(
            out=sbf[bb][:], in_=src_dram[i_ * 128:(i_ + 1) * 128, :]))(i, buf),
            deps=[prev_cast.get(("b", buf))], inc=x_ld[buf], inc_n=16)
        prev_cast[buf] = e_c
        for j in range(NC):
            pb = j % 2
            e_t = sch.add("tensor", (lambda bb, j_, pb_: lambda te: te.transpose(
                ctx.psum_bf16[pb_][:], sbf[bb][:, j_ * 128:(j_ + 1) * 128],
                ctx.consts["ident_bf"][:]))(buf, j, pb),
                deps=[e_c, prev_ps.get(pb)] + [ctx.consts["evs"][1]], inc=mm)
            e_e = sch.add("vector", (lambda j_, i_, pb_: lambda v: v.tensor_copy(
                obuf[j_][:, i_ * 128:(i_ + 1) * 128], ctx.psum_bf16[pb_][:]))(j, i, pb),
                deps=[e_t], inc=vs)
            prev_ps[pb] = e_e
            ev_by_j[j].append(e_e)
        prev_cast[("b", buf)] = ev_by_j[NC - 1][-1]
    for j in range(NC):
        sl = j % 4
        if a2a_split:
            H = R // 2
            for bb in range(2):
                sch.add("gpsimd", (lambda j_, b_: lambda g: g.dma_start(
                    out=dst_dram[b_ * Cc + j_ * 128: b_ * Cc + (j_ + 1) * 128, :],
                    in_=obuf[j_][:, b_ * H:(b_ + 1) * H]))(j, bb),
                    deps=ev_by_j[j], inc=st[sl], inc_n=16)
        else:
            sch.add("gpsimd", (lambda j_: lambda g: g.dma_start(
                out=dst_dram[j_ * 128:(j_ + 1) * 128, :], in_=obuf[j_][:]))(j),
                deps=ev_by_j[j], inc=st[sl], inc_n=16)
    ctx.reset(m0)


def emit_collective(ctx, S, kind, in_ap, out_ap, op=None):
    sch = ctx.sch
    op = ALU.bypass if op is None else op
    def f(g):
        with allow_ici_all_to_all():
            return g.collective_compute(
                kind, op,
                replica_groups=[[0, 1], [2, 3], [4, 5], [6, 7]],
                ins=[in_ap.ap().opt()], outs=[out_ap.ap().opt()])
    return sch.add("gpsimd", f, inc=S["cc"], inc_n=1)


def emit_attn_phase(ctx, S, layer, cc_x_out, wq, wk, wv, afm_tiles, x_dep=None,
                    rc_ap=None, rs_ap=None):
    sch = ctx.sch
    m0 = ctx.mark()
    NH = 8 if layer == 1 else 4
    NBR = 1 if layer == 1 else 2
    DV = 128 if layer == 1 else 256
    DVA = DV + 1
    FH = 1024 // NH
    KC = D // 128
    NT = T_SEQ // 128
    NQ5 = T_SEQ // 512
    C = ctx.consts
    x_ldS, w_lds, castS, mmS, vsS, ssS, stS = (S["x_ld"], S["w_ld"], S["cast"],
                                               S["mm"], S["vs"], S["ss"], S["st"])
    sc_ld = S["sc_ld"]

    x_bf = [ctx.tile(T_SEQ, BF16) for _ in range(KC)]
    w_bf = [{nm: ctx.tile(KC * FH, BF16) for nm in ("wq", "wk", "wv")} for _ in range(2)]
    q_bf = [ctx.tile(T_SEQ, BF16) for _ in range(NBR)]
    k_bf = [ctx.tile(T_SEQ, BF16) for _ in range(NBR)]
    v_tm = ctx.tile(NT * DVA, BF16)
    pt = ctx.tile(NT * 512, BF16)
    rec = [ctx.tile(1, F32) for _ in range(NBR)]
    a_f32 = [ctx.tile(DV, F32) for _ in range(2)]
    a_bfs_t = ctx.tile(DV, BF16)
    a_bfs = [a_bfs_t, a_bfs_t]
    if layer == 1:
        rope_sc = [ctx.tile(512, F32) for _ in range(3)]
        cos_l = ctx.tile(2048, F32)
        sin_l = ctx.tile(2048, F32)
    else:
        nlam_rec = ctx.tile(1, F32)
        sq_s = ctx.tile(256, F32)
        o_sb0 = ctx.tile(4 * 256, F32)
        o1_sc = ctx.tile(256, F32)
        cmb = ctx.tile(256, F32)
        rms = {nm: ctx.tile(1, F32) for nm in ("sumsq", "ms", "std", "rstd")}
    ps_proj = [ctx.psum_f32[0], ctx.psum_f32[1]]
    ps_sc = [ctx.psum_f32[2], ctx.psum_f32[3]]
    ps_pv = [ctx.psum_f32[4][:, 0:DVA], ctx.psum_f32[5][:, 0:DVA]]

    x_evs = []
    for k in range(KC):
        buf = k % 2
        dep = [x_evs[k - 2][1]] if k >= 2 else []
        if x_dep is not None:
            dep = dep + [x_dep]
        e0 = sch.add("sync", (lambda kk: lambda s: s.dma_start(
            out=x_bf[kk][:, 0:1024], in_=cc_x_out[kk * 128:(kk + 1) * 128, :]))(k),
            deps=dep, inc=x_ldS[buf], inc_n=16)
        e1 = sch.add("sync", (lambda kk: lambda s: s.dma_start(
            out=x_bf[kk][:, 1024:2048], in_=cc_x_out[2048 + kk * 128: 2048 + (kk + 1) * 128, :]))(k),
            deps=[e0], inc=x_ldS[buf], inc_n=16)
        x_evs.append((e0, e1))
    x_done = [(x_ldS[0], x_ldS[0].count), (x_ldS[1], x_ldS[1].count)]

    st_prev = [None] * 4
    st_ctr = [0]
    def next_store_slot():
        i = st_ctr[0] % 4
        st_ctr[0] += 1
        return i

    prev_tp = {}
    prev_abfs = {}
    NDS = DV // 128
    def emit_a_out(e_src, ab, h, tq, qs):
        e_c = sch.add("vector", (lambda ab_: lambda v: v.tensor_copy(a_bfs[ab_][:], a_f32[ab_][:]))(ab),
                      deps=[e_src, prev_abfs.get(0), prev_abfs.get(1)], inc=vsS)
        last = None
        for dsub in range(NDS):
            tb = (st_ctr[0] + dsub) % 2
            fc = h * NDS + dsub
            qcol = (tq * 4 + qs) * 128
            e_t = sch.add("tensor", (lambda ab_, d_, tb_: lambda te: te.transpose(
                ctx.psum_bf16[tb_][:], a_bfs[ab_][:, d_ * 128:(d_ + 1) * 128],
                C["ident_bf"][:]))(ab, dsub, tb),
                deps=[e_c, prev_tp.get(tb), C["evs"][1]], inc=mmS)
            e_e = sch.add("vector", (lambda fc_, qc_, tb_: lambda v: v.tensor_copy(
                afm_tiles[fc_][:, qc_:qc_ + 128], ctx.psum_bf16[tb_][:]))(fc, qcol, tb),
                deps=[e_t], inc=vsS)
            prev_tp[tb] = e_e
            last = e_e
        st_ctr[0] += NDS
        prev_abfs[ab] = last
        return last

    prev_w_use = {}
    prev_head_end = [None]
    w_ev_by_head = {}
    prev_pproj = {}
    prev_psc = {}
    prev_ppv = {}
    prev_af = {}
    prev_rope = [None]
    prev_rms = [None]
    prev_osb0 = {}
    prev_o1 = [None]
    prev_cmb = [None]
    prev_rec = {}
    prev_nlam = [None]
    e_mc, e_ic, e_hc, e_sh, e_ep, e_lbc = C["evs"]
    lam_evs = [e_lbc]
    eps_t = C["eps"]; shift_t = C["shift"]
    mask_bf = C["mask_bf"]; ident_bf = C["ident_bf"]
    subln_t = C["subln"]; lam_b = C["lam_b"]
    subln_ev = C["loaded"]
    rope_evs = []
    if layer == 1:
        cos_t, sin_t = cos_l, sin_l
        e_rc = sch.add("sync", lambda s: s.dma_start(out=cos_t[:], in_=rc_ap), inc=S["r_ld"][0], inc_n=16)
        e_rs = sch.add("sync", lambda s: s.dma_start(out=sin_t[:], in_=rs_ap), deps=[e_rc], inc=S["r_ld"][1], inc_n=16)
        rope_evs = [e_rc, e_rs]
    else:
        cos_t = sin_t = None

    def load_head_weights(h):
        hb = h % 2
        evs = {}
        for wi, (nm, wap) in enumerate((("wq", wq), ("wk", wk), ("wv", wv))):
            e_lw = sch.add("gpsimd", (lambda w_, c0, c1, nm_, hb_: lambda g: g.dma_start(
                out=w_bf[hb_][nm_][:].rearrange("p (c f) -> p c f", c=KC),
                in_=w_[:, c0:c1].rearrange("(c p) f -> p c f", p=128)))(
                    wap, h * FH, (h + 1) * FH, nm, hb),
                deps=[prev_w_use.get((nm, hb))], inc=w_lds[wi + 3 * hb], inc_n=16)
            evs[nm] = e_lw
        w_ev_by_head[h] = evs

    load_head_weights(0)
    if NH > 1:
        load_head_weights(1)

    for h in range(NH):
        hb = h % 2
        w_evs = w_ev_by_head[h]

        qk_done = []
        pcnt = 0
        for nm, dest in (("wq", q_bf), ("wk", k_bf)):
            for br in range(NBR):
                for tsl in range(4):
                    pp = pcnt % 2
                    pcnt += 1
                    def proj(nm_, br_, tsl_, pp_, hb_=hb):
                        def f(te):
                            ins = None
                            for kc in range(KC):
                                ins = te.matmul(ps_proj[pp_][:],
                                                w_bf[hb_][nm_][:, kc * FH + br_ * 128: kc * FH + br_ * 128 + 128],
                                                x_bf[kc][:, tsl_ * 512:(tsl_ + 1) * 512],
                                                start=(kc == 0), stop=(kc == KC - 1))
                            return ins
                        return f
                    e_mm = sch.add("tensor", proj(nm, br, tsl, pp),
                                   deps=[w_evs[nm], prev_pproj.get(pp)] + x_done, inc=mmS)
                    if layer == 1:
                        e_r1 = sch.add("vector", (lambda pp_: lambda v: v.tensor_scalar(
                            rope_sc[0][0:64, :], ps_proj[pp_][64:128, :], -1.0, None, ALU.mult))(pp),
                            deps=[e_mm, prev_rope[0]], inc=vsS)
                        e_r2 = sch.add("vector", (lambda pp_: lambda v: v.tensor_copy(
                            rope_sc[0][64:128, :], ps_proj[pp_][0:64, :]))(pp),
                            deps=[e_mm, e_r1], inc=vsS)
                        e_r3 = sch.add("vector", (lambda pp_, t_: lambda v: v.tensor_tensor(
                            rope_sc[1][:], ps_proj[pp_][:], cos_t[:, t_ * 512:(t_ + 1) * 512], ALU.mult))(pp, tsl),
                            deps=[e_mm, e_r1] + rope_evs, inc=vsS)
                        e_r4 = sch.add("vector", (lambda t_: lambda v: v.tensor_tensor(
                            rope_sc[2][:], rope_sc[0][:], sin_t[:, t_ * 512:(t_ + 1) * 512], ALU.mult))(tsl),
                            deps=[e_r2, e_r3] + rope_evs, inc=vsS)
                        e_evq = sch.add("vector", (lambda d_, t_: lambda v: v.tensor_tensor(
                            d_[:, t_ * 512:(t_ + 1) * 512], rope_sc[1][:], rope_sc[2][:], ALU.add))(dest[br], tsl),
                            deps=[e_r4, prev_head_end[0]], inc=vsS)
                        prev_rope[0] = e_evq
                    else:
                        e_evq = sch.add("scalar", (lambda d_, t_, pp_: lambda s: s.copy(
                            d_[:, t_ * 512:(t_ + 1) * 512], ps_proj[pp_][:]))(dest[br], tsl, pp),
                            deps=[e_mm, prev_head_end[0]], inc=ssS)
                    prev_pproj[pp] = e_evq
                    qk_done.append(e_evq)
        qk_all = qk_done[-1]

        e_vms = sch.add("vector", lambda v: v.memset(v_tm[:], 1.0),
                        deps=[prev_head_end[0]], inc=vsS)
        v_done = []
        for kt in range(NT):
            pp = pcnt % 2
            pcnt += 1
            def vproj(kt_, pp_, hb_=hb):
                def f(te):
                    ins = None
                    for kc in range(KC):
                        ins = te.matmul(ps_proj[pp_][:, 0:DV],
                                        x_bf[kc][:, kt_ * 128:(kt_ + 1) * 128],
                                        w_bf[hb_]["wv"][:, kc * FH: kc * FH + DV],
                                        start=(kc == 0), stop=(kc == KC - 1))
                    return ins
                return f
            e_vm = sch.add("tensor", vproj(kt, pp),
                           deps=[w_evs["wv"], prev_pproj.get(pp)] + x_done, inc=mmS)
            e_ve = sch.add("scalar", (lambda kt_, pp_: lambda s: s.copy(
                v_tm[:, kt_ * DVA: kt_ * DVA + DV], ps_proj[pp_][:, 0:DV]))(kt, pp),
                deps=[e_vm, e_vms], inc=ssS)
            prev_pproj[pp] = e_ve
            v_done.append(e_ve)
        v_all = v_done[-1]

        head_last = None
        for tq in range(NQ5):
            nch = 4 * tq + 4
            osb0_evs = {}
            for br in range(NBR):
                pt_evs = []
                for j in range(nch):
                    pb = j % 2
                    def sct(j_, tq_, br_, pb_):
                        def f(te):
                            return te.matmul(ps_sc[pb_][:],
                                             k_bf[br_][:, j_ * 128:(j_ + 1) * 128],
                                             q_bf[br_][:, tq_ * 512:(tq_ + 1) * 512],
                                             start=True, stop=(j_ < 4 * tq_))
                        return f
                    e_s = sch.add("tensor", sct(j, tq, br, pb),
                                  deps=[qk_all, prev_psc.get(pb)], inc=mmS)
                    if j >= 4 * tq:
                        r_i = j - 4 * tq
                        def mskmm(r_, pb_):
                            def f(te):
                                return te.matmul(ps_sc[pb_][:], ident_bf[:],
                                                 mask_bf[:, r_ * 512:(r_ + 1) * 512],
                                                 start=False, stop=True)
                            return f
                        e_s = sch.add("tensor", mskmm(r_i, pb), deps=[e_s, e_mc, e_ic], inc=mmS)
                    e_x = sch.add("scalar", (lambda j_, pb_: lambda s: s.activation(
                        pt[:, j_ * 512:(j_ + 1) * 512], ps_sc[pb_][:], AF.Exp,
                        bias=shift_t[:], scale=SM_SCALE))(j, pb),
                        deps=[e_s, e_sh], inc=ssS)
                    prev_psc[pb] = e_x
                    pt_evs.append(e_x)
                pt_all = pt_evs[-1]

                for qs in range(4):
                    pb = qs % 2
                    def pvg(qs_, ncl_, pb_):
                        def f(te):
                            ins = None
                            for j in range(ncl_):
                                ins = te.matmul(ps_pv[pb_][:],
                                                pt[:, j * 512 + qs_ * 128: j * 512 + qs_ * 128 + 128],
                                                v_tm[:, j * DVA:(j + 1) * DVA],
                                                start=(j == 0), stop=(j == ncl_ - 1))
                            return ins
                        return f
                    e_pv = sch.add("tensor", pvg(qs, nch, pb),
                                   deps=[pt_all, v_all, prev_ppv.get(pb), e_lbc], inc=mmS)
                    e_rc2 = sch.add("vector", (lambda pb_, br_: lambda v: v.reciprocal(
                        rec[br_][:], ps_pv[pb_][:, DV:DVA]))(pb, br),
                        deps=[e_pv, prev_rec.get(br)], inc=vsS)
                    if layer == 1:
                        ab = st_ctr[0] % 2
                        e_n = sch.add("vector", (lambda pb_, ab_: lambda v: v.tensor_scalar(
                            a_f32[ab_][:, 0:DV], ps_pv[pb_][:, 0:DV], rec[0][:], None, ALU.mult))(pb, ab),
                            deps=[e_rc2, prev_af.get(ab)], inc=vsS)
                        e_st = emit_a_out(e_n, ab, h, tq, qs)
                        prev_af[ab] = e_st
                        prev_rec[0] = e_n
                        prev_ppv[pb] = e_n
                        head_last = e_st
                    elif br == 0:
                        e_n = sch.add("vector", (lambda pb_, qs_: lambda v: v.tensor_scalar(
                            o_sb0[:, qs_ * DV:(qs_ + 1) * DV], ps_pv[pb_][:, 0:DV], rec[0][:], None, ALU.mult))(pb, qs),
                            deps=[e_rc2, prev_osb0.get(qs)], inc=vsS)
                        osb0_evs[qs] = e_n
                        prev_rec[0] = e_n
                        prev_ppv[pb] = e_n
                    else:
                        e_c1 = sch.add("vector", lambda v: v.tensor_tensor(
                            nlam_rec[:], rec[1][:], lam_b[:], ALU.mult),
                            deps=[e_rc2, prev_nlam[0]] + lam_evs, inc=vsS)
                        e_c2 = sch.add("vector", (lambda pb_: lambda v: v.tensor_scalar(
                            o1_sc[:], ps_pv[pb_][:, 0:DV], nlam_rec[:], None, ALU.mult))(pb),
                            deps=[e_c1, prev_o1[0]], inc=vsS)
                        prev_rec[1] = e_c2
                        prev_ppv[pb] = e_c2
                        prev_nlam[0] = e_c2
                        e_c3 = sch.add("vector", (lambda qs_: lambda v: v.tensor_tensor(
                            cmb[:], o_sb0[:, qs_ * 256:(qs_ + 1) * 256], o1_sc[:], ALU.subtract))(qs),
                            deps=[e_c2, osb0_evs[qs], prev_cmb[0]], inc=vsS)
                        prev_osb0[qs] = e_c3
                        prev_o1[0] = e_c3
                        e_q1 = sch.add("scalar", lambda s: s.activation(
                            sq_s[:], cmb[:], AF.Square, accum_out=rms["sumsq"][:]),
                            deps=[e_c3, prev_rms[0]], inc=ssS)
                        e_q2 = sch.add("vector", lambda v: v.tensor_scalar(
                            rms["ms"][:], rms["sumsq"][:], 1.0 / DV, None, ALU.mult),
                            deps=[e_q1], inc=vsS)
                        e_q3 = sch.add("scalar", lambda s: s.activation(
                            rms["std"][:], rms["ms"][:], AF.Sqrt, bias=eps_t[:]),
                            deps=[e_q2, e_ep], inc=ssS)
                        e_q4 = sch.add("vector", lambda v: v.reciprocal(rms["rstd"][:], rms["std"][:]),
                                       deps=[e_q3], inc=vsS)
                        e_q5 = sch.add("vector", lambda v: v.tensor_scalar(
                            rms["rstd"][:], rms["rstd"][:], 1.0 - LAMBDA_INIT, None, ALU.mult),
                            deps=[e_q4], inc=vsS)
                        ab = st_ctr[0] % 2
                        e_q6 = sch.add("vector", (lambda ab_: lambda v: v.tensor_scalar(
                            a_f32[ab_][:], cmb[:], rms["rstd"][:], None, ALU.mult))(ab),
                            deps=[e_q5, e_c3, prev_af.get(ab)], inc=vsS)
                        e_q7 = sch.add("vector", (lambda ab_: lambda v: v.tensor_tensor(
                            a_f32[ab_][:], a_f32[ab_][:], subln_t[:], ALU.mult))(ab),
                            deps=[e_q6, subln_ev], inc=vsS)
                        prev_rms[0] = e_q7
                        prev_cmb[0] = e_q7
                        e_st = emit_a_out(e_q7, ab, h, tq, qs)
                        prev_af[ab] = e_st
                        head_last = e_st

        prev_head_end[0] = head_last
        for nm in ("wq", "wk", "wv"):
            prev_w_use[(nm, hb)] = head_last
        if h + 2 < NH:
            load_head_weights(h + 2)
    ctx.reset(m0)


def emit_pmm_phase(ctx, S, kch, a_src, w_src, out_dram, a_is_bf16=True,
                   p_src=None, resid_src=None, colscale_src=None, scale64=False,
                   finalize=False):
    sch = ctx.sch
    m0 = ctx.mark()
    TT = TOK_CORE // 128
    OC = D // 512
    NB = 1 if kch > 16 else 2
    a_ld, w_ld, r_ld, p_ld = S["x_ld"], S["w_ld"], S["r_ld"], S["p_ld"]
    c_ld, cast, mm, vs, ss = S["c_ld"], S["cast"], S["mm"], S["vs"], S["ss"]
    st_sems = S["st"]
    a_bf = [ctx.tile(TOK_CORE, BF16) for _ in range(kch)]
    w_bf = [ctx.tile(D, BF16) for _ in range(kch)]
    a_f32 = None if a_is_bf16 else [ctx.tile(TOK_CORE, F32) for _ in range(NB)]
    r_f32 = [ctx.tile(D, F32) for _ in range(NB)] if finalize else None
    p_f32 = [ctx.tile(D, F32) for _ in range(NB)] if p_src is not None else None
    cs_t = ctx.tile(D, F32) if colscale_src is not None else None
    y = [ctx.tile(D, F32) for _ in range(NB)]
    o_t = y
    stats = None
    if finalize:
        stats = {nm: ctx.tile(1, F32) for nm in ("sum1", "sumsq", "mean", "var", "m2", "std", "rstd", "nmr")}
        stats["sq_scratch"] = ctx.tile(D, F32)
    eps_t = ctx.consts["eps"]
    ps = [ctx.psum_f32[i] for i in range(4)]

    cs_ev = None
    if colscale_src is not None:
        e_cl = sch.add("sync", lambda s: s.dma_start(out=cs_t[:], in_=colscale_src), inc=c_ld, inc_n=16)
        if scale64:
            cs_ev = sch.add("vector", lambda v: v.tensor_scalar(cs_t[:], cs_t[:], 64.0, None, ALU.mult),
                            deps=[e_cl], inc=vs)
        else:
            cs_ev = e_cl

    cast_evs = []
    for k in range(kch):
        buf = k % NB
        abuf = k % 4
        dep_a = [cast_evs[k - 4][0]] if k >= 4 else []
        dep_w = [cast_evs[k - 4][1]] if k >= 4 else []
        if a_is_bf16:
            e_ca = sch.add("sync", (lambda kk: lambda s: s.dma_start(
                out=a_bf[kk][:], in_=a_src(kk)))(k),
                deps=dep_a, inc=a_ld[abuf], inc_n=16)
        else:
            e_la = sch.add("sync", (lambda kk, bb: lambda s: s.dma_start(
                out=a_f32[bb][:], in_=a_src(kk)))(k, buf),
                deps=[cast_evs[k - NB][0]] if k >= NB else [], inc=a_ld[buf], inc_n=16)
            e_ca = sch.add("vector", (lambda kk, bb: lambda v: v.tensor_copy(a_bf[kk][:], a_f32[bb][:]))(k, buf),
                           deps=[e_la], inc=cast)
        e_cw = sch.add("gpsimd", (lambda kk: lambda g: g.dma_start(
            out=w_bf[kk][:], in_=w_src(kk)))(k),
            deps=dep_w, inc=w_ld[k % 2], inc_n=16)
        cast_evs.append((e_ca, e_cw))
    kh = kch // 2
    loads_h1 = [e for pair in cast_evs[:kh] for e in pair if e is not None]
    loads_done = [cast_evs[-1][0], cast_evs[-1][1], (cast, cast.count)] + \
                 [(a_ld[i], a_ld[i].count) for i in range(4)] + \
                 [(w_ld[0], w_ld[0].count), (w_ld[1], w_ld[1].count)]

    prev_ev = {}
    prev_r = {}
    prev_p = {}
    prev_out = {}
    prev_ln_out = None
    for t in range(TT):
        buf = t % NB
        e_lr = None
        e_lp = None
        if finalize:
            e_lr = sch.add("sync", (lambda tt, bb: lambda s: s.dma_start(
                out=r_f32[bb][:], in_=resid_src[tt * 128:(tt + 1) * 128, :]))(t, buf),
                deps=[prev_r.get(buf)], inc=r_ld[buf % 2], inc_n=16)
        if p_src is not None:
            e_lp = sch.add("sync", (lambda tt, bb: lambda s: s.dma_start(
                out=p_f32[bb][:], in_=p_src[tt * 128:(tt + 1) * 128, :]))(t, buf),
                deps=[prev_p.get(buf)], inc=p_ld[buf % 2], inc_n=16)
        mm_evs = []
        for o in range(OC):
            def mmgroup(tt, oo, k0, k1):
                def f(te):
                    ins = None
                    for k in range(k0, k1):
                        ins = te.matmul(ps[oo][:],
                                        a_bf[k][:, tt * 128:(tt + 1) * 128],
                                        w_bf[k][:, oo * 512:(oo + 1) * 512],
                                        start=(k == 0), stop=(k == kch - 1))
                    return ins
                return f
            e_mm = sch.add("tensor", mmgroup(t, o, 0, kch),
                           deps=loads_done + [prev_ev.get(o)], inc=mm)
            mm_evs.append(e_mm)
        ev_evs = []
        for o in range(OC):
            sl = slice(o * 512, (o + 1) * 512)
            if p_src is not None:
                e_a = sch.add("vector", (lambda oo, bb, sl_: lambda v: v.tensor_tensor(
                    y[bb][:, sl_], ps[oo][:], p_f32[bb][:, sl_], ALU.add))(o, buf, sl),
                    deps=[mm_evs[o], e_lp, prev_ln_out, prev_out.get(buf)], inc=vs)
            else:
                e_a = sch.add("vector", (lambda oo, bb, sl_: lambda v: v.tensor_copy(
                    y[bb][:, sl_], ps[oo][:]))(o, buf, sl),
                    deps=[mm_evs[o], prev_ln_out, prev_out.get(buf)], inc=vs)
            step = e_a
            if finalize:
                if colscale_src is not None:
                    step = sch.add("vector", (lambda oo, bb, sl_: lambda v: v.tensor_tensor(
                        y[bb][:, sl_], y[bb][:, sl_], cs_t[:, sl_], ALU.mult))(o, buf, sl),
                        deps=[step, cs_ev], inc=vs)
                step = sch.add("vector", (lambda oo, bb, sl_: lambda v: v.tensor_tensor(
                    y[bb][:, sl_], y[bb][:, sl_], r_f32[bb][:, sl_], ALU.add))(o, buf, sl),
                    deps=[step, e_lr], inc=vs)
            prev_ev[o] = e_a
            ev_evs.append(step)
        if finalize:
            e_out = emit_layernorm(sch, y[buf], o_t[buf], stats, eps_t, D,
                                   deps=ev_evs + [ctx.consts["evs"][4]], v_sem=vs, s_sem=ss,
                                   out_free_dep=prev_out.get(buf))
            prev_ln_out = e_out
            src_tile = o_t[buf]
        else:
            e_out = ev_evs[-1]
            src_tile = y[buf]
        e_st = sch.add("gpsimd", (lambda tt, s_: lambda g: g.dma_start(
            out=out_dram[tt * 128:(tt + 1) * 128, :], in_=s_[:]))(t, src_tile),
            deps=ev_evs + [e_out], inc=st_sems[buf % 2], inc_n=16)
        prev_r[buf] = ev_evs[-1]
        prev_p[buf] = ev_evs[-1]
        prev_out[buf] = e_st
    ctx.reset(m0)


def emit_fwht_phase(ctx, S, x_src, su_src, t_out, h2048):
    sch = ctx.sch
    m0 = ctx.mark()
    KC = D // 128
    x_ld, w_ld, mm, vs, ss = S["x_ld"], S["w_ld"], S["mm"], S["vs"], S["ss"]
    sc_ld, st_sems = S["sc_ld"], S["st"]
    xs_bf = [ctx.tile(TOK_CORE, BF16) for _ in range(KC)]
    h_bf = [ctx.tile(D, BF16) for _ in range(KC)]
    su_t = ctx.tile(KC, F32)
    to_t = [ctx.tile(TOK_CORE, BF16) for _ in range(2)]
    ps2 = [ctx.psum_f32[0], ctx.psum_f32[1]]

    for k in range(KC):
        sch.add("sync", (lambda kk: lambda s: s.dma_start(
            out=su_t[:, kk:kk + 1], in_=su_src[kk * 128:(kk + 1) * 128, :]))(k),
            inc=sc_ld[0], inc_n=16)
    e_su = (sc_ld[0], sc_ld[0].count)

    xev = []
    hev = []
    for k in range(KC):
        buf = k % 2
        e_lx = sch.add("sync", (lambda kk: lambda s: s.dma_start(
            out=xs_bf[kk][:], in_=x_src[kk * 128:(kk + 1) * 128, :]))(k),
            deps=[xev[k - 2]] if k >= 2 else [], inc=x_ld[buf], inc_n=16)
        e_ps = sch.add("vector", (lambda kk: lambda v: v.tensor_scalar(
            xs_bf[kk][:], xs_bf[kk][:], su_t[:, kk:kk + 1], None, ALU.mult))(k),
            deps=[e_lx, e_su], inc=vs)
        xev.append(e_ps)
        e_lh = sch.add("gpsimd", (lambda kk: lambda g: g.dma_start(
            out=h_bf[kk][:], in_=h2048[kk * 128:(kk + 1) * 128, :]))(k),
            deps=[hev[k - 2]] if k >= 2 else [], inc=w_ld[k % 2], inc_n=16)
        hev.append(e_lh)
    loads = [xev[-1], hev[-1], (vs, vs.count),
             (x_ld[0], x_ld[0].count), (x_ld[1], x_ld[1].count),
             (w_ld[0], w_ld[0].count), (w_ld[1], w_ld[1].count)]

    st_prev = [None, None]
    prev_ps = {}
    for fo in range(KC):
        sl2 = fo % 2
        ev_th = []
        for th in range(2):
            pp = th
            def mmg(fo_, th_, pp_):
                def f(te):
                    ins = None
                    for fi in range(KC):
                        ins = te.matmul(ps2[pp_][:],
                                        h_bf[fi][:, fo_ * 128:(fo_ + 1) * 128],
                                        xs_bf[fi][:, th_ * 512:(th_ + 1) * 512],
                                        start=(fi == 0), stop=(fi == KC - 1))
                    return ins
                return f
            e_m = sch.add("tensor", mmg(fo, th, pp), deps=loads + [prev_ps.get(pp)], inc=mm)
            e_e = sch.add("scalar", (lambda th_, pp_, sl_: lambda s: s.activation(
                to_t[sl_][:, th_ * 512:(th_ + 1) * 512], ps2[pp_][:], AF.Copy,
                scale=FWHT_SCALE_2048))(th, pp, sl2),
                deps=[e_m, st_prev[sl2]], inc=ss)
            prev_ps[pp] = e_e
            ev_th.append(e_e)
        e_st = sch.add("gpsimd", (lambda fo_, sl_: lambda g: g.dma_start(
            out=t_out[fo_ * 128:(fo_ + 1) * 128, :], in_=to_t[sl_][:]))(fo, sl2),
            deps=ev_th, inc=st_sems[sl2], inc_n=16)
        st_prev[sl2] = e_st
    ctx.reset(m0)


def emit_ffn_up_phase(ctx, S, layer, x_src, gw, uw, gs, us, su, h_out):
    sch = ctx.sch
    m0 = ctx.mark()
    NI = (INTER1 if layer == 1 else INTER0) // 128
    KC = D // 128
    x_ld, gw_ld, uw_ld, sc_ld = S["x_ld"], S["w_ld"][0:2], S["w_ld"][2:4], S["sc_ld"]
    c_ld, cast, mm, vs, ss, st_sems = S["c_ld"], S["cast"], S["mm"], S["vs"], S["ss"], S["st"]
    x_bf = [ctx.tile(TOK_CORE, BF16) for _ in range(KC)]
    gw_bf = [ctx.tile(2048, BF16) for _ in range(2)]
    uw_bf = [ctx.tile(2048, BF16) for _ in range(2)]
    scs = [ctx.tile(4, F32) for _ in range(2)]
    sg_t = [ctx.tile(512, F32) for _ in range(2)]
    if layer == 1:
        hbuf = [ctx.tile(TOK_CORE, BF16) for _ in range(NI)]
        btmp = ctx.tile(TOK_CORE, BF16)
        ho_t = [ctx.tile(512, BF16) for _ in range(4)]
    else:
        ho_t = [ctx.tile(TOK_CORE, BF16) for _ in range(4)]
    ps_g = [ctx.psum_f32[0], ctx.psum_f32[1]]
    ps_u = [ctx.psum_f32[2], ctx.psum_f32[3]]
    h128_b = ctx.consts["h128_bf"]
    h_evs = [ctx.consts["evs"][2]]

    x_cast_evs = []
    for k in range(KC):
        buf = k % 2
        e_cx = sch.add("sync", (lambda kk: lambda s: s.dma_start(
            out=x_bf[kk][:], in_=x_src[kk * 128:(kk + 1) * 128, :]))(k),
            deps=[x_cast_evs[k - 2]] if k >= 2 else [], inc=x_ld[buf], inc_n=16)
        x_cast_evs.append(e_cx)
    x_done = [(x_ld[0], x_ld[0].count), (x_ld[1], x_ld[1].count)]

    st_prev = [None] * 4
    prev_wcast = {}
    prev_wuse = {}
    prev_scuse = {}
    prev_sg = {}
    prev_psg = {}
    prev_psu = {}
    prev_hbuf = {}
    hbuf_evs = []
    tile_ev = {}
    prev_btmp = [None]

    for i in range(NI):
        wb = i % 2
        e_cg = sch.add("gpsimd", (lambda i_, bb: lambda g: g.dma_start(
            out=gw_bf[bb][:].rearrange("p (c f) -> p c f", c=KC),
            in_=gw[:, i_ * 128:(i_ + 1) * 128].rearrange("(c p) f -> p c f", p=128)))(i, wb),
            deps=[prev_wuse.get(("g", wb))], inc=gw_ld[0], inc_n=16)
        e_cu = sch.add("gpsimd", (lambda i_, bb: lambda g: g.dma_start(
            out=uw_bf[bb][:].rearrange("p (c f) -> p c f", c=KC),
            in_=uw[:, i_ * 128:(i_ + 1) * 128].rearrange("(c p) f -> p c f", p=128)))(i, wb),
            deps=[prev_wuse.get(("u", wb)), e_cg], inc=uw_ld[0], inc_n=16)
        wb2 = i % 2
        e_ls = sch.add("sync", (lambda i_, bb: lambda s: s.dma_start(
            out=scs[bb][:, 0:1], in_=gs[i_ * 128:(i_ + 1) * 128, :]))(i, wb2),
            deps=[prev_scuse.get(wb2)], inc=sc_ld[wb2], inc_n=16)
        e_ls2 = sch.add("sync", (lambda i_, bb: lambda s: s.dma_start(
            out=scs[bb][:, 1:2], in_=us[i_ * 128:(i_ + 1) * 128, :]))(i, wb2),
            deps=[e_ls], inc=sc_ld[wb2], inc_n=16)
        if layer == 1:
            e_ls3 = sch.add("sync", (lambda i_, bb: lambda s: s.dma_start(
                out=scs[bb][:, 2:3], in_=su[i_ * 128:(i_ + 1) * 128, :]))(i, wb2),
                deps=[e_ls2], inc=sc_ld[wb2], inc_n=16)
            e_sc1 = sch.add("vector", (lambda bb: lambda v: v.tensor_scalar(
                scs[bb][:, 3:4], scs[bb][:, 1:2], 64.0, None, ALU.mult))(wb2),
                deps=[e_ls3], inc=vs)
            e_sc2 = sch.add("vector", (lambda bb: lambda v: v.tensor_tensor(
                scs[bb][:, 3:4], scs[bb][:, 3:4], scs[bb][:, 2:3], ALU.mult))(wb2),
                deps=[e_sc1], inc=vs)
            e_sc3 = sch.add("vector", (lambda bb: lambda v: v.tensor_scalar(
                scs[bb][:, 0:1], scs[bb][:, 0:1], 64.0, None, ALU.mult))(wb2),
                deps=[e_sc2], inc=vs)
            e_scales = e_sc3
            comb_col = 3
        else:
            e_scales = e_ls2
            comb_col = 1
        for th in range(2):
            tsl = slice(th * 512, (th + 1) * 512)
            pp = th
            def gmm(th_, bb, pp_, which):
                wsrc = gw_bf if which == "g" else uw_bf
                pdst = ps_g if which == "g" else ps_u
                def f(te):
                    ins = None
                    for kc in range(KC):
                        ins = te.matmul(pdst[pp_][:],
                                        wsrc[bb][:, kc * 128:(kc + 1) * 128],
                                        x_bf[kc][:, th_ * 512:(th_ + 1) * 512],
                                        start=(kc == 0), stop=(kc == KC - 1))
                    return ins
                return f
            e_gm = sch.add("tensor", gmm(th, wb, pp, "g"),
                           deps=x_done + [e_cg, prev_psg.get(pp)], inc=mm)
            e_um = sch.add("tensor", gmm(th, wb, pp, "u"),
                           deps=x_done + [e_cu, prev_psu.get(pp)], inc=mm)
            e_sg = sch.add("scalar", (lambda bb, pp_, sgb: lambda s: s.activation(
                sg_t[sgb][:], ps_g[pp_][:], AF.Silu, scale=scs[bb][:, 0:1]))(wb2, pp, th),
                deps=[e_gm, e_scales, prev_sg.get(th)], inc=ss)
            prev_psg[pp] = e_sg
            if layer == 1:
                dst = hbuf[i]
                ddep = prev_hbuf.get(i)
            else:
                dst = ho_t[2 * (i % 2) + th]
                ddep = st_prev[2 * (i % 2) + th]
            e_h2 = sch.add("vector", (lambda bb, pp_, sgb, d_, t2: lambda v: v.scalar_tensor_tensor(
                d_[:, t2] if layer == 1 else d_[:, 0:512],
                ps_u[pp_][:], scs[bb][:, comb_col:comb_col + 1], sg_t[sgb][:],
                ALU.mult, ALU.mult))(wb2, pp, th, dst, tsl),
                deps=[e_um, e_sg, e_scales, ddep], inc=vs)
            prev_psu[pp] = e_h2
            prev_sg[th] = e_h2
            if layer == 1:
                prev_hbuf[i] = e_h2
                hbuf_evs.append(e_h2)
            prev_wuse[("g", wb)] = e_sg
            prev_wuse[("u", wb)] = e_h2
            prev_scuse[wb2] = e_h2
            if layer == 0:
                sl4 = 2 * (i % 2) + th
                e_st = sch.add("gpsimd", (lambda i_, th_, d_: lambda g: g.dma_start(
                    out=h_out[i_ * 128:(i_ + 1) * 128, th_ * 512:(th_ + 1) * 512],
                    in_=d_[:, 0:512]))(i, th, dst),
                    deps=[e_h2, st_prev[sl4]], inc=st_sems[sl4], inc_n=16)
                st_prev[sl4] = e_st
        if layer == 1:
            tile_ev[i] = hbuf_evs[2 * i + 1]
            g2 = i
            for s in range(6):
                span = 1 << (s + 1)
                if (g2 + 1) % span == 0:
                    base = g2 + 1 - span
                    for c in range(base, base + (span // 2)):
                        a, b = c, c + (span // 2)
                        e1 = sch.add("vector", (lambda a_, b_: lambda v: v.tensor_tensor(
                            btmp[:], hbuf[a_][:], hbuf[b_][:], ALU.subtract))(a, b),
                            deps=[tile_ev[a], tile_ev[b], prev_btmp[0]], inc=vs)
                        e2 = sch.add("vector", (lambda a_, b_: lambda v: v.tensor_tensor(
                            hbuf[a_][:], hbuf[a_][:], hbuf[b_][:], ALU.add))(a, b),
                            deps=[e1], inc=vs)
                        e3 = sch.add("vector", (lambda b_: lambda v: v.tensor_copy(hbuf[b_][:], btmp[:]))(b),
                            deps=[e2], inc=vs)
                        tile_ev[a] = e2
                        tile_ev[b] = e3
                        prev_btmp[0] = e3

    if layer == 1:
        prev_ps = {}
        for c in range(NI):
            for th in range(2):
                pp = th
                e_m = sch.add("tensor", (lambda c_, th_, pp_: lambda te: te.matmul(
                    ps_g[pp_][:], h128_b[:], hbuf[c_][:, th_ * 512:(th_ + 1) * 512],
                    start=True, stop=True))(c, th, pp),
                    deps=[tile_ev[c]] + h_evs + [prev_ps.get(pp)], inc=mm)
                sl4 = 2 * (c % 2) + th
                e_e = sch.add("scalar", (lambda pp_, sl_: lambda s: s.activation(
                    ho_t[sl_][:], ps_g[pp_][:], AF.Copy, scale=FWHT_SCALE_8192))(pp, sl4),
                    deps=[e_m, st_prev[sl4]], inc=ss)
                prev_ps[pp] = e_e
                e_st = sch.add("gpsimd", (lambda c_, th_, sl_: lambda g: g.dma_start(
                    out=h_out[c_ * 128:(c_ + 1) * 128, th_ * 512:(th_ + 1) * 512],
                    in_=ho_t[sl_][:]))(c, th, sl4),
                    deps=[e_e], inc=st_sems[sl4], inc_n=16)
                st_prev[sl4] = e_st
    ctx.reset(m0)



def emit_wopart_phase(ctx, S, a_tiles, wo_my, part_out_d):
    sch = ctx.sch
    m0 = ctx.mark()
    a_ld, w_ld, cast, mm, vs, ss, st_sems = (S["x_ld"], S["w_ld"], S["cast"],
                                             S["mm"], S["vs"], S["ss"], S["st"])
    a_bf = a_tiles
    w_bf = [ctx.tile(D, BF16) for _ in range(8)]
    ctile = [ctx.tile(D, BF16) for _ in range(2)]
    ps = [ctx.psum_f32[i] for i in range(4)]
    cast_evs = []
    for k in range(8):
        buf = k % 2
        e_cw = sch.add("gpsimd", (lambda kk: lambda g: g.dma_start(
            out=w_bf[kk][:], in_=wo_my[kk * 128:(kk + 1) * 128, :]))(k),
            deps=[cast_evs[k - 2] if k >= 2 else None], inc=w_ld[buf], inc_n=16)
        cast_evs.append(e_cw)
    loads = [cast_evs[-1],
             (w_ld[0], w_ld[0].count), (w_ld[1], w_ld[1].count)]
    prev_ev = {}
    prev_ct = {}
    for tt in range(16):
        cb = tt % 2
        ev_o = []
        for o in range(4):
            def mmg(tt_, oo):
                def f(te):
                    ins = None
                    for k in range(8):
                        ins = te.matmul(ps[oo][:],
                                        a_bf[k][:, tt_ * 128:(tt_ + 1) * 128],
                                        w_bf[k][:, oo * 512:(oo + 1) * 512],
                                        start=(k == 0), stop=(k == 7))
                    return ins
                return f
            e_mm = sch.add("tensor", mmg(tt, o), deps=loads + [prev_ev.get(o)], inc=mm)
            e_ev = sch.add("scalar", (lambda oo, cb_: lambda s: s.copy(
                ctile[cb_][:, oo * 512:(oo + 1) * 512], ps[oo][:]))(o, cb),
                deps=[e_mm, prev_ct.get(cb)], inc=ss)
            prev_ev[o] = e_ev
            ev_o.append(e_ev)
        e_st = sch.add("gpsimd", (lambda tt_, cb_: lambda g: g.dma_start(
            out=part_out_d[tt_ * 128:(tt_ + 1) * 128, :], in_=ctile[cb_][:]))(tt, cb),
            deps=ev_o, inc=st_sems[cb], inc_n=16)
        prev_ct[cb] = e_st
    ctx.reset(m0)


def emit_resln_phase(ctx, S, rs_out_d, resid_src, out_d, x_dep=None):
    sch = ctx.sch
    m0 = ctx.mark()
    a_ld, r_ld, cast, vs, ss, st_sems = (S["x_ld"], S["r_ld"], S["cast"],
                                         S["vs"], S["ss"], S["st"])
    w_bf = [ctx.tile(D, BF16) for _ in range(2)]
    r_f32 = [ctx.tile(D, F32) for _ in range(2)]
    y = [ctx.tile(D, F32) for _ in range(2)]
    stats = {nm: ctx.tile(1, F32) for nm in ("sum1", "sumsq", "mean", "var", "m2", "std", "rstd", "nmr")}
    stats["sq_scratch"] = ctx.tile(D, F32)
    eps_t = ctx.consts["eps"]
    prev = {}
    prev_ln = None
    for t in range(8):
        buf = t % 2
        e_lw = sch.add("sync", (lambda tt, bb: lambda s: s.dma_start(
            out=w_bf[bb][:], in_=rs_out_d[tt * 128:(tt + 1) * 128, :]))(t, buf),
            deps=[prev.get(("w", buf)), x_dep], inc=a_ld[buf], inc_n=16)
        e_lr = sch.add("sync", (lambda tt, bb: lambda s: s.dma_start(
            out=r_f32[bb][:], in_=resid_src[tt * 128:(tt + 1) * 128, :]))(t, buf),
            deps=[prev.get(("r", buf))], inc=r_ld[buf], inc_n=16)
        e_y = sch.add("vector", (lambda bb: lambda v: v.tensor_tensor(
            y[bb][:], w_bf[bb][:], r_f32[bb][:], ALU.add))(buf),
            deps=[e_lw, e_lr, prev_ln, prev.get(("st", buf))], inc=S["vs"])
        e_out = emit_layernorm(sch, y[buf], y[buf], stats, eps_t, D,
                               deps=[e_y, ctx.consts["evs"][4]], v_sem=S["vs"], s_sem=S["ss"])
        prev_ln = e_out
        e_st = sch.add("gpsimd", (lambda tt, bb: lambda g: g.dma_start(
            out=out_d[tt * 128:(tt + 1) * 128, :], in_=y[bb][:]))(t, buf),
            deps=[e_out], inc=st_sems[buf], inc_n=16)
        prev[("w", buf)] = e_y
        prev[("r", buf)] = e_y
        prev[("st", buf)] = e_st
    ctx.reset(m0)


def build_fused_program(dbg=False):
    nc = bass.Bass("TRN2", target_bir_lowering=False, debug=False)
    f32 = F32
    ap = {}
    ap["x_rows"] = nc.dram_tensor("x_rows", [TOK_CORE, D], f32, kind="ExternalInput").ap()
    for nm in ("wq0", "wk0", "wv0"):
        ap[nm] = nc.dram_tensor(nm, [D, 1024], f32, kind="ExternalInput").ap()
    ap["wo0"] = nc.dram_tensor("wo0", [1024, D], f32, kind="ExternalInput").ap()
    ap["gwT0"] = nc.dram_tensor("gwT0", [D, INTER0], f32, kind="ExternalInput").ap()
    ap["uwT0"] = nc.dram_tensor("uwT0", [D, INTER0], f32, kind="ExternalInput").ap()
    ap["dwT0"] = nc.dram_tensor("dwT0", [INTER0, D], f32, kind="ExternalInput").ap()
    ap["gs0"] = nc.dram_tensor("gs0", [INTER0, 1], f32, kind="ExternalInput").ap()
    ap["us0"] = nc.dram_tensor("us0", [INTER0, 1], f32, kind="ExternalInput").ap()
    ap["ds_b"] = nc.dram_tensor("ds_b", [128, D], f32, kind="ExternalInput").ap()
    for nm in ("wq1", "wk1", "wv1"):
        ap[nm] = nc.dram_tensor(nm, [D, 1024], f32, kind="ExternalInput").ap()
    ap["wo1T"] = nc.dram_tensor("wo1T", [1024, D], f32, kind="ExternalInput").ap()
    ap["su_ug"] = nc.dram_tensor("su_ug", [D, 1], f32, kind="ExternalInput").ap()
    ap["gwT1"] = nc.dram_tensor("gwT1", [D, INTER1], f32, kind="ExternalInput").ap()
    ap["uwT1"] = nc.dram_tensor("uwT1", [D, INTER1], f32, kind="ExternalInput").ap()
    ap["gs1"] = nc.dram_tensor("gs1", [INTER1, 1], f32, kind="ExternalInput").ap()
    ap["us1"] = nc.dram_tensor("us1", [INTER1, 1], f32, kind="ExternalInput").ap()
    ap["su_dp"] = nc.dram_tensor("su_dp", [INTER1, 1], f32, kind="ExternalInput").ap()
    ap["wdT1"] = nc.dram_tensor("wdT1", [INTER1, D], f32, kind="ExternalInput").ap()
    ap["wdp_b"] = nc.dram_tensor("wdp_b", [128, D], f32, kind="ExternalInput").ap()
    ap["masks"] = nc.dram_tensor("masks", [4, 128, 512], f32, kind="ExternalInput").ap()
    ap["ident"] = nc.dram_tensor("ident", [128, 128], f32, kind="ExternalInput").ap()
    ap["h128"] = nc.dram_tensor("h128", [128, 128], f32, kind="ExternalInput").ap()
    ap["h2048"] = nc.dram_tensor("h2048", [D, D], BF16, kind="ExternalInput").ap()
    ap["rc"] = nc.dram_tensor("rc", [128, T_SEQ], f32, kind="ExternalInput").ap()
    ap["rs"] = nc.dram_tensor("rs", [128, T_SEQ], f32, kind="ExternalInput").ap()
    ap["lamv"] = nc.dram_tensor("lamv", [1, 512], f32, kind="ExternalInput").ap()
    ap["subln_b"] = nc.dram_tensor("subln_b", [128, 256], f32, kind="ExternalInput").ap()
    out_rows = nc.dram_tensor("out_rows", [TOK_CORE, D], f32, kind="ExternalOutput").ap()

    dr = lambda nm, shp, dt: nc.dram_tensor(nm, shp, dt)
    cc_x_in = [dr(f"cc_x_in{l}", [D, TOK_CORE], BF16) for l in range(2)]
    cc_x_out = [dr(f"cc_x_out{l}", [2 * D, TOK_CORE], BF16) for l in range(2)]
    cc_p_in = [dr(f"cc_p_in{l}", [T_SEQ, D], BF16) for l in range(2)]
    cc_p_out = [dr(f"cc_p_out{l}", [TOK_CORE, D], BF16) for l in range(2)]
    r1_d = dr("r1_d", [TOK_CORE, D], f32)
    r2_d = dr("r2_d", [TOK_CORE, D], f32)
    r3_d = dr("r3_d", [TOK_CORE, D], f32)
    xfm_d = dr("xfm_d", [D, TOK_CORE], BF16)
    t_d = dr("t_d", [D, TOK_CORE], BF16)
    h0_d = dr("h0_d", [INTER0, TOK_CORE], BF16)
    h2_d = dr("h2_d", [INTER1, TOK_CORE], BF16)
    pa_d = dr("pa_d", [TOK_CORE, D], f32)
    pb_d = dr("pb_d", [TOK_CORE, D], f32)

    sch = Sched()
    S = fsem_pool(sch)

    with contextlib.ExitStack() as stack:
        slab = stack.enter_context(nc.sbuf_tensor("slab", [128, SLAB_BF16_ELEMS], BF16))
        psum_f32 = [stack.enter_context(nc.psum_tensor(f"pb_{i}", [128, 512], F32)) for i in range(6)]
        psum_bf16 = [stack.enter_context(nc.psum_tensor(f"pbb_{i}", [128, 128], BF16)) for i in range(2)]
        ctx = Fctx(nc, sch, slab, [p[:] for p in psum_f32] and [p for p in psum_f32], psum_bf16)
        ctx.psum_f32 = [p[:] for p in psum_f32]
        ctx.psum_bf16 = [p[:] for p in psum_bf16]

        emit_const_phase(ctx, S, ap)
        base_mark = ctx.mark()
        sch.barrier()

        def bar():
            sch.barrier()

        emit_transpose_phase(ctx, S, ap["x_rows"], cc_x_in[0].ap(), TOK_CORE, D); bar()
        cc_ev = emit_collective(ctx, S, "AllGather", cc_x_in[0], cc_x_out[0])
        sch.barrier(exclude=[S["cc"]])
        afm_mark = ctx.mark()
        afm_tiles = [ctx.tile(T_SEQ, BF16) for _ in range(8)]
        emit_attn_phase(ctx, S, 0, cc_x_out[0].ap(), ap["wq0"], ap["wk0"], ap["wv0"], afm_tiles,
                        x_dep=cc_ev); bar()
        emit_wopart_phase(ctx, S, afm_tiles, ap["wo0"], cc_p_in[0].ap()); bar()
        ctx.reset(afm_mark)
        cc_ev = emit_collective(ctx, S, "ReduceScatter", cc_p_in[0], cc_p_out[0], op=ALU.add)
        sch.barrier(exclude=[S["cc"]])
        emit_resln_phase(ctx, S, cc_p_out[0].ap(), ap["x_rows"], r1_d.ap(), x_dep=cc_ev); bar()
        emit_transpose_phase(ctx, S, r1_d.ap(), xfm_d.ap(), TOK_CORE, D); bar()
        emit_ffn_up_phase(ctx, S, 0, xfm_d.ap(), ap["gwT0"], ap["uwT0"],
                          ap["gs0"], ap["us0"], None, h0_d.ap()); bar()
        emit_pmm_phase(ctx, S, 22,
                       a_src=lambda k: h0_d.ap()[k * 128:(k + 1) * 128, :],
                       w_src=lambda k: ap["dwT0"][k * 128:(k + 1) * 128, :],
                       out_dram=pa_d.ap()); bar()
        emit_pmm_phase(ctx, S, 22,
                       a_src=lambda k: h0_d.ap()[(22 + k) * 128:(23 + k) * 128, :],
                       w_src=lambda k: ap["dwT0"][(22 + k) * 128:(23 + k) * 128, :],
                       out_dram=r2_d.ap(), p_src=pa_d.ap(), resid_src=r1_d.ap(),
                       colscale_src=ap["ds_b"], scale64=False, finalize=True); bar()

        emit_transpose_phase(ctx, S, r2_d.ap(), cc_x_in[1].ap(), TOK_CORE, D); bar()
        cc_ev = emit_collective(ctx, S, "AllGather", cc_x_in[1], cc_x_out[1])
        sch.barrier(exclude=[S["cc"]])
        afm_mark = ctx.mark()
        afm_tiles = [ctx.tile(T_SEQ, BF16) for _ in range(8)]
        emit_attn_phase(ctx, S, 1, cc_x_out[1].ap(), ap["wq1"], ap["wk1"], ap["wv1"], afm_tiles,
                        x_dep=cc_ev, rc_ap=ap["rc"], rs_ap=ap["rs"]); bar()
        emit_wopart_phase(ctx, S, afm_tiles, ap["wo1T"], cc_p_in[1].ap()); bar()
        ctx.reset(afm_mark)
        cc_ev = emit_collective(ctx, S, "ReduceScatter", cc_p_in[1], cc_p_out[1], op=ALU.add)
        sch.barrier(exclude=[S["cc"]])
        emit_resln_phase(ctx, S, cc_p_out[1].ap(), r2_d.ap(), r3_d.ap(), x_dep=cc_ev); bar()
        emit_transpose_phase(ctx, S, r3_d.ap(), xfm_d.ap(), TOK_CORE, D); bar()
        emit_fwht_phase(ctx, S, xfm_d.ap(), ap["su_ug"], t_d.ap(), ap["h2048"]); bar()
        emit_ffn_up_phase(ctx, S, 1, t_d.ap(), ap["gwT1"], ap["uwT1"],
                          ap["gs1"], ap["us1"], ap["su_dp"], h2_d.ap()); bar()
        pcur, pnxt = pa_d, pb_d
        emit_pmm_phase(ctx, S, 16,
                       a_src=lambda k: h2_d.ap()[k * 128:(k + 1) * 128, :],
                       w_src=lambda k: ap["wdT1"][k * 128:(k + 1) * 128, :],
                       out_dram=pcur.ap()); bar()
        for seg in (1, 2):
            emit_pmm_phase(ctx, S, 16,
                           a_src=(lambda sg: lambda k: h2_d.ap()[(sg * 16 + k) * 128:(sg * 16 + k + 1) * 128, :])(seg),
                           w_src=(lambda sg: lambda k: ap["wdT1"][(sg * 16 + k) * 128:(sg * 16 + k + 1) * 128, :])(seg),
                           out_dram=pnxt.ap(), p_src=pcur.ap()); bar()
            pcur, pnxt = pnxt, pcur
        emit_pmm_phase(ctx, S, 16,
                       a_src=lambda k: h2_d.ap()[(48 + k) * 128:(49 + k) * 128, :],
                       w_src=lambda k: ap["wdT1"][(48 + k) * 128:(49 + k) * 128, :],
                       out_dram=out_rows, p_src=pcur.ap(), resid_src=r3_d.ap(),
                       colscale_src=ap["wdp_b"], scale64=True, finalize=True)

        if dbg:
            sch.barrier()
            dbg_outs = {
                "dbg_xfm0": (cc_x_in[0], [D, TOK_CORE], BF16),
                "dbg_xag0": (cc_x_out[0], [2 * D, TOK_CORE], BF16),
                "dbg_rs0": (cc_p_out[0], [TOK_CORE, D], BF16),
                "dbg_r1": (r1_d, [TOK_CORE, D], f32),
                "dbg_h0": (h0_d, [INTER0, TOK_CORE], BF16),
                "dbg_r2": (r2_d, [TOK_CORE, D], f32),
                "dbg_r3": (r3_d, [TOK_CORE, D], f32),
                "dbg_t": (t_d, [D, TOK_CORE], BF16),
                "dbg_h2": (h2_d, [INTER1, TOK_CORE], BF16),
            }
            for i, (nm, (src_t, shp, dt_)) in enumerate(dbg_outs.items()):
                o = nc.dram_tensor(nm, shp, dt_, kind="ExternalOutput").ap()
                sch.add("sync", (lambda o_, s_: lambda s: s.dma_start(out=o_, in_=s_.ap()))(o, src_t),
                        inc=S["cc"], inc_n=16)
        sch.add("gpsimd", lambda g: None,
                deps=[(s, s.count) for s in S["st"]] + [(S["cc"], S["cc"].count)])
        sch.emit(nc)
    return nc


def kernel_fused(**inputs):
    f32 = np.float32
    g = {k: np.asarray(v, f32) for k, v in inputs.items()}
    x = g["L0_attn_x"]
    x_flat = x.reshape(TOK_GLOBAL, D)
    masks = host_masks()
    ident = np.eye(128, dtype=f32)
    H128 = host_hadamard128()
    import ml_dtypes
    H2 = np.array([[1.0]], dtype=np.float64)
    while H2.shape[0] < D:
        H2 = np.block([[H2, H2], [H2, -H2]])
    H2048 = np.ascontiguousarray(H2.astype(ml_dtypes.bfloat16))
    cosT, sinT = host_rope_tables()
    lamv = np.concatenate([g["L0_attn_lambda_q1"], g["L0_attn_lambda_k1"],
                           g["L0_attn_lambda_q2"], g["L0_attn_lambda_k2"]])[None, :]
    subln_b = _ct(np.broadcast_to(g["L0_attn_subln_weight"], (128, 256)))
    ds_b = _ct(np.broadcast_to(g["L0_ffn_down_s"], (128, D)))
    wdp_b = _ct(np.broadcast_to(g["L1_ffn_Wscale_dp"], (128, D)))
    gwT0 = _ct(g["L0_ffn_gate_w"].T); uwT0 = _ct(g["L0_ffn_up_w"].T)
    dwT0 = _ct(g["L0_ffn_down_w"].T)
    gwT1 = _ct(g["L1_ffn_W_gate"].T); uwT1 = _ct(g["L1_ffn_W_up"].T)
    wdT1 = _ct(g["L1_ffn_W_down"].T)
    wo1T = _ct(g["L1_attn_wo"].T)
    wsc = g["L1_ffn_Wscale_ug"]

    import os
    dbg = bool(os.environ.get("FUSED_DBG"))
    nc = _prog("fused_dbg" if dbg else "fused", build_fused_program, dbg)
    in_maps = []
    for c in range(N_CORES):
        b, s = c // 2, c % 2
        hs = slice(s * 1024, (s + 1) * 1024)
        in_maps.append({
            "x_rows": _ct(x_flat[c * TOK_CORE:(c + 1) * TOK_CORE]),
            "wq0": _ct(g["L0_attn_Wq"][:, hs]), "wk0": _ct(g["L0_attn_Wk"][:, hs]),
            "wv0": _ct(g["L0_attn_Wv"][:, hs]), "wo0": _ct(g["L0_attn_Wo"][hs, :]),
            "gwT0": gwT0, "uwT0": uwT0, "dwT0": dwT0,
            "gs0": _ct(g["L0_ffn_gate_s"][:, None]), "us0": _ct(g["L0_ffn_up_s"][:, None]),
            "ds_b": ds_b,
            "wq1": _ct(g["L1_attn_wq"][hs, :].T), "wk1": _ct(g["L1_attn_wk"][hs, :].T),
            "wv1": _ct(g["L1_attn_wv"][hs, :].T), "wo1T": _ct(wo1T[hs, :]),
            "su_ug": _ct(g["L1_ffn_SU_ug"][:, None]),
            "gwT1": gwT1, "uwT1": uwT1,
            "gs1": _ct(wsc[INTER1:][:, None]), "us1": _ct(wsc[:INTER1][:, None]),
            "su_dp": _ct(g["L1_ffn_SU_dp"][:, None]), "wdT1": wdT1, "wdp_b": wdp_b,
            "masks": masks, "ident": ident, "h128": H128, "h2048": H2048, "rc": cosT, "rs": sinT,
            "lamv": _ct(lamv), "subln_b": subln_b,
        })
    res = _run(nc, in_maps)
    if dbg:
        kernel_fused.dbg_results = res.results
    out = np.concatenate([res.results[c]["out_rows"] for c in range(N_CORES)], axis=0)
    return out.reshape(B_SZ, T_SEQ, D)


def kernel(**inputs):
    return kernel_fused(**inputs)
